# version 01
# baseline (speedup 1.0000x reference)
import contextlib
import math

import numpy as np
import ml_dtypes

import concourse.bass as bass
import concourse.mybir as mybir
from concourse.bass_utils import run_bass_kernel_spmd

F32 = mybir.dt.float32
BF16 = mybir.dt.bfloat16
AF = mybir.ActivationFunctionType
ALU = mybir.AluOpType
AX = mybir.AxisListType

NCORE = 8
D = 2048
SEQ = 8192
T = SEQ // NCORE
HW = 512
CH = D // 128
INTER = 8192
ICH = INTER // 128
NH, DK, DV = 16, 128, 128
CHUNK = 128
NCHK = T // CHUNK
SUB = 16
NSUB = CHUNK // SUB
EPS = 1e-5
DENOM = math.sqrt(2048.0) / 8192.0
LN_C = math.log(DK ** -0.5)



class Prog:

    def __init__(self):
        self.nc = bass.Bass("TRN2", target_bir_lowering=False,
                            detect_race_conditions=False)
        self.ctx = contextlib.ExitStack()
        self.hnd = {}
        self.cnt = {}
        self.block = None
        self.haz = {k: {"w": [], "r": [], "last": None}
                    for k in ("vector", "scalar", "gpsimd")}
        for s in ("gd", "mm", "ev", "ac", "st", "vsy", "ssy", "gsy"):
            self.sem(s)

    def sem(self, name):
        if name not in self.hnd:
            self.hnd[name] = self.ctx.enter_context(self.nc.semaphore(name))
            self.cnt[name] = 0
        return self.hnd[name]

    def inc(self, inst, name, k=1):
        inst.then_inc(self.hnd[name], k)
        self.cnt[name] += k
        return self.cnt[name]

    def val(self, name):
        return self.cnt[name]

    def start_block(self):
        self.block = self.ctx.enter_context(self.nc.Block())

    def on(self, eng):
        def deco(fn):
            getattr(self.block, eng)(fn)
        return deco

    def snap(self):
        return dict(self.cnt)

    def wait(self, eng, snap):
        for name, v in snap.items():
            if v > 0:
                eng.wait_ge(self.hnd[name], v)

    def sb(self, name, shape, dt):
        return self.ctx.enter_context(self.nc.sbuf_tensor(name, list(shape), dt))

    def psb(self, name, shape, dt):
        return self.ctx.enter_context(self.nc.psum_tensor(name, list(shape), dt))

    def din(self, name, shape, dt):
        return self.nc.dram_tensor(name, list(shape), dt, kind="ExternalInput").ap()

    def dout(self, name, shape, dt):
        return self.nc.dram_tensor(name, list(shape), dt, kind="ExternalOutput").ap()

    def dtmp(self, name, shape, dt):
        return self.nc.dram_tensor(name, list(shape), dt).ap()


_AUTOSEM = {"vector": "vsy", "scalar": "ssy", "gpsimd": "gsy"}


def _region(a):
    try:
        name = a.tensor.name
        es = mybir.dt.size(a.dtype)
        apl = [(int(s), int(c)) for s, c in a.ap]
    except (AttributeError, TypeError):
        return None
    lo = int(a.offset) * es
    if len(apl) > 1:
        skip = max(range(len(apl)), key=lambda i: apl[i][0])
    else:
        skip = -1
    span = es
    for i, (s, c) in enumerate(apl):
        if i != skip:
            span += es * abs(s) * (c - 1)
    return (name, lo, lo + span)


def _overlap(r1, r2):
    return r1[0] == r2[0] and r1[1] < r2[2] and r2[1] < r1[2]


class Eng:

    def __init__(self, p, e, kind):
        self.p, self.e, self.kind = p, e, kind
        self.h = p.haz[kind]

    def _sync(self):
        h = self.h
        if h["last"] is not None:
            inst, sname, sval = h["last"]
            if sname is None:
                sname = _AUTOSEM[self.kind]
                sval = self.p.inc(inst, sname)
                h["last"] = (inst, sname, sval)
            self.e.wait_ge(self.p.hnd[sname], sval)
        h["w"].clear()
        h["r"].clear()

    def _op(self, call, wr, rd):
        h = self.h
        wn = [r for r in (_region(a) for a in wr) if r]
        rn = [r for r in (_region(a) for a in rd) if r]
        conflict = (
            any(_overlap(r, w) for r in rn for w in h["w"])
            or any(_overlap(w1, w2) for w1 in wn for w2 in h["w"])
            or any(_overlap(w, r) for w in wn for r in h["r"])
        )
        if conflict:
            self._sync()
        inst = call()
        h["w"].extend(wn)
        h["r"].extend(rn)
        h["last"] = (inst, None, None)
        return inst

    def mark(self, sem, k=1):
        h = self.h
        inst, sname, sval = h["last"]
        assert sname is None, "op already has an inc"
        n = self.p.inc(inst, sem, k)
        h["last"] = (inst, sem, n)
        return n

    def barrier(self):
        self._sync()

    def wait_ge(self, semh, v):
        self.e.wait_ge(semh, v)

    def wtok(self, t):
        if t is not None:
            self.e.wait_ge(self.p.hnd[t[0]], t[1])

    def wsnap(self, snap):
        self.p.wait(self.e, snap)

    def tt(self, out, a, b2, op):
        return self._op(lambda: self.e.tensor_tensor(out, a, b2, op), [out], [a, b2])

    def ts(self, out, a, s1, s2, op0, op1=None):
        rd = [a] + [s for s in (s1, s2) if hasattr(s, "tensor")]
        if op1 is None:
            return self._op(lambda: self.e.tensor_scalar(out, a, s1, s2, op0), [out], rd)
        return self._op(lambda: self.e.tensor_scalar(out, a, s1, s2, op0, op1), [out], rd)

    def cp(self, out, a):
        return self._op(lambda: self.e.tensor_copy(out, a), [out], [a])

    def red(self, out, a, op):
        return self._op(lambda: self.e.tensor_reduce(out, a, axis=AX.X, op=op), [out], [a])

    def mset(self, out, v):
        return self._op(lambda: self.e.memset(out, v), [out], [])

    def act(self, out, a, func, bias=None, scale=None, accum_out=None):
        kw = {}
        if bias is not None:
            kw["bias"] = bias
        if scale is not None:
            kw["scale"] = scale
        if accum_out is not None:
            kw["accum_out"] = accum_out
        rd = [a] + [x for x in (bias, scale) if hasattr(x, "tensor")]
        wr = [out] + ([accum_out] if accum_out is not None else [])
        return self._op(lambda: self.e.activation(out, a, func, **kw), wr, rd)


def chs(ch, w=T):
    return slice(ch * w, (ch + 1) * w)


def tok(ch, hh, w=T):
    return slice(ch * w + hh * HW, ch * w + (hh + 1) * HW)


def hsl(hh):
    return slice(hh * HW, (hh + 1) * HW)


def wait_tok(eng, p, t):
    if t is not None:
        eng.wait_ge(p.hnd[t[0]], t[1])


def make_shared(p):
    b = {}
    b["banks"] = [p.psb(f"pb{i}", [128, 512], F32) for i in range(8)]
    b["bank_free"] = [None] * 8
    b["ones_bf"] = p.sb("ones_bf", [128, 1], BF16)
    b["ones_row_f"] = p.sb("ones_row_f", [1, 128], F32)
    b["xbf"] = [p.sb("xbf0", [128, T], BF16)]
    b["xsqb"] = [p.sb("xsqb0", [128, T], BF16)]
    b["rows"] = p.sb("rows", [1, 3 * T], F32)
    b["mb"] = p.sb("mbb", [128, T], F32)
    b["rb"] = p.sb("rbb", [128, T], F32)
    b["aptmp"] = p.sb("aptmp", [128, HW], F32)
    b["strip"] = [p.sb(f"strip{i}", [128, 16 * 512], BF16) for i in range(2)]
    b["strip_free"] = [None, None]
    b["consts"] = p.sb("consts", [128, 4], F32)
    return b


def row(b, i):
    return b["rows"][:, i * T:(i + 1) * T]


def fill_consts(p, b):
    @p.on("vector")
    def _(ve):
        v = Eng(p, ve, "vector")
        v.mset(b["consts"][:, 0:1], EPS)
        v.mset(b["consts"][:, 1:2], 1.0)
        v.mset(b["consts"][:, 2:3], LN_C)
        v.mark("ev")


def load_consts(p, b, pairs):
    @p.on("sync")
    def _(sync):
        for dst, src in pairs:
            p.inc(sync.dma_start(out=dst, in_=src), "lx", 16)


def cast_weight(p, dst_bf, src_f32, pieces=4):
    rows_ = src_f32.shape[0]
    step = rows_ // pieces
    @p.on("gpsimd")
    def _(g):
        for i in range(pieces):
            p.inc(g.dma_start(out=dst_bf[i * step:(i + 1) * step, :],
                              in_=src_f32[i * step:(i + 1) * step, :]), "gd", 16)


def emit_ln(p, b, X, XLN, label):
    ps_sum = [b["banks"][0], b["banks"][1]]
    ps_sq = [b["banks"][2], b["banks"][3]]
    ps_mb = [b["banks"][4], b["banks"][5]]
    ps_rb = [b["banks"][6], b["banks"][7]]
    pre = p.snap()
    mm_by_ch = {}

    for ch in range(CH):
        cur = 0
        hold = {}

        @p.on("vector")
        def _(ve, ch=ch, cur=cur):
            v = Eng(p, ve, "vector")
            if ch == 0:
                v.wsnap(pre)
            if ch >= 1:
                v.wait_ge(p.hnd["mm"], mm_by_ch[ch - 1])
            v.cp(b["xbf"][cur][:, :], X[:, chs(ch)])
            v.tt(b["xsqb"][cur][:, :], X[:, chs(ch)], X[:, chs(ch)], ALU.mult)
            hold["v"] = v.mark("ev")

        @p.on("tensor")
        def _(t, ch=ch, cur=cur):
            if ch == 0:
                for i in range(8):
                    wait_tok(t, p, b["bank_free"][i])
            t.wait_ge(p.hnd["ev"], hold["v"])
            last = None
            for hh in (0, 1):
                t.matmul(ps_sum[hh][0:1, :], b["ones_bf"][:, :], b["xbf"][cur][:, hsl(hh)],
                         start=(ch == 0), stop=(ch == CH - 1))
                last = t.matmul(ps_sq[hh][0:1, :], b["ones_bf"][:, :], b["xsqb"][cur][:, hsl(hh)],
                                start=(ch == 0), stop=(ch == CH - 1))
            mm_by_ch[ch] = p.inc(last, "mm")

    mm_done = p.val("mm")
    hold2 = {}

    @p.on("vector")
    def _(ve):
        v = Eng(p, ve, "vector")
        v.wait_ge(p.hnd["mm"], mm_done)
        for hh in (0, 1):
            v.cp(row(b, 0)[:, hsl(hh)], ps_sum[hh][0:1, :])
            v.cp(row(b, 1)[:, hsl(hh)], ps_sq[hh][0:1, :])
        v.ts(row(b, 0), row(b, 0), 1.0 / D, None, ALU.mult)
        v.ts(row(b, 1), row(b, 1), 1.0 / D, None, ALU.mult)
        v.tt(row(b, 2), row(b, 0), row(b, 0), ALU.mult)
        v.tt(row(b, 1), row(b, 1), row(b, 2), ALU.subtract)
        hold2["v"] = v.mark("ev")

    hold3 = {}

    @p.on("scalar")
    def _(se):
        s = Eng(p, se, "scalar")
        s.wait_ge(p.hnd["ev"], hold2["v"])
        s.act(row(b, 2), row(b, 1), AF.Ln, bias=b["consts"][0:1, 0:1])
        s.act(row(b, 1), row(b, 2), AF.Exp, scale=-0.5)
        hold3["v"] = s.mark("ac")

    @p.on("tensor")
    def _(t):
        t.wait_ge(p.hnd["ac"], hold3["v"])
        last = None
        for hh in (0, 1):
            t.matmul(ps_mb[hh][:, :], b["ones_row_f"][:, :], row(b, 0)[0:1, hsl(hh)],
                     start=True, stop=True)
            last = t.matmul(ps_rb[hh][:, :], b["ones_row_f"][:, :], row(b, 1)[0:1, hsl(hh)],
                            start=True, stop=True)
        p.inc(last, "mm")
    mmb = p.val("mm")

    @p.on("vector")
    def _(ve):
        v = Eng(p, ve, "vector")
        v.wait_ge(p.hnd["mm"], mmb)
        for hh in (0, 1):
            v.cp(b["mb"][:, hsl(hh)], ps_mb[hh][:, :])
            v.cp(b["rb"][:, hsl(hh)], ps_rb[hh][:, :])
        for ch in range(CH):
            for hh in (0, 1):
                v.tt(b["aptmp"][:, :], X[:, tok(ch, hh)], b["mb"][:, hsl(hh)], ALU.subtract)
                v.tt(XLN[:, tok(ch, hh)], b["aptmp"][:, :], b["rb"][:, hsl(hh)], ALU.mult)
        v.mark("ev")
    evd = p.val("ev")
    for i in range(8):
        b["bank_free"][i] = ("ev", evd)
    return ("ev", evd)


def load_strip(p, b, sbuf_idx, w_dram, col0, k_chunks, kc0=0):
    prev = b["strip_free"][sbuf_idx]
    strip = b["strip"][sbuf_idx]
    semn = f"lw{sbuf_idx}"
    p.sem(semn)

    @p.on("sync")
    def _(sync):
        wait_tok(sync, p, prev)
        src = w_dram[kc0 * 128:(kc0 + k_chunks) * 128, col0:col0 + 512] \
            .rearrange("(k p) c -> p k c", p=128)
        dst = strip[:, 0:k_chunks * 512].rearrange("p (k c) -> p k c", k=k_chunks)
        p.inc(sync.dma_start(out=dst, in_=src), semn, 16)
    return (semn, p.val(semn)), strip


def proj_group(p, b, strip, ldv, src, k_chunks, start, stop, kc_shift=0):
    mmv = {}

    @p.on("tensor")
    def _(t):
        wait_tok(t, p, ldv)
        if start:
            for i in range(8):
                wait_tok(t, p, b["bank_free"][i])
        for oi in range(4):
            for hh in (0, 1):
                ps = b["banks"][oi * 2 + hh]
                last = None
                for kc in range(k_chunks):
                    last = t.matmul(
                        ps[:, :],
                        strip[:, kc * 512 + oi * 128: kc * 512 + (oi + 1) * 128],
                        src[:, (kc + kc_shift) * T + hh * HW:(kc + kc_shift) * T + (hh + 1) * HW],
                        start=(start and kc == 0), stop=(stop and kc == k_chunks - 1))
                mmv[(oi, hh)] = p.inc(last, "mm")
    return mmv



def build_l1():
    p = Prog()
    p.sem("lx")
    x_in = p.din("x", [D, T], F32)
    wkT = p.din("wkT", [D, D], F32)
    wvT = p.din("wvT", [D, D], F32)
    ones_bf_in = p.din("ones_bf_in", [128, 1], BF16)
    ones_row_f_in = p.din("ones_row_f_in", [1, 128], F32)

    gu_tap = [p.din(f"gu_tap{k}", [512, INTER], F32) for k in range(4)]
    gu_c = [p.din(f"gu_c{k}", [256, INTER], F32) for k in range(4)]
    dn_tap = [p.din(f"dn_tap{k}", [1024, D], F32) for k in range(4)]
    dn_c = [p.din(f"dn_c{k}", [1024, D], F32) for k in range(4)]

    out_ss = p.dout("slicesum", [128, CH], F32)
    out_kvt = p.dout("kvt", [128, CH * T], BF16)
    out_gu = p.dout("recon_gu", [512, INTER], BF16)
    out_dn = p.dout("recon_dn", [1024, D], BF16)

    wk_bf = p.dtmp("wk_bf", [D, D], BF16)
    wv_bf = p.dtmp("wv_bf", [D, D], BF16)

    X = p.sb("X", [128, CH * T], F32)
    XLN = p.sb("XLN", [128, CH * T], BF16)
    KVTILE = [p.sb(f"kvtile{i}", [128, T], BF16) for i in range(2)]
    TMPK = p.sb("TMPK", [128, 4 * T], F32)
    KVF = p.sb("KVF", [128, T], F32)
    CS2 = p.sb("CS2", [128, T], F32)
    SS = p.sb("SS", [128, CH], F32)
    RW = 256
    RSL = [p.sb(f"rsl{i}", [128, 8 * RW], F32) for i in range(2)]
    RT = [p.sb(f"rt{i}", [128, RW], F32) for i in range(4)]
    RO = [p.sb(f"ro{i}", [128, RW], BF16) for i in range(2)]

    p.start_block()
    b = make_shared(p)
    fill_consts(p, b)

    load_consts(p, b, [
        (b["ones_bf"][:, :], ones_bf_in[:, :]),
        (b["ones_row_f"][:, :], ones_row_f_in[:, :]),
    ])

    @p.on("sync")
    def _(sync):
        for ch in range(CH):
            p.inc(sync.dma_start(out=X[:, chs(ch)], in_=x_in[ch * 128:(ch + 1) * 128, :]), "lx", 16)

    cast_weight(p, wk_bf, wkT)
    cast_weight(p, wv_bf, wvT)
    cast_done = p.val("gd")

    emit_ln(p, b, X, XLN, "ln1")

    @p.on("sync")
    def _(sync):
        sync.wait_ge(p.hnd["gd"], cast_done)

    b["kvtile_free"] = [None, None]
    for g in range(4):
        ldk, stripk = load_strip(p, b, 0, wk_bf, g * 512, CH)
        mmk = proj_group(p, b, stripk, ldk, XLN, CH, start=True, stop=True)

        @p.on("vector")
        def _(ve, mmk=mmk):
            v = Eng(p, ve, "vector")
            for oi in range(4):
                for hh in (0, 1):
                    v.wait_ge(p.hnd["mm"], mmk[(oi, hh)])
                    v.cp(TMPK[:, tok(oi, hh)], b["banks"][oi * 2 + hh][:, :])
            v.mark("ev")
        for i in range(8):
            b["bank_free"][i] = ("ev", p.val("ev"))
        b["strip_free"][0] = ("mm", mmk[(3, 1)])

        ldv_, stripv = load_strip(p, b, 1, wv_bf, g * 512, CH)
        mmv = proj_group(p, b, stripv, ldv_, XLN, CH, start=True, stop=True)
        b["strip_free"][1] = ("mm", mmv[(3, 1)])

        for oi in range(4):
            ch = g * 4 + oi
            ktile = KVTILE[ch % 2]
            ktile_free = b["kvtile_free"][ch % 2]

            @p.on("vector")
            def _(ve, ch=ch, oi=oi, mmv=mmv, ktile=ktile, ktile_free=ktile_free):
                v = Eng(p, ve, "vector")
                for hh in (0, 1):
                    v.wait_ge(p.hnd["mm"], mmv[(oi, hh)])
                    v.tt(KVF[:, hsl(hh)], TMPK[:, tok(oi, hh)],
                         b["banks"][oi * 2 + hh][:, :], ALU.mult)
                v.red(SS[:, ch:ch + 1], KVF[:, :], ALU.add)
                v.wtok(ktile_free)
                bufs = [KVF, CS2]
                for si, s in enumerate([1, 2, 4, 8, 16, 32, 64, 128, 256, 512]):
                    a = bufs[si % 2]
                    o = ktile if si == 9 else bufs[(si + 1) % 2]
                    v.tt(o[:, s:T], a[:, s:T], a[:, 0:T - s], ALU.add)
                    v.cp(o[:, 0:s], a[:, 0:s])
                v.mark("ev")
            b["bank_free"][oi * 2] = ("ev", p.val("ev"))
            b["bank_free"][oi * 2 + 1] = ("ev", p.val("ev"))
            evk = p.val("ev")

            semk = f"skv{ch % 2}"
            p.sem(semk)

            @p.on("gpsimd")
            def _(gq, ch=ch, ktile=ktile, evk=evk, semk=semk):
                gq.wait_ge(p.hnd["ev"], evk)
                p.inc(gq.dma_start(out=out_kvt[:, chs(ch)], in_=ktile[:, :]), semk, 16)
            b["kvtile_free"][ch % 2] = (semk, p.val(semk))

    tiles = []
    for rc in range(4):
        for sg in range(INTER // RW):
            tiles.append(("gu", rc, sg))
    for rc in range(8):
        for sg in range(D // RW):
            tiles.append(("dn", rc, sg))

    rsl_free = [None, None]
    ro_free = [None, None]
    for ti, (kind, rc, sg) in enumerate(tiles):
        cur = ti % 2
        semr = f"lr{cur}"
        p.sem(semr)

        @p.on("sync")
        def _(sync, kind=kind, rc=rc, sg=sg, cur=cur, semr=semr):
            wait_tok(sync, p, rsl_free[cur])
            for k in range(4):
                if kind == "gu":
                    tapsrc = gu_tap[k][rc * 128:(rc + 1) * 128, sg * RW:(sg + 1) * RW]
                    csrc = gu_c[k][(rc % 2) * 128:(rc % 2 + 1) * 128, sg * RW:(sg + 1) * RW]
                else:
                    tapsrc = dn_tap[k][rc * 128:(rc + 1) * 128, sg * RW:(sg + 1) * RW]
                    csrc = dn_c[k][rc * 128:(rc + 1) * 128, sg * RW:(sg + 1) * RW]
                p.inc(sync.dma_start(out=RSL[cur][:, k * RW:(k + 1) * RW], in_=tapsrc), semr, 16)
                p.inc(sync.dma_start(out=RSL[cur][:, (4 + k) * RW:(5 + k) * RW], in_=csrc), semr, 16)
        ldr = p.val(semr)

        @p.on("vector")
        def _(ve, cur=cur, ldr=ldr, semr=semr):
            v = Eng(p, ve, "vector")
            v.wait_ge(p.hnd[semr], ldr)
            v.wtok(ro_free[cur])
            sl = RSL[cur]
            v.tt(RT[0][:, :], sl[:, 0 * RW:1 * RW], sl[:, 4 * RW:5 * RW], ALU.mult)
            v.tt(RT[1][:, :], sl[:, 1 * RW:2 * RW], sl[:, 5 * RW:6 * RW], ALU.mult)
            v.tt(RT[0][:, :], RT[0][:, :], RT[1][:, :], ALU.add)
            v.tt(RT[2][:, :], sl[:, 2 * RW:3 * RW], sl[:, 6 * RW:7 * RW], ALU.mult)
            v.tt(RT[3][:, :], sl[:, 3 * RW:4 * RW], sl[:, 7 * RW:8 * RW], ALU.mult)
            v.tt(RT[2][:, :], RT[2][:, :], RT[3][:, :], ALU.add)
            v.tt(RO[cur][:, :], RT[0][:, :], RT[2][:, :], ALU.add)
            v.mark("ev")
        evr = p.val("ev")
        rsl_free[cur] = ("ev", evr)

        sems = f"sro{cur}"
        p.sem(sems)

        @p.on("gpsimd")
        def _(g, kind=kind, rc=rc, sg=sg, cur=cur, evr=evr, sems=sems):
            g.wait_ge(p.hnd["ev"], evr)
            dst = out_gu if kind == "gu" else out_dn
            p.inc(g.dma_start(out=dst[rc * 128:(rc + 1) * 128, sg * RW:(sg + 1) * RW],
                              in_=RO[cur][:, :]), sems, 16)
        ro_free[cur] = (sems, p.val(sems))

    fin = p.snap()

    @p.on("gpsimd")
    def _(g):
        p.wait(g, fin)
        p.inc(g.dma_start(out=out_ss[:, :], in_=SS[:, :]), "st", 16)
        g.wait_ge(p.hnd["st"], p.val("st"))

    return p



_GRIDS = {}


def _spline_host(gate_cp, up_cp, down_cp):
    n = INTER * D
    nct = gate_cp.shape[0]
    if "idx" not in _GRIDS:
        t = np.linspace(0.0, nct - 1.0, n, dtype=np.float64)
        i = np.clip(np.floor(t).astype(np.int64), 0, nct - 2)
        f = (t - i).astype(np.float32)
        idx = np.stack([np.clip(i + k, 0, nct - 1) for k in (-1, 0, 1, 2)], axis=0).astype(np.int32)
        f2 = f * f
        f3 = f2 * f
        c0 = 0.5 * (-f + 2.0 * f2 - f3)
        c1 = 0.5 * (2.0 - 5.0 * f2 + 3.0 * f3)
        c2 = 0.5 * (f + 4.0 * f2 - 3.0 * f3)
        c3 = 0.5 * (-f2 + f3)
        c = np.stack([c0, c1, c2, c3], 0).astype(np.float32)
        cguT = np.ascontiguousarray(c.reshape(4, INTER, D).transpose(0, 2, 1))
        cdnT = np.ascontiguousarray(c.reshape(4, D, INTER).transpose(0, 2, 1))
        _GRIDS.update(idx=idx, cguT=cguT, cdnT=cdnT)
    idx = _GRIDS["idx"]
    cguT = _GRIDS["cguT"]
    cdnT = _GRIDS["cdnT"]

    out = {"gu_tap": [[] for _ in range(4)], "gu_c": [[] for _ in range(4)],
           "dn_tap": [[] for _ in range(4)], "dn_c": [[] for _ in range(4)]}
    for k in range(4):
        g_t = np.ascontiguousarray(gate_cp[idx[k]].reshape(INTER, D).T)
        u_t = np.ascontiguousarray(up_cp[idx[k]].reshape(INTER, D).T)
        d_t = np.ascontiguousarray(down_cp[idx[k]].reshape(D, INTER).T)
        for ccore in range(NCORE):
            gr = slice(ccore * 256, (ccore + 1) * 256)
            dr = slice(ccore * 1024, (ccore + 1) * 1024)
            out["gu_tap"][k].append(np.ascontiguousarray(
                np.concatenate([g_t[gr], u_t[gr]], 0)))
            out["gu_c"][k].append(np.ascontiguousarray(cguT[k][gr]))
            out["dn_tap"][k].append(np.ascontiguousarray(d_t[dr]))
            out["dn_c"][k].append(np.ascontiguousarray(cdnT[k][dr]))
    return out


def _fm_slices(x):
    xs = x.reshape(SEQ, D)
    return [np.ascontiguousarray(xs[c * T:(c + 1) * T, :].T) for c in range(NCORE)]


def _consts():
    return {
        "ones_bf_in": np.ones((128, 1), dtype=ml_dtypes.bfloat16),
        "ones_row_f_in": np.ones((1, 128), dtype=np.float32),
    }


def run_l1(inputs):
    x = np.asarray(inputs["L0_attn_a_x"], dtype=np.float32)
    xs = _fm_slices(x)
    wkT = np.ascontiguousarray(np.asarray(inputs["L0_attn_a_Wk"], np.float32).T)
    wvT = np.ascontiguousarray(np.asarray(inputs["L0_attn_a_Wv"], np.float32).T)
    sp = _spline_host(np.asarray(inputs["L0_ffn_gate_cp"], np.float32),
                      np.asarray(inputs["L0_ffn_up_cp"], np.float32),
                      np.asarray(inputs["L0_ffn_down_cp"], np.float32))
    cst = _consts()
    in_maps = []
    for c in range(NCORE):
        m = dict(x=xs[c], wkT=wkT, wvT=wvT, **cst)
        for k in range(4):
            m[f"gu_tap{k}"] = sp["gu_tap"][k][c]
            m[f"gu_c{k}"] = sp["gu_c"][k][c]
            m[f"dn_tap{k}"] = sp["dn_tap"][k][c]
            m[f"dn_c{k}"] = sp["dn_c"][k][c]
        in_maps.append(m)
    prog = build_l1()
    res = run_bass_kernel_spmd(prog.nc, in_maps, core_ids=list(range(NCORE)))
    return res.results


def kernel(**inputs):
    raise NotImplementedError("L2/L3 not yet wired")



_L23_CACHE = {}

class Carver:

    def __init__(self, base_ap, total_bytes):
        self.base = base_ap
        self.total = total_bytes
        self.cur = 0

    def take(self, shape, dt):
        es = mybir.dt.size(dt)
        nb = int(np.prod(shape[1:])) * es if len(shape) > 1 else es
        nb = int(np.prod(shape[1:])) * es
        self.cur = (self.cur + 3) // 4 * 4
        c0 = self.cur // 2
        self.cur += nb
        assert self.cur <= self.total, f"carver overflow {self.cur} > {self.total}"
        view = self.base[:, c0:c0 + nb // 2]
        if dt != BF16:
            view = view.bitcast(dt)
        if len(shape) == 3:
            view = view.rearrange("p (a c) -> p a c", a=shape[1])
        return view

    def reset(self, to=0):
        self.cur = to



def emit_ffn(p, b, X, XLN, Hbuf, gate_w, up_w, dn_w, sc_x, out_dst, XR, XOT, FE4):
    @p.on("gpsimd")
    def _(g):
        p.wait(g, {"ev": p.val("ev")})
        for ch in range(CH):
            p.inc(g.dma_start(out=sc_x[:, chs(ch)], in_=X[:, chs(ch)]), "gd", 16)
    spill_done = ("gd", p.val("gd"))

    b["xr_free"] = [None, None]
    b["xot_free"] = [None, None]
    prev_silu_consumed = None
    for hh in (0, 1):
        for g in range(ICH // 4):
            ldg, stripg = load_strip(p, b, 0, gate_w, g * 512, CH)
            ldu, stripu = load_strip(p, b, 1, up_w, g * 512, CH)
            mmg = {}

            @p.on("tensor")
            def _(t, g=g, hh=hh, ldg=ldg, ldu=ldu, stripg=stripg, stripu=stripu):
                wait_tok(t, p, ldg)
                wait_tok(t, p, ldu)
                if g == 0 and hh == 0:
                    wait_tok(t, p, spill_done)
                for i in range(8):
                    wait_tok(t, p, b["bank_free"][i])
                for oi in range(4):
                    last = None
                    for kc in range(CH):
                        last = t.matmul(b["banks"][oi][:, :],
                                        stripg[:, kc * 512 + oi * 128: kc * 512 + (oi + 1) * 128],
                                        XLN[:, tok(kc, hh)],
                                        start=(kc == 0), stop=(kc == CH - 1))
                    mmg[("g", oi)] = p.inc(last, "mm")
                for oi in range(4):
                    last = None
                    for kc in range(CH):
                        last = t.matmul(b["banks"][4 + oi][:, :],
                                        stripu[:, kc * 512 + oi * 128: kc * 512 + (oi + 1) * 128],
                                        XLN[:, tok(kc, hh)],
                                        start=(kc == 0), stop=(kc == CH - 1))
                    mmg[("u", oi)] = p.inc(last, "mm")

            holds = {}

            @p.on("scalar")
            def _(se, mmg=mmg, prev=prev_silu_consumed):
                s = Eng(p, se, "scalar")
                if prev is not None:
                    s.wait_ge(p.hnd["ev"], prev)
                for oi in range(4):
                    s.wait_ge(p.hnd["mm"], mmg[("g", oi)])
                    s.act(FE4[:, oi * HW:(oi + 1) * HW], b["banks"][oi][:, :], AF.Silu)
                    holds[oi] = s.mark("ac")

            @p.on("vector")
            def _(ve, g=g, hh=hh, mmg=mmg):
                v = Eng(p, ve, "vector")
                for oi in range(4):
                    ic = g * 4 + oi
                    v.wait_ge(p.hnd["ac"], holds[oi])
                    v.wait_ge(p.hnd["mm"], mmg[("u", oi)])
                    v.tt(Hbuf[:, ic * HW:(ic + 1) * HW],
                         FE4[:, oi * HW:(oi + 1) * HW], b["banks"][4 + oi][:, :], ALU.mult)
                v.mark("ev")
            evg = p.val("ev")
            prev_silu_consumed = evg
            for i in range(8):
                b["bank_free"][i] = ("ev", evg)
            b["strip_free"][0] = ("mm", mmg[("g", 3)])
            b["strip_free"][1] = ("mm", mmg[("u", 3)])

        for ocg in range(4):
            mmd = {}
            for piece in range(4):
                sbf = piece % 2
                ldd, stripd = load_strip(p, b, sbf, dn_w, ocg * 512, CH, kc0=piece * 16)

                @p.on("tensor")
                def _(t, piece=piece, ldd=ldd, stripd=stripd, ocg=ocg, hh=hh):
                    wait_tok(t, p, ldd)
                    if piece == 0:
                        for i in range(4):
                            wait_tok(t, p, b["bank_free"][i])
                    last = None
                    for oi in range(4):
                        for kc in range(CH):
                            kabs = piece * 16 + kc
                            last = t.matmul(
                                b["banks"][oi][:, :],
                                stripd[:, kc * 512 + oi * 128: kc * 512 + (oi + 1) * 128],
                                Hbuf[:, kabs * HW:(kabs + 1) * HW],
                                start=(kabs == 0), stop=(kabs == ICH - 1),
                                skip_group_check=True)
                        if piece == 3:
                            mmd[oi] = p.inc(last, "mm")
                            last = None
                    if piece != 3:
                        p.inc(last, "mm")
                b["strip_free"][sbf] = ("mm", p.val("mm"))

            for oi in range(4):
                oc = ocg * 4 + oi
                sl = oc % 2

                @p.on("sync")
                def _(sync, oc=oc, hh=hh, sl=sl):
                    wait_tok(sync, p, b["xr_free"][sl])
                    p.inc(sync.dma_start(out=XR[sl][:, :], in_=sc_x[:, tok(oc, hh)]), f"lxr{sl}", 16)
                lxr = p.val(f"lxr{sl}")
                xot_free = b["xot_free"][sl]

                @p.on("vector")
                def _(ve, oi=oi, sl=sl, lxr=lxr, mmd=mmd, xot_free=xot_free):
                    v = Eng(p, ve, "vector")
                    v.wait_ge(p.hnd["mm"], mmd[oi])
                    v.wait_ge(p.hnd[f"lxr{sl}"], lxr)
                    v.wtok(xot_free)
                    v.tt(XOT[sl][:, :], b["banks"][oi][:, :], XR[sl][:, :], ALU.add)
                    v.mark("ev")
                evd = p.val("ev")
                b["xr_free"][sl] = ("ev", evd)
                b["bank_free"][oi] = ("ev", evd)

                semx = "sx3a" if sl == 0 else "sx3b"

                @p.on("gpsimd")
                def _(gq, oc=oc, hh=hh, sl=sl, evd=evd, semx=semx):
                    gq.wait_ge(p.hnd["ev"], evd)
                    p.inc(gq.dma_start(out=out_dst[:, tok(oc, hh)], in_=XOT[sl][:, :]), semx, 16)
                b["xot_free"][sl] = (semx, p.val(semx))



def build_l2():
    p = Prog()
    for s in ("lx", "lkv0", "lkv1", "lxr0", "lxr1", "sx3a", "sx3b", "lx3", "lssg",
              "sqs0", "sqs1", "sk0", "sk1", "sgp0", "sgp1", "svv0", "svv1",
              "lgp0", "lgp1", "lkk0", "lkk1", "lvv0", "lvv1", "lqq0", "lqq1",
              "sqt0", "sqt1", "soc0", "soc1"):
        p.sem(s)
    x_in = p.din("x", [D, T], F32)
    wqaT = p.din("wqaT", [D, D], F32)
    woaT = p.din("woaT", [D, D], F32)
    wqb_in = p.din("wqb", [D, D], F32)
    wfb_in = p.din("wfb", [D, D], F32)
    wib_in = p.din("wib", [D, D], F32)
    kvt_in = p.din("kvt", [128, CH * T], BF16)
    ssg_in = p.din("ssg", [8, D], F32)
    mpfx_in = p.din("mpfx", [8, 1], F32)
    gate_w = p.din("gate_w", [D, INTER], BF16)
    up_w = p.din("up_w", [D, INTER], BF16)
    dn_w = p.din("dn_w", [INTER, D], BF16)
    ones_bf_in = p.din("ones_bf_in", [128, 1], BF16)
    ones_row_f_in = p.din("ones_row_f_in", [1, 128], F32)
    ident_in = p.din("ident_bf", [128, 128], BF16)
    maskf_in = p.din("maskf", [128, 128], F32)

    out_x3 = p.dout("x3", [128, CH * T], F32)
    out_op = p.dout("opart", [128, CH * T], BF16)
    out_qt = p.dout("qtil", [128, NCHK * D], BF16)
    out_st = p.dout("sstate", [128, NH * 128], F32)
    out_ld = p.dout("logd", [128, NH], F32)
    out_gm = p.dout("gamlog", [128, NH * NCHK], F32)

    wqa_bf = p.dtmp("wqa_bf", [D, D], BF16)
    woa_bf = p.dtmp("woa_bf", [D, D], BF16)
    wqb_bf = p.dtmp("wqb_bf", [D, D], BF16)
    wfb_bf = p.dtmp("wfb_bf", [D, D], BF16)
    wib_bf = p.dtmp("wib_bf", [D, D], BF16)
    sc_x = p.dtmp("sc_x", [128, CH * T], F32)
    sc_qs = p.dtmp("sc_qs", [128, CH * T], BF16)
    sc_k = p.dtmp("sc_k", [128, CH * T], BF16)
    sc_v = p.dtmp("sc_v", [128, CH * T], BF16)
    sc_gp = p.dtmp("sc_gp", [128, CH * T], F32)

    ARE = p.sb("ARE", [128, 49152], BF16)
    X = ARE[:, 0:32768].bitcast(F32)
    XLN = ARE[:, 32768:49152]
    Hbuf = ARE[:, 0:32768]
    SCR = p.sb("SCR", [128, 16384], BF16)
    VHAT = SCR[:, :]
    OFF = p.sb("OFF", [128, CH], F32)
    MPFX = p.sb("MPFX", [8, 1], F32)
    IDE = p.sb("IDE", [128, 128], BF16)
    MASKF = p.sb("MASKF", [128, 128], F32)
    KVR = [p.sb(f"kvr{i}", [128, HW], BF16) for i in range(2)]
    XR = [p.sb(f"xr{i}", [128, HW], F32) for i in range(2)]
    XOT = [p.sb(f"xot{i}", [128, HW], F32) for i in range(2)]
    FE4 = p.sb("FE4", [128, 4 * HW], F32)

    p.start_block()
    b = make_shared(p)
    fill_consts(p, b)

    load_consts(p, b, [
        (b["ones_bf"][:, :], ones_bf_in[:, :]),
        (b["ones_row_f"][:, :], ones_row_f_in[:, :]),
        (IDE[:, :], ident_in[:, :]),
        (MASKF[:, :], maskf_in[:, :]),
        (MPFX[:, :], mpfx_in[:, :]),
    ])

    @p.on("sync")
    def _(sync):
        for ch in range(CH):
            p.inc(sync.dma_start(out=X[:, chs(ch)], in_=x_in[ch * 128:(ch + 1) * 128, :]), "lx", 16)

    for dst, srcw in ((wqa_bf, wqaT), (woa_bf, woaT), (wqb_bf, wqb_in),
                      (wfb_bf, wfb_in), (wib_bf, wib_in)):
        cast_weight(p, dst, srcw)
    cast_done = p.val("gd")

    emit_ln(p, b, X, XLN, "ln1")

    SSG = SCR[:, 0:4096].bitcast(F32)[0:8, 0:2048]

    @p.on("sync")
    def _(sync):
        p.inc(sync.dma_start(out=SSG, in_=ssg_in[:, :]), "lssg", 16)
    ldssg = p.val("lssg")

    @p.on("tensor")
    def _(t):
        t.wait_ge(p.hnd["lssg"], ldssg)
        for i in range(8):
            wait_tok(t, p, b["bank_free"][i])
        last = None
        for ch in range(CH):
            last = t.matmul(b["banks"][0][:, ch:ch + 1],
                            SSG[0:8, ch * 128:(ch + 1) * 128], MPFX[0:8, 0:1],
                            start=True, stop=True)
        p.inc(last, "mm")
    mmoff = p.val("mm")

    @p.on("vector")
    def _(ve):
        v = Eng(p, ve, "vector")
        v.wait_ge(p.hnd["mm"], mmoff)
        v.cp(OFF[:, :], b["banks"][0][:, 0:CH])
        v.mark("ev")
    b["bank_free"][0] = ("ev", p.val("ev"))

    @p.on("sync")
    def _(sync):
        sync.wait_ge(p.hnd["gd"], cast_done)

    kvr_free = [None, None]
    for g in range(4):
        ldq, stripq = load_strip(p, b, 0, wqa_bf, g * 512, CH)
        mmq = proj_group(p, b, stripq, ldq, XLN, CH, start=True, stop=True)
        b["strip_free"][0] = ("mm", mmq[(3, 1)])

        for oi in range(4):
            ch = g * 4 + oi
            for hh in (0, 1):
                sl = (ch * 2 + hh) % 2

                @p.on("sync")
                def _(sync, ch=ch, hh=hh, sl=sl):
                    wait_tok(sync, p, kvr_free[sl])
                    p.inc(sync.dma_start(out=KVR[sl][:, :], in_=kvt_in[:, tok(ch, hh)]),
                          f"lkv{sl}", 16)
                lkvv = p.val(f"lkv{sl}")

                @p.on("vector")
                def _(ve, ch=ch, oi=oi, hh=hh, sl=sl, mmq=mmq, lkvv=lkvv):
                    v = Eng(p, ve, "vector")
                    v.wait_ge(p.hnd[f"lkv{sl}"], lkvv)
                    v.wait_ge(p.hnd["mm"], mmq[(oi, hh)])
                    v.ts(b["aptmp"][:, :], KVR[sl][:, :], OFF[:, ch:ch + 1], DENOM,
                         ALU.add, ALU.mult)
                    v.tt(VHAT[:, tok(ch, hh)], b["banks"][oi * 2 + hh][:, :],
                         b["aptmp"][:, :], ALU.mult)
                    v.mark("ev")
                b["bank_free"][oi * 2 + hh] = ("ev", p.val("ev"))
                kvr_free[sl] = ("ev", p.val("ev"))

    for g in range(4):
        ldo, stripo = load_strip(p, b, 1, woa_bf, g * 512, CH)
        mmo = proj_group(p, b, stripo, ldo, VHAT, CH, start=True, stop=True)
        b["strip_free"][1] = ("mm", mmo[(3, 1)])

        @p.on("vector")
        def _(ve, g=g, mmo=mmo):
            v = Eng(p, ve, "vector")
            for oi in range(4):
                oc = g * 4 + oi
                for hh in (0, 1):
                    v.wait_ge(p.hnd["mm"], mmo[(oi, hh)])
                    v.tt(X[:, tok(oc, hh)], b["banks"][oi * 2 + hh][:, :],
                         X[:, tok(oc, hh)], ALU.add)
            v.mark("ev")
        for i in range(8):
            b["bank_free"][i] = ("ev", p.val("ev"))

    emit_ln(p, b, X, XLN, "ln2")

    emit_ffn(p, b, X, XLN, Hbuf, gate_w, up_w, dn_w, sc_x, out_x3, XR, XOT, FE4)

    fin_ffn = p.snap()

    @p.on("sync")
    def _(sync):
        p.wait(sync, fin_ffn)
        for ch in range(CH):
            p.inc(sync.dma_start(out=X[:, chs(ch)], in_=out_x3[:, chs(ch)]), "lx3", 16)

    emit_ln(p, b, X, XLN, "ln3")

    FEG = SCR[:, 0:8192].bitcast(F32)
    KSTG = SCR[:, 8192:12288]
    QSTG = SCR[:, 12288:13312]
    VSTG = SCR[:, 13312:14336]

    qst_free = [None, None]
    for g in range(4):
        ldq, stripq = load_strip(p, b, 0, wqb_bf, g * 512, CH)
        mmq = proj_group(p, b, stripq, ldq, XLN, CH, start=True, stop=True)
        b["strip_free"][0] = ("mm", mmq[(3, 1)])
        for oi in range(4):
            ch = g * 4 + oi
            for hh in (0, 1):
                sl = (ch * 2 + hh) % 2
                hold = {}

                @p.on("scalar")
                def _(se, oi=oi, hh=hh, sl=sl, mmq=mmq, hold=hold):
                    s = Eng(p, se, "scalar")
                    s.wait_ge(p.hnd["mm"], mmq[(oi, hh)])
                    s.wtok(qst_free[sl])
                    s.act(QSTG[:, sl * HW:(sl + 1) * HW], b["banks"][oi * 2 + hh][:, :], AF.Silu)
                    hold["v"] = s.mark("ac")
                b["bank_free"][oi * 2 + hh] = ("ac", hold["v"])
                semq = f"sqs{sl}"

                @p.on("gpsimd")
                def _(gq, ch=ch, hh=hh, sl=sl, hold=hold, semq=semq):
                    gq.wait_ge(p.hnd["ac"], hold["v"])
                    p.inc(gq.dma_start(out=sc_qs[:, tok(ch, hh)],
                                       in_=QSTG[:, sl * HW:(sl + 1) * HW]), semq, 16)
                qst_free[sl] = (semq, p.val(semq))

    kst_free = None
    gst_free = None
    for g in range(4):
        ldf, stripf = load_strip(p, b, 1, wfb_bf, g * 512, CH)
        mmf = proj_group(p, b, stripf, ldf, XLN, CH, start=True, stop=True)
        b["strip_free"][1] = ("mm", mmf[(3, 1)])
        holds = {}

        @p.on("scalar")
        def _(se, g=g, mmf=mmf, holds=holds, kst_free=kst_free, gst_free=gst_free):
            s = Eng(p, se, "scalar")
            s.wtok(gst_free)
            s.wtok(kst_free)
            for oi in range(4):
                for hh in (0, 1):
                    j = oi * 2 + hh
                    s.wait_ge(p.hnd["mm"], mmf[(oi, hh)])
                    s.act(FEG[:, j * HW:(j + 1) * HW], b["banks"][j][:, :], AF.Exp, scale=-1.0)
            for oi in range(4):
                for hh in (0, 1):
                    j = oi * 2 + hh
                    s.act(KSTG[:, j * HW:(j + 1) * HW], b["banks"][j][:, :],
                          AF.Sigmoid, scale=-1.0)
                    holds[("k", oi, hh)] = s.mark("ac")
            for oi in range(4):
                for hh in (0, 1):
                    j = oi * 2 + hh
                    s.act(FEG[:, j * HW:(j + 1) * HW], FEG[:, j * HW:(j + 1) * HW],
                          AF.Ln, bias=b["consts"][:, 1:2])
                    holds[("gp", oi, hh)] = s.mark("ac")

        for oi in range(4):
            for hh in (0, 1):
                ch = g * 4 + oi
                j = oi * 2 + hh
                b["bank_free"][j] = ("ac", holds[("k", oi, hh)])
                hk = holds[("k", oi, hh)]
                hg = holds[("gp", oi, hh)]
                semk = f"sk{j % 2}"
                semg = f"sgp{j % 2}"

                @p.on("gpsimd")
                def _(gq, ch=ch, hh=hh, j=j, hk=hk, hg=hg, semk=semk, semg=semg):
                    gq.wait_ge(p.hnd["ac"], hk)
                    p.inc(gq.dma_start(out=sc_k[:, tok(ch, hh)],
                                       in_=KSTG[:, j * HW:(j + 1) * HW]), semk, 16)
                    gq.wait_ge(p.hnd["ac"], hg)
                    p.inc(gq.dma_start(out=sc_gp[:, tok(ch, hh)],
                                       in_=FEG[:, j * HW:(j + 1) * HW]), semg, 16)
        kst_free = ("sk1", p.val("sk1"))
        gst_free = ("sgp1", p.val("sgp1"))
        kst_free2 = ("sk0", p.val("sk0"))
        gst_free2 = ("sgp0", p.val("sgp0"))

        @p.on("scalar")
        def _(se, kst_free2=kst_free2, gst_free2=gst_free2):
            se.wait_ge(p.hnd[kst_free2[0]], kst_free2[1])
            se.wait_ge(p.hnd[gst_free2[0]], gst_free2[1])

    vst_free = [None, None]
    for g in range(4):
        ldi, stripi = load_strip(p, b, 0, wib_bf, g * 512, CH)
        mmi = proj_group(p, b, stripi, ldi, XLN, CH, start=True, stop=True)
        b["strip_free"][0] = ("mm", mmi[(3, 1)])
        for oi in range(4):
            ch = g * 4 + oi
            for hh in (0, 1):
                sl = (ch * 2 + hh) % 2
                hold = {}

                @p.on("vector")
                def _(ve, oi=oi, hh=hh, sl=sl, mmi=mmi, hold=hold):
                    v = Eng(p, ve, "vector")
                    v.wait_ge(p.hnd["mm"], mmi[(oi, hh)])
                    v.wtok(vst_free[sl])
                    v.cp(VSTG[:, sl * HW:(sl + 1) * HW], b["banks"][oi * 2 + hh][:, :])
                    hold["v"] = v.mark("ev")
                b["bank_free"][oi * 2 + hh] = ("ev", hold["v"])
                semv = f"svv{sl}"

                @p.on("gpsimd")
                def _(gq, ch=ch, hh=hh, sl=sl, hold=hold, semv=semv):
                    gq.wait_ge(p.hnd["ev"], hold["v"])
                    p.inc(gq.dma_start(out=sc_v[:, tok(ch, hh)],
                                       in_=VSTG[:, sl * HW:(sl + 1) * HW]), semv, 16)
                vst_free[sl] = (semv, p.val(semv))

    cva = Carver(ARE[:, :], 98304)
    HST = cva.take([128, NH, 128], F32)
    GPA = cva.take([128, NH, 128], F32)
    GPB = cva.take([128, NH, 128], F32)
    EKT = cva.take([128, NH, 128], F32)
    SUBT = cva.take([128, NH, 128], F32)
    KHAT = cva.take([128, NH, 128], BF16)
    KTT = cva.take([128, NH, 128], BF16)
    VTT = cva.take([128, NH, 128], BF16)
    HBF = cva.take([128, NH, 128], BF16)
    QTC = cva.take([128, NH, 128], BF16)
    KTIL = cva.take([128, NH, 128], BF16)
    QIB = cva.take([128, NH, SUB], BF16)
    ABM = cva.take([128, NH, 128], BF16)
    OCH = cva.take([128, NH, 128], BF16)
    SUBQ = cva.take([128, NH, SUB], F32)
    EQB = cva.take([128, NH, SUB], F32)
    GCT = cva.take([128, NH], F32)
    NGC = cva.take([128, NH], F32)
    EDEC = cva.take([128, NH], F32)
    SGP = cva.take([128, NH], F32)
    GAMT = cva.take([128, NH, NCHK], F32)
    LOGD = cva.take([128, NH], F32)
    QR3 = [cva.take([128, NH, 128], BF16) for _ in range(2)]
    cvs = Carver(SCR[:, :], 32768)
    GPR = [cvs.take([128, NH, 128], F32) for _ in range(2)]
    KR3 = [cvs.take([128, NH, 128], BF16) for _ in range(2)]
    VR3 = [cvs.take([128, NH, 128], BF16) for _ in range(2)]

    glastart = p.snap()

    @p.on("vector")
    def _(ve):
        v = Eng(p, ve, "vector")
        v.wsnap(glastart)
        v.mset(HST[:, :, :], 0.0)
        v.mset(SGP[:, :], 0.0)
        v.mark("ev")
    gla0 = p.val("ev")

    ring_free = {k: [None, None] for k in ("gp", "kk", "vv", "qq")}
    scmap = {"gp": sc_gp, "kk": sc_k, "vv": sc_v, "qq": sc_qs}
    dstmap = {"gp": GPR, "kk": KR3, "vv": VR3, "qq": QR3}

    def load_chunk(m):
        sl = m % 2
        for key in ("gp", "kk", "vv", "qq"):
            @p.on("sync")
            def _(sync, key=key, m=m, sl=sl):
                if m <= 1:
                    p.wait(sync, {"ev": gla0})
                wait_tok(sync, p, ring_free[key][sl])
                src3 = scmap[key].rearrange("p (h t) -> p h t", t=T)[:, :, m * 128:(m + 1) * 128]
                p.inc(sync.dma_start(out=dstmap[key][sl][:, :, :], in_=src3), f"l{key}{sl}", 16)

    load_chunk(0)
    och_free = [None, None]
    qtc_free = None
    prev_gla_mm = 0
    for m in range(NCHK):
        sl = m % 2
        if m + 1 < NCHK:
            load_chunk(m + 1)
        ldv = {k: p.val(f"l{k}{sl}") for k in ("gp", "kk", "vv", "qq")}

        hold = {}

        @p.on("vector")
        def _(ve, m=m, sl=sl, ldv=ldv, prev_gla_mm=prev_gla_mm):
            v = Eng(p, ve, "vector")
            if prev_gla_mm:
                v.wait_ge(p.hnd["mm"], prev_gla_mm)
            for k in ("gp", "kk", "vv", "qq"):
                v.wait_ge(p.hnd[f"l{k}{sl}"], ldv[k])
            src = GPR[sl]
            pp = [GPA, GPB]
            for si, s in enumerate([1, 2, 4, 8, 16, 32, 64]):
                a = src if si == 0 else pp[(si + 1) % 2]
                o = pp[si % 2]
                v.tt(o[:, :, s:128], a[:, :, s:128], a[:, :, 0:128 - s], ALU.add)
                v.cp(o[:, :, 0:s], a[:, :, 0:s])
            v.cp(GCT[:, :].rearrange("p (h o) -> p h o", o=1), GPA[:, :, 127:128])
            v.ts(NGC[:, :], GCT[:, :], -1.0, None, ALU.mult)
            hold["v"] = v.mark("ev")
        evc = p.val("ev")
        gposf = GPA

        hold2 = {}

        @p.on("scalar")
        def _(se):
            s = Eng(p, se, "scalar")
            s.wait_ge(p.hnd["ev"], hold["v"])
            s.act(EDEC[:, :], GCT[:, :], AF.Exp, scale=-1.0)
            for h in range(NH):
                s.act(EKT[:, h, :], gposf[:, h, :], AF.Exp, bias=NGC[:, h:h + 1])
            hold2["v"] = s.mark("ac")

        hold3 = {}

        @p.on("vector")
        def _(ve, sl=sl):
            v = Eng(p, ve, "vector")
            v.wait_ge(p.hnd["ac"], hold2["v"])
            v.tt(KHAT[:, :, :], KR3[sl][:, :, :], EKT[:, :, :], ALU.mult)
            v.cp(HBF[:, :, :], HST[:, :, :])
            hold3["v"] = v.mark("ev")
        evkh = p.val("ev")

        ps_bf = [b["banks"][i][:, :].bitcast(BF16) for i in range(4)]

        @p.on("tensor")
        def _(t, sl=sl):
            t.wait_ge(p.hnd["ev"], evkh)
            for i in range(4):
                wait_tok(t, p, b["bank_free"][i])
            last = None
            for h in range(NH):
                last = t.transpose(ps_bf[h // 8][:, (h % 8) * 128:(h % 8 + 1) * 128],
                                   KHAT[:, h, :], IDE[:, :])
            for h in range(NH):
                last = t.transpose(ps_bf[2 + h // 8][:, (h % 8) * 128:(h % 8 + 1) * 128],
                                   VR3[sl][:, h, :], IDE[:, :])
            p.inc(last, "mm")
        mmtr = p.val("mm")

        @p.on("vector")
        def _(ve):
            v = Eng(p, ve, "vector")
            v.wait_ge(p.hnd["mm"], mmtr)
            v.cp(KTT[:, 0:8, :], ps_bf[0].rearrange("p (h t) -> p h t", h=8))
            v.cp(KTT[:, 8:16, :], ps_bf[1].rearrange("p (h t) -> p h t", h=8))
            v.cp(VTT[:, 0:8, :], ps_bf[2].rearrange("p (h t) -> p h t", h=8))
            v.cp(VTT[:, 8:16, :], ps_bf[3].rearrange("p (h t) -> p h t", h=8))
            v.mark("ev")
        evtt = p.val("ev")
        for i in range(4):
            b["bank_free"][i] = ("ev", evtt)

        @p.on("tensor")
        def _(t):
            t.wait_ge(p.hnd["ev"], evtt)
            for i in range(4, 8):
                wait_tok(t, p, b["bank_free"][i])
            last = None
            for h in range(NH):
                ps = b["banks"][4 + h // 4]
                last = t.matmul(ps[:, (h % 4) * 128:(h % 4 + 1) * 128],
                                KTT[:, h, :], VTT[:, h, :], start=True, stop=True,
                                skip_group_check=True)
            p.inc(last, "mm")
        mmkv = p.val("mm")

        @p.on("vector")
        def _(ve):
            v = Eng(p, ve, "vector")
            v.wait_ge(p.hnd["mm"], mmkv)
            for h in range(NH):
                ps = b["banks"][4 + h // 4]
                v.ts(SUBT[:, h, :], HST[:, h, :], EDEC[:, h:h + 1], None, ALU.mult)
                v.tt(HST[:, h, :], SUBT[:, h, :], ps[:, (h % 4) * 128:(h % 4 + 1) * 128], ALU.add)
            v.mark("ev")
        evst = p.val("ev")
        for i in range(4, 8):
            b["bank_free"][i] = ("ev", evst)

        hold6 = {}

        @p.on("scalar")
        def _(se):
            s = Eng(p, se, "scalar")
            s.wait_ge(p.hnd["ev"], evst)
            s.act(EKT[:, :, :], gposf[:, :, :], AF.Exp, scale=-1.0, bias=b["consts"][:, 2:3])
            hold6["v"] = s.mark("ac")

        @p.on("vector")
        def _(ve, sl=sl, qtc_free=qtc_free):
            v = Eng(p, ve, "vector")
            v.wait_ge(p.hnd["ac"], hold6["v"])
            v.wtok(qtc_free)
            v.tt(QTC[:, :, :], QR3[sl][:, :, :], EKT[:, :, :], ALU.mult)
            v.mark("ev")
        evqt = p.val("ev")
        semqt = f"sqt{m % 2}"

        @p.on("gpsimd")
        def _(gq, m=m, semqt=semqt):
            gq.wait_ge(p.hnd["ev"], evqt)
            p.inc(gq.dma_start(out=out_qt[:, m * D:(m + 1) * D],
                               in_=QTC[:, :, :].rearrange("p h t -> p (h t)")), semqt, 16)
        qtc_free = (semqt, p.val(semqt))

        prev_amm = {}
        for I in range(NSUB):
            endi = (I + 1) * SUB

            @p.on("vector")
            def _(ve, I=I, endi=endi):
                v = Eng(p, ve, "vector")
                if I > 0:
                    v.wait_ge(p.hnd["mm"], prev_amm[I - 1])
                else:
                    v.wait_ge(p.hnd["ev"], evqt)
                if I > 0:
                    gb = gposf[:, :, I * SUB - 1:I * SUB]
                    v.tt(SUBT[:, :, 0:endi], gposf[:, :, 0:endi],
                         gb.broadcast_to([128, NH, endi]), ALU.subtract)
                    v.tt(SUBQ[:, :, :], gb.broadcast_to([128, NH, SUB]),
                         gposf[:, :, I * SUB:(I + 1) * SUB], ALU.subtract)
                    v.mark("ev")
            evsub = p.val("ev")

            @p.on("scalar")
            def _(se, I=I, endi=endi):
                s = Eng(p, se, "scalar")
                s.wait_ge(p.hnd["ev"], evsub)
                if I == 0:
                    s.act(EKT[:, :, 0:endi], gposf[:, :, 0:endi], AF.Exp)
                    s.act(EQB[:, :, :], gposf[:, :, 0:SUB], AF.Exp, scale=-1.0,
                          bias=b["consts"][:, 2:3])
                else:
                    s.act(EKT[:, :, 0:endi], SUBT[:, :, 0:endi], AF.Exp)
                    s.act(EQB[:, :, :], SUBQ[:, :, :], AF.Exp, bias=b["consts"][:, 2:3])
                s.mark("ac")
            evek = p.val("ac")

            @p.on("vector")
            def _(ve, I=I, endi=endi, sl=sl):
                v = Eng(p, ve, "vector")
                v.wait_ge(p.hnd["ac"], evek)
                v.tt(KTIL[:, :, 0:endi], KR3[sl][:, :, 0:endi], EKT[:, :, 0:endi], ALU.mult)
                v.tt(QIB[:, :, :], QR3[sl][:, :, I * SUB:(I + 1) * SUB], EQB[:, :, :], ALU.mult)
                v.mark("ev")
            evqi = p.val("ev")

            @p.on("tensor")
            def _(t, I=I, endi=endi):
                t.wait_ge(p.hnd["ev"], evqi)
                if I == 0:
                    for i in (2, 3, 4, 5):
                        wait_tok(t, p, b["bank_free"][i])
                last = None
                for h in range(NH):
                    ab = b["banks"][2 + h // 4]
                    last = t.matmul(ab[0:endi, (h % 4) * 128 + I * SUB:(h % 4) * 128 + (I + 1) * SUB],
                                    KTIL[:, h, 0:endi], QIB[:, h, :],
                                    start=True, stop=True, skip_group_check=True)
                prev_amm[I] = p.inc(last, "mm")
        mmA = p.val("mm")

        @p.on("vector")
        def _(ve):
            v = Eng(p, ve, "vector")
            v.wait_ge(p.hnd["mm"], mmA)
            for h in range(NH):
                ab = b["banks"][2 + h // 4]
                v.tt(ABM[:, h, :], ab[:, (h % 4) * 128:(h % 4 + 1) * 128], MASKF[:, :], ALU.mult)
            v.mark("ev")
        evab = p.val("ev")
        for i in (2, 3, 4, 5):
            b["bank_free"][i] = ("ev", evab)


        oevs = {}
        for h in range(NH):
            ob = b["banks"][6 + h % 2]

            @p.on("tensor")
            def _(t, h=h, ob=ob, m=m):
                if h == 0:
                    t.wait_ge(p.hnd["ev"], evab)
                    for i in (6, 7):
                        wait_tok(t, p, b["bank_free"][i])
                if h >= 2:
                    t.wait_ge(p.hnd["ev"], oevs[h - 2])
                last = t.matmul(ob[:, 0:128], HBF[:, h, :], QTC[:, h, :],
                                start=True, stop=False, skip_group_check=True)
                for I in range(NSUB):
                    endi = (I + 1) * SUB
                    last = t.matmul(ob[:, I * SUB:(I + 1) * SUB],
                                    VTT[0:endi, h, :], ABM[0:endi, h, I * SUB:(I + 1) * SUB],
                                    start=False, stop=(I == NSUB - 1), skip_group_check=True)
                p.inc(last, "mm")
            mmoh = p.val("mm")

            @p.on("vector")
            def _(ve, h=h, ob=ob, mmoh=mmoh, m=m):
                v = Eng(p, ve, "vector")
                v.wait_ge(p.hnd["mm"], mmoh)
                if h <= 1:
                    v.wtok(och_free[m % 2])
                v.cp(OCH[:, h, :], ob[:, 0:128])
                oevs[h] = v.mark("ev")
        evo = p.val("ev")
        b["bank_free"][6] = ("ev", evo)
        b["bank_free"][7] = ("ev", evo)
        prev_gla_mm = p.val("mm")

        semoc = f"soc{m % 2}"

        @p.on("gpsimd")
        def _(gq, m=m, semoc=semoc):
            gq.wait_ge(p.hnd["ev"], evo)
            dst3 = out_op.rearrange("p (h t) -> p h t", t=T)[:, :, m * 128:(m + 1) * 128]
            p.inc(gq.dma_start(out=dst3, in_=OCH[:, :, :]), semoc, 16)
        och_free[m % 2] = (semoc, p.val(semoc))

        @p.on("vector")
        def _(ve, m=m):
            v = Eng(p, ve, "vector")
            v.wait_ge(p.hnd["ev"], evc)
            v.ts(GAMT[:, :, m:m + 1].rearrange("p h o -> p (h o)"), SGP[:, :], -1.0,
                 None, ALU.mult)
            v.tt(SGP[:, :], SGP[:, :], GCT[:, :], ALU.add)
            v.mark("ev")
        endchunk = p.val("ev")
        for k in ("gp", "kk", "vv", "qq"):
            ring_free[k][sl] = ("ev", endchunk)

    @p.on("vector")
    def _(ve):
        v = Eng(p, ve, "vector")
        v.ts(LOGD[:, :], SGP[:, :], -1.0, None, ALU.mult)
        v.mark("ev")

    fin = p.snap()

    @p.on("gpsimd")
    def _(g):
        p.wait(g, fin)
        p.inc(g.dma_start(out=out_st[:, :], in_=HST[:, :, :].rearrange("p h t -> p (h t)")), "st", 16)
        p.inc(g.dma_start(out=out_ld[:, :], in_=LOGD[:, :]), "st", 16)
        p.inc(g.dma_start(out=out_gm[:, :], in_=GAMT[:, :, :].rearrange("p h t -> p (h t)")), "st", 16)
        g.wait_ge(p.hnd["st"], p.val("st"))

    return p


def _l2_host(inputs, l1res):
    x = np.asarray(inputs["L0_attn_a_x"], dtype=np.float32)
    xs = _fm_slices(x)
    wqaT = np.ascontiguousarray(np.asarray(inputs["L0_attn_a_Wq"], np.float32).T)
    woaT = np.ascontiguousarray(np.asarray(inputs["L0_attn_a_Wo"], np.float32).T)
    wqb = np.ascontiguousarray(np.asarray(inputs["L0_attn_b_Wq"], np.float32))
    wfb = np.ascontiguousarray(np.asarray(inputs["L0_attn_b_Wf"], np.float32))
    wib = np.ascontiguousarray(np.asarray(inputs["L0_attn_b_Wi"], np.float32))
    ssg = np.stack([np.ascontiguousarray(l1res[j]["slicesum"]).T.reshape(D)
                    for j in range(NCORE)]).astype(np.float32)
    gate_full = np.ascontiguousarray(np.concatenate(
        [np.asarray(l1res[c]["recon_gu"])[:256] for c in range(NCORE)], 0))
    up_full = np.ascontiguousarray(np.concatenate(
        [np.asarray(l1res[c]["recon_gu"])[256:512] for c in range(NCORE)], 0))
    dn_full = np.ascontiguousarray(np.concatenate(
        [np.asarray(l1res[c]["recon_dn"]) for c in range(NCORE)], 0))
    _L23_CACHE.update(gate_full=gate_full, up_full=up_full, dn_full=dn_full)
    maskf = np.zeros((128, 128), np.float32)
    for tau in range(128):
        for t in range(128):
            if tau <= t:
                maskf[tau, t] = 1.0
    ident = np.eye(128, dtype=ml_dtypes.bfloat16)
    cst = _consts()
    in_maps = []
    for c in range(NCORE):
        mpfx = (np.arange(8) < c).astype(np.float32).reshape(8, 1)
        m = dict(x=xs[c], wqaT=wqaT, woaT=woaT, wqb=wqb, wfb=wfb, wib=wib,
                 kvt=np.ascontiguousarray(np.asarray(l1res[c]["kvt"])),
                 ssg=ssg, mpfx=mpfx, gate_w=gate_full, up_w=up_full, dn_w=dn_full,
                 ident_bf=ident, maskf=maskf, **cst)
        in_maps.append(m)
    return in_maps


def run_l2(inputs, l1res):
    in_maps = _l2_host(inputs, l1res)
    prog = build_l2()
    res = run_bass_kernel_spmd(prog.nc, in_maps, core_ids=list(range(NCORE)))
    return res.results



def build_l3():
    p = Prog()
    for s in ("lx", "lxr0", "lxr1", "sx3a", "sx3b", "lld", "lsst0", "lsst1",
              "lqt0", "lqt1", "lgmt", "lop"):
        p.sem(s)
    x3_in = p.din("x3", [128, CH * T], F32)
    op_in = p.din("opart", [128, CH * T], BF16)
    qt_in = p.din("qtil", [128, NCHK * D], BF16)
    sall_in = p.din("sall", [8 * 128, NH * 128], F32)
    ldall_in = p.din("ldall", [8, D], F32)
    wscanT_in = p.din("wscanT", [8, 8], F32)
    biarep_in = p.din("biarep", [128, 128], F32)
    gamlog_in = p.din("gamlog", [128, NH * NCHK], F32)
    gw_in = p.din("gw", [128, NH], F32)
    wob_in = p.din("wob", [D, D], F32)
    gate_w = p.din("gate_w", [D, INTER], BF16)
    up_w = p.din("up_w", [D, INTER], BF16)
    dn_w = p.din("dn_w", [INTER, D], BF16)
    ones_bf_in = p.din("ones_bf_in", [128, 1], BF16)
    ones_row_f_in = p.din("ones_row_f_in", [1, 128], F32)

    out_y = p.dout("y", [128, CH * T], F32)

    wob_bf = p.dtmp("wob_bf", [D, D], BF16)
    sc_x = p.dtmp("sc_x", [128, CH * T], F32)

    ARE = p.sb("ARE", [128, 49152], BF16)
    X = ARE[:, 0:32768].bitcast(F32)
    XLN = ARE[:, 32768:49152]
    Hbuf = ARE[:, 0:32768]
    SCR = p.sb("SCR", [128, 16384], BF16)
    OBUF = SCR[:, :]
    HIN = XLN[:, 0:4096].bitcast(F32)
    LDA = XLN[:, 4096:8192].bitcast(F32)[0:8, 0:2048]
    SSTR = [XLN[:, 8192 + i * 4096: 8192 + (i + 1) * 4096].bitcast(F32) for i in range(2)]
    QTR = [XLN[:, 8192 + i * 2048: 8192 + (i + 1) * 2048] for i in range(2)]
    XR = [p.sb(f"xr{i}", [128, HW], F32) for i in range(2)]
    XOT = [p.sb(f"xot{i}", [128, HW], F32) for i in range(2)]
    FE4 = p.sb("FE4", [128, 4 * HW], F32)
    WSC = p.sb("WSC", [8, 8], F32)
    BIA = p.sb("BIA", [128, 128], F32)
    GMT = p.sb("GMT", [128, 128], F32)
    GW = p.sb("GW", [128, NH], F32)
    COEF = p.sb("COEF", [128, 128], F32)
    HCB = p.sb("HCB", [128, 128], BF16)
    EG = p.sb("EG", [128, NH], F32)

    p.start_block()
    b = make_shared(p)
    fill_consts(p, b)

    load_consts(p, b, [
        (b["ones_bf"][:, :], ones_bf_in[:, :]),
        (b["ones_row_f"][:, :], ones_row_f_in[:, :]),
        (WSC[:, :], wscanT_in[:, :]),
        (BIA[:, :], biarep_in[:, :]),
        (GMT[:, :], gamlog_in[:, :]),
        (GW[:, :], gw_in[:, :]),
    ])

    @p.on("sync")
    def _(sync):
        for ch in range(CH):
            p.inc(sync.dma_start(out=X[:, chs(ch)], in_=x3_in[:, chs(ch)]), "lx", 16)
        p.inc(sync.dma_start(out=LDA, in_=ldall_in[:, :]), "lld", 16)

    @p.on("gpsimd")
    def _(g):
        p.inc(g.dma_start(out=OBUF[:, :], in_=op_in[:, :]), "lop", 16)
    cast_weight(p, wob_bf, wob_in)
    cast_done = p.val("gd")
    lop_done = p.val("lop")

    lld = p.val("lld")
    lxv = p.val("lx")

    @p.on("tensor")
    def _(t):
        t.wait_ge(p.hnd["lld"], lld)
        t.wait_ge(p.hnd["lx"], lxv)
        p.wait(t, {"ev": p.val("ev"), "ac": p.val("ac")})
        last = None
        for h in range(NH):
            last = t.matmul(b["banks"][0][:, h * 8:(h + 1) * 8],
                            LDA[0:8, h * 128:(h + 1) * 128], WSC[0:8, 0:8],
                            start=True, stop=True, skip_group_check=True)
        p.inc(last, "mm")
    mme = p.val("mm")

    hold = {}

    @p.on("vector")
    def _(ve):
        v = Eng(p, ve, "vector")
        v.wait_ge(p.hnd["mm"], mme)
        v.tt(COEF[:, :], b["banks"][0][:, 0:128], BIA[:, :], ALU.add)
        hold["v"] = v.mark("ev")
    b["bank_free"][0] = ("ev", p.val("ev"))

    @p.on("scalar")
    def _(se):
        s = Eng(p, se, "scalar")
        s.wait_ge(p.hnd["ev"], hold["v"])
        s.act(COEF[:, :], COEF[:, :], AF.Exp)
        s.mark("ac")
    ace = p.val("ac")

    sst_free = [None, None]
    for j in range(NCORE):
        sl = j % 2

        @p.on("sync")
        def _(sync, j=j, sl=sl):
            wait_tok(sync, p, sst_free[sl])
            p.inc(sync.dma_start(out=SSTR[sl][:, :],
                                 in_=sall_in[j * 128:(j + 1) * 128, :]), f"lsst{sl}", 16)
        lss = p.val(f"lsst{sl}")

        @p.on("vector")
        def _(ve, j=j, sl=sl, lss=lss):
            v = Eng(p, ve, "vector")
            v.wait_ge(p.hnd[f"lsst{sl}"], lss)
            if j == 0:
                v.wait_ge(p.hnd["ac"], ace)
            for h in range(NH):
                hsx = slice(h * 128, (h + 1) * 128)
                v.ts(b["aptmp"][:, 0:128], SSTR[sl][:, hsx], COEF[:, h * 8 + j:h * 8 + j + 1],
                     None, ALU.mult)
                if j == 0:
                    v.cp(HIN[:, hsx], b["aptmp"][:, 0:128])
                else:
                    v.tt(HIN[:, hsx], HIN[:, hsx], b["aptmp"][:, 0:128], ALU.add)
            v.mark("ev")
        sst_free[sl] = ("ev", p.val("ev"))
    hin_done = p.val("ev")

    qtr_free = [None, None]
    prev_corr_mm = 0
    for m in range(NCHK):
        sl = m % 2

        @p.on("sync")
        def _(sync, m=m, sl=sl):
            if m <= 1:
                p.wait(sync, {"ev": hin_done})
            wait_tok(sync, p, qtr_free[sl])
            p.inc(sync.dma_start(out=QTR[sl][:, :], in_=qt_in[:, m * D:(m + 1) * D]),
                  f"lqt{sl}", 16)
        lqt = p.val(f"lqt{sl}")

        holdg = {}

        @p.on("scalar")
        def _(se, m=m, prev_hc=prev_corr_mm):
            s = Eng(p, se, "scalar")
            if m == 0:
                s.wait_ge(p.hnd["ev"], hin_done)
            else:
                s.wait_ge(p.hnd["ev"], prev_hc)
            g3 = GMT[:, :].rearrange("p (h mm) -> p h mm", mm=NCHK)
            s.act(EG[:, :].rearrange("p (h o) -> p h o", o=1), g3[:, :, m:m + 1], AF.Exp)
            holdg["v"] = s.mark("ac")

        bank0 = (m % 2) * 4
        mmc = {}
        evcs = {}
        for h in range(NH):
            holdh = {}

            @p.on("vector")
            def _(ve, h=h, m=m, holdg=holdg):
                v = Eng(p, ve, "vector")
                if h == 0:
                    v.wait_ge(p.hnd["ac"], holdg["v"])
                    if m == 0:
                        v.wait_ge(p.hnd["lop"], lop_done)
                if h >= 1:
                    v.wait_ge(p.hnd["mm"], mmc[h - 1])
                v.ts(HCB[:, :], HIN[:, h * 128:(h + 1) * 128], EG[:, h:h + 1], None, ALU.mult)
                holdh["v"] = v.mark("ev")
            evh = p.val("ev")

            ps = b["banks"][bank0 + h // 4]

            @p.on("tensor")
            def _(t, h=h, sl=sl, evh=evh, ps=ps, lqt=lqt):
                t.wait_ge(p.hnd["ev"], evh)
                if h == 0:
                    t.wait_ge(p.hnd["lqt" + str(sl)], lqt)
                    for i in range(bank0, bank0 + 4):
                        wait_tok(t, p, b["bank_free"][i])
                last = t.matmul(ps[:, (h % 4) * 128:(h % 4 + 1) * 128],
                                HCB[:, :], QTR[sl][:, h * 128:(h + 1) * 128],
                                start=True, stop=True, skip_group_check=True)
                mmc[h] = p.inc(last, "mm")

            @p.on("vector")
            def _(ve, h=h, m=m, ps=ps):
                v = Eng(p, ve, "vector")
                v.wait_ge(p.hnd["mm"], mmc[h])
                osl = slice(h * T + m * 128, h * T + (m + 1) * 128)
                v.tt(OBUF[:, osl], OBUF[:, osl], ps[:, (h % 4) * 128:(h % 4 + 1) * 128], ALU.add)
                evcs[h] = v.mark("ev")
        evchunk = p.val("ev")
        for i in range(bank0, bank0 + 4):
            b["bank_free"][i] = ("ev", evchunk)
        qtr_free[sl] = ("ev", evchunk)
        prev_corr_mm = evchunk

    corr_done = p.val("ev")
    ps_sq = [b["banks"][0], b["banks"][1]]
    mm_by_h = {}
    for h in range(NH):
        holds = {}

        @p.on("vector")
        def _(ve, h=h):
            v = Eng(p, ve, "vector")
            if h == 0:
                v.wait_ge(p.hnd["ev"], corr_done)
            if h >= 1:
                v.wait_ge(p.hnd["mm"], mm_by_h[h - 1])
            v.tt(b["xbf"][0][:, :], OBUF[:, chs(h)], OBUF[:, chs(h)], ALU.mult)
            holds["v"] = v.mark("ev")

        @p.on("tensor")
        def _(t, h=h):
            if h == 0:
                for i in (0, 1):
                    wait_tok(t, p, b["bank_free"][i])
            t.wait_ge(p.hnd["ev"], holds["v"])
            last = None
            for hh in (0, 1):
                last = t.matmul(ps_sq[hh][0:1, :], b["ones_bf"][:, :],
                                b["xbf"][0][:, hsl(hh)],
                                start=(h == 0), stop=(h == NH - 1))
            mm_by_h[h] = p.inc(last, "mm")
    mm_sq = p.val("mm")

    hold2 = {}

    @p.on("vector")
    def _(ve):
        v = Eng(p, ve, "vector")
        v.wait_ge(p.hnd["mm"], mm_sq)
        for hh in (0, 1):
            v.cp(row(b, 0)[:, hsl(hh)], ps_sq[hh][0:1, :])
        v.ts(row(b, 0), row(b, 0), 1.0 / D, None, ALU.mult)
        hold2["v"] = v.mark("ev")

    hold3 = {}

    @p.on("scalar")
    def _(se):
        s = Eng(p, se, "scalar")
        s.wait_ge(p.hnd["ev"], hold2["v"])
        s.act(row(b, 2), row(b, 0), AF.Ln, bias=b["consts"][0:1, 0:1])
        s.act(row(b, 1), row(b, 2), AF.Exp, scale=-0.5)
        hold3["v"] = s.mark("ac")

    @p.on("tensor")
    def _(t):
        t.wait_ge(p.hnd["ac"], hold3["v"])
        for i in (2, 3):
            wait_tok(t, p, b["bank_free"][i])
        last = None
        for hh in (0, 1):
            last = t.matmul(b["banks"][2 + hh][:, :], b["ones_row_f"][:, :],
                            row(b, 1)[0:1, hsl(hh)], start=True, stop=True)
        p.inc(last, "mm")
    mmrb = p.val("mm")

    @p.on("vector")
    def _(ve):
        v = Eng(p, ve, "vector")
        v.wait_ge(p.hnd["mm"], mmrb)
        for hh in (0, 1):
            v.cp(b["rb"][:, hsl(hh)], b["banks"][2 + hh][:, :])
        for h in range(NH):
            for hh in (0, 1):
                v.ts(b["aptmp"][:, :], OBUF[:, tok(h, hh)], GW[:, h:h + 1], None, ALU.mult)
                v.tt(OBUF[:, tok(h, hh)], b["aptmp"][:, :], b["rb"][:, hsl(hh)], ALU.mult)
        v.mark("ev")
    nrm_done = p.val("ev")
    for i in range(8):
        b["bank_free"][i] = ("ev", nrm_done)

    @p.on("sync")
    def _(sync):
        sync.wait_ge(p.hnd["gd"], cast_done)

    for g in range(4):
        ldo, stripo = load_strip(p, b, 1, wob_bf, g * 512, CH)
        mmo = proj_group(p, b, stripo, ldo, OBUF, CH, start=True, stop=True)
        b["strip_free"][1] = ("mm", mmo[(3, 1)])

        @p.on("vector")
        def _(ve, g=g, mmo=mmo):
            v = Eng(p, ve, "vector")
            for oi in range(4):
                oc = g * 4 + oi
                for hh in (0, 1):
                    v.wait_ge(p.hnd["mm"], mmo[(oi, hh)])
                    v.tt(X[:, tok(oc, hh)], b["banks"][oi * 2 + hh][:, :],
                         X[:, tok(oc, hh)], ALU.add)
            v.mark("ev")
        for i in range(8):
            b["bank_free"][i] = ("ev", p.val("ev"))

    emit_ln(p, b, X, XLN, "ln4")

    emit_ffn(p, b, X, XLN, Hbuf, gate_w, up_w, dn_w, sc_x, out_y, XR, XOT, FE4)

    fin = p.snap()

    @p.on("gpsimd")
    def _(g):
        p.wait(g, fin)

    return p


def _l3_host(inputs, l2res):
    gate_full = _L23_CACHE["gate_full"]
    up_full = _L23_CACHE["up_full"]
    dn_full = _L23_CACHE["dn_full"]
    ldall = np.stack([np.ascontiguousarray(np.asarray(l2res[j]["logd"])).T.reshape(D)
                      for j in range(NCORE)]).astype(np.float32)
    sall = np.ascontiguousarray(np.concatenate(
        [np.asarray(l2res[j]["sstate"]) for j in range(NCORE)], 0))
    gwv = np.asarray(inputs["L0_attn_b_g_weight"], np.float32)
    gw = np.ascontiguousarray(gwv.reshape(NH, 128).T)
    wob = np.ascontiguousarray(np.asarray(inputs["L0_attn_b_Wo"], np.float32))
    cst = _consts()
    in_maps = []
    for c in range(NCORE):
        wscanT = np.zeros((8, 8), np.float32)
        for j in range(8):
            for i in range(8):
                if j < i < c:
                    wscanT[i, j] = 1.0
        bia = np.where(np.arange(8) < c, 0.0, -1e30).astype(np.float32)
        biarep = np.tile(bia, 16)[None, :].repeat(128, 0)
        r = l2res[c]
        m = dict(x3=np.ascontiguousarray(np.asarray(r["x3"])),
                 opart=np.ascontiguousarray(np.asarray(r["opart"])),
                 qtil=np.ascontiguousarray(np.asarray(r["qtil"])),
                 sall=sall, ldall=ldall, wscanT=wscanT, biarep=biarep,
                 gamlog=np.ascontiguousarray(np.asarray(r["gamlog"])),
                 gw=gw, wob=wob, gate_w=gate_full, up_w=up_full, dn_w=dn_full,
                 **cst)
        in_maps.append(m)
    return in_maps


def run_l3(inputs, l2res):
    in_maps = _l3_host(inputs, l2res)
    prog = build_l3()
    res = run_bass_kernel_spmd(prog.nc, in_maps, core_ids=list(range(NCORE)))
    return res.results


def kernel(**inputs):
    l1res = run_l1(inputs)
    l2res = run_l2(inputs, l1res)
    l3res = run_l3(inputs, l2res)
    outs = []
    for c in range(NCORE):
        y = np.asarray(l3res[c]["y"], np.float32)
        outs.append(y.reshape(128, CH, T).transpose(2, 1, 0).reshape(T, D))
    return np.concatenate(outs, 0).reshape(1, SEQ, D).astype(np.float32)


# version 03
# speedup vs baseline: 1.1873x; 1.1873x over previous
import contextlib
import math

import numpy as np
import ml_dtypes

import concourse.bass as bass
import concourse.mybir as mybir
from concourse.bass_utils import run_bass_kernel_spmd

F32 = mybir.dt.float32
BF16 = mybir.dt.bfloat16
AF = mybir.ActivationFunctionType
ALU = mybir.AluOpType
AX = mybir.AxisListType

NCORE = 8
D = 2048
SEQ = 8192
T = SEQ // NCORE
HW = 512
CH = D // 128
INTER = 8192
ICH = INTER // 128
NH, DK, DV = 16, 128, 128
CHUNK = 128
NCHK = T // CHUNK
SUB = 16
NSUB = CHUNK // SUB
EPS = 1e-5
DENOM = math.sqrt(2048.0) / 8192.0
LN_C = math.log(DK ** -0.5)



class Prog:

    def __init__(self):
        self.nc = bass.Bass("TRN2", target_bir_lowering=False,
                            detect_race_conditions=False)
        self.ctx = contextlib.ExitStack()
        self.hnd = {}
        self.cnt = {}
        self.block = None
        self.haz = {k: {"w": [], "r": [], "last": None}
                    for k in ("vector", "scalar", "gpsimd")}
        for s in ("gd", "mm", "ev", "ac", "st", "vsy", "ssy", "gsy"):
            self.sem(s)

    def sem(self, name):
        if name not in self.hnd:
            self.hnd[name] = self.ctx.enter_context(self.nc.semaphore(name))
            self.cnt[name] = 0
        return self.hnd[name]

    def inc(self, inst, name, k=1):
        inst.then_inc(self.hnd[name], k)
        self.cnt[name] += k
        return self.cnt[name]

    def val(self, name):
        return self.cnt[name]

    def start_block(self):
        self.block = self.ctx.enter_context(self.nc.Block())

    def on(self, eng):
        def deco(fn):
            getattr(self.block, eng)(fn)
        return deco

    def snap(self):
        return dict(self.cnt)

    def wait(self, eng, snap):
        for name, v in snap.items():
            if v > 0:
                eng.wait_ge(self.hnd[name], v)

    def sb(self, name, shape, dt):
        return self.ctx.enter_context(self.nc.sbuf_tensor(name, list(shape), dt))

    def psb(self, name, shape, dt):
        return self.ctx.enter_context(self.nc.psum_tensor(name, list(shape), dt))

    def din(self, name, shape, dt):
        return self.nc.dram_tensor(name, list(shape), dt, kind="ExternalInput").ap()

    def dout(self, name, shape, dt):
        return self.nc.dram_tensor(name, list(shape), dt, kind="ExternalOutput").ap()

    def dtmp(self, name, shape, dt):
        return self.nc.dram_tensor(name, list(shape), dt).ap()


_AUTOSEM = {"vector": "vsy", "scalar": "ssy", "gpsimd": "gsy"}


def _region(a):
    try:
        name = a.tensor.name
        es = mybir.dt.size(a.dtype)
        apl = [(int(s), int(c)) for s, c in a.ap]
    except (AttributeError, TypeError):
        return None
    lo = int(a.offset) * es
    if len(apl) > 1:
        skip = max(range(len(apl)), key=lambda i: apl[i][0])
    else:
        skip = -1
    span = es
    for i, (s, c) in enumerate(apl):
        if i != skip:
            span += es * abs(s) * (c - 1)
    return (name, lo, lo + span)


def _overlap(r1, r2):
    return r1[0] == r2[0] and r1[1] < r2[2] and r2[1] < r1[2]


class Eng:

    def __init__(self, p, e, kind):
        self.p, self.e, self.kind = p, e, kind
        self.h = p.haz[kind]

    def _sync(self):
        h = self.h
        if h["last"] is not None:
            inst, sname, sval = h["last"]
            if sname is None:
                sname = _AUTOSEM[self.kind]
                sval = self.p.inc(inst, sname)
                h["last"] = (inst, sname, sval)
            self.e.wait_ge(self.p.hnd[sname], sval)
        h["w"].clear()
        h["r"].clear()

    def _op(self, call, wr, rd):
        h = self.h
        wn = [r for r in (_region(a) for a in wr) if r]
        rn = [r for r in (_region(a) for a in rd) if r]
        conflict = (
            any(_overlap(r, w) for r in rn for w in h["w"])
            or any(_overlap(w1, w2) for w1 in wn for w2 in h["w"])
            or any(_overlap(w, r) for w in wn for r in h["r"])
        )
        if conflict:
            self._sync()
        inst = call()
        h["w"].extend(wn)
        h["r"].extend(rn)
        h["last"] = (inst, None, None)
        return inst

    def mark(self, sem, k=1):
        h = self.h
        inst, sname, sval = h["last"]
        assert sname is None, "op already has an inc"
        n = self.p.inc(inst, sem, k)
        h["last"] = (inst, sem, n)
        return n

    def barrier(self):
        self._sync()

    def wait_ge(self, semh, v):
        self.e.wait_ge(semh, v)

    def wtok(self, t):
        if t is not None:
            self.e.wait_ge(self.p.hnd[t[0]], t[1])

    def wsnap(self, snap):
        self.p.wait(self.e, snap)

    def tt(self, out, a, b2, op):
        return self._op(lambda: self.e.tensor_tensor(out, a, b2, op), [out], [a, b2])

    def ts(self, out, a, s1, s2, op0, op1=None):
        rd = [a] + [s for s in (s1, s2) if hasattr(s, "tensor")]
        if op1 is None:
            return self._op(lambda: self.e.tensor_scalar(out, a, s1, s2, op0), [out], rd)
        return self._op(lambda: self.e.tensor_scalar(out, a, s1, s2, op0, op1), [out], rd)

    def cp(self, out, a):
        return self._op(lambda: self.e.tensor_copy(out, a), [out], [a])

    def red(self, out, a, op):
        return self._op(lambda: self.e.tensor_reduce(out, a, axis=AX.X, op=op), [out], [a])

    def mset(self, out, v):
        return self._op(lambda: self.e.memset(out, v), [out], [])

    def act(self, out, a, func, bias=None, scale=None, accum_out=None):
        kw = {}
        if bias is not None:
            kw["bias"] = bias
        if scale is not None:
            kw["scale"] = scale
        if accum_out is not None:
            kw["accum_out"] = accum_out
        rd = [a] + [x for x in (bias, scale) if hasattr(x, "tensor")]
        wr = [out] + ([accum_out] if accum_out is not None else [])
        return self._op(lambda: self.e.activation(out, a, func, **kw), wr, rd)


def chs(ch, w=T):
    return slice(ch * w, (ch + 1) * w)


def tok(ch, hh, w=T):
    return slice(ch * w + hh * HW, ch * w + (hh + 1) * HW)


def hsl(hh):
    return slice(hh * HW, (hh + 1) * HW)


def wait_tok(eng, p, t):
    if t is not None:
        eng.wait_ge(p.hnd[t[0]], t[1])


def make_shared(p):
    b = {}
    b["banks"] = [p.psb(f"pb{i}", [128, 512], F32) for i in range(8)]
    b["bank_free"] = [None] * 8
    b["ones_bf"] = p.sb("ones_bf", [128, 1], BF16)
    b["ones_row_f"] = p.sb("ones_row_f", [1, 128], F32)
    b["xbf"] = [p.sb("xbf0", [128, T], BF16)]
    b["xsqb"] = [p.sb("xsqb0", [128, T], BF16)]
    b["rows"] = p.sb("rows", [1, 3 * T], F32)
    b["mb"] = p.sb("mbb", [128, T], F32)
    b["rb"] = p.sb("rbb", [128, T], F32)
    b["aptmp"] = p.sb("aptmp", [128, HW], F32)
    b["strip"] = [p.sb(f"strip{i}", [128, 16 * 512], BF16) for i in range(2)]
    b["strip_free"] = [None, None]
    b["consts"] = p.sb("consts", [128, 4], F32)
    return b


def row(b, i):
    return b["rows"][:, i * T:(i + 1) * T]


def fill_consts(p, b):
    @p.on("vector")
    def _(ve):
        v = Eng(p, ve, "vector")
        v.mset(b["consts"][:, 0:1], EPS)
        v.mset(b["consts"][:, 1:2], 1.0)
        v.mset(b["consts"][:, 2:3], LN_C)
        v.mark("ev")


def load_consts(p, b, pairs):
    @p.on("sync")
    def _(sync):
        for dst, src in pairs:
            p.inc(sync.dma_start(out=dst, in_=src), "lx", 16)


def cast_weight(p, dst_bf, src_f32, pieces=4):
    rows_ = src_f32.shape[0]
    step = rows_ // pieces
    @p.on("gpsimd")
    def _(g):
        for i in range(pieces):
            p.inc(g.dma_start(out=dst_bf[i * step:(i + 1) * step, :],
                              in_=src_f32[i * step:(i + 1) * step, :]), "gd", 16)


def emit_ln(p, b, X, XLN, label):
    ps_sum = [b["banks"][0], b["banks"][1]]
    ps_sq = [b["banks"][2], b["banks"][3]]
    ps_mb = [b["banks"][4], b["banks"][5]]
    ps_rb = [b["banks"][6], b["banks"][7]]
    pre = p.snap()
    mm_by_ch = {}

    for ch in range(CH):
        cur = 0
        hold = {}

        @p.on("vector")
        def _(ve, ch=ch, cur=cur):
            v = Eng(p, ve, "vector")
            if ch == 0:
                v.wsnap(pre)
            if ch >= 1:
                v.wait_ge(p.hnd["mm"], mm_by_ch[ch - 1])
            v.cp(b["xbf"][cur][:, :], X[:, chs(ch)])
            v.tt(b["xsqb"][cur][:, :], X[:, chs(ch)], X[:, chs(ch)], ALU.mult)
            hold["v"] = v.mark("ev")

        @p.on("tensor")
        def _(t, ch=ch, cur=cur):
            if ch == 0:
                for i in range(8):
                    wait_tok(t, p, b["bank_free"][i])
            t.wait_ge(p.hnd["ev"], hold["v"])
            last = None
            for hh in (0, 1):
                t.matmul(ps_sum[hh][0:1, :], b["ones_bf"][:, :], b["xbf"][cur][:, hsl(hh)],
                         start=(ch == 0), stop=(ch == CH - 1))
                last = t.matmul(ps_sq[hh][0:1, :], b["ones_bf"][:, :], b["xsqb"][cur][:, hsl(hh)],
                                start=(ch == 0), stop=(ch == CH - 1))
            mm_by_ch[ch] = p.inc(last, "mm")

    mm_done = p.val("mm")
    hold2 = {}

    @p.on("vector")
    def _(ve):
        v = Eng(p, ve, "vector")
        v.wait_ge(p.hnd["mm"], mm_done)
        for hh in (0, 1):
            v.cp(row(b, 0)[:, hsl(hh)], ps_sum[hh][0:1, :])
            v.cp(row(b, 1)[:, hsl(hh)], ps_sq[hh][0:1, :])
        v.ts(row(b, 0), row(b, 0), 1.0 / D, None, ALU.mult)
        v.ts(row(b, 1), row(b, 1), 1.0 / D, None, ALU.mult)
        v.tt(row(b, 2), row(b, 0), row(b, 0), ALU.mult)
        v.tt(row(b, 1), row(b, 1), row(b, 2), ALU.subtract)
        hold2["v"] = v.mark("ev")

    hold3 = {}

    @p.on("scalar")
    def _(se):
        s = Eng(p, se, "scalar")
        s.wait_ge(p.hnd["ev"], hold2["v"])
        s.act(row(b, 2), row(b, 1), AF.Ln, bias=b["consts"][0:1, 0:1])
        s.act(row(b, 1), row(b, 2), AF.Exp, scale=-0.5)
        hold3["v"] = s.mark("ac")

    @p.on("tensor")
    def _(t):
        t.wait_ge(p.hnd["ac"], hold3["v"])
        last = None
        for hh in (0, 1):
            t.matmul(ps_mb[hh][:, :], b["ones_row_f"][:, :], row(b, 0)[0:1, hsl(hh)],
                     start=True, stop=True)
            last = t.matmul(ps_rb[hh][:, :], b["ones_row_f"][:, :], row(b, 1)[0:1, hsl(hh)],
                            start=True, stop=True)
        p.inc(last, "mm")
    mmb = p.val("mm")

    @p.on("vector")
    def _(ve):
        v = Eng(p, ve, "vector")
        v.wait_ge(p.hnd["mm"], mmb)
        for hh in (0, 1):
            v.cp(b["mb"][:, hsl(hh)], ps_mb[hh][:, :])
            v.cp(b["rb"][:, hsl(hh)], ps_rb[hh][:, :])
        for ch in range(CH):
            for hh in (0, 1):
                v.tt(b["aptmp"][:, :], X[:, tok(ch, hh)], b["mb"][:, hsl(hh)], ALU.subtract)
                v.tt(XLN[:, tok(ch, hh)], b["aptmp"][:, :], b["rb"][:, hsl(hh)], ALU.mult)
        v.mark("ev")
    evd = p.val("ev")
    for i in range(8):
        b["bank_free"][i] = ("ev", evd)
    return ("ev", evd)


def load_strip(p, b, sbuf_idx, w_dram, col0, k_chunks, kc0=0, q="sync"):
    prev = b["strip_free"][sbuf_idx]
    strip = b["strip"][sbuf_idx]
    semn = f"lw{sbuf_idx}"
    p.sem(semn)

    @p.on(q)
    def _(sync):
        wait_tok(sync, p, prev)
        src = w_dram[kc0 * 128:(kc0 + k_chunks) * 128, col0:col0 + 512] \
            .rearrange("(k p) c -> p k c", p=128)
        dst = strip[:, 0:k_chunks * 512].rearrange("p (k c) -> p k c", k=k_chunks)
        p.inc(sync.dma_start(out=dst, in_=src), semn, 16)
    return (semn, p.val(semn)), strip


def proj_group(p, b, strip, ldv, src, k_chunks, start, stop, kc_shift=0):
    mmv = {}

    @p.on("tensor")
    def _(t):
        wait_tok(t, p, ldv)
        for oi in range(4):
            for hh in (0, 1):
                ps = b["banks"][oi * 2 + hh]
                if start:
                    wait_tok(t, p, b["bank_free"][oi * 2 + hh])
                last = None
                for kc in range(k_chunks):
                    last = t.matmul(
                        ps[:, :],
                        strip[:, kc * 512 + oi * 128: kc * 512 + (oi + 1) * 128],
                        src[:, (kc + kc_shift) * T + hh * HW:(kc + kc_shift) * T + (hh + 1) * HW],
                        start=(start and kc == 0), stop=(stop and kc == k_chunks - 1))
                mmv[(oi, hh)] = p.inc(last, "mm")
    return mmv



def build_l1():
    p = Prog()
    p.sem("lx")
    for s in ("lr0", "lr1", "lr2", "sro0", "sro1", "skv0", "skv1"):
        p.sem(s)
    x_in = p.din("x", [D, T], F32)
    wkT = p.din("wkT", [D, D], F32)
    wvT = p.din("wvT", [D, D], F32)
    ones_bf_in = p.din("ones_bf_in", [128, 1], BF16)
    ones_row_f_in = p.din("ones_row_f_in", [1, 128], F32)

    gu_tap = [p.din(f"gu_tap{k}", [512, INTER], F32) for k in range(4)]
    gu_c = [p.din(f"gu_c{k}", [256, INTER], F32) for k in range(4)]
    dn_tap = [p.din(f"dn_tap{k}", [1024, D], F32) for k in range(4)]
    dn_c = [p.din(f"dn_c{k}", [1024, D], F32) for k in range(4)]

    out_ss = p.dout("slicesum", [128, CH], F32)
    out_kvt = p.dout("kvt", [128, CH * T], BF16)
    out_gu = p.dout("recon_gu", [512, INTER], BF16)
    out_dn = p.dout("recon_dn", [1024, D], BF16)

    X = p.sb("X", [128, CH * T], F32)
    XLN = p.sb("XLN", [128, CH * T], BF16)
    KVTILE = [p.sb(f"kvtile{i}", [128, T], BF16) for i in range(2)]
    TMPK = p.sb("TMPK", [128, 4 * T], F32)
    KVF = p.sb("KVF", [128, T], F32)
    CS2 = p.sb("CS2", [128, T], F32)
    SS = p.sb("SS", [128, CH], F32)
    RW = 512
    RSL = [X[:, i * 4096:(i + 1) * 4096] for i in range(3)]
    RT4 = X[:, 12288:14336]
    AB = X[:, 14336:15360]
    ROv = X[:, 15360:15872].bitcast(BF16)
    RO = [ROv[:, 0:512], ROv[:, 512:1024]]

    p.start_block()
    b = make_shared(p)
    fill_consts(p, b)

    load_consts(p, b, [
        (b["ones_bf"][:, :], ones_bf_in[:, :]),
        (b["ones_row_f"][:, :], ones_row_f_in[:, :]),
    ])

    @p.on("sync")
    def _(sync):
        for ch in range(CH):
            p.inc(sync.dma_start(out=X[:, chs(ch)], in_=x_in[ch * 128:(ch + 1) * 128, :]), "lx", 16)

    ln1_tok = emit_ln(p, b, X, XLN, "ln1")

    tiles = []
    for rc in range(4):
        for sg in range(INTER // RW):
            tiles.append(("gu", rc, sg))
    for rc in range(8):
        for sg in range(D // RW):
            tiles.append(("dn", rc, sg))
    rst = {"i": 0, "slot_free": [ln1_tok, ln1_tok, ln1_tok], "ro_free": [None, None]}

    def emit_recon(n):
        for _ in range(n):
            ti = rst["i"]
            if ti >= len(tiles):
                return
            rst["i"] = ti + 1
            kind, rc, sg = tiles[ti]
            slot = ti % 3
            q = ["sync", "scalar"][ti % 2] if ti < 56 else \
                ["sync", "scalar", "gpsimd"][ti % 3]
            semr = f"lr{slot}"
            sfree = rst["slot_free"][slot]

            @p.on(q)
            def _(qe, kind=kind, rc=rc, sg=sg, slot=slot, semr=semr, sfree=sfree):
                wait_tok(qe, p, sfree)
                for k in range(4):
                    if kind == "gu":
                        tapsrc = gu_tap[k][rc * 128:(rc + 1) * 128, sg * RW:(sg + 1) * RW]
                        csrc = gu_c[k][(rc % 2) * 128:(rc % 2 + 1) * 128, sg * RW:(sg + 1) * RW]
                    else:
                        tapsrc = dn_tap[k][rc * 128:(rc + 1) * 128, sg * RW:(sg + 1) * RW]
                        csrc = dn_c[k][rc * 128:(rc + 1) * 128, sg * RW:(sg + 1) * RW]
                    p.inc(qe.dma_start(out=RSL[slot][:, k * RW:(k + 1) * RW], in_=tapsrc), semr, 16)
                    p.inc(qe.dma_start(out=RSL[slot][:, (4 + k) * RW:(5 + k) * RW], in_=csrc), semr, 16)
            ldr = p.val(semr)
            ri = ti % 2
            rfree = rst["ro_free"][ri]

            @p.on("vector")
            def _(ve, slot=slot, ldr=ldr, semr=semr, ri=ri, rfree=rfree):
                v = Eng(p, ve, "vector")
                v.wait_ge(p.hnd[semr], ldr)
                v.wtok(rfree)
                sl = RSL[slot]
                v.tt(RT4[:, :], sl[:, 0:4 * RW], sl[:, 4 * RW:8 * RW], ALU.mult)
                v.tt(AB[:, 0:RW], RT4[:, 0:RW], RT4[:, RW:2 * RW], ALU.add)
                v.tt(AB[:, RW:2 * RW], RT4[:, 2 * RW:3 * RW], RT4[:, 3 * RW:4 * RW], ALU.add)
                v.tt(RO[ri][:, :], AB[:, 0:RW], AB[:, RW:2 * RW], ALU.add)
                v.mark("ev")
            evr = p.val("ev")
            rst["slot_free"][slot] = ("ev", evr)
            sems = f"sro{ri}"

            @p.on("gpsimd")
            def _(g, kind=kind, rc=rc, sg=sg, ri=ri, evr=evr, sems=sems):
                g.wait_ge(p.hnd["ev"], evr)
                dst = out_gu if kind == "gu" else out_dn
                p.inc(g.dma_start(out=dst[rc * 128:(rc + 1) * 128, sg * RW:(sg + 1) * RW],
                                  in_=RO[ri][:, :]), sems, 16)
            rst["ro_free"][ri] = (sems, p.val(sems))

    b["kvtile_free"] = [None, None]
    for g in range(4):
        ldk, stripk = load_strip(p, b, 0, wkT, g * 512, CH, q="gpsimd")
        mmk = proj_group(p, b, stripk, ldk, XLN, CH, start=True, stop=True)

        @p.on("vector")
        def _(ve, mmk=mmk):
            v = Eng(p, ve, "vector")
            for oi in range(4):
                for hh in (0, 1):
                    v.wait_ge(p.hnd["mm"], mmk[(oi, hh)])
                    v.cp(TMPK[:, tok(oi, hh)], b["banks"][oi * 2 + hh][:, :])
            v.mark("ev")
        for i in range(8):
            b["bank_free"][i] = ("ev", p.val("ev"))
        b["strip_free"][0] = ("mm", mmk[(3, 1)])

        ldv_, stripv = load_strip(p, b, 1, wvT, g * 512, CH, q="gpsimd")
        mmv = proj_group(p, b, stripv, ldv_, XLN, CH, start=True, stop=True)
        b["strip_free"][1] = ("mm", mmv[(3, 1)])

        for oi in range(4):
            ch = g * 4 + oi
            ktile = KVTILE[ch % 2]
            ktile_free = b["kvtile_free"][ch % 2]

            @p.on("vector")
            def _(ve, ch=ch, oi=oi, mmv=mmv, ktile=ktile, ktile_free=ktile_free):
                v = Eng(p, ve, "vector")
                for hh in (0, 1):
                    v.wait_ge(p.hnd["mm"], mmv[(oi, hh)])
                    v.tt(KVF[:, hsl(hh)], TMPK[:, tok(oi, hh)],
                         b["banks"][oi * 2 + hh][:, :], ALU.mult)
                v.red(SS[:, ch:ch + 1], KVF[:, :], ALU.add)
                v.wtok(ktile_free)
                bufs = [KVF, CS2]
                for si, s in enumerate([1, 2, 4, 8, 16, 32, 64, 128, 256, 512]):
                    a = bufs[si % 2]
                    o = ktile if si == 9 else bufs[(si + 1) % 2]
                    v.tt(o[:, s:T], a[:, s:T], a[:, 0:T - s], ALU.add)
                    v.cp(o[:, 0:s], a[:, 0:s])
                v.mark("ev")
            b["bank_free"][oi * 2] = ("ev", p.val("ev"))
            b["bank_free"][oi * 2 + 1] = ("ev", p.val("ev"))
            evk = p.val("ev")

            semk = f"skv{ch % 2}"

            @p.on("scalar")
            def _(gq, ch=ch, ktile=ktile, evk=evk, semk=semk):
                gq.wait_ge(p.hnd["ev"], evk)
                p.inc(gq.dma_start(out=out_kvt[:, chs(ch)], in_=ktile[:, :]), semk, 16)
            b["kvtile_free"][ch % 2] = (semk, p.val(semk))
            emit_recon(3)
        emit_recon(2)

    emit_recon(len(tiles))

    fin = p.snap()

    @p.on("gpsimd")
    def _(g):
        p.wait(g, fin)
        p.inc(g.dma_start(out=out_ss[:, :], in_=SS[:, :]), "st", 16)
        g.wait_ge(p.hnd["st"], p.val("st"))

    return p



_GRIDS = {}


def _spline_host(gate_cp, up_cp, down_cp):
    n = INTER * D
    nct = gate_cp.shape[0]
    if "idx" not in _GRIDS:
        t = np.linspace(0.0, nct - 1.0, n, dtype=np.float64)
        i = np.clip(np.floor(t).astype(np.int64), 0, nct - 2)
        f = (t - i).astype(np.float32)
        idx = np.stack([np.clip(i + k, 0, nct - 1) for k in (-1, 0, 1, 2)], axis=0).astype(np.int32)
        f2 = f * f
        f3 = f2 * f
        c0 = 0.5 * (-f + 2.0 * f2 - f3)
        c1 = 0.5 * (2.0 - 5.0 * f2 + 3.0 * f3)
        c2 = 0.5 * (f + 4.0 * f2 - 3.0 * f3)
        c3 = 0.5 * (-f2 + f3)
        c = np.stack([c0, c1, c2, c3], 0).astype(np.float32)
        cguT = np.ascontiguousarray(c.reshape(4, INTER, D).transpose(0, 2, 1))
        cdnT = np.ascontiguousarray(c.reshape(4, D, INTER).transpose(0, 2, 1))
        _GRIDS.update(idx=idx, cguT=cguT, cdnT=cdnT)
    idx = _GRIDS["idx"]
    cguT = _GRIDS["cguT"]
    cdnT = _GRIDS["cdnT"]

    out = {"gu_tap": [[] for _ in range(4)], "gu_c": [[] for _ in range(4)],
           "dn_tap": [[] for _ in range(4)], "dn_c": [[] for _ in range(4)]}
    for k in range(4):
        g_t = np.ascontiguousarray(gate_cp[idx[k]].reshape(INTER, D).T)
        u_t = np.ascontiguousarray(up_cp[idx[k]].reshape(INTER, D).T)
        d_t = np.ascontiguousarray(down_cp[idx[k]].reshape(D, INTER).T)
        for ccore in range(NCORE):
            gr = slice(ccore * 256, (ccore + 1) * 256)
            dr = slice(ccore * 1024, (ccore + 1) * 1024)
            out["gu_tap"][k].append(np.ascontiguousarray(
                np.concatenate([g_t[gr], u_t[gr]], 0)))
            out["gu_c"][k].append(np.ascontiguousarray(cguT[k][gr]))
            out["dn_tap"][k].append(np.ascontiguousarray(d_t[dr]))
            out["dn_c"][k].append(np.ascontiguousarray(cdnT[k][dr]))
    return out


def _fm_slices(x):
    xs = x.reshape(SEQ, D)
    return [np.ascontiguousarray(xs[c * T:(c + 1) * T, :].T) for c in range(NCORE)]


def _consts():
    return {
        "ones_bf_in": np.ones((128, 1), dtype=ml_dtypes.bfloat16),
        "ones_row_f_in": np.ones((1, 128), dtype=np.float32),
    }


def run_l1(inputs):
    x = np.asarray(inputs["L0_attn_a_x"], dtype=np.float32)
    xs = _fm_slices(x)
    wkT = np.ascontiguousarray(np.asarray(inputs["L0_attn_a_Wk"], np.float32).T)
    wvT = np.ascontiguousarray(np.asarray(inputs["L0_attn_a_Wv"], np.float32).T)
    sp = _spline_host(np.asarray(inputs["L0_ffn_gate_cp"], np.float32),
                      np.asarray(inputs["L0_ffn_up_cp"], np.float32),
                      np.asarray(inputs["L0_ffn_down_cp"], np.float32))
    cst = _consts()
    in_maps = []
    for c in range(NCORE):
        m = dict(x=xs[c], wkT=wkT, wvT=wvT, **cst)
        for k in range(4):
            m[f"gu_tap{k}"] = sp["gu_tap"][k][c]
            m[f"gu_c{k}"] = sp["gu_c"][k][c]
            m[f"dn_tap{k}"] = sp["dn_tap"][k][c]
            m[f"dn_c{k}"] = sp["dn_c"][k][c]
        in_maps.append(m)
    prog = build_l1()
    res = run_bass_kernel_spmd(prog.nc, in_maps, core_ids=list(range(NCORE)))
    return res.results


def kernel(**inputs):
    raise NotImplementedError("L2/L3 not yet wired")



_L23_CACHE = {}

class Carver:

    def __init__(self, base_ap, total_bytes):
        self.base = base_ap
        self.total = total_bytes
        self.cur = 0

    def take(self, shape, dt):
        es = mybir.dt.size(dt)
        nb = int(np.prod(shape[1:])) * es if len(shape) > 1 else es
        nb = int(np.prod(shape[1:])) * es
        self.cur = (self.cur + 3) // 4 * 4
        c0 = self.cur // 2
        self.cur += nb
        assert self.cur <= self.total, f"carver overflow {self.cur} > {self.total}"
        view = self.base[:, c0:c0 + nb // 2]
        if dt != BF16:
            view = view.bitcast(dt)
        if len(shape) == 3:
            view = view.rearrange("p (a c) -> p a c", a=shape[1])
        return view

    def reset(self, to=0):
        self.cur = to



def load_stripx(p, buf, free_tok, semn, w_dram, col0, k_chunks, kc0=0):
    p.sem(semn)

    @p.on("sync")
    def _(sync):
        wait_tok(sync, p, free_tok)
        src = w_dram[kc0 * 128:(kc0 + k_chunks) * 128, col0:col0 + 512] \
            .rearrange("(k p) c -> p k c", p=128)
        dst = buf[:, 0:k_chunks * 512].rearrange("p (k c) -> p k c", k=k_chunks)
        p.inc(sync.dma_start(out=dst, in_=src), semn, 16)
    return (semn, p.val(semn))


def emit_ffn(p, b, X, XLN, Hbuf, gate_w, up_w, dn_w, sc_x, out_dst, XR, XOT, FE4,
             xstrips=None):
    S = [b["strip"][0][:, :], b["strip"][1][:, :]] + (xstrips or [])
    NS = len(S)
    SF = [b["strip_free"][0], b["strip_free"][1]] + [None] * (NS - 2)
    @p.on("gpsimd")
    def _(g):
        p.wait(g, {"ev": p.val("ev")})
        for ch in range(CH):
            p.inc(g.dma_start(out=sc_x[:, chs(ch)], in_=X[:, chs(ch)]), "gd", 16)
    spill_done = ("gd", p.val("gd"))

    b["xr_free"] = [None, None]
    b["xot_free"] = [None, None]
    prev_silu_consumed = None
    gidx = 0
    for hh in (0, 1):
        for g in range(ICH // 4):
            sg_i = (2 * gidx) % NS
            su_i = (2 * gidx + 1) % NS
            gidx += 1
            stripg = S[sg_i]
            stripu = S[su_i]
            ldg = load_stripx(p, stripg, SF[sg_i], f"lw{sg_i}", gate_w, g * 512, CH)
            ldu = load_stripx(p, stripu, SF[su_i], f"lw{su_i}", up_w, g * 512, CH)
            mmg = {}

            @p.on("tensor")
            def _(t, g=g, hh=hh, ldg=ldg, ldu=ldu, stripg=stripg, stripu=stripu):
                wait_tok(t, p, ldg)
                if g == 0 and hh == 0:
                    wait_tok(t, p, spill_done)
                for oi in range(4):
                    wait_tok(t, p, b["bank_free"][oi])
                    last = None
                    for kc in range(CH):
                        last = t.matmul(b["banks"][oi][:, :],
                                        stripg[:, kc * 512 + oi * 128: kc * 512 + (oi + 1) * 128],
                                        XLN[:, tok(kc, hh)],
                                        start=(kc == 0), stop=(kc == CH - 1))
                    mmg[("g", oi)] = p.inc(last, "mm")
                wait_tok(t, p, ldu)
                for oi in range(4):
                    wait_tok(t, p, b["bank_free"][4 + oi])
                    last = None
                    for kc in range(CH):
                        last = t.matmul(b["banks"][4 + oi][:, :],
                                        stripu[:, kc * 512 + oi * 128: kc * 512 + (oi + 1) * 128],
                                        XLN[:, tok(kc, hh)],
                                        start=(kc == 0), stop=(kc == CH - 1))
                    mmg[("u", oi)] = p.inc(last, "mm")

            holds = {}

            @p.on("scalar")
            def _(se, mmg=mmg, prev=prev_silu_consumed):
                s = Eng(p, se, "scalar")
                if prev is not None:
                    s.wait_ge(p.hnd["ev"], prev)
                for oi in range(4):
                    s.wait_ge(p.hnd["mm"], mmg[("g", oi)])
                    s.act(FE4[:, oi * HW:(oi + 1) * HW], b["banks"][oi][:, :], AF.Silu)
                    holds[oi] = s.mark("ac")

            evs = {}

            @p.on("vector")
            def _(ve, g=g, hh=hh, mmg=mmg):
                v = Eng(p, ve, "vector")
                for oi in range(4):
                    ic = g * 4 + oi
                    v.wait_ge(p.hnd["ac"], holds[oi])
                    v.wait_ge(p.hnd["mm"], mmg[("u", oi)])
                    v.tt(Hbuf[:, ic * HW:(ic + 1) * HW],
                         FE4[:, oi * HW:(oi + 1) * HW], b["banks"][4 + oi][:, :], ALU.mult)
                    evs[oi] = v.mark("ev")
            evg = p.val("ev")
            prev_silu_consumed = evg
            for oi in range(4):
                b["bank_free"][oi] = ("ac", holds[oi])
                b["bank_free"][4 + oi] = ("ev", evs[oi])
            SF[sg_i] = ("mm", mmg[("g", 3)])
            SF[su_i] = ("mm", mmg[("u", 3)])

        for ocg in range(4):
            mmd = {}
            for piece in range(4):
                sbf = (gidx * 2 + piece) % NS
                stripd = S[sbf]
                ldd = load_stripx(p, stripd, SF[sbf], f"lw{sbf}", dn_w, ocg * 512, CH,
                                  kc0=piece * 16)

                @p.on("tensor")
                def _(t, piece=piece, ldd=ldd, stripd=stripd, ocg=ocg, hh=hh):
                    wait_tok(t, p, ldd)
                    if piece == 0:
                        for i in range(4):
                            wait_tok(t, p, b["bank_free"][i])
                    last = None
                    for oi in range(4):
                        for kc in range(CH):
                            kabs = piece * 16 + kc
                            last = t.matmul(
                                b["banks"][oi][:, :],
                                stripd[:, kc * 512 + oi * 128: kc * 512 + (oi + 1) * 128],
                                Hbuf[:, kabs * HW:(kabs + 1) * HW],
                                start=(kabs == 0), stop=(kabs == ICH - 1),
                                skip_group_check=True)
                        if piece == 3:
                            mmd[oi] = p.inc(last, "mm")
                            last = None
                    if piece != 3:
                        p.inc(last, "mm")
                SF[sbf] = ("mm", p.val("mm"))
            gidx += 2

            for oi in range(4):
                oc = ocg * 4 + oi
                sl = oc % 2

                @p.on("sync")
                def _(sync, oc=oc, hh=hh, sl=sl):
                    wait_tok(sync, p, b["xr_free"][sl])
                    p.inc(sync.dma_start(out=XR[sl][:, :], in_=sc_x[:, tok(oc, hh)]), f"lxr{sl}", 16)
                lxr = p.val(f"lxr{sl}")
                xot_free = b["xot_free"][sl]

                @p.on("vector")
                def _(ve, oi=oi, sl=sl, lxr=lxr, mmd=mmd, xot_free=xot_free):
                    v = Eng(p, ve, "vector")
                    v.wait_ge(p.hnd["mm"], mmd[oi])
                    v.wait_ge(p.hnd[f"lxr{sl}"], lxr)
                    v.wtok(xot_free)
                    v.tt(XOT[sl][:, :], b["banks"][oi][:, :], XR[sl][:, :], ALU.add)
                    v.mark("ev")
                evd = p.val("ev")
                b["xr_free"][sl] = ("ev", evd)
                b["bank_free"][oi] = ("ev", evd)

                semx = "sx3a" if sl == 0 else "sx3b"

                @p.on("gpsimd")
                def _(gq, oc=oc, hh=hh, sl=sl, evd=evd, semx=semx):
                    gq.wait_ge(p.hnd["ev"], evd)
                    p.inc(gq.dma_start(out=out_dst[:, tok(oc, hh)], in_=XOT[sl][:, :]), semx, 16)
                b["xot_free"][sl] = (semx, p.val(semx))



def build_l2():
    p = Prog()
    for s in ("lx", "lkv0", "lkv1", "lxr0", "lxr1", "sx3a", "sx3b", "lx3", "lssg",
              "sqs0", "sqs1", "sk0", "sk1", "sgp0", "sgp1", "svv0", "svv1",
              "lgp0", "lgp1", "lkk0", "lkk1", "lvv0", "lvv1", "lqq0", "lqq1",
              "sqt0", "sqt1", "soc0", "soc1"):
        p.sem(s)
    x_in = p.din("x", [D, T], F32)
    wqaT = p.din("wqaT", [D, D], F32)
    woaT = p.din("woaT", [D, D], F32)
    wqb_in = p.din("wqb", [D, D], F32)
    wfb_in = p.din("wfb", [D, D], F32)
    wib_in = p.din("wib", [D, D], F32)
    kvt_in = p.din("kvt", [128, CH * T], BF16)
    ssg_in = p.din("ssg", [8, D], F32)
    mpfx_in = p.din("mpfx", [8, 1], F32)
    gate_w = p.din("gate_w", [D, INTER], BF16)
    up_w = p.din("up_w", [D, INTER], BF16)
    dn_w = p.din("dn_w", [INTER, D], BF16)
    ones_bf_in = p.din("ones_bf_in", [128, 1], BF16)
    ones_row_f_in = p.din("ones_row_f_in", [1, 128], F32)
    ident_in = p.din("ident_bf", [128, 128], BF16)
    maskf_in = p.din("maskf", [128, 128], F32)

    out_x3 = p.dout("x3", [128, CH * T], F32)
    out_op = p.dout("opart", [128, CH * T], BF16)
    out_qt = p.dout("qtil", [128, NCHK * D], BF16)
    out_st = p.dout("sstate", [128, NH * 128], F32)
    out_ld = p.dout("logd", [128, NH], F32)
    out_gm = p.dout("gamlog", [128, NH * NCHK], F32)

    wqa_bf = p.dtmp("wqa_bf", [D, D], BF16)
    woa_bf = p.dtmp("woa_bf", [D, D], BF16)
    wqb_bf = p.dtmp("wqb_bf", [D, D], BF16)
    wfb_bf = p.dtmp("wfb_bf", [D, D], BF16)
    wib_bf = p.dtmp("wib_bf", [D, D], BF16)
    sc_x = p.dtmp("sc_x", [128, CH * T], F32)
    sc_qs = p.dtmp("sc_qs", [128, CH * T], BF16)
    sc_k = p.dtmp("sc_k", [128, CH * T], BF16)
    sc_v = p.dtmp("sc_v", [128, CH * T], BF16)
    sc_gp = p.dtmp("sc_gp", [128, CH * T], F32)

    ARE = p.sb("ARE", [128, 49152], BF16)
    X = ARE[:, 0:32768].bitcast(F32)
    XLN = ARE[:, 32768:49152]
    Hbuf = ARE[:, 0:32768]
    SCR = p.sb("SCR", [128, 16384], BF16)
    VHAT = SCR[:, :]
    OFF = p.sb("OFF", [128, CH], F32)
    MPFX = p.sb("MPFX", [8, 1], F32)
    IDE = p.sb("IDE", [128, 128], BF16)
    MASKF = p.sb("MASKF", [128, 128], F32)
    KVR = [p.sb(f"kvr{i}", [128, HW], BF16) for i in range(2)]
    XR = [p.sb(f"xr{i}", [128, HW], F32) for i in range(2)]
    XOT = [p.sb(f"xot{i}", [128, HW], F32) for i in range(2)]
    FE4 = p.sb("FE4", [128, 4 * HW], F32)

    p.start_block()
    b = make_shared(p)
    fill_consts(p, b)

    load_consts(p, b, [
        (b["ones_bf"][:, :], ones_bf_in[:, :]),
        (b["ones_row_f"][:, :], ones_row_f_in[:, :]),
        (IDE[:, :], ident_in[:, :]),
        (MASKF[:, :], maskf_in[:, :]),
        (MPFX[:, :], mpfx_in[:, :]),
    ])

    @p.on("sync")
    def _(sync):
        for ch in range(CH):
            p.inc(sync.dma_start(out=X[:, chs(ch)], in_=x_in[ch * 128:(ch + 1) * 128, :]), "lx", 16)

    emit_ln(p, b, X, XLN, "ln1")

    cast_tok = {}
    for nm, dst, srcw in (("wqa", wqa_bf, wqaT), ("woa", woa_bf, woaT),
                          ("wqb", wqb_bf, wqb_in), ("wfb", wfb_bf, wfb_in),
                          ("wib", wib_bf, wib_in)):
        cast_weight(p, dst, srcw)
        cast_tok[nm] = ("gd", p.val("gd"))

    SSG = SCR[:, 0:4096].bitcast(F32)[0:8, 0:2048]

    @p.on("sync")
    def _(sync):
        p.inc(sync.dma_start(out=SSG, in_=ssg_in[:, :]), "lssg", 16)
    ldssg = p.val("lssg")

    @p.on("tensor")
    def _(t):
        t.wait_ge(p.hnd["lssg"], ldssg)
        for i in range(8):
            wait_tok(t, p, b["bank_free"][i])
        last = None
        for ch in range(CH):
            last = t.matmul(b["banks"][0][:, ch:ch + 1],
                            SSG[0:8, ch * 128:(ch + 1) * 128], MPFX[0:8, 0:1],
                            start=True, stop=True)
        p.inc(last, "mm")
    mmoff = p.val("mm")

    @p.on("vector")
    def _(ve):
        v = Eng(p, ve, "vector")
        v.wait_ge(p.hnd["mm"], mmoff)
        v.cp(OFF[:, :], b["banks"][0][:, 0:CH])
        v.mark("ev")
    b["bank_free"][0] = ("ev", p.val("ev"))

    @p.on("sync")
    def _(sync):
        wait_tok(sync, p, cast_tok["wqa"])

    kvr_free = [None, None]
    for g in range(4):
        ldq, stripq = load_strip(p, b, g % 2, wqa_bf, g * 512, CH)
        mmq = proj_group(p, b, stripq, ldq, XLN, CH, start=True, stop=True)
        b["strip_free"][g % 2] = ("mm", mmq[(3, 1)])

        for oi in range(4):
            ch = g * 4 + oi
            for hh in (0, 1):
                sl = (ch * 2 + hh) % 2

                @p.on("sync")
                def _(sync, ch=ch, hh=hh, sl=sl):
                    wait_tok(sync, p, kvr_free[sl])
                    p.inc(sync.dma_start(out=KVR[sl][:, :], in_=kvt_in[:, tok(ch, hh)]),
                          f"lkv{sl}", 16)
                lkvv = p.val(f"lkv{sl}")

                @p.on("vector")
                def _(ve, ch=ch, oi=oi, hh=hh, sl=sl, mmq=mmq, lkvv=lkvv):
                    v = Eng(p, ve, "vector")
                    v.wait_ge(p.hnd[f"lkv{sl}"], lkvv)
                    v.wait_ge(p.hnd["mm"], mmq[(oi, hh)])
                    v.ts(b["aptmp"][:, :], KVR[sl][:, :], OFF[:, ch:ch + 1], DENOM,
                         ALU.add, ALU.mult)
                    v.tt(VHAT[:, tok(ch, hh)], b["banks"][oi * 2 + hh][:, :],
                         b["aptmp"][:, :], ALU.mult)
                    v.mark("ev")
                b["bank_free"][oi * 2 + hh] = ("ev", p.val("ev"))
                kvr_free[sl] = ("ev", p.val("ev"))

    @p.on("sync")
    def _(sync):
        wait_tok(sync, p, cast_tok["woa"])
    for g in range(4):
        ldo, stripo = load_strip(p, b, g % 2, woa_bf, g * 512, CH)
        mmo = proj_group(p, b, stripo, ldo, VHAT, CH, start=True, stop=True)
        b["strip_free"][g % 2] = ("mm", mmo[(3, 1)])

        @p.on("vector")
        def _(ve, g=g, mmo=mmo):
            v = Eng(p, ve, "vector")
            for oi in range(4):
                oc = g * 4 + oi
                for hh in (0, 1):
                    v.wait_ge(p.hnd["mm"], mmo[(oi, hh)])
                    v.tt(X[:, tok(oc, hh)], b["banks"][oi * 2 + hh][:, :],
                         X[:, tok(oc, hh)], ALU.add)
            v.mark("ev")
        for i in range(8):
            b["bank_free"][i] = ("ev", p.val("ev"))

    emit_ln(p, b, X, XLN, "ln2")

    emit_ffn(p, b, X, XLN, Hbuf, gate_w, up_w, dn_w, sc_x, out_x3, XR, XOT, FE4,
             xstrips=[SCR[:, 0:8192], SCR[:, 8192:16384]])

    fin_ffn = p.snap()

    @p.on("sync")
    def _(sync):
        p.wait(sync, fin_ffn)
        for ch in range(CH):
            p.inc(sync.dma_start(out=X[:, chs(ch)], in_=out_x3[:, chs(ch)]), "lx3", 16)

    emit_ln(p, b, X, XLN, "ln3")

    FEG = SCR[:, 0:8192].bitcast(F32)
    KSTG = SCR[:, 8192:12288]
    QSTG = SCR[:, 12288:13312]
    VSTG = SCR[:, 13312:14336]

    qst_free = [None, None]

    @p.on("sync")
    def _(sync):
        wait_tok(sync, p, cast_tok["wqb"])
    for g in range(4):
        ldq, stripq = load_strip(p, b, g % 2, wqb_bf, g * 512, CH)
        mmq = proj_group(p, b, stripq, ldq, XLN, CH, start=True, stop=True)
        b["strip_free"][g % 2] = ("mm", mmq[(3, 1)])
        for oi in range(4):
            ch = g * 4 + oi
            for hh in (0, 1):
                sl = (ch * 2 + hh) % 2
                hold = {}

                @p.on("scalar")
                def _(se, oi=oi, hh=hh, sl=sl, mmq=mmq, hold=hold):
                    s = Eng(p, se, "scalar")
                    s.wait_ge(p.hnd["mm"], mmq[(oi, hh)])
                    s.wtok(qst_free[sl])
                    s.act(QSTG[:, sl * HW:(sl + 1) * HW], b["banks"][oi * 2 + hh][:, :], AF.Silu)
                    hold["v"] = s.mark("ac")
                b["bank_free"][oi * 2 + hh] = ("ac", hold["v"])
                semq = f"sqs{sl}"

                @p.on("gpsimd")
                def _(gq, ch=ch, hh=hh, sl=sl, hold=hold, semq=semq):
                    gq.wait_ge(p.hnd["ac"], hold["v"])
                    p.inc(gq.dma_start(out=sc_qs[:, tok(ch, hh)],
                                       in_=QSTG[:, sl * HW:(sl + 1) * HW]), semq, 16)
                qst_free[sl] = (semq, p.val(semq))

    kst_free = None
    gst_free = None

    @p.on("sync")
    def _(sync):
        wait_tok(sync, p, cast_tok["wfb"])
    for g in range(4):
        ldf, stripf = load_strip(p, b, g % 2, wfb_bf, g * 512, CH)
        mmf = proj_group(p, b, stripf, ldf, XLN, CH, start=True, stop=True)
        b["strip_free"][g % 2] = ("mm", mmf[(3, 1)])
        holds = {}

        @p.on("scalar")
        def _(se, g=g, mmf=mmf, holds=holds, kst_free=kst_free, gst_free=gst_free):
            s = Eng(p, se, "scalar")
            s.wtok(gst_free)
            s.wtok(kst_free)
            for oi in range(4):
                for hh in (0, 1):
                    j = oi * 2 + hh
                    s.wait_ge(p.hnd["mm"], mmf[(oi, hh)])
                    s.act(FEG[:, j * HW:(j + 1) * HW], b["banks"][j][:, :], AF.Exp, scale=-1.0)
            for oi in range(4):
                for hh in (0, 1):
                    j = oi * 2 + hh
                    s.act(KSTG[:, j * HW:(j + 1) * HW], b["banks"][j][:, :],
                          AF.Sigmoid, scale=-1.0)
                    holds[("k", oi, hh)] = s.mark("ac")
            for oi in range(4):
                for hh in (0, 1):
                    j = oi * 2 + hh
                    s.act(FEG[:, j * HW:(j + 1) * HW], FEG[:, j * HW:(j + 1) * HW],
                          AF.Ln, bias=b["consts"][:, 1:2])
                    holds[("gp", oi, hh)] = s.mark("ac")

        for oi in range(4):
            for hh in (0, 1):
                ch = g * 4 + oi
                j = oi * 2 + hh
                b["bank_free"][j] = ("ac", holds[("k", oi, hh)])
                hk = holds[("k", oi, hh)]
                hg = holds[("gp", oi, hh)]
                semk = f"sk{j % 2}"
                semg = f"sgp{j % 2}"

                @p.on("gpsimd")
                def _(gq, ch=ch, hh=hh, j=j, hk=hk, hg=hg, semk=semk, semg=semg):
                    gq.wait_ge(p.hnd["ac"], hk)
                    p.inc(gq.dma_start(out=sc_k[:, tok(ch, hh)],
                                       in_=KSTG[:, j * HW:(j + 1) * HW]), semk, 16)
                    gq.wait_ge(p.hnd["ac"], hg)
                    p.inc(gq.dma_start(out=sc_gp[:, tok(ch, hh)],
                                       in_=FEG[:, j * HW:(j + 1) * HW]), semg, 16)
        kst_free = ("sk1", p.val("sk1"))
        gst_free = ("sgp1", p.val("sgp1"))
        kst_free2 = ("sk0", p.val("sk0"))
        gst_free2 = ("sgp0", p.val("sgp0"))

        @p.on("scalar")
        def _(se, kst_free2=kst_free2, gst_free2=gst_free2):
            se.wait_ge(p.hnd[kst_free2[0]], kst_free2[1])
            se.wait_ge(p.hnd[gst_free2[0]], gst_free2[1])

    vst_free = [None, None]

    @p.on("sync")
    def _(sync):
        wait_tok(sync, p, cast_tok["wib"])
    for g in range(4):
        ldi, stripi = load_strip(p, b, g % 2, wib_bf, g * 512, CH)
        mmi = proj_group(p, b, stripi, ldi, XLN, CH, start=True, stop=True)
        b["strip_free"][g % 2] = ("mm", mmi[(3, 1)])
        for oi in range(4):
            ch = g * 4 + oi
            for hh in (0, 1):
                sl = (ch * 2 + hh) % 2
                hold = {}

                @p.on("vector")
                def _(ve, oi=oi, hh=hh, sl=sl, mmi=mmi, hold=hold):
                    v = Eng(p, ve, "vector")
                    v.wait_ge(p.hnd["mm"], mmi[(oi, hh)])
                    v.wtok(vst_free[sl])
                    v.cp(VSTG[:, sl * HW:(sl + 1) * HW], b["banks"][oi * 2 + hh][:, :])
                    hold["v"] = v.mark("ev")
                b["bank_free"][oi * 2 + hh] = ("ev", hold["v"])
                semv = f"svv{sl}"

                @p.on("gpsimd")
                def _(gq, ch=ch, hh=hh, sl=sl, hold=hold, semv=semv):
                    gq.wait_ge(p.hnd["ev"], hold["v"])
                    p.inc(gq.dma_start(out=sc_v[:, tok(ch, hh)],
                                       in_=VSTG[:, sl * HW:(sl + 1) * HW]), semv, 16)
                vst_free[sl] = (semv, p.val(semv))

    cva = Carver(ARE[:, :], 98304)
    HST = cva.take([128, NH, 128], F32)
    GPA = cva.take([128, NH, 128], F32)
    GPB = cva.take([128, NH, 128], F32)
    EKT = cva.take([128, NH, 128], F32)
    SUBT = cva.take([128, NH, 128], F32)
    KHAT = cva.take([128, NH, 128], BF16)
    KTT = cva.take([128, NH, 128], BF16)
    VTT = cva.take([128, NH, 128], BF16)
    HBF = cva.take([128, NH, 128], BF16)
    QTC = cva.take([128, NH, 128], BF16)
    KTIL = cva.take([128, NH, 128], BF16)
    QIB = cva.take([128, NH, SUB], BF16)
    ABM = cva.take([128, NH, 128], BF16)
    OCH = cva.take([128, NH, 128], BF16)
    SUBQ = cva.take([128, NH, SUB], F32)
    EQB = cva.take([128, NH, SUB], F32)
    GCT = cva.take([128, NH], F32)
    NGC = cva.take([128, NH], F32)
    EDEC = cva.take([128, NH], F32)
    SGP = cva.take([128, NH], F32)
    GAMT = cva.take([128, NH, NCHK], F32)
    LOGD = cva.take([128, NH], F32)
    QR3 = [cva.take([128, NH, 128], BF16) for _ in range(2)]
    cvs = Carver(SCR[:, :], 32768)
    GPR = [cvs.take([128, NH, 128], F32) for _ in range(2)]
    KR3 = [cvs.take([128, NH, 128], BF16) for _ in range(2)]
    VR3 = [cvs.take([128, NH, 128], BF16) for _ in range(2)]

    glastart = p.snap()

    @p.on("vector")
    def _(ve):
        v = Eng(p, ve, "vector")
        v.wsnap(glastart)
        v.mset(HST[:, :, :], 0.0)
        v.mset(SGP[:, :], 0.0)
        v.mark("ev")
    gla0 = p.val("ev")

    ring_free = {k: [None, None] for k in ("gp", "kk", "vv", "qq")}
    scmap = {"gp": sc_gp, "kk": sc_k, "vv": sc_v, "qq": sc_qs}
    dstmap = {"gp": GPR, "kk": KR3, "vv": VR3, "qq": QR3}

    def load_chunk(m):
        sl = m % 2
        for key in ("gp", "kk", "vv", "qq"):
            @p.on("sync")
            def _(sync, key=key, m=m, sl=sl):
                if m <= 1:
                    p.wait(sync, {"ev": gla0})
                wait_tok(sync, p, ring_free[key][sl])
                src3 = scmap[key].rearrange("p (h t) -> p h t", t=T)[:, :, m * 128:(m + 1) * 128]
                p.inc(sync.dma_start(out=dstmap[key][sl][:, :, :], in_=src3), f"l{key}{sl}", 16)

    load_chunk(0)
    och_free = [None, None]
    qtc_free = None
    prev_gla_mm = 0
    for m in range(NCHK):
        sl = m % 2
        if m + 1 < NCHK:
            load_chunk(m + 1)
        ldv = {k: p.val(f"l{k}{sl}") for k in ("gp", "kk", "vv", "qq")}

        hold = {}

        @p.on("vector")
        def _(ve, m=m, sl=sl, ldv=ldv, prev_gla_mm=prev_gla_mm):
            v = Eng(p, ve, "vector")
            if prev_gla_mm:
                v.wait_ge(p.hnd["mm"], prev_gla_mm)
            for k in ("gp", "kk", "vv", "qq"):
                v.wait_ge(p.hnd[f"l{k}{sl}"], ldv[k])
            src = GPR[sl]
            pp = [GPA, GPB]
            for si, s in enumerate([1, 2, 4, 8, 16, 32, 64]):
                a = src if si == 0 else pp[(si + 1) % 2]
                o = pp[si % 2]
                v.tt(o[:, :, s:128], a[:, :, s:128], a[:, :, 0:128 - s], ALU.add)
                v.cp(o[:, :, 0:s], a[:, :, 0:s])
            v.cp(GCT[:, :].rearrange("p (h o) -> p h o", o=1), GPA[:, :, 127:128])
            v.ts(NGC[:, :], GCT[:, :], -1.0, None, ALU.mult)
            hold["v"] = v.mark("ev")
        evc = p.val("ev")
        gposf = GPA

        hold2 = {}

        @p.on("scalar")
        def _(se):
            s = Eng(p, se, "scalar")
            s.wait_ge(p.hnd["ev"], hold["v"])
            s.act(EDEC[:, :], GCT[:, :], AF.Exp, scale=-1.0)
            for h in range(NH):
                s.act(EKT[:, h, :], gposf[:, h, :], AF.Exp, bias=NGC[:, h:h + 1])
            hold2["v"] = s.mark("ac")

        hold3 = {}

        @p.on("vector")
        def _(ve, sl=sl):
            v = Eng(p, ve, "vector")
            v.wait_ge(p.hnd["ac"], hold2["v"])
            v.tt(KHAT[:, :, :], KR3[sl][:, :, :], EKT[:, :, :], ALU.mult)
            v.cp(HBF[:, :, :], HST[:, :, :])
            hold3["v"] = v.mark("ev")
        evkh = p.val("ev")

        ps_bf = [b["banks"][i][:, :].bitcast(BF16) for i in range(4)]

        @p.on("tensor")
        def _(t, sl=sl):
            t.wait_ge(p.hnd["ev"], evkh)
            for i in range(4):
                wait_tok(t, p, b["bank_free"][i])
            last = None
            for h in range(NH):
                last = t.transpose(ps_bf[h // 8][:, (h % 8) * 128:(h % 8 + 1) * 128],
                                   KHAT[:, h, :], IDE[:, :])
            for h in range(NH):
                last = t.transpose(ps_bf[2 + h // 8][:, (h % 8) * 128:(h % 8 + 1) * 128],
                                   VR3[sl][:, h, :], IDE[:, :])
            p.inc(last, "mm")
        mmtr = p.val("mm")

        @p.on("vector")
        def _(ve):
            v = Eng(p, ve, "vector")
            v.wait_ge(p.hnd["mm"], mmtr)
            v.cp(KTT[:, 0:8, :], ps_bf[0].rearrange("p (h t) -> p h t", h=8))
            v.cp(KTT[:, 8:16, :], ps_bf[1].rearrange("p (h t) -> p h t", h=8))
            v.cp(VTT[:, 0:8, :], ps_bf[2].rearrange("p (h t) -> p h t", h=8))
            v.cp(VTT[:, 8:16, :], ps_bf[3].rearrange("p (h t) -> p h t", h=8))
            v.mark("ev")
        evtt = p.val("ev")
        for i in range(4):
            b["bank_free"][i] = ("ev", evtt)

        @p.on("tensor")
        def _(t):
            t.wait_ge(p.hnd["ev"], evtt)
            for i in range(4, 8):
                wait_tok(t, p, b["bank_free"][i])
            last = None
            for h in range(NH):
                ps = b["banks"][4 + h // 4]
                last = t.matmul(ps[:, (h % 4) * 128:(h % 4 + 1) * 128],
                                KTT[:, h, :], VTT[:, h, :], start=True, stop=True,
                                skip_group_check=True)
            p.inc(last, "mm")
        mmkv = p.val("mm")

        @p.on("vector")
        def _(ve):
            v = Eng(p, ve, "vector")
            v.wait_ge(p.hnd["mm"], mmkv)
            for h in range(NH):
                ps = b["banks"][4 + h // 4]
                v.ts(SUBT[:, h, :], HST[:, h, :], EDEC[:, h:h + 1], None, ALU.mult)
                v.tt(HST[:, h, :], SUBT[:, h, :], ps[:, (h % 4) * 128:(h % 4 + 1) * 128], ALU.add)
            v.mark("ev")
        evst = p.val("ev")
        for i in range(4, 8):
            b["bank_free"][i] = ("ev", evst)

        hold6 = {}

        @p.on("scalar")
        def _(se):
            s = Eng(p, se, "scalar")
            s.wait_ge(p.hnd["ev"], evst)
            s.act(EKT[:, :, :], gposf[:, :, :], AF.Exp, scale=-1.0, bias=b["consts"][:, 2:3])
            hold6["v"] = s.mark("ac")

        @p.on("vector")
        def _(ve, sl=sl, qtc_free=qtc_free):
            v = Eng(p, ve, "vector")
            v.wait_ge(p.hnd["ac"], hold6["v"])
            v.wtok(qtc_free)
            v.tt(QTC[:, :, :], QR3[sl][:, :, :], EKT[:, :, :], ALU.mult)
            v.mark("ev")
        evqt = p.val("ev")
        semqt = f"sqt{m % 2}"

        @p.on("gpsimd")
        def _(gq, m=m, semqt=semqt):
            gq.wait_ge(p.hnd["ev"], evqt)
            p.inc(gq.dma_start(out=out_qt[:, m * D:(m + 1) * D],
                               in_=QTC[:, :, :].rearrange("p h t -> p (h t)")), semqt, 16)
        qtc_free = (semqt, p.val(semqt))

        prev_amm = {}
        for I in range(NSUB):
            endi = (I + 1) * SUB

            @p.on("vector")
            def _(ve, I=I, endi=endi):
                v = Eng(p, ve, "vector")
                if I > 0:
                    v.wait_ge(p.hnd["mm"], prev_amm[I - 1])
                else:
                    v.wait_ge(p.hnd["ev"], evqt)
                if I > 0:
                    gb = gposf[:, :, I * SUB - 1:I * SUB]
                    v.tt(SUBT[:, :, 0:endi], gposf[:, :, 0:endi],
                         gb.broadcast_to([128, NH, endi]), ALU.subtract)
                    v.tt(SUBQ[:, :, :], gb.broadcast_to([128, NH, SUB]),
                         gposf[:, :, I * SUB:(I + 1) * SUB], ALU.subtract)
                    v.mark("ev")
            evsub = p.val("ev")

            @p.on("scalar")
            def _(se, I=I, endi=endi):
                s = Eng(p, se, "scalar")
                s.wait_ge(p.hnd["ev"], evsub)
                if I == 0:
                    s.act(EKT[:, :, 0:endi], gposf[:, :, 0:endi], AF.Exp)
                    s.act(EQB[:, :, :], gposf[:, :, 0:SUB], AF.Exp, scale=-1.0,
                          bias=b["consts"][:, 2:3])
                else:
                    s.act(EKT[:, :, 0:endi], SUBT[:, :, 0:endi], AF.Exp)
                    s.act(EQB[:, :, :], SUBQ[:, :, :], AF.Exp, bias=b["consts"][:, 2:3])
                s.mark("ac")
            evek = p.val("ac")

            @p.on("vector")
            def _(ve, I=I, endi=endi, sl=sl):
                v = Eng(p, ve, "vector")
                v.wait_ge(p.hnd["ac"], evek)
                v.tt(KTIL[:, :, 0:endi], KR3[sl][:, :, 0:endi], EKT[:, :, 0:endi], ALU.mult)
                v.tt(QIB[:, :, :], QR3[sl][:, :, I * SUB:(I + 1) * SUB], EQB[:, :, :], ALU.mult)
                v.mark("ev")
            evqi = p.val("ev")

            @p.on("tensor")
            def _(t, I=I, endi=endi):
                t.wait_ge(p.hnd["ev"], evqi)
                if I == 0:
                    for i in (2, 3, 4, 5):
                        wait_tok(t, p, b["bank_free"][i])
                last = None
                for h in range(NH):
                    ab = b["banks"][2 + h // 4]
                    last = t.matmul(ab[0:endi, (h % 4) * 128 + I * SUB:(h % 4) * 128 + (I + 1) * SUB],
                                    KTIL[:, h, 0:endi], QIB[:, h, :],
                                    start=True, stop=True, skip_group_check=True)
                prev_amm[I] = p.inc(last, "mm")
        mmA = p.val("mm")

        @p.on("vector")
        def _(ve):
            v = Eng(p, ve, "vector")
            v.wait_ge(p.hnd["mm"], mmA)
            for h in range(NH):
                ab = b["banks"][2 + h // 4]
                v.tt(ABM[:, h, :], ab[:, (h % 4) * 128:(h % 4 + 1) * 128], MASKF[:, :], ALU.mult)
            v.mark("ev")
        evab = p.val("ev")
        for i in (2, 3, 4, 5):
            b["bank_free"][i] = ("ev", evab)


        oevs = {}
        for h in range(NH):
            ob = b["banks"][6 + h % 2]

            @p.on("tensor")
            def _(t, h=h, ob=ob, m=m):
                if h == 0:
                    t.wait_ge(p.hnd["ev"], evab)
                    for i in (6, 7):
                        wait_tok(t, p, b["bank_free"][i])
                if h >= 2:
                    t.wait_ge(p.hnd["ev"], oevs[h - 2])
                last = t.matmul(ob[:, 0:128], HBF[:, h, :], QTC[:, h, :],
                                start=True, stop=False, skip_group_check=True)
                for I in range(NSUB):
                    endi = (I + 1) * SUB
                    last = t.matmul(ob[:, I * SUB:(I + 1) * SUB],
                                    VTT[0:endi, h, :], ABM[0:endi, h, I * SUB:(I + 1) * SUB],
                                    start=False, stop=(I == NSUB - 1), skip_group_check=True)
                p.inc(last, "mm")
            mmoh = p.val("mm")

            @p.on("vector")
            def _(ve, h=h, ob=ob, mmoh=mmoh, m=m):
                v = Eng(p, ve, "vector")
                v.wait_ge(p.hnd["mm"], mmoh)
                if h <= 1:
                    v.wtok(och_free[m % 2])
                v.cp(OCH[:, h, :], ob[:, 0:128])
                oevs[h] = v.mark("ev")
        evo = p.val("ev")
        b["bank_free"][6] = ("ev", evo)
        b["bank_free"][7] = ("ev", evo)
        prev_gla_mm = p.val("mm")

        semoc = f"soc{m % 2}"

        @p.on("gpsimd")
        def _(gq, m=m, semoc=semoc):
            gq.wait_ge(p.hnd["ev"], evo)
            dst3 = out_op.rearrange("p (h t) -> p h t", t=T)[:, :, m * 128:(m + 1) * 128]
            p.inc(gq.dma_start(out=dst3, in_=OCH[:, :, :]), semoc, 16)
        och_free[m % 2] = (semoc, p.val(semoc))

        @p.on("vector")
        def _(ve, m=m):
            v = Eng(p, ve, "vector")
            v.wait_ge(p.hnd["ev"], evc)
            v.ts(GAMT[:, :, m:m + 1].rearrange("p h o -> p (h o)"), SGP[:, :], -1.0,
                 None, ALU.mult)
            v.tt(SGP[:, :], SGP[:, :], GCT[:, :], ALU.add)
            v.mark("ev")
        endchunk = p.val("ev")
        for k in ("gp", "kk", "vv", "qq"):
            ring_free[k][sl] = ("ev", endchunk)

    @p.on("vector")
    def _(ve):
        v = Eng(p, ve, "vector")
        v.ts(LOGD[:, :], SGP[:, :], -1.0, None, ALU.mult)
        v.mark("ev")

    fin = p.snap()

    @p.on("gpsimd")
    def _(g):
        p.wait(g, fin)
        p.inc(g.dma_start(out=out_st[:, :], in_=HST[:, :, :].rearrange("p h t -> p (h t)")), "st", 16)
        p.inc(g.dma_start(out=out_ld[:, :], in_=LOGD[:, :]), "st", 16)
        p.inc(g.dma_start(out=out_gm[:, :], in_=GAMT[:, :, :].rearrange("p h t -> p (h t)")), "st", 16)
        g.wait_ge(p.hnd["st"], p.val("st"))

    return p


def _l2_host(inputs, l1res):
    x = np.asarray(inputs["L0_attn_a_x"], dtype=np.float32)
    xs = _fm_slices(x)
    wqaT = np.ascontiguousarray(np.asarray(inputs["L0_attn_a_Wq"], np.float32).T)
    woaT = np.ascontiguousarray(np.asarray(inputs["L0_attn_a_Wo"], np.float32).T)
    wqb = np.ascontiguousarray(np.asarray(inputs["L0_attn_b_Wq"], np.float32))
    wfb = np.ascontiguousarray(np.asarray(inputs["L0_attn_b_Wf"], np.float32))
    wib = np.ascontiguousarray(np.asarray(inputs["L0_attn_b_Wi"], np.float32))
    ssg = np.stack([np.ascontiguousarray(l1res[j]["slicesum"]).T.reshape(D)
                    for j in range(NCORE)]).astype(np.float32)
    gate_full = np.ascontiguousarray(np.concatenate(
        [np.asarray(l1res[c]["recon_gu"])[:256] for c in range(NCORE)], 0))
    up_full = np.ascontiguousarray(np.concatenate(
        [np.asarray(l1res[c]["recon_gu"])[256:512] for c in range(NCORE)], 0))
    dn_full = np.ascontiguousarray(np.concatenate(
        [np.asarray(l1res[c]["recon_dn"]) for c in range(NCORE)], 0))
    _L23_CACHE.update(gate_full=gate_full, up_full=up_full, dn_full=dn_full)
    maskf = np.zeros((128, 128), np.float32)
    for tau in range(128):
        for t in range(128):
            if tau <= t:
                maskf[tau, t] = 1.0
    ident = np.eye(128, dtype=ml_dtypes.bfloat16)
    cst = _consts()
    in_maps = []
    for c in range(NCORE):
        mpfx = (np.arange(8) < c).astype(np.float32).reshape(8, 1)
        m = dict(x=xs[c], wqaT=wqaT, woaT=woaT, wqb=wqb, wfb=wfb, wib=wib,
                 kvt=np.ascontiguousarray(np.asarray(l1res[c]["kvt"])),
                 ssg=ssg, mpfx=mpfx, gate_w=gate_full, up_w=up_full, dn_w=dn_full,
                 ident_bf=ident, maskf=maskf, **cst)
        in_maps.append(m)
    return in_maps


def run_l2(inputs, l1res):
    in_maps = _l2_host(inputs, l1res)
    prog = build_l2()
    res = run_bass_kernel_spmd(prog.nc, in_maps, core_ids=list(range(NCORE)))
    return res.results



def build_l3():
    p = Prog()
    for s in ("lx", "lxr0", "lxr1", "sx3a", "sx3b", "lld", "lsst0", "lsst1",
              "lqt0", "lqt1", "lgmt", "lop"):
        p.sem(s)
    x3_in = p.din("x3", [128, CH * T], F32)
    op_in = p.din("opart", [128, CH * T], BF16)
    qt_in = p.din("qtil", [128, NCHK * D], BF16)
    sall_in = p.din("sall", [8 * 128, NH * 128], F32)
    ldall_in = p.din("ldall", [8, D], F32)
    wscanT_in = p.din("wscanT", [8, 8], F32)
    biarep_in = p.din("biarep", [128, 128], F32)
    gamlog_in = p.din("gamlog", [128, NH * NCHK], F32)
    gw_in = p.din("gw", [128, NH], F32)
    wob_in = p.din("wob", [D, D], F32)
    gate_w = p.din("gate_w", [D, INTER], BF16)
    up_w = p.din("up_w", [D, INTER], BF16)
    dn_w = p.din("dn_w", [INTER, D], BF16)
    ones_bf_in = p.din("ones_bf_in", [128, 1], BF16)
    ones_row_f_in = p.din("ones_row_f_in", [1, 128], F32)

    out_y = p.dout("y", [128, CH * T], F32)

    wob_bf = p.dtmp("wob_bf", [D, D], BF16)
    sc_x = p.dtmp("sc_x", [128, CH * T], F32)

    ARE = p.sb("ARE", [128, 49152], BF16)
    X = ARE[:, 0:32768].bitcast(F32)
    XLN = ARE[:, 32768:49152]
    Hbuf = ARE[:, 0:32768]
    SCR = p.sb("SCR", [128, 16384], BF16)
    OBUF = SCR[:, :]
    HIN = XLN[:, 0:4096].bitcast(F32)
    LDA = XLN[:, 4096:8192].bitcast(F32)[0:8, 0:2048]
    SSTR = [XLN[:, 8192 + i * 4096: 8192 + (i + 1) * 4096].bitcast(F32) for i in range(2)]
    QTR = [XLN[:, 8192 + i * 2048: 8192 + (i + 1) * 2048] for i in range(2)]
    XR = [p.sb(f"xr{i}", [128, HW], F32) for i in range(2)]
    XOT = [p.sb(f"xot{i}", [128, HW], F32) for i in range(2)]
    FE4 = p.sb("FE4", [128, 4 * HW], F32)
    WSC = p.sb("WSC", [8, 8], F32)
    BIA = p.sb("BIA", [128, 128], F32)
    GMT = p.sb("GMT", [128, 128], F32)
    GW = p.sb("GW", [128, NH], F32)
    COEF = p.sb("COEF", [128, 128], F32)
    HCB = p.sb("HCB", [128, 128], BF16)
    EG = p.sb("EG", [128, NH], F32)

    p.start_block()
    b = make_shared(p)
    fill_consts(p, b)

    load_consts(p, b, [
        (b["ones_bf"][:, :], ones_bf_in[:, :]),
        (b["ones_row_f"][:, :], ones_row_f_in[:, :]),
        (WSC[:, :], wscanT_in[:, :]),
        (BIA[:, :], biarep_in[:, :]),
        (GMT[:, :], gamlog_in[:, :]),
        (GW[:, :], gw_in[:, :]),
    ])

    @p.on("sync")
    def _(sync):
        for ch in range(CH):
            p.inc(sync.dma_start(out=X[:, chs(ch)], in_=x3_in[:, chs(ch)]), "lx", 16)
        p.inc(sync.dma_start(out=LDA, in_=ldall_in[:, :]), "lld", 16)

    @p.on("gpsimd")
    def _(g):
        p.inc(g.dma_start(out=OBUF[:, :], in_=op_in[:, :]), "lop", 16)
    cast_weight(p, wob_bf, wob_in)
    cast_done = p.val("gd")
    lop_done = p.val("lop")

    lld = p.val("lld")
    lxv = p.val("lx")

    @p.on("tensor")
    def _(t):
        t.wait_ge(p.hnd["lld"], lld)
        t.wait_ge(p.hnd["lx"], lxv)
        p.wait(t, {"ev": p.val("ev"), "ac": p.val("ac")})
        last = None
        for h in range(NH):
            last = t.matmul(b["banks"][0][:, h * 8:(h + 1) * 8],
                            LDA[0:8, h * 128:(h + 1) * 128], WSC[0:8, 0:8],
                            start=True, stop=True, skip_group_check=True)
        p.inc(last, "mm")
    mme = p.val("mm")

    hold = {}

    @p.on("vector")
    def _(ve):
        v = Eng(p, ve, "vector")
        v.wait_ge(p.hnd["mm"], mme)
        v.tt(COEF[:, :], b["banks"][0][:, 0:128], BIA[:, :], ALU.add)
        hold["v"] = v.mark("ev")
    b["bank_free"][0] = ("ev", p.val("ev"))

    @p.on("scalar")
    def _(se):
        s = Eng(p, se, "scalar")
        s.wait_ge(p.hnd["ev"], hold["v"])
        s.act(COEF[:, :], COEF[:, :], AF.Exp)
        s.mark("ac")
    ace = p.val("ac")

    sst_free = [None, None]
    for j in range(NCORE):
        sl = j % 2

        @p.on("sync")
        def _(sync, j=j, sl=sl):
            wait_tok(sync, p, sst_free[sl])
            p.inc(sync.dma_start(out=SSTR[sl][:, :],
                                 in_=sall_in[j * 128:(j + 1) * 128, :]), f"lsst{sl}", 16)
        lss = p.val(f"lsst{sl}")

        @p.on("vector")
        def _(ve, j=j, sl=sl, lss=lss):
            v = Eng(p, ve, "vector")
            v.wait_ge(p.hnd[f"lsst{sl}"], lss)
            if j == 0:
                v.wait_ge(p.hnd["ac"], ace)
            for h in range(NH):
                hsx = slice(h * 128, (h + 1) * 128)
                v.ts(b["aptmp"][:, 0:128], SSTR[sl][:, hsx], COEF[:, h * 8 + j:h * 8 + j + 1],
                     None, ALU.mult)
                if j == 0:
                    v.cp(HIN[:, hsx], b["aptmp"][:, 0:128])
                else:
                    v.tt(HIN[:, hsx], HIN[:, hsx], b["aptmp"][:, 0:128], ALU.add)
            v.mark("ev")
        sst_free[sl] = ("ev", p.val("ev"))
    hin_done = p.val("ev")

    qtr_free = [None, None]
    prev_corr_mm = 0
    for m in range(NCHK):
        sl = m % 2

        @p.on("sync")
        def _(sync, m=m, sl=sl):
            if m <= 1:
                p.wait(sync, {"ev": hin_done})
            wait_tok(sync, p, qtr_free[sl])
            p.inc(sync.dma_start(out=QTR[sl][:, :], in_=qt_in[:, m * D:(m + 1) * D]),
                  f"lqt{sl}", 16)
        lqt = p.val(f"lqt{sl}")

        holdg = {}

        @p.on("scalar")
        def _(se, m=m, prev_hc=prev_corr_mm):
            s = Eng(p, se, "scalar")
            if m == 0:
                s.wait_ge(p.hnd["ev"], hin_done)
            else:
                s.wait_ge(p.hnd["ev"], prev_hc)
            g3 = GMT[:, :].rearrange("p (h mm) -> p h mm", mm=NCHK)
            s.act(EG[:, :].rearrange("p (h o) -> p h o", o=1), g3[:, :, m:m + 1], AF.Exp)
            holdg["v"] = s.mark("ac")

        bank0 = (m % 2) * 4
        mmc = {}
        evcs = {}
        for h in range(NH):
            holdh = {}

            @p.on("vector")
            def _(ve, h=h, m=m, holdg=holdg):
                v = Eng(p, ve, "vector")
                if h == 0:
                    v.wait_ge(p.hnd["ac"], holdg["v"])
                    if m == 0:
                        v.wait_ge(p.hnd["lop"], lop_done)
                if h >= 1:
                    v.wait_ge(p.hnd["mm"], mmc[h - 1])
                v.ts(HCB[:, :], HIN[:, h * 128:(h + 1) * 128], EG[:, h:h + 1], None, ALU.mult)
                holdh["v"] = v.mark("ev")
            evh = p.val("ev")

            ps = b["banks"][bank0 + h // 4]

            @p.on("tensor")
            def _(t, h=h, sl=sl, evh=evh, ps=ps, lqt=lqt):
                t.wait_ge(p.hnd["ev"], evh)
                if h == 0:
                    t.wait_ge(p.hnd["lqt" + str(sl)], lqt)
                    for i in range(bank0, bank0 + 4):
                        wait_tok(t, p, b["bank_free"][i])
                last = t.matmul(ps[:, (h % 4) * 128:(h % 4 + 1) * 128],
                                HCB[:, :], QTR[sl][:, h * 128:(h + 1) * 128],
                                start=True, stop=True, skip_group_check=True)
                mmc[h] = p.inc(last, "mm")

            @p.on("vector")
            def _(ve, h=h, m=m, ps=ps):
                v = Eng(p, ve, "vector")
                v.wait_ge(p.hnd["mm"], mmc[h])
                osl = slice(h * T + m * 128, h * T + (m + 1) * 128)
                v.tt(OBUF[:, osl], OBUF[:, osl], ps[:, (h % 4) * 128:(h % 4 + 1) * 128], ALU.add)
                evcs[h] = v.mark("ev")
        evchunk = p.val("ev")
        for i in range(bank0, bank0 + 4):
            b["bank_free"][i] = ("ev", evchunk)
        qtr_free[sl] = ("ev", evchunk)
        prev_corr_mm = evchunk

    corr_done = p.val("ev")
    ps_sq = [b["banks"][0], b["banks"][1]]
    mm_by_h = {}
    for h in range(NH):
        holds = {}

        @p.on("vector")
        def _(ve, h=h):
            v = Eng(p, ve, "vector")
            if h == 0:
                v.wait_ge(p.hnd["ev"], corr_done)
            if h >= 1:
                v.wait_ge(p.hnd["mm"], mm_by_h[h - 1])
            v.tt(b["xbf"][0][:, :], OBUF[:, chs(h)], OBUF[:, chs(h)], ALU.mult)
            holds["v"] = v.mark("ev")

        @p.on("tensor")
        def _(t, h=h):
            if h == 0:
                for i in (0, 1):
                    wait_tok(t, p, b["bank_free"][i])
            t.wait_ge(p.hnd["ev"], holds["v"])
            last = None
            for hh in (0, 1):
                last = t.matmul(ps_sq[hh][0:1, :], b["ones_bf"][:, :],
                                b["xbf"][0][:, hsl(hh)],
                                start=(h == 0), stop=(h == NH - 1))
            mm_by_h[h] = p.inc(last, "mm")
    mm_sq = p.val("mm")

    hold2 = {}

    @p.on("vector")
    def _(ve):
        v = Eng(p, ve, "vector")
        v.wait_ge(p.hnd["mm"], mm_sq)
        for hh in (0, 1):
            v.cp(row(b, 0)[:, hsl(hh)], ps_sq[hh][0:1, :])
        v.ts(row(b, 0), row(b, 0), 1.0 / D, None, ALU.mult)
        hold2["v"] = v.mark("ev")

    hold3 = {}

    @p.on("scalar")
    def _(se):
        s = Eng(p, se, "scalar")
        s.wait_ge(p.hnd["ev"], hold2["v"])
        s.act(row(b, 2), row(b, 0), AF.Ln, bias=b["consts"][0:1, 0:1])
        s.act(row(b, 1), row(b, 2), AF.Exp, scale=-0.5)
        hold3["v"] = s.mark("ac")

    @p.on("tensor")
    def _(t):
        t.wait_ge(p.hnd["ac"], hold3["v"])
        for i in (2, 3):
            wait_tok(t, p, b["bank_free"][i])
        last = None
        for hh in (0, 1):
            last = t.matmul(b["banks"][2 + hh][:, :], b["ones_row_f"][:, :],
                            row(b, 1)[0:1, hsl(hh)], start=True, stop=True)
        p.inc(last, "mm")
    mmrb = p.val("mm")

    @p.on("vector")
    def _(ve):
        v = Eng(p, ve, "vector")
        v.wait_ge(p.hnd["mm"], mmrb)
        for hh in (0, 1):
            v.cp(b["rb"][:, hsl(hh)], b["banks"][2 + hh][:, :])
        for h in range(NH):
            for hh in (0, 1):
                v.ts(b["aptmp"][:, :], OBUF[:, tok(h, hh)], GW[:, h:h + 1], None, ALU.mult)
                v.tt(OBUF[:, tok(h, hh)], b["aptmp"][:, :], b["rb"][:, hsl(hh)], ALU.mult)
        v.mark("ev")
    nrm_done = p.val("ev")
    for i in range(8):
        b["bank_free"][i] = ("ev", nrm_done)

    @p.on("sync")
    def _(sync):
        sync.wait_ge(p.hnd["gd"], cast_done)

    for g in range(4):
        ldo, stripo = load_strip(p, b, g % 2, wob_bf, g * 512, CH)
        mmo = proj_group(p, b, stripo, ldo, OBUF, CH, start=True, stop=True)
        b["strip_free"][g % 2] = ("mm", mmo[(3, 1)])

        @p.on("vector")
        def _(ve, g=g, mmo=mmo):
            v = Eng(p, ve, "vector")
            for oi in range(4):
                oc = g * 4 + oi
                for hh in (0, 1):
                    v.wait_ge(p.hnd["mm"], mmo[(oi, hh)])
                    v.tt(X[:, tok(oc, hh)], b["banks"][oi * 2 + hh][:, :],
                         X[:, tok(oc, hh)], ALU.add)
            v.mark("ev")
        for i in range(8):
            b["bank_free"][i] = ("ev", p.val("ev"))

    emit_ln(p, b, X, XLN, "ln4")

    emit_ffn(p, b, X, XLN, Hbuf, gate_w, up_w, dn_w, sc_x, out_y, XR, XOT, FE4)

    fin = p.snap()

    @p.on("gpsimd")
    def _(g):
        p.wait(g, fin)

    return p


def _l3_host(inputs, l2res):
    gate_full = _L23_CACHE["gate_full"]
    up_full = _L23_CACHE["up_full"]
    dn_full = _L23_CACHE["dn_full"]
    ldall = np.stack([np.ascontiguousarray(np.asarray(l2res[j]["logd"])).T.reshape(D)
                      for j in range(NCORE)]).astype(np.float32)
    sall = np.ascontiguousarray(np.concatenate(
        [np.asarray(l2res[j]["sstate"]) for j in range(NCORE)], 0))
    gwv = np.asarray(inputs["L0_attn_b_g_weight"], np.float32)
    gw = np.ascontiguousarray(gwv.reshape(NH, 128).T)
    wob = np.ascontiguousarray(np.asarray(inputs["L0_attn_b_Wo"], np.float32))
    cst = _consts()
    in_maps = []
    for c in range(NCORE):
        wscanT = np.zeros((8, 8), np.float32)
        for j in range(8):
            for i in range(8):
                if j < i < c:
                    wscanT[i, j] = 1.0
        bia = np.where(np.arange(8) < c, 0.0, -1e30).astype(np.float32)
        biarep = np.tile(bia, 16)[None, :].repeat(128, 0)
        r = l2res[c]
        m = dict(x3=np.ascontiguousarray(np.asarray(r["x3"])),
                 opart=np.ascontiguousarray(np.asarray(r["opart"])),
                 qtil=np.ascontiguousarray(np.asarray(r["qtil"])),
                 sall=sall, ldall=ldall, wscanT=wscanT, biarep=biarep,
                 gamlog=np.ascontiguousarray(np.asarray(r["gamlog"])),
                 gw=gw, wob=wob, gate_w=gate_full, up_w=up_full, dn_w=dn_full,
                 **cst)
        in_maps.append(m)
    return in_maps


def run_l3(inputs, l2res):
    in_maps = _l3_host(inputs, l2res)
    prog = build_l3()
    res = run_bass_kernel_spmd(prog.nc, in_maps, core_ids=list(range(NCORE)))
    return res.results


def kernel(**inputs):
    l1res = run_l1(inputs)
    l2res = run_l2(inputs, l1res)
    l3res = run_l3(inputs, l2res)
    outs = []
    for c in range(NCORE):
        y = np.asarray(l3res[c]["y"], np.float32)
        outs.append(y.reshape(128, CH, T).transpose(2, 1, 0).reshape(T, D))
    return np.concatenate(outs, 0).reshape(1, SEQ, D).astype(np.float32)


# version 04
# speedup vs baseline: 1.2589x; 1.0603x over previous
import contextlib
import math

import numpy as np
import ml_dtypes

import concourse.bass as bass
import concourse.mybir as mybir
from concourse.bass_utils import run_bass_kernel_spmd

F32 = mybir.dt.float32
BF16 = mybir.dt.bfloat16
AF = mybir.ActivationFunctionType
ALU = mybir.AluOpType
AX = mybir.AxisListType

NCORE = 8
D = 2048
SEQ = 8192
T = SEQ // NCORE
HW = 512
CH = D // 128
INTER = 8192
ICH = INTER // 128
NH, DK, DV = 16, 128, 128
CHUNK = 128
NCHK = T // CHUNK
SUB = 16
NSUB = CHUNK // SUB
EPS = 1e-5
DENOM = math.sqrt(2048.0) / 8192.0
LN_C = math.log(DK ** -0.5)



class Prog:

    def __init__(self):
        self.nc = bass.Bass("TRN2", target_bir_lowering=False,
                            detect_race_conditions=False)
        self.ctx = contextlib.ExitStack()
        self.hnd = {}
        self.cnt = {}
        self.block = None
        self.haz = {k: {"w": [], "r": [], "last": None}
                    for k in ("vector", "scalar", "gpsimd")}
        for s in ("gd", "mm", "ev", "ac", "st", "vsy", "ssy", "gsy"):
            self.sem(s)

    def sem(self, name):
        if name not in self.hnd:
            self.hnd[name] = self.ctx.enter_context(self.nc.semaphore(name))
            self.cnt[name] = 0
        return self.hnd[name]

    def inc(self, inst, name, k=1):
        inst.then_inc(self.hnd[name], k)
        self.cnt[name] += k
        return self.cnt[name]

    def val(self, name):
        return self.cnt[name]

    def start_block(self):
        self.block = self.ctx.enter_context(self.nc.Block())

    def on(self, eng):
        def deco(fn):
            getattr(self.block, eng)(fn)
        return deco

    def snap(self):
        return dict(self.cnt)

    def wait(self, eng, snap):
        for name, v in snap.items():
            if v > 0:
                eng.wait_ge(self.hnd[name], v)

    def sb(self, name, shape, dt):
        return self.ctx.enter_context(self.nc.sbuf_tensor(name, list(shape), dt))

    def psb(self, name, shape, dt):
        return self.ctx.enter_context(self.nc.psum_tensor(name, list(shape), dt))

    def din(self, name, shape, dt):
        return self.nc.dram_tensor(name, list(shape), dt, kind="ExternalInput").ap()

    def dout(self, name, shape, dt):
        return self.nc.dram_tensor(name, list(shape), dt, kind="ExternalOutput").ap()

    def dtmp(self, name, shape, dt):
        return self.nc.dram_tensor(name, list(shape), dt).ap()


_AUTOSEM = {"vector": "vsy", "scalar": "ssy", "gpsimd": "gsy"}


def _region(a):
    try:
        name = a.tensor.name
        es = mybir.dt.size(a.dtype)
        apl = [(int(s), int(c)) for s, c in a.ap]
    except (AttributeError, TypeError):
        return None
    lo = int(a.offset) * es
    if len(apl) > 1:
        skip = max(range(len(apl)), key=lambda i: apl[i][0])
    else:
        skip = -1
    span = es
    for i, (s, c) in enumerate(apl):
        if i != skip:
            span += es * abs(s) * (c - 1)
    return (name, lo, lo + span)


def _overlap(r1, r2):
    return r1[0] == r2[0] and r1[1] < r2[2] and r2[1] < r1[2]


class Eng:

    def __init__(self, p, e, kind):
        self.p, self.e, self.kind = p, e, kind
        self.h = p.haz[kind]

    def _sync(self):
        h = self.h
        if h["last"] is not None:
            inst, sname, sval = h["last"]
            if sname is None:
                sname = _AUTOSEM[self.kind]
                sval = self.p.inc(inst, sname)
                h["last"] = (inst, sname, sval)
            self.e.wait_ge(self.p.hnd[sname], sval)
        h["w"].clear()
        h["r"].clear()

    def _op(self, call, wr, rd):
        h = self.h
        wn = [r for r in (_region(a) for a in wr) if r]
        rn = [r for r in (_region(a) for a in rd) if r]
        conflict = (
            any(_overlap(r, w) for r in rn for w in h["w"])
            or any(_overlap(w1, w2) for w1 in wn for w2 in h["w"])
            or any(_overlap(w, r) for w in wn for r in h["r"])
        )
        if conflict:
            self._sync()
        inst = call()
        h["w"].extend(wn)
        h["r"].extend(rn)
        h["last"] = (inst, None, None)
        return inst

    def mark(self, sem, k=1):
        h = self.h
        inst, sname, sval = h["last"]
        assert sname is None, "op already has an inc"
        n = self.p.inc(inst, sem, k)
        h["last"] = (inst, sem, n)
        return n

    def barrier(self):
        self._sync()

    def wait_ge(self, semh, v):
        self.e.wait_ge(semh, v)

    def wtok(self, t):
        if t is not None:
            self.e.wait_ge(self.p.hnd[t[0]], t[1])

    def wsnap(self, snap):
        self.p.wait(self.e, snap)

    def tt(self, out, a, b2, op):
        return self._op(lambda: self.e.tensor_tensor(out, a, b2, op), [out], [a, b2])

    def ts(self, out, a, s1, s2, op0, op1=None):
        rd = [a] + [s for s in (s1, s2) if hasattr(s, "tensor")]
        if op1 is None:
            return self._op(lambda: self.e.tensor_scalar(out, a, s1, s2, op0), [out], rd)
        return self._op(lambda: self.e.tensor_scalar(out, a, s1, s2, op0, op1), [out], rd)

    def cp(self, out, a):
        return self._op(lambda: self.e.tensor_copy(out, a), [out], [a])

    def red(self, out, a, op):
        return self._op(lambda: self.e.tensor_reduce(out, a, axis=AX.X, op=op), [out], [a])

    def mset(self, out, v):
        return self._op(lambda: self.e.memset(out, v), [out], [])

    def act(self, out, a, func, bias=None, scale=None, accum_out=None):
        kw = {}
        if bias is not None:
            kw["bias"] = bias
        if scale is not None:
            kw["scale"] = scale
        if accum_out is not None:
            kw["accum_out"] = accum_out
        rd = [a] + [x for x in (bias, scale) if hasattr(x, "tensor")]
        wr = [out] + ([accum_out] if accum_out is not None else [])
        return self._op(lambda: self.e.activation(out, a, func, **kw), wr, rd)


def chs(ch, w=T):
    return slice(ch * w, (ch + 1) * w)


def tok(ch, hh, w=T):
    return slice(ch * w + hh * HW, ch * w + (hh + 1) * HW)


def hsl(hh):
    return slice(hh * HW, (hh + 1) * HW)


def wait_tok(eng, p, t):
    if t is not None:
        eng.wait_ge(p.hnd[t[0]], t[1])


def make_shared(p):
    b = {}
    b["banks"] = [p.psb(f"pb{i}", [128, 512], F32) for i in range(8)]
    b["bank_free"] = [None] * 8
    b["ones_bf"] = p.sb("ones_bf", [128, 1], BF16)
    b["ones_row_f"] = p.sb("ones_row_f", [1, 128], F32)
    b["xbf"] = [p.sb("xbf0", [128, T], BF16)]
    b["xsqb"] = [p.sb("xsqb0", [128, T], BF16)]
    b["rows"] = p.sb("rows", [1, 3 * T], F32)
    b["mb"] = p.sb("mbb", [128, T], F32)
    b["rb"] = p.sb("rbb", [128, T], F32)
    b["aptmp"] = p.sb("aptmp", [128, HW], F32)
    b["strip"] = [p.sb(f"strip{i}", [128, 16 * 512], BF16) for i in range(2)]
    b["strip_free"] = [None, None]
    b["consts"] = p.sb("consts", [128, 4], F32)
    return b


def row(b, i):
    return b["rows"][:, i * T:(i + 1) * T]


def fill_consts(p, b):
    @p.on("vector")
    def _(ve):
        v = Eng(p, ve, "vector")
        v.mset(b["consts"][:, 0:1], EPS)
        v.mset(b["consts"][:, 1:2], 1.0)
        v.mset(b["consts"][:, 2:3], LN_C)
        v.mark("ev")


def load_consts(p, b, pairs):
    @p.on("sync")
    def _(sync):
        for dst, src in pairs:
            p.inc(sync.dma_start(out=dst, in_=src), "lx", 16)


def cast_weight(p, dst_bf, src_f32, pieces=4):
    rows_ = src_f32.shape[0]
    step = rows_ // pieces
    @p.on("gpsimd")
    def _(g):
        for i in range(pieces):
            p.inc(g.dma_start(out=dst_bf[i * step:(i + 1) * step, :],
                              in_=src_f32[i * step:(i + 1) * step, :]), "gd", 16)


def emit_ln(p, b, X, XLN, label):
    ps_sum = [b["banks"][0], b["banks"][1]]
    ps_sq = [b["banks"][2], b["banks"][3]]
    ps_mb = [b["banks"][4], b["banks"][5]]
    ps_rb = [b["banks"][6], b["banks"][7]]
    pre = p.snap()
    mm_by_ch = {}

    for ch in range(CH):
        cur = 0
        hold = {}

        @p.on("vector")
        def _(ve, ch=ch, cur=cur):
            v = Eng(p, ve, "vector")
            if ch == 0:
                v.wsnap(pre)
            if ch >= 1:
                v.wait_ge(p.hnd["mm"], mm_by_ch[ch - 1])
            v.cp(b["xbf"][cur][:, :], X[:, chs(ch)])
            v.tt(b["xsqb"][cur][:, :], X[:, chs(ch)], X[:, chs(ch)], ALU.mult)
            hold["v"] = v.mark("ev")

        @p.on("tensor")
        def _(t, ch=ch, cur=cur):
            if ch == 0:
                for i in range(8):
                    wait_tok(t, p, b["bank_free"][i])
            t.wait_ge(p.hnd["ev"], hold["v"])
            last = None
            for hh in (0, 1):
                t.matmul(ps_sum[hh][0:1, :], b["ones_bf"][:, :], b["xbf"][cur][:, hsl(hh)],
                         start=(ch == 0), stop=(ch == CH - 1))
                last = t.matmul(ps_sq[hh][0:1, :], b["ones_bf"][:, :], b["xsqb"][cur][:, hsl(hh)],
                                start=(ch == 0), stop=(ch == CH - 1))
            mm_by_ch[ch] = p.inc(last, "mm")

    mm_done = p.val("mm")
    hold2 = {}

    @p.on("vector")
    def _(ve):
        v = Eng(p, ve, "vector")
        v.wait_ge(p.hnd["mm"], mm_done)
        for hh in (0, 1):
            v.cp(row(b, 0)[:, hsl(hh)], ps_sum[hh][0:1, :])
            v.cp(row(b, 1)[:, hsl(hh)], ps_sq[hh][0:1, :])
        v.ts(row(b, 0), row(b, 0), 1.0 / D, None, ALU.mult)
        v.ts(row(b, 1), row(b, 1), 1.0 / D, None, ALU.mult)
        v.tt(row(b, 2), row(b, 0), row(b, 0), ALU.mult)
        v.tt(row(b, 1), row(b, 1), row(b, 2), ALU.subtract)
        hold2["v"] = v.mark("ev")

    hold3 = {}

    @p.on("scalar")
    def _(se):
        s = Eng(p, se, "scalar")
        s.wait_ge(p.hnd["ev"], hold2["v"])
        s.act(row(b, 2), row(b, 1), AF.Ln, bias=b["consts"][0:1, 0:1])
        s.act(row(b, 1), row(b, 2), AF.Exp, scale=-0.5)
        hold3["v"] = s.mark("ac")

    @p.on("tensor")
    def _(t):
        t.wait_ge(p.hnd["ac"], hold3["v"])
        last = None
        for hh in (0, 1):
            t.matmul(ps_mb[hh][:, :], b["ones_row_f"][:, :], row(b, 0)[0:1, hsl(hh)],
                     start=True, stop=True)
            last = t.matmul(ps_rb[hh][:, :], b["ones_row_f"][:, :], row(b, 1)[0:1, hsl(hh)],
                            start=True, stop=True)
        p.inc(last, "mm")
    mmb = p.val("mm")

    @p.on("vector")
    def _(ve):
        v = Eng(p, ve, "vector")
        v.wait_ge(p.hnd["mm"], mmb)
        for hh in (0, 1):
            v.cp(b["mb"][:, hsl(hh)], ps_mb[hh][:, :])
            v.cp(b["rb"][:, hsl(hh)], ps_rb[hh][:, :])
        for ch in range(CH):
            for hh in (0, 1):
                v.tt(b["aptmp"][:, :], X[:, tok(ch, hh)], b["mb"][:, hsl(hh)], ALU.subtract)
                v.tt(XLN[:, tok(ch, hh)], b["aptmp"][:, :], b["rb"][:, hsl(hh)], ALU.mult)
        v.mark("ev")
    evd = p.val("ev")
    for i in range(8):
        b["bank_free"][i] = ("ev", evd)
    return ("ev", evd)


def load_strip(p, b, sbuf_idx, w_dram, col0, k_chunks, kc0=0, q="sync"):
    prev = b["strip_free"][sbuf_idx]
    strip = b["strip"][sbuf_idx]
    semn = f"lw{sbuf_idx}"
    p.sem(semn)

    @p.on(q)
    def _(sync):
        wait_tok(sync, p, prev)
        src = w_dram[kc0 * 128:(kc0 + k_chunks) * 128, col0:col0 + 512] \
            .rearrange("(k p) c -> p k c", p=128)
        dst = strip[:, 0:k_chunks * 512].rearrange("p (k c) -> p k c", k=k_chunks)
        p.inc(sync.dma_start(out=dst, in_=src), semn, 16)
    return (semn, p.val(semn)), strip


def proj_group(p, b, strip, ldv, src, k_chunks, start, stop, kc_shift=0):
    mmv = {}

    @p.on("tensor")
    def _(t):
        wait_tok(t, p, ldv)
        for oi in range(4):
            for hh in (0, 1):
                ps = b["banks"][oi * 2 + hh]
                if start:
                    wait_tok(t, p, b["bank_free"][oi * 2 + hh])
                last = None
                for kc in range(k_chunks):
                    last = t.matmul(
                        ps[:, :],
                        strip[:, kc * 512 + oi * 128: kc * 512 + (oi + 1) * 128],
                        src[:, (kc + kc_shift) * T + hh * HW:(kc + kc_shift) * T + (hh + 1) * HW],
                        start=(start and kc == 0), stop=(stop and kc == k_chunks - 1))
                mmv[(oi, hh)] = p.inc(last, "mm")
    return mmv



def build_l1():
    p = Prog()
    p.sem("lx")
    for s in ("lr0", "lr1", "lr2", "sro0", "sro1", "skv0", "skv1"):
        p.sem(s)
    x_in = p.din("x", [D, T], F32)
    wkT = p.din("wkT", [D, D], F32)
    wvT = p.din("wvT", [D, D], F32)
    ones_bf_in = p.din("ones_bf_in", [128, 1], BF16)
    ones_row_f_in = p.din("ones_row_f_in", [1, 128], F32)

    gu_tap = [p.din(f"gu_tap{k}", [512, INTER], F32) for k in range(4)]
    gu_c = [p.din(f"gu_c{k}", [256, INTER], F32) for k in range(4)]
    dn_tap = [p.din(f"dn_tap{k}", [1024, D], F32) for k in range(4)]
    dn_c = [p.din(f"dn_c{k}", [1024, D], F32) for k in range(4)]

    out_ss = p.dout("slicesum", [128, CH], F32)
    out_kvt = p.dout("kvt", [128, CH * T], BF16)
    out_gu = p.dout("recon_gu", [512, INTER], BF16)
    out_dn = p.dout("recon_dn", [1024, D], BF16)

    X = p.sb("X", [128, CH * T], F32)
    XLN = p.sb("XLN", [128, CH * T], BF16)
    KVTILE = [p.sb(f"kvtile{i}", [128, T], BF16) for i in range(2)]
    TMPK = p.sb("TMPK", [128, 4 * T], F32)
    KVF = p.sb("KVF", [128, T], F32)
    CS2 = p.sb("CS2", [128, T], F32)
    SS = p.sb("SS", [128, CH], F32)
    RW = 512
    RSL = [X[:, i * 4096:(i + 1) * 4096] for i in range(3)]
    RT4 = X[:, 12288:14336]
    AB = X[:, 14336:15360]
    ROv = X[:, 15360:15872].bitcast(BF16)
    RO = [ROv[:, 0:512], ROv[:, 512:1024]]

    p.start_block()
    b = make_shared(p)
    fill_consts(p, b)

    load_consts(p, b, [
        (b["ones_bf"][:, :], ones_bf_in[:, :]),
        (b["ones_row_f"][:, :], ones_row_f_in[:, :]),
    ])

    @p.on("sync")
    def _(sync):
        for ch in range(CH):
            p.inc(sync.dma_start(out=X[:, chs(ch)], in_=x_in[ch * 128:(ch + 1) * 128, :]), "lx", 16)

    ln1_tok = emit_ln(p, b, X, XLN, "ln1")

    tiles = []
    for rc in range(4):
        for sg in range(INTER // RW):
            tiles.append(("gu", rc, sg))
    for rc in range(8):
        for sg in range(D // RW):
            tiles.append(("dn", rc, sg))
    rst = {"i": 0, "slot_free": [ln1_tok, ln1_tok, ln1_tok], "ro_free": [None, None]}

    def emit_recon(n):
        for _ in range(n):
            ti = rst["i"]
            if ti >= len(tiles):
                return
            rst["i"] = ti + 1
            kind, rc, sg = tiles[ti]
            slot = ti % 3
            q = ["sync", "scalar"][ti % 2] if ti < 56 else \
                ["sync", "scalar", "gpsimd"][ti % 3]
            semr = f"lr{slot}"
            sfree = rst["slot_free"][slot]

            @p.on(q)
            def _(qe, kind=kind, rc=rc, sg=sg, slot=slot, semr=semr, sfree=sfree):
                wait_tok(qe, p, sfree)
                for k in range(4):
                    if kind == "gu":
                        tapsrc = gu_tap[k][rc * 128:(rc + 1) * 128, sg * RW:(sg + 1) * RW]
                        csrc = gu_c[k][(rc % 2) * 128:(rc % 2 + 1) * 128, sg * RW:(sg + 1) * RW]
                    else:
                        tapsrc = dn_tap[k][rc * 128:(rc + 1) * 128, sg * RW:(sg + 1) * RW]
                        csrc = dn_c[k][rc * 128:(rc + 1) * 128, sg * RW:(sg + 1) * RW]
                    p.inc(qe.dma_start(out=RSL[slot][:, k * RW:(k + 1) * RW], in_=tapsrc), semr, 16)
                    p.inc(qe.dma_start(out=RSL[slot][:, (4 + k) * RW:(5 + k) * RW], in_=csrc), semr, 16)
            ldr = p.val(semr)
            ri = ti % 2
            rfree = rst["ro_free"][ri]

            @p.on("vector")
            def _(ve, slot=slot, ldr=ldr, semr=semr, ri=ri, rfree=rfree):
                v = Eng(p, ve, "vector")
                v.wait_ge(p.hnd[semr], ldr)
                v.wtok(rfree)
                sl = RSL[slot]
                v.tt(RT4[:, :], sl[:, 0:4 * RW], sl[:, 4 * RW:8 * RW], ALU.mult)
                v.tt(AB[:, 0:RW], RT4[:, 0:RW], RT4[:, RW:2 * RW], ALU.add)
                v.tt(AB[:, RW:2 * RW], RT4[:, 2 * RW:3 * RW], RT4[:, 3 * RW:4 * RW], ALU.add)
                v.tt(RO[ri][:, :], AB[:, 0:RW], AB[:, RW:2 * RW], ALU.add)
                v.mark("ev")
            evr = p.val("ev")
            rst["slot_free"][slot] = ("ev", evr)
            sems = f"sro{ri}"

            @p.on("gpsimd")
            def _(g, kind=kind, rc=rc, sg=sg, ri=ri, evr=evr, sems=sems):
                g.wait_ge(p.hnd["ev"], evr)
                dst = out_gu if kind == "gu" else out_dn
                p.inc(g.dma_start(out=dst[rc * 128:(rc + 1) * 128, sg * RW:(sg + 1) * RW],
                                  in_=RO[ri][:, :]), sems, 16)
            rst["ro_free"][ri] = (sems, p.val(sems))

    b["kvtile_free"] = [None, None]
    for g in range(4):
        ldk, stripk = load_strip(p, b, 0, wkT, g * 512, CH, q="gpsimd")
        mmk = proj_group(p, b, stripk, ldk, XLN, CH, start=True, stop=True)

        @p.on("vector")
        def _(ve, mmk=mmk):
            v = Eng(p, ve, "vector")
            for oi in range(4):
                for hh in (0, 1):
                    v.wait_ge(p.hnd["mm"], mmk[(oi, hh)])
                    v.cp(TMPK[:, tok(oi, hh)], b["banks"][oi * 2 + hh][:, :])
            v.mark("ev")
        for i in range(8):
            b["bank_free"][i] = ("ev", p.val("ev"))
        b["strip_free"][0] = ("mm", mmk[(3, 1)])

        ldv_, stripv = load_strip(p, b, 1, wvT, g * 512, CH, q="gpsimd")
        mmv = proj_group(p, b, stripv, ldv_, XLN, CH, start=True, stop=True)
        b["strip_free"][1] = ("mm", mmv[(3, 1)])

        for oi in range(4):
            ch = g * 4 + oi
            ktile = KVTILE[ch % 2]
            ktile_free = b["kvtile_free"][ch % 2]

            @p.on("vector")
            def _(ve, ch=ch, oi=oi, mmv=mmv, ktile=ktile, ktile_free=ktile_free):
                v = Eng(p, ve, "vector")
                for hh in (0, 1):
                    v.wait_ge(p.hnd["mm"], mmv[(oi, hh)])
                    v.tt(KVF[:, hsl(hh)], TMPK[:, tok(oi, hh)],
                         b["banks"][oi * 2 + hh][:, :], ALU.mult)
                v.red(SS[:, ch:ch + 1], KVF[:, :], ALU.add)
                v.wtok(ktile_free)
                bufs = [KVF, CS2]
                for si, s in enumerate([1, 2, 4, 8, 16, 32, 64, 128, 256, 512]):
                    a = bufs[si % 2]
                    o = ktile if si == 9 else bufs[(si + 1) % 2]
                    v.tt(o[:, s:T], a[:, s:T], a[:, 0:T - s], ALU.add)
                    v.cp(o[:, 0:s], a[:, 0:s])
                v.mark("ev")
            b["bank_free"][oi * 2] = ("ev", p.val("ev"))
            b["bank_free"][oi * 2 + 1] = ("ev", p.val("ev"))
            evk = p.val("ev")

            semk = f"skv{ch % 2}"

            @p.on("scalar")
            def _(gq, ch=ch, ktile=ktile, evk=evk, semk=semk):
                gq.wait_ge(p.hnd["ev"], evk)
                p.inc(gq.dma_start(out=out_kvt[:, chs(ch)], in_=ktile[:, :]), semk, 16)
            b["kvtile_free"][ch % 2] = (semk, p.val(semk))
            emit_recon(3)
        emit_recon(2)

    emit_recon(len(tiles))

    fin = p.snap()

    @p.on("gpsimd")
    def _(g):
        p.wait(g, fin)
        p.inc(g.dma_start(out=out_ss[:, :], in_=SS[:, :]), "st", 16)
        g.wait_ge(p.hnd["st"], p.val("st"))

    return p



_GRIDS = {}


def _spline_host(gate_cp, up_cp, down_cp):
    n = INTER * D
    nct = gate_cp.shape[0]
    if "idx" not in _GRIDS:
        t = np.linspace(0.0, nct - 1.0, n, dtype=np.float64)
        i = np.clip(np.floor(t).astype(np.int64), 0, nct - 2)
        f = (t - i).astype(np.float32)
        idx = np.stack([np.clip(i + k, 0, nct - 1) for k in (-1, 0, 1, 2)], axis=0).astype(np.int32)
        f2 = f * f
        f3 = f2 * f
        c0 = 0.5 * (-f + 2.0 * f2 - f3)
        c1 = 0.5 * (2.0 - 5.0 * f2 + 3.0 * f3)
        c2 = 0.5 * (f + 4.0 * f2 - 3.0 * f3)
        c3 = 0.5 * (-f2 + f3)
        c = np.stack([c0, c1, c2, c3], 0).astype(np.float32)
        cguT = np.ascontiguousarray(c.reshape(4, INTER, D).transpose(0, 2, 1))
        cdnT = np.ascontiguousarray(c.reshape(4, D, INTER).transpose(0, 2, 1))
        _GRIDS.update(idx=idx, cguT=cguT, cdnT=cdnT)
    idx = _GRIDS["idx"]
    cguT = _GRIDS["cguT"]
    cdnT = _GRIDS["cdnT"]

    out = {"gu_tap": [[] for _ in range(4)], "gu_c": [[] for _ in range(4)],
           "dn_tap": [[] for _ in range(4)], "dn_c": [[] for _ in range(4)]}
    for k in range(4):
        g_t = np.ascontiguousarray(gate_cp[idx[k]].reshape(INTER, D).T)
        u_t = np.ascontiguousarray(up_cp[idx[k]].reshape(INTER, D).T)
        d_t = np.ascontiguousarray(down_cp[idx[k]].reshape(D, INTER).T)
        for ccore in range(NCORE):
            gr = slice(ccore * 256, (ccore + 1) * 256)
            dr = slice(ccore * 1024, (ccore + 1) * 1024)
            out["gu_tap"][k].append(np.ascontiguousarray(
                np.concatenate([g_t[gr], u_t[gr]], 0)))
            out["gu_c"][k].append(np.ascontiguousarray(cguT[k][gr]))
            out["dn_tap"][k].append(np.ascontiguousarray(d_t[dr]))
            out["dn_c"][k].append(np.ascontiguousarray(cdnT[k][dr]))
    return out


def _fm_slices(x):
    xs = x.reshape(SEQ, D)
    return [np.ascontiguousarray(xs[c * T:(c + 1) * T, :].T) for c in range(NCORE)]


def _consts():
    return {
        "ones_bf_in": np.ones((128, 1), dtype=ml_dtypes.bfloat16),
        "ones_row_f_in": np.ones((1, 128), dtype=np.float32),
    }


def run_l1(inputs):
    x = np.asarray(inputs["L0_attn_a_x"], dtype=np.float32)
    xs = _fm_slices(x)
    wkT = np.ascontiguousarray(np.asarray(inputs["L0_attn_a_Wk"], np.float32).T)
    wvT = np.ascontiguousarray(np.asarray(inputs["L0_attn_a_Wv"], np.float32).T)
    sp = _spline_host(np.asarray(inputs["L0_ffn_gate_cp"], np.float32),
                      np.asarray(inputs["L0_ffn_up_cp"], np.float32),
                      np.asarray(inputs["L0_ffn_down_cp"], np.float32))
    cst = _consts()
    in_maps = []
    for c in range(NCORE):
        m = dict(x=xs[c], wkT=wkT, wvT=wvT, **cst)
        for k in range(4):
            m[f"gu_tap{k}"] = sp["gu_tap"][k][c]
            m[f"gu_c{k}"] = sp["gu_c"][k][c]
            m[f"dn_tap{k}"] = sp["dn_tap"][k][c]
            m[f"dn_c{k}"] = sp["dn_c"][k][c]
        in_maps.append(m)
    prog = build_l1()
    res = run_bass_kernel_spmd(prog.nc, in_maps, core_ids=list(range(NCORE)))
    return res.results


def kernel(**inputs):
    raise NotImplementedError("L2/L3 not yet wired")



_L23_CACHE = {}

class Carver:

    def __init__(self, base_ap, total_bytes):
        self.base = base_ap
        self.total = total_bytes
        self.cur = 0

    def take(self, shape, dt):
        es = mybir.dt.size(dt)
        nb = int(np.prod(shape[1:])) * es if len(shape) > 1 else es
        nb = int(np.prod(shape[1:])) * es
        self.cur = (self.cur + 3) // 4 * 4
        c0 = self.cur // 2
        self.cur += nb
        assert self.cur <= self.total, f"carver overflow {self.cur} > {self.total}"
        view = self.base[:, c0:c0 + nb // 2]
        if dt != BF16:
            view = view.bitcast(dt)
        if len(shape) == 3:
            view = view.rearrange("p (a c) -> p a c", a=shape[1])
        return view

    def reset(self, to=0):
        self.cur = to



def load_stripx(p, buf, free_tok, semn, w_dram, col0, k_chunks, kc0=0):
    p.sem(semn)

    @p.on("sync")
    def _(sync):
        wait_tok(sync, p, free_tok)
        src = w_dram[kc0 * 128:(kc0 + k_chunks) * 128, col0:col0 + 512] \
            .rearrange("(k p) c -> p k c", p=128)
        dst = buf[:, 0:k_chunks * 512].rearrange("p (k c) -> p k c", k=k_chunks)
        p.inc(sync.dma_start(out=dst, in_=src), semn, 16)
    return (semn, p.val(semn))


def emit_ffn(p, b, X, XLN, Hbuf, gate_w, up_w, dn_w, sc_x, out_dst, XR, XOT, FE4,
             xstrips=None):
    S = [b["strip"][0][:, :], b["strip"][1][:, :]] + (xstrips or [])
    NS = len(S)
    SF = [b["strip_free"][0], b["strip_free"][1]] + [None] * (NS - 2)
    @p.on("gpsimd")
    def _(g):
        p.wait(g, {"ev": p.val("ev")})
        for ch in range(CH):
            p.inc(g.dma_start(out=sc_x[:, chs(ch)], in_=X[:, chs(ch)]), "gd", 16)
    spill_done = ("gd", p.val("gd"))

    b["xr_free"] = [None, None]
    b["xot_free"] = [None, None]
    prev_silu_consumed = None
    gidx = 0
    for hh in (0, 1):
        for g in range(ICH // 4):
            sg_i = (2 * gidx) % NS
            su_i = (2 * gidx + 1) % NS
            gidx += 1
            stripg = S[sg_i]
            stripu = S[su_i]
            ldg = load_stripx(p, stripg, SF[sg_i], f"lw{sg_i}", gate_w, g * 512, CH)
            ldu = load_stripx(p, stripu, SF[su_i], f"lw{su_i}", up_w, g * 512, CH)
            mmg = {}

            @p.on("tensor")
            def _(t, g=g, hh=hh, ldg=ldg, ldu=ldu, stripg=stripg, stripu=stripu):
                wait_tok(t, p, ldg)
                if g == 0 and hh == 0:
                    wait_tok(t, p, spill_done)
                for oi in range(4):
                    wait_tok(t, p, b["bank_free"][oi])
                    last = None
                    for kc in range(CH):
                        last = t.matmul(b["banks"][oi][:, :],
                                        stripg[:, kc * 512 + oi * 128: kc * 512 + (oi + 1) * 128],
                                        XLN[:, tok(kc, hh)],
                                        start=(kc == 0), stop=(kc == CH - 1))
                    mmg[("g", oi)] = p.inc(last, "mm")
                wait_tok(t, p, ldu)
                for oi in range(4):
                    wait_tok(t, p, b["bank_free"][4 + oi])
                    last = None
                    for kc in range(CH):
                        last = t.matmul(b["banks"][4 + oi][:, :],
                                        stripu[:, kc * 512 + oi * 128: kc * 512 + (oi + 1) * 128],
                                        XLN[:, tok(kc, hh)],
                                        start=(kc == 0), stop=(kc == CH - 1))
                    mmg[("u", oi)] = p.inc(last, "mm")

            holds = {}

            @p.on("scalar")
            def _(se, mmg=mmg, prev=prev_silu_consumed):
                s = Eng(p, se, "scalar")
                if prev is not None:
                    s.wait_ge(p.hnd["ev"], prev)
                for oi in range(4):
                    s.wait_ge(p.hnd["mm"], mmg[("g", oi)])
                    s.act(FE4[:, oi * HW:(oi + 1) * HW], b["banks"][oi][:, :], AF.Silu)
                    holds[oi] = s.mark("ac")

            evs = {}

            @p.on("vector")
            def _(ve, g=g, hh=hh, mmg=mmg):
                v = Eng(p, ve, "vector")
                for oi in range(4):
                    ic = g * 4 + oi
                    v.wait_ge(p.hnd["ac"], holds[oi])
                    v.wait_ge(p.hnd["mm"], mmg[("u", oi)])
                    v.tt(Hbuf[:, ic * HW:(ic + 1) * HW],
                         FE4[:, oi * HW:(oi + 1) * HW], b["banks"][4 + oi][:, :], ALU.mult)
                    evs[oi] = v.mark("ev")
            evg = p.val("ev")
            prev_silu_consumed = evg
            for oi in range(4):
                b["bank_free"][oi] = ("ac", holds[oi])
                b["bank_free"][4 + oi] = ("ev", evs[oi])
            SF[sg_i] = ("mm", mmg[("g", 3)])
            SF[su_i] = ("mm", mmg[("u", 3)])

        for ocg in range(4):
            mmd = {}
            for piece in range(4):
                sbf = (gidx * 2 + piece) % NS
                stripd = S[sbf]
                ldd = load_stripx(p, stripd, SF[sbf], f"lw{sbf}", dn_w, ocg * 512, CH,
                                  kc0=piece * 16)

                @p.on("tensor")
                def _(t, piece=piece, ldd=ldd, stripd=stripd, ocg=ocg, hh=hh):
                    wait_tok(t, p, ldd)
                    if piece == 0:
                        for i in range(4):
                            wait_tok(t, p, b["bank_free"][i])
                    last = None
                    for oi in range(4):
                        for kc in range(CH):
                            kabs = piece * 16 + kc
                            last = t.matmul(
                                b["banks"][oi][:, :],
                                stripd[:, kc * 512 + oi * 128: kc * 512 + (oi + 1) * 128],
                                Hbuf[:, kabs * HW:(kabs + 1) * HW],
                                start=(kabs == 0), stop=(kabs == ICH - 1),
                                skip_group_check=True)
                        if piece == 3:
                            mmd[oi] = p.inc(last, "mm")
                            last = None
                    if piece != 3:
                        p.inc(last, "mm")
                SF[sbf] = ("mm", p.val("mm"))
            gidx += 2

            for oi in range(4):
                oc = ocg * 4 + oi
                sl = oc % 2

                @p.on("sync")
                def _(sync, oc=oc, hh=hh, sl=sl):
                    wait_tok(sync, p, b["xr_free"][sl])
                    p.inc(sync.dma_start(out=XR[sl][:, :], in_=sc_x[:, tok(oc, hh)]), f"lxr{sl}", 16)
                lxr = p.val(f"lxr{sl}")
                xot_free = b["xot_free"][sl]

                @p.on("vector")
                def _(ve, oi=oi, sl=sl, lxr=lxr, mmd=mmd, xot_free=xot_free):
                    v = Eng(p, ve, "vector")
                    v.wait_ge(p.hnd["mm"], mmd[oi])
                    v.wait_ge(p.hnd[f"lxr{sl}"], lxr)
                    v.wtok(xot_free)
                    v.tt(XOT[sl][:, :], b["banks"][oi][:, :], XR[sl][:, :], ALU.add)
                    v.mark("ev")
                evd = p.val("ev")
                b["xr_free"][sl] = ("ev", evd)
                b["bank_free"][oi] = ("ev", evd)

                semx = "sx3a" if sl == 0 else "sx3b"

                @p.on("gpsimd")
                def _(gq, oc=oc, hh=hh, sl=sl, evd=evd, semx=semx):
                    gq.wait_ge(p.hnd["ev"], evd)
                    p.inc(gq.dma_start(out=out_dst[:, tok(oc, hh)], in_=XOT[sl][:, :]), semx, 16)
                b["xot_free"][sl] = (semx, p.val(semx))



def build_l2():
    p = Prog()
    for s in ("lx", "lkv0", "lkv1", "lxr0", "lxr1", "sx3a", "sx3b", "lx3", "lssg",
              "sqs0", "sqs1", "sk0", "sk1", "sgp0", "sgp1", "svv0", "svv1",
              "lgp0", "lgp1", "lkk0", "lkk1", "lvv0", "lvv1", "lqq0", "lqq1",
              "sqt0", "sqt1", "soc0", "soc1"):
        p.sem(s)
    x_in = p.din("x", [D, T], F32)
    wqaT = p.din("wqaT", [D, D], F32)
    woaT = p.din("woaT", [D, D], F32)
    wqb_in = p.din("wqb", [D, D], F32)
    wfb_in = p.din("wfb", [D, D], F32)
    wib_in = p.din("wib", [D, D], F32)
    kvt_in = p.din("kvt", [128, CH * T], BF16)
    ssg_in = p.din("ssg", [8, D], F32)
    mpfx_in = p.din("mpfx", [8, 1], F32)
    gate_w = p.din("gate_w", [D, INTER], BF16)
    up_w = p.din("up_w", [D, INTER], BF16)
    dn_w = p.din("dn_w", [INTER, D], BF16)
    ones_bf_in = p.din("ones_bf_in", [128, 1], BF16)
    ones_row_f_in = p.din("ones_row_f_in", [1, 128], F32)
    ident_in = p.din("ident_bf", [128, 128], BF16)
    maskf_in = p.din("maskf", [128, 128], F32)

    out_x3 = p.dout("x3", [128, CH * T], F32)
    out_op = p.dout("opart", [128, CH * T], BF16)
    out_qt = p.dout("qtil", [128, NCHK * D], BF16)
    out_st = p.dout("sstate", [128, NH * 128], F32)
    out_ld = p.dout("logd", [128, NH], F32)
    out_gm = p.dout("gamlog", [128, NH * NCHK], F32)

    sc_x = p.dtmp("sc_x", [128, CH * T], F32)
    sc_qs = p.dtmp("sc_qs", [128, CH * T], BF16)
    sc_k = p.dtmp("sc_k", [128, CH * T], BF16)
    sc_v = p.dtmp("sc_v", [128, CH * T], BF16)
    sc_gp = p.dtmp("sc_gp", [128, CH * T], F32)

    ARE = p.sb("ARE", [128, 49152], BF16)
    X = ARE[:, 0:32768].bitcast(F32)
    XLN = ARE[:, 32768:49152]
    Hbuf = ARE[:, 0:32768]
    SCR = p.sb("SCR", [128, 16384], BF16)
    VHAT = SCR[:, :]
    OFF = p.sb("OFF", [128, CH], F32)
    MPFX = p.sb("MPFX", [8, 1], F32)
    IDE = p.sb("IDE", [128, 128], BF16)
    MASKF = p.sb("MASKF", [128, 128], F32)
    KVR = [p.sb(f"kvr{i}", [128, HW], BF16) for i in range(2)]
    XR = [p.sb(f"xr{i}", [128, HW], F32) for i in range(2)]
    XOT = [p.sb(f"xot{i}", [128, HW], F32) for i in range(2)]
    FE4 = p.sb("FE4", [128, 4 * HW], F32)

    p.start_block()
    b = make_shared(p)
    fill_consts(p, b)

    load_consts(p, b, [
        (b["ones_bf"][:, :], ones_bf_in[:, :]),
        (b["ones_row_f"][:, :], ones_row_f_in[:, :]),
        (IDE[:, :], ident_in[:, :]),
        (MASKF[:, :], maskf_in[:, :]),
        (MPFX[:, :], mpfx_in[:, :]),
    ])

    @p.on("sync")
    def _(sync):
        for ch in range(CH):
            p.inc(sync.dma_start(out=X[:, chs(ch)], in_=x_in[ch * 128:(ch + 1) * 128, :]), "lx", 16)

    emit_ln(p, b, X, XLN, "ln1")

    SSG = SCR[:, 0:4096].bitcast(F32)[0:8, 0:2048]

    @p.on("sync")
    def _(sync):
        p.inc(sync.dma_start(out=SSG, in_=ssg_in[:, :]), "lssg", 16)
    ldssg = p.val("lssg")

    @p.on("tensor")
    def _(t):
        t.wait_ge(p.hnd["lssg"], ldssg)
        for i in range(8):
            wait_tok(t, p, b["bank_free"][i])
        last = None
        for ch in range(CH):
            last = t.matmul(b["banks"][0][:, ch:ch + 1],
                            SSG[0:8, ch * 128:(ch + 1) * 128], MPFX[0:8, 0:1],
                            start=True, stop=True)
        p.inc(last, "mm")
    mmoff = p.val("mm")

    @p.on("vector")
    def _(ve):
        v = Eng(p, ve, "vector")
        v.wait_ge(p.hnd["mm"], mmoff)
        v.cp(OFF[:, :], b["banks"][0][:, 0:CH])
        v.mark("ev")
    b["bank_free"][0] = ("ev", p.val("ev"))


    kvr_free = [None, None]
    for g in range(4):
        ldq, stripq = load_strip(p, b, g % 2, wqaT, g * 512, CH, q="gpsimd")
        mmq = proj_group(p, b, stripq, ldq, XLN, CH, start=True, stop=True)
        b["strip_free"][g % 2] = ("mm", mmq[(3, 1)])

        for oi in range(4):
            ch = g * 4 + oi
            for hh in (0, 1):
                sl = (ch * 2 + hh) % 2

                @p.on("sync")
                def _(sync, ch=ch, hh=hh, sl=sl):
                    wait_tok(sync, p, kvr_free[sl])
                    p.inc(sync.dma_start(out=KVR[sl][:, :], in_=kvt_in[:, tok(ch, hh)]),
                          f"lkv{sl}", 16)
                lkvv = p.val(f"lkv{sl}")

                @p.on("vector")
                def _(ve, ch=ch, oi=oi, hh=hh, sl=sl, mmq=mmq, lkvv=lkvv):
                    v = Eng(p, ve, "vector")
                    v.wait_ge(p.hnd[f"lkv{sl}"], lkvv)
                    v.wait_ge(p.hnd["mm"], mmq[(oi, hh)])
                    v.ts(b["aptmp"][:, :], KVR[sl][:, :], OFF[:, ch:ch + 1], DENOM,
                         ALU.add, ALU.mult)
                    v.tt(VHAT[:, tok(ch, hh)], b["banks"][oi * 2 + hh][:, :],
                         b["aptmp"][:, :], ALU.mult)
                    v.mark("ev")
                b["bank_free"][oi * 2 + hh] = ("ev", p.val("ev"))
                kvr_free[sl] = ("ev", p.val("ev"))

    for g in range(4):
        ldo, stripo = load_strip(p, b, g % 2, woaT, g * 512, CH, q="gpsimd")
        mmo = proj_group(p, b, stripo, ldo, VHAT, CH, start=True, stop=True)
        b["strip_free"][g % 2] = ("mm", mmo[(3, 1)])

        @p.on("vector")
        def _(ve, g=g, mmo=mmo):
            v = Eng(p, ve, "vector")
            for oi in range(4):
                oc = g * 4 + oi
                for hh in (0, 1):
                    v.wait_ge(p.hnd["mm"], mmo[(oi, hh)])
                    v.tt(X[:, tok(oc, hh)], b["banks"][oi * 2 + hh][:, :],
                         X[:, tok(oc, hh)], ALU.add)
            v.mark("ev")
        for i in range(8):
            b["bank_free"][i] = ("ev", p.val("ev"))

    emit_ln(p, b, X, XLN, "ln2")

    emit_ffn(p, b, X, XLN, Hbuf, gate_w, up_w, dn_w, sc_x, out_x3, XR, XOT, FE4,
             xstrips=[SCR[:, 0:8192], SCR[:, 8192:16384]])

    fin_ffn = p.snap()

    @p.on("sync")
    def _(sync):
        p.wait(sync, fin_ffn)
        for ch in range(CH):
            p.inc(sync.dma_start(out=X[:, chs(ch)], in_=out_x3[:, chs(ch)]), "lx3", 16)

    emit_ln(p, b, X, XLN, "ln3")

    FEG = SCR[:, 0:8192].bitcast(F32)
    KSTG = SCR[:, 8192:12288]
    QSTG = SCR[:, 12288:13312]
    VSTG = SCR[:, 13312:14336]

    qst_free = [None, None]

    for g in range(4):
        ldq, stripq = load_strip(p, b, g % 2, wqb_in, g * 512, CH, q="gpsimd")
        mmq = proj_group(p, b, stripq, ldq, XLN, CH, start=True, stop=True)
        b["strip_free"][g % 2] = ("mm", mmq[(3, 1)])
        for oi in range(4):
            ch = g * 4 + oi
            for hh in (0, 1):
                sl = (ch * 2 + hh) % 2
                hold = {}

                @p.on("scalar")
                def _(se, oi=oi, hh=hh, sl=sl, mmq=mmq, hold=hold):
                    s = Eng(p, se, "scalar")
                    s.wait_ge(p.hnd["mm"], mmq[(oi, hh)])
                    s.wtok(qst_free[sl])
                    s.act(QSTG[:, sl * HW:(sl + 1) * HW], b["banks"][oi * 2 + hh][:, :], AF.Silu)
                    hold["v"] = s.mark("ac")
                b["bank_free"][oi * 2 + hh] = ("ac", hold["v"])
                semq = f"sqs{sl}"

                @p.on("sync")
                def _(gq, ch=ch, hh=hh, sl=sl, hold=hold, semq=semq):
                    gq.wait_ge(p.hnd["ac"], hold["v"])
                    p.inc(gq.dma_start(out=sc_qs[:, tok(ch, hh)],
                                       in_=QSTG[:, sl * HW:(sl + 1) * HW]), semq, 16)
                qst_free[sl] = (semq, p.val(semq))

    kst_free = None
    gst_free = None

    for g in range(4):
        ldf, stripf = load_strip(p, b, g % 2, wfb_in, g * 512, CH, q="gpsimd")
        mmf = proj_group(p, b, stripf, ldf, XLN, CH, start=True, stop=True)
        b["strip_free"][g % 2] = ("mm", mmf[(3, 1)])
        holds = {}

        @p.on("scalar")
        def _(se, g=g, mmf=mmf, holds=holds, kst_free=kst_free, gst_free=gst_free):
            s = Eng(p, se, "scalar")
            s.wtok(gst_free)
            s.wtok(kst_free)
            for oi in range(4):
                for hh in (0, 1):
                    j = oi * 2 + hh
                    s.wait_ge(p.hnd["mm"], mmf[(oi, hh)])
                    s.act(FEG[:, j * HW:(j + 1) * HW], b["banks"][j][:, :], AF.Exp, scale=-1.0)
            for oi in range(4):
                for hh in (0, 1):
                    j = oi * 2 + hh
                    s.act(KSTG[:, j * HW:(j + 1) * HW], b["banks"][j][:, :],
                          AF.Sigmoid, scale=-1.0)
                    holds[("k", oi, hh)] = s.mark("ac")
            for oi in range(4):
                for hh in (0, 1):
                    j = oi * 2 + hh
                    s.act(FEG[:, j * HW:(j + 1) * HW], FEG[:, j * HW:(j + 1) * HW],
                          AF.Ln, bias=b["consts"][:, 1:2])
                    holds[("gp", oi, hh)] = s.mark("ac")

        for oi in range(4):
            for hh in (0, 1):
                ch = g * 4 + oi
                j = oi * 2 + hh
                b["bank_free"][j] = ("ac", holds[("k", oi, hh)])
                hk = holds[("k", oi, hh)]
                hg = holds[("gp", oi, hh)]
                semk = f"sk{j % 2}"
                semg = f"sgp{j % 2}"

                @p.on("sync")
                def _(gq, ch=ch, hh=hh, j=j, hk=hk, hg=hg, semk=semk, semg=semg):
                    gq.wait_ge(p.hnd["ac"], hk)
                    p.inc(gq.dma_start(out=sc_k[:, tok(ch, hh)],
                                       in_=KSTG[:, j * HW:(j + 1) * HW]), semk, 16)
                    gq.wait_ge(p.hnd["ac"], hg)
                    p.inc(gq.dma_start(out=sc_gp[:, tok(ch, hh)],
                                       in_=FEG[:, j * HW:(j + 1) * HW]), semg, 16)
        kst_free = ("sk1", p.val("sk1"))
        gst_free = ("sgp1", p.val("sgp1"))
        kst_free2 = ("sk0", p.val("sk0"))
        gst_free2 = ("sgp0", p.val("sgp0"))

        @p.on("scalar")
        def _(se, kst_free2=kst_free2, gst_free2=gst_free2):
            se.wait_ge(p.hnd[kst_free2[0]], kst_free2[1])
            se.wait_ge(p.hnd[gst_free2[0]], gst_free2[1])

    vst_free = [None, None]

    for g in range(4):
        ldi, stripi = load_strip(p, b, g % 2, wib_in, g * 512, CH, q="gpsimd")
        mmi = proj_group(p, b, stripi, ldi, XLN, CH, start=True, stop=True)
        b["strip_free"][g % 2] = ("mm", mmi[(3, 1)])
        for oi in range(4):
            ch = g * 4 + oi
            for hh in (0, 1):
                sl = (ch * 2 + hh) % 2
                hold = {}

                @p.on("vector")
                def _(ve, oi=oi, hh=hh, sl=sl, mmi=mmi, hold=hold):
                    v = Eng(p, ve, "vector")
                    v.wait_ge(p.hnd["mm"], mmi[(oi, hh)])
                    v.wtok(vst_free[sl])
                    v.cp(VSTG[:, sl * HW:(sl + 1) * HW], b["banks"][oi * 2 + hh][:, :])
                    hold["v"] = v.mark("ev")
                b["bank_free"][oi * 2 + hh] = ("ev", hold["v"])
                semv = f"svv{sl}"

                @p.on("sync")
                def _(gq, ch=ch, hh=hh, sl=sl, hold=hold, semv=semv):
                    gq.wait_ge(p.hnd["ev"], hold["v"])
                    p.inc(gq.dma_start(out=sc_v[:, tok(ch, hh)],
                                       in_=VSTG[:, sl * HW:(sl + 1) * HW]), semv, 16)
                vst_free[sl] = (semv, p.val(semv))

    cva = Carver(ARE[:, :], 98304)
    HST = cva.take([128, NH, 128], F32)
    GPA = cva.take([128, NH, 128], F32)
    GPB = cva.take([128, NH, 128], F32)
    EKT = cva.take([128, NH, 128], F32)
    SUBT = cva.take([128, NH, 128], F32)
    KHAT = cva.take([128, NH, 128], BF16)
    KTT = cva.take([128, NH, 128], BF16)
    VTT = cva.take([128, NH, 128], BF16)
    HBF = cva.take([128, NH, 128], BF16)
    QTC = cva.take([128, NH, 128], BF16)
    KTIL = cva.take([128, NH, 128], BF16)
    QIB = cva.take([128, NH, SUB], BF16)
    ABM = cva.take([128, NH, 128], BF16)
    OCH = cva.take([128, NH, 128], BF16)
    SUBQ = cva.take([128, NH, SUB], F32)
    EQB = cva.take([128, NH, SUB], F32)
    GCT = cva.take([128, NH], F32)
    NGC = cva.take([128, NH], F32)
    EDEC = cva.take([128, NH], F32)
    SGP = cva.take([128, NH], F32)
    GAMT = cva.take([128, NH, NCHK], F32)
    LOGD = cva.take([128, NH], F32)
    QR3 = [cva.take([128, NH, 128], BF16) for _ in range(2)]
    cvs = Carver(SCR[:, :], 32768)
    GPR = [cvs.take([128, NH, 128], F32) for _ in range(2)]
    KR3 = [cvs.take([128, NH, 128], BF16) for _ in range(2)]
    VR3 = [cvs.take([128, NH, 128], BF16) for _ in range(2)]

    glastart = p.snap()

    @p.on("vector")
    def _(ve):
        v = Eng(p, ve, "vector")
        v.wsnap(glastart)
        v.mset(HST[:, :, :], 0.0)
        v.mset(SGP[:, :], 0.0)
        v.mark("ev")
    gla0 = p.val("ev")

    ring_free = {k: [None, None] for k in ("gp", "kk", "vv", "qq")}
    scmap = {"gp": sc_gp, "kk": sc_k, "vv": sc_v, "qq": sc_qs}
    dstmap = {"gp": GPR, "kk": KR3, "vv": VR3, "qq": QR3}

    def load_chunk(m):
        sl = m % 2
        for key in ("gp", "kk", "vv", "qq"):
            @p.on("sync")
            def _(sync, key=key, m=m, sl=sl):
                if m <= 1:
                    p.wait(sync, {"ev": gla0})
                wait_tok(sync, p, ring_free[key][sl])
                src3 = scmap[key].rearrange("p (h t) -> p h t", t=T)[:, :, m * 128:(m + 1) * 128]
                p.inc(sync.dma_start(out=dstmap[key][sl][:, :, :], in_=src3), f"l{key}{sl}", 16)

    load_chunk(0)
    och_free = [None, None]
    qtc_free = None
    prev_gla_mm = 0
    for m in range(NCHK):
        sl = m % 2
        if m + 1 < NCHK:
            load_chunk(m + 1)
        ldv = {k: p.val(f"l{k}{sl}") for k in ("gp", "kk", "vv", "qq")}

        hold = {}

        @p.on("vector")
        def _(ve, m=m, sl=sl, ldv=ldv, prev_gla_mm=prev_gla_mm):
            v = Eng(p, ve, "vector")
            if prev_gla_mm:
                v.wait_ge(p.hnd["mm"], prev_gla_mm)
            for k in ("gp", "kk", "vv", "qq"):
                v.wait_ge(p.hnd[f"l{k}{sl}"], ldv[k])
            src = GPR[sl]
            pp = [GPA, GPB]
            for si, s in enumerate([1, 2, 4, 8, 16, 32, 64]):
                a = src if si == 0 else pp[(si + 1) % 2]
                o = pp[si % 2]
                v.tt(o[:, :, s:128], a[:, :, s:128], a[:, :, 0:128 - s], ALU.add)
                v.cp(o[:, :, 0:s], a[:, :, 0:s])
            v.cp(GCT[:, :].rearrange("p (h o) -> p h o", o=1), GPA[:, :, 127:128])
            v.ts(NGC[:, :], GCT[:, :], -1.0, None, ALU.mult)
            hold["v"] = v.mark("ev")
        evc = p.val("ev")
        gposf = GPA

        hold2 = {}

        @p.on("scalar")
        def _(se):
            s = Eng(p, se, "scalar")
            s.wait_ge(p.hnd["ev"], hold["v"])
            s.act(EDEC[:, :], GCT[:, :], AF.Exp, scale=-1.0)
            for h in range(NH):
                s.act(EKT[:, h, :], gposf[:, h, :], AF.Exp, bias=NGC[:, h:h + 1])
            hold2["v"] = s.mark("ac")

        hold3 = {}

        @p.on("vector")
        def _(ve, sl=sl):
            v = Eng(p, ve, "vector")
            v.wait_ge(p.hnd["ac"], hold2["v"])
            v.tt(KHAT[:, :, :], KR3[sl][:, :, :], EKT[:, :, :], ALU.mult)
            v.cp(HBF[:, :, :], HST[:, :, :])
            hold3["v"] = v.mark("ev")
        evkh = p.val("ev")

        ps_bf = [b["banks"][i][:, :].bitcast(BF16) for i in range(4)]

        @p.on("tensor")
        def _(t, sl=sl):
            t.wait_ge(p.hnd["ev"], evkh)
            for i in range(4):
                wait_tok(t, p, b["bank_free"][i])
            last = None
            for h in range(NH):
                last = t.transpose(ps_bf[h // 8][:, (h % 8) * 128:(h % 8 + 1) * 128],
                                   KHAT[:, h, :], IDE[:, :])
            for h in range(NH):
                last = t.transpose(ps_bf[2 + h // 8][:, (h % 8) * 128:(h % 8 + 1) * 128],
                                   VR3[sl][:, h, :], IDE[:, :])
            p.inc(last, "mm")
        mmtr = p.val("mm")

        @p.on("vector")
        def _(ve):
            v = Eng(p, ve, "vector")
            v.wait_ge(p.hnd["mm"], mmtr)
            v.cp(KTT[:, 0:8, :], ps_bf[0].rearrange("p (h t) -> p h t", h=8))
            v.cp(KTT[:, 8:16, :], ps_bf[1].rearrange("p (h t) -> p h t", h=8))
            v.cp(VTT[:, 0:8, :], ps_bf[2].rearrange("p (h t) -> p h t", h=8))
            v.cp(VTT[:, 8:16, :], ps_bf[3].rearrange("p (h t) -> p h t", h=8))
            v.mark("ev")
        evtt = p.val("ev")
        for i in range(4):
            b["bank_free"][i] = ("ev", evtt)

        @p.on("tensor")
        def _(t):
            t.wait_ge(p.hnd["ev"], evtt)
            for i in range(4, 8):
                wait_tok(t, p, b["bank_free"][i])
            last = None
            for h in range(NH):
                ps = b["banks"][4 + h // 4]
                last = t.matmul(ps[:, (h % 4) * 128:(h % 4 + 1) * 128],
                                KTT[:, h, :], VTT[:, h, :], start=True, stop=True,
                                skip_group_check=True)
            p.inc(last, "mm")
        mmkv = p.val("mm")

        @p.on("vector")
        def _(ve):
            v = Eng(p, ve, "vector")
            v.wait_ge(p.hnd["mm"], mmkv)
            for h in range(NH):
                ps = b["banks"][4 + h // 4]
                v.ts(SUBT[:, h, :], HST[:, h, :], EDEC[:, h:h + 1], None, ALU.mult)
                v.tt(HST[:, h, :], SUBT[:, h, :], ps[:, (h % 4) * 128:(h % 4 + 1) * 128], ALU.add)
            v.mark("ev")
        evst = p.val("ev")
        for i in range(4, 8):
            b["bank_free"][i] = ("ev", evst)

        hold6 = {}

        @p.on("scalar")
        def _(se):
            s = Eng(p, se, "scalar")
            s.wait_ge(p.hnd["ev"], evst)
            s.act(EKT[:, :, :], gposf[:, :, :], AF.Exp, scale=-1.0, bias=b["consts"][:, 2:3])
            hold6["v"] = s.mark("ac")

        @p.on("vector")
        def _(ve, sl=sl, qtc_free=qtc_free):
            v = Eng(p, ve, "vector")
            v.wait_ge(p.hnd["ac"], hold6["v"])
            v.wtok(qtc_free)
            v.tt(QTC[:, :, :], QR3[sl][:, :, :], EKT[:, :, :], ALU.mult)
            v.mark("ev")
        evqt = p.val("ev")
        semqt = f"sqt{m % 2}"

        @p.on("gpsimd")
        def _(gq, m=m, semqt=semqt):
            gq.wait_ge(p.hnd["ev"], evqt)
            p.inc(gq.dma_start(out=out_qt[:, m * D:(m + 1) * D],
                               in_=QTC[:, :, :].rearrange("p h t -> p (h t)")), semqt, 16)
        qtc_free = (semqt, p.val(semqt))

        prev_amm = {}
        for I in range(NSUB):
            endi = (I + 1) * SUB

            @p.on("vector")
            def _(ve, I=I, endi=endi):
                v = Eng(p, ve, "vector")
                if I > 0:
                    v.wait_ge(p.hnd["mm"], prev_amm[I - 1])
                else:
                    v.wait_ge(p.hnd["ev"], evqt)
                if I > 0:
                    gb = gposf[:, :, I * SUB - 1:I * SUB]
                    v.tt(SUBT[:, :, 0:endi], gposf[:, :, 0:endi],
                         gb.broadcast_to([128, NH, endi]), ALU.subtract)
                    v.tt(SUBQ[:, :, :], gb.broadcast_to([128, NH, SUB]),
                         gposf[:, :, I * SUB:(I + 1) * SUB], ALU.subtract)
                    v.mark("ev")
            evsub = p.val("ev")

            @p.on("scalar")
            def _(se, I=I, endi=endi):
                s = Eng(p, se, "scalar")
                s.wait_ge(p.hnd["ev"], evsub)
                if I == 0:
                    s.act(EKT[:, :, 0:endi], gposf[:, :, 0:endi], AF.Exp)
                    s.act(EQB[:, :, :], gposf[:, :, 0:SUB], AF.Exp, scale=-1.0,
                          bias=b["consts"][:, 2:3])
                else:
                    s.act(EKT[:, :, 0:endi], SUBT[:, :, 0:endi], AF.Exp)
                    s.act(EQB[:, :, :], SUBQ[:, :, :], AF.Exp, bias=b["consts"][:, 2:3])
                s.mark("ac")
            evek = p.val("ac")

            @p.on("vector")
            def _(ve, I=I, endi=endi, sl=sl):
                v = Eng(p, ve, "vector")
                v.wait_ge(p.hnd["ac"], evek)
                v.tt(KTIL[:, :, 0:endi], KR3[sl][:, :, 0:endi], EKT[:, :, 0:endi], ALU.mult)
                v.tt(QIB[:, :, :], QR3[sl][:, :, I * SUB:(I + 1) * SUB], EQB[:, :, :], ALU.mult)
                v.mark("ev")
            evqi = p.val("ev")

            @p.on("tensor")
            def _(t, I=I, endi=endi):
                t.wait_ge(p.hnd["ev"], evqi)
                if I == 0:
                    for i in (2, 3, 4, 5):
                        wait_tok(t, p, b["bank_free"][i])
                last = None
                for h in range(NH):
                    ab = b["banks"][2 + h // 4]
                    last = t.matmul(ab[0:endi, (h % 4) * 128 + I * SUB:(h % 4) * 128 + (I + 1) * SUB],
                                    KTIL[:, h, 0:endi], QIB[:, h, :],
                                    start=True, stop=True, skip_group_check=True)
                prev_amm[I] = p.inc(last, "mm")
        mmA = p.val("mm")

        @p.on("vector")
        def _(ve):
            v = Eng(p, ve, "vector")
            v.wait_ge(p.hnd["mm"], mmA)
            for h in range(NH):
                ab = b["banks"][2 + h // 4]
                v.tt(ABM[:, h, :], ab[:, (h % 4) * 128:(h % 4 + 1) * 128], MASKF[:, :], ALU.mult)
            v.mark("ev")
        evab = p.val("ev")
        for i in (2, 3, 4, 5):
            b["bank_free"][i] = ("ev", evab)


        oevs = {}
        for h in range(NH):
            ob = b["banks"][6 + h % 2]

            @p.on("tensor")
            def _(t, h=h, ob=ob, m=m):
                if h == 0:
                    t.wait_ge(p.hnd["ev"], evab)
                    for i in (6, 7):
                        wait_tok(t, p, b["bank_free"][i])
                if h >= 2:
                    t.wait_ge(p.hnd["ev"], oevs[h - 2])
                last = t.matmul(ob[:, 0:128], HBF[:, h, :], QTC[:, h, :],
                                start=True, stop=False, skip_group_check=True)
                for I in range(NSUB):
                    endi = (I + 1) * SUB
                    last = t.matmul(ob[:, I * SUB:(I + 1) * SUB],
                                    VTT[0:endi, h, :], ABM[0:endi, h, I * SUB:(I + 1) * SUB],
                                    start=False, stop=(I == NSUB - 1), skip_group_check=True)
                p.inc(last, "mm")
            mmoh = p.val("mm")

            @p.on("vector")
            def _(ve, h=h, ob=ob, mmoh=mmoh, m=m):
                v = Eng(p, ve, "vector")
                v.wait_ge(p.hnd["mm"], mmoh)
                if h <= 1:
                    v.wtok(och_free[m % 2])
                v.cp(OCH[:, h, :], ob[:, 0:128])
                oevs[h] = v.mark("ev")
        evo = p.val("ev")
        b["bank_free"][6] = ("ev", evo)
        b["bank_free"][7] = ("ev", evo)
        prev_gla_mm = p.val("mm")

        semoc = f"soc{m % 2}"

        @p.on("gpsimd")
        def _(gq, m=m, semoc=semoc):
            gq.wait_ge(p.hnd["ev"], evo)
            dst3 = out_op.rearrange("p (h t) -> p h t", t=T)[:, :, m * 128:(m + 1) * 128]
            p.inc(gq.dma_start(out=dst3, in_=OCH[:, :, :]), semoc, 16)
        och_free[m % 2] = (semoc, p.val(semoc))

        @p.on("vector")
        def _(ve, m=m):
            v = Eng(p, ve, "vector")
            v.wait_ge(p.hnd["ev"], evc)
            v.ts(GAMT[:, :, m:m + 1].rearrange("p h o -> p (h o)"), SGP[:, :], -1.0,
                 None, ALU.mult)
            v.tt(SGP[:, :], SGP[:, :], GCT[:, :], ALU.add)
            v.mark("ev")
        endchunk = p.val("ev")
        for k in ("gp", "kk", "vv", "qq"):
            ring_free[k][sl] = ("ev", endchunk)

    @p.on("vector")
    def _(ve):
        v = Eng(p, ve, "vector")
        v.ts(LOGD[:, :], SGP[:, :], -1.0, None, ALU.mult)
        v.mark("ev")

    fin = p.snap()

    @p.on("gpsimd")
    def _(g):
        p.wait(g, fin)
        p.inc(g.dma_start(out=out_st[:, :], in_=HST[:, :, :].rearrange("p h t -> p (h t)")), "st", 16)
        p.inc(g.dma_start(out=out_ld[:, :], in_=LOGD[:, :]), "st", 16)
        p.inc(g.dma_start(out=out_gm[:, :], in_=GAMT[:, :, :].rearrange("p h t -> p (h t)")), "st", 16)
        g.wait_ge(p.hnd["st"], p.val("st"))

    return p


def _l2_host(inputs, l1res):
    x = np.asarray(inputs["L0_attn_a_x"], dtype=np.float32)
    xs = _fm_slices(x)
    wqaT = np.ascontiguousarray(np.asarray(inputs["L0_attn_a_Wq"], np.float32).T)
    woaT = np.ascontiguousarray(np.asarray(inputs["L0_attn_a_Wo"], np.float32).T)
    wqb = np.ascontiguousarray(np.asarray(inputs["L0_attn_b_Wq"], np.float32))
    wfb = np.ascontiguousarray(np.asarray(inputs["L0_attn_b_Wf"], np.float32))
    wib = np.ascontiguousarray(np.asarray(inputs["L0_attn_b_Wi"], np.float32))
    ssg = np.stack([np.ascontiguousarray(l1res[j]["slicesum"]).T.reshape(D)
                    for j in range(NCORE)]).astype(np.float32)
    gate_full = np.ascontiguousarray(np.concatenate(
        [np.asarray(l1res[c]["recon_gu"])[:256] for c in range(NCORE)], 0))
    up_full = np.ascontiguousarray(np.concatenate(
        [np.asarray(l1res[c]["recon_gu"])[256:512] for c in range(NCORE)], 0))
    dn_full = np.ascontiguousarray(np.concatenate(
        [np.asarray(l1res[c]["recon_dn"]) for c in range(NCORE)], 0))
    _L23_CACHE.update(gate_full=gate_full, up_full=up_full, dn_full=dn_full)
    maskf = np.zeros((128, 128), np.float32)
    for tau in range(128):
        for t in range(128):
            if tau <= t:
                maskf[tau, t] = 1.0
    ident = np.eye(128, dtype=ml_dtypes.bfloat16)
    cst = _consts()
    in_maps = []
    for c in range(NCORE):
        mpfx = (np.arange(8) < c).astype(np.float32).reshape(8, 1)
        m = dict(x=xs[c], wqaT=wqaT, woaT=woaT, wqb=wqb, wfb=wfb, wib=wib,
                 kvt=np.ascontiguousarray(np.asarray(l1res[c]["kvt"])),
                 ssg=ssg, mpfx=mpfx, gate_w=gate_full, up_w=up_full, dn_w=dn_full,
                 ident_bf=ident, maskf=maskf, **cst)
        in_maps.append(m)
    return in_maps


def run_l2(inputs, l1res):
    in_maps = _l2_host(inputs, l1res)
    prog = build_l2()
    res = run_bass_kernel_spmd(prog.nc, in_maps, core_ids=list(range(NCORE)))
    return res.results



def build_l3():
    p = Prog()
    for s in ("lx", "lxr0", "lxr1", "sx3a", "sx3b", "lld", "lsst0", "lsst1",
              "lqt0", "lqt1", "lgmt", "lop"):
        p.sem(s)
    x3_in = p.din("x3", [128, CH * T], F32)
    op_in = p.din("opart", [128, CH * T], BF16)
    qt_in = p.din("qtil", [128, NCHK * D], BF16)
    sall_in = p.din("sall", [8 * 128, NH * 128], F32)
    ldall_in = p.din("ldall", [8, D], F32)
    wscanT_in = p.din("wscanT", [8, 8], F32)
    biarep_in = p.din("biarep", [128, 128], F32)
    gamlog_in = p.din("gamlog", [128, NH * NCHK], F32)
    gw_in = p.din("gw", [128, NH], F32)
    wob_in = p.din("wob", [D, D], F32)
    gate_w = p.din("gate_w", [D, INTER], BF16)
    up_w = p.din("up_w", [D, INTER], BF16)
    dn_w = p.din("dn_w", [INTER, D], BF16)
    ones_bf_in = p.din("ones_bf_in", [128, 1], BF16)
    ones_row_f_in = p.din("ones_row_f_in", [1, 128], F32)

    out_y = p.dout("y", [128, CH * T], F32)

    sc_x = p.dtmp("sc_x", [128, CH * T], F32)

    ARE = p.sb("ARE", [128, 49152], BF16)
    X = ARE[:, 0:32768].bitcast(F32)
    XLN = ARE[:, 32768:49152]
    Hbuf = ARE[:, 0:32768]
    SCR = p.sb("SCR", [128, 16384], BF16)
    OBUF = SCR[:, :]
    HIN = XLN[:, 0:4096].bitcast(F32)
    LDA = XLN[:, 4096:8192].bitcast(F32)[0:8, 0:2048]
    SSTR = [XLN[:, 8192 + i * 4096: 8192 + (i + 1) * 4096].bitcast(F32) for i in range(2)]
    QTR = [XLN[:, 8192 + i * 2048: 8192 + (i + 1) * 2048] for i in range(2)]
    XR = [p.sb(f"xr{i}", [128, HW], F32) for i in range(2)]
    XOT = [p.sb(f"xot{i}", [128, HW], F32) for i in range(2)]
    FE4 = p.sb("FE4", [128, 4 * HW], F32)
    WSC = p.sb("WSC", [8, 8], F32)
    BIA = p.sb("BIA", [128, 128], F32)
    GMT = p.sb("GMT", [128, 128], F32)
    GW = p.sb("GW", [128, NH], F32)
    COEF = p.sb("COEF", [128, 128], F32)
    HCB = p.sb("HCB", [128, 128], BF16)
    EG = p.sb("EG", [128, 2 * NH], F32)

    p.start_block()
    b = make_shared(p)
    fill_consts(p, b)

    load_consts(p, b, [
        (b["ones_bf"][:, :], ones_bf_in[:, :]),
        (b["ones_row_f"][:, :], ones_row_f_in[:, :]),
        (WSC[:, :], wscanT_in[:, :]),
        (BIA[:, :], biarep_in[:, :]),
        (GMT[:, :], gamlog_in[:, :]),
        (GW[:, :], gw_in[:, :]),
    ])

    @p.on("sync")
    def _(sync):
        for ch in range(CH):
            p.inc(sync.dma_start(out=X[:, chs(ch)], in_=x3_in[:, chs(ch)]), "lx", 16)
        p.inc(sync.dma_start(out=LDA, in_=ldall_in[:, :]), "lld", 16)

    @p.on("gpsimd")
    def _(g):
        p.inc(g.dma_start(out=OBUF[:, :], in_=op_in[:, :]), "lop", 16)
    lop_done = p.val("lop")

    lld = p.val("lld")
    lxv = p.val("lx")

    @p.on("tensor")
    def _(t):
        t.wait_ge(p.hnd["lld"], lld)
        t.wait_ge(p.hnd["lx"], lxv)
        p.wait(t, {"ev": p.val("ev"), "ac": p.val("ac")})
        last = None
        for h in range(NH):
            last = t.matmul(b["banks"][0][:, h * 8:(h + 1) * 8],
                            LDA[0:8, h * 128:(h + 1) * 128], WSC[0:8, 0:8],
                            start=True, stop=True, skip_group_check=True)
        p.inc(last, "mm")
    mme = p.val("mm")

    hold = {}

    @p.on("vector")
    def _(ve):
        v = Eng(p, ve, "vector")
        v.wait_ge(p.hnd["mm"], mme)
        v.tt(COEF[:, :], b["banks"][0][:, 0:128], BIA[:, :], ALU.add)
        hold["v"] = v.mark("ev")
    b["bank_free"][0] = ("ev", p.val("ev"))

    @p.on("scalar")
    def _(se):
        s = Eng(p, se, "scalar")
        s.wait_ge(p.hnd["ev"], hold["v"])
        s.act(COEF[:, :], COEF[:, :], AF.Exp)
        s.mark("ac")
    ace = p.val("ac")

    TMPH = XLN[:, 4096:8192].bitcast(F32)
    HCB16 = [XLN[:, 4096:6144], XLN[:, 12288:14336]]
    sst_free = [None, None]
    for j in range(NCORE):
        sl = j % 2

        @p.on("sync")
        def _(sync, j=j, sl=sl):
            wait_tok(sync, p, sst_free[sl])
            p.inc(sync.dma_start(out=SSTR[sl][:, :],
                                 in_=sall_in[j * 128:(j + 1) * 128, :]), f"lsst{sl}", 16)
        lss = p.val(f"lsst{sl}")

        @p.on("vector")
        def _(ve, j=j, sl=sl, lss=lss):
            v = Eng(p, ve, "vector")
            v.wait_ge(p.hnd[f"lsst{sl}"], lss)
            if j == 0:
                v.wait_ge(p.hnd["ac"], ace)
            sv = SSTR[sl].rearrange("p (h t) -> p h t", t=128)
            cj = COEF.rearrange("p (h j) -> p h j", j=8)[:, :, j:j + 1] \
                .broadcast_to([128, NH, 128])
            tmp = TMPH.rearrange("p (h t) -> p h t", t=128)
            hv = HIN.rearrange("p (h t) -> p h t", t=128)
            if j == 0:
                v.tt(hv, sv, cj, ALU.mult)
            else:
                v.tt(tmp, sv, cj, ALU.mult)
                v.tt(hv, hv, tmp, ALU.add)
            v.mark("ev")
        sst_free[sl] = ("ev", p.val("ev"))
    hin_done = p.val("ev")

    qtr_free = [None, None]
    hcb_free = [None, None]
    eg_free = [None, None]
    OB3 = OBUF.rearrange("p (h t) -> p h t", t=T)
    for m in range(NCHK):
        sl = m % 2

        @p.on("sync")
        def _(sync, m=m, sl=sl):
            if m <= 1:
                p.wait(sync, {"ev": hin_done})
            wait_tok(sync, p, qtr_free[sl])
            p.inc(sync.dma_start(out=QTR[sl][:, :], in_=qt_in[:, m * D:(m + 1) * D]),
                  f"lqt{sl}", 16)
        lqt = p.val(f"lqt{sl}")

        holdg = {}

        @p.on("scalar")
        def _(se, m=m, sl=sl):
            s = Eng(p, se, "scalar")
            if m == 0:
                s.wait_ge(p.hnd["ev"], hin_done)
            s.wtok(eg_free[sl])
            g3 = GMT[:, :].rearrange("p (h mm) -> p h mm", mm=NCHK)
            s.act(EG[:, sl * NH:(sl + 1) * NH].rearrange("p (h o) -> p h o", o=1),
                  g3[:, :, m:m + 1], AF.Exp)
            holdg["v"] = s.mark("ac")

        holdh = {}

        @p.on("vector")
        def _(ve, m=m, sl=sl, holdg=holdg):
            v = Eng(p, ve, "vector")
            v.wait_ge(p.hnd["ac"], holdg["v"])
            if m == 0:
                v.wait_ge(p.hnd["lop"], lop_done)
            v.wtok(hcb_free[sl])
            egb = EG[:, sl * NH:(sl + 1) * NH] \
                .rearrange("p (h o) -> p h o", o=1).broadcast_to([128, NH, 128])
            v.tt(HCB16[sl].rearrange("p (h t) -> p h t", t=128),
                 HIN.rearrange("p (h t) -> p h t", t=128), egb, ALU.mult)
            holdh["v"] = v.mark("ev")
        eg_free[sl] = ("ev", p.val("ev"))

        bank0 = (m % 2) * 4
        mmc = {}

        @p.on("tensor")
        def _(t, m=m, sl=sl, holdh=holdh, lqt=lqt, bank0=bank0):
            t.wait_ge(p.hnd["ev"], holdh["v"])
            t.wait_ge(p.hnd["lqt" + str(sl)], lqt)
            for i in range(bank0, bank0 + 4):
                wait_tok(t, p, b["bank_free"][i])
            hcv = HCB16[sl].rearrange("p (h t) -> p h t", t=128)
            for h in range(NH):
                ps = b["banks"][bank0 + h // 4]
                last = t.matmul(ps[:, (h % 4) * 128:(h % 4 + 1) * 128],
                                hcv[:, h, :], QTR[sl][:, h * 128:(h + 1) * 128],
                                start=True, stop=True, skip_group_check=True)
                if h % 4 == 3:
                    mmc[h // 4] = p.inc(last, "mm")
        hcb_free[sl] = ("mm", mmc[3])
        qtr_free[sl] = ("mm", mmc[3])

        evcs = {}

        @p.on("vector")
        def _(ve, m=m, bank0=bank0):
            v = Eng(p, ve, "vector")
            for k in range(4):
                v.wait_ge(p.hnd["mm"], mmc[k])
                bv = b["banks"][bank0 + k].rearrange("p (h t) -> p h t", t=128)
                ob = OB3[:, 4 * k:4 * (k + 1), m * 128:(m + 1) * 128]
                v.tt(ob, ob, bv, ALU.add)
                evcs[k] = v.mark("ev")
        for k in range(4):
            b["bank_free"][bank0 + k] = ("ev", evcs[k])

    corr_done = p.val("ev")
    ps_sq = [b["banks"][0], b["banks"][1]]
    mm_by_h = {}
    for h in range(NH):
        holds = {}

        @p.on("vector")
        def _(ve, h=h):
            v = Eng(p, ve, "vector")
            if h == 0:
                v.wait_ge(p.hnd["ev"], corr_done)
            if h >= 1:
                v.wait_ge(p.hnd["mm"], mm_by_h[h - 1])
            v.tt(b["xbf"][0][:, :], OBUF[:, chs(h)], OBUF[:, chs(h)], ALU.mult)
            holds["v"] = v.mark("ev")

        @p.on("tensor")
        def _(t, h=h):
            if h == 0:
                for i in (0, 1):
                    wait_tok(t, p, b["bank_free"][i])
            t.wait_ge(p.hnd["ev"], holds["v"])
            last = None
            for hh in (0, 1):
                last = t.matmul(ps_sq[hh][0:1, :], b["ones_bf"][:, :],
                                b["xbf"][0][:, hsl(hh)],
                                start=(h == 0), stop=(h == NH - 1))
            mm_by_h[h] = p.inc(last, "mm")
    mm_sq = p.val("mm")

    hold2 = {}

    @p.on("vector")
    def _(ve):
        v = Eng(p, ve, "vector")
        v.wait_ge(p.hnd["mm"], mm_sq)
        for hh in (0, 1):
            v.cp(row(b, 0)[:, hsl(hh)], ps_sq[hh][0:1, :])
        v.ts(row(b, 0), row(b, 0), 1.0 / D, None, ALU.mult)
        hold2["v"] = v.mark("ev")

    hold3 = {}

    @p.on("scalar")
    def _(se):
        s = Eng(p, se, "scalar")
        s.wait_ge(p.hnd["ev"], hold2["v"])
        s.act(row(b, 2), row(b, 0), AF.Ln, bias=b["consts"][0:1, 0:1])
        s.act(row(b, 1), row(b, 2), AF.Exp, scale=-0.5)
        hold3["v"] = s.mark("ac")

    @p.on("tensor")
    def _(t):
        t.wait_ge(p.hnd["ac"], hold3["v"])
        for i in (2, 3):
            wait_tok(t, p, b["bank_free"][i])
        last = None
        for hh in (0, 1):
            last = t.matmul(b["banks"][2 + hh][:, :], b["ones_row_f"][:, :],
                            row(b, 1)[0:1, hsl(hh)], start=True, stop=True)
        p.inc(last, "mm")
    mmrb = p.val("mm")

    @p.on("vector")
    def _(ve):
        v = Eng(p, ve, "vector")
        v.wait_ge(p.hnd["mm"], mmrb)
        for hh in (0, 1):
            v.cp(b["rb"][:, hsl(hh)], b["banks"][2 + hh][:, :])
        for h in range(NH):
            for hh in (0, 1):
                v.ts(b["aptmp"][:, :], OBUF[:, tok(h, hh)], GW[:, h:h + 1], None, ALU.mult)
                v.tt(OBUF[:, tok(h, hh)], b["aptmp"][:, :], b["rb"][:, hsl(hh)], ALU.mult)
        v.mark("ev")
    nrm_done = p.val("ev")
    for i in range(8):
        b["bank_free"][i] = ("ev", nrm_done)

    for g in range(4):
        ldo, stripo = load_strip(p, b, g % 2, wob_in, g * 512, CH, q="gpsimd")
        mmo = proj_group(p, b, stripo, ldo, OBUF, CH, start=True, stop=True)
        b["strip_free"][g % 2] = ("mm", mmo[(3, 1)])

        @p.on("vector")
        def _(ve, g=g, mmo=mmo):
            v = Eng(p, ve, "vector")
            for oi in range(4):
                oc = g * 4 + oi
                for hh in (0, 1):
                    v.wait_ge(p.hnd["mm"], mmo[(oi, hh)])
                    v.tt(X[:, tok(oc, hh)], b["banks"][oi * 2 + hh][:, :],
                         X[:, tok(oc, hh)], ALU.add)
            v.mark("ev")
        for i in range(8):
            b["bank_free"][i] = ("ev", p.val("ev"))

    emit_ln(p, b, X, XLN, "ln4")

    emit_ffn(p, b, X, XLN, Hbuf, gate_w, up_w, dn_w, sc_x, out_y, XR, XOT, FE4)

    fin = p.snap()

    @p.on("gpsimd")
    def _(g):
        p.wait(g, fin)

    return p


def _l3_host(inputs, l2res):
    gate_full = _L23_CACHE["gate_full"]
    up_full = _L23_CACHE["up_full"]
    dn_full = _L23_CACHE["dn_full"]
    ldall = np.stack([np.ascontiguousarray(np.asarray(l2res[j]["logd"])).T.reshape(D)
                      for j in range(NCORE)]).astype(np.float32)
    sall = np.ascontiguousarray(np.concatenate(
        [np.asarray(l2res[j]["sstate"]) for j in range(NCORE)], 0))
    gwv = np.asarray(inputs["L0_attn_b_g_weight"], np.float32)
    gw = np.ascontiguousarray(gwv.reshape(NH, 128).T)
    wob = np.ascontiguousarray(np.asarray(inputs["L0_attn_b_Wo"], np.float32))
    cst = _consts()
    in_maps = []
    for c in range(NCORE):
        wscanT = np.zeros((8, 8), np.float32)
        for j in range(8):
            for i in range(8):
                if j < i < c:
                    wscanT[i, j] = 1.0
        bia = np.where(np.arange(8) < c, 0.0, -1e30).astype(np.float32)
        biarep = np.tile(bia, 16)[None, :].repeat(128, 0)
        r = l2res[c]
        m = dict(x3=np.ascontiguousarray(np.asarray(r["x3"])),
                 opart=np.ascontiguousarray(np.asarray(r["opart"])),
                 qtil=np.ascontiguousarray(np.asarray(r["qtil"])),
                 sall=sall, ldall=ldall, wscanT=wscanT, biarep=biarep,
                 gamlog=np.ascontiguousarray(np.asarray(r["gamlog"])),
                 gw=gw, wob=wob, gate_w=gate_full, up_w=up_full, dn_w=dn_full,
                 **cst)
        in_maps.append(m)
    return in_maps


def run_l3(inputs, l2res):
    in_maps = _l3_host(inputs, l2res)
    prog = build_l3()
    res = run_bass_kernel_spmd(prog.nc, in_maps, core_ids=list(range(NCORE)))
    return res.results


def kernel(**inputs):
    l1res = run_l1(inputs)
    l2res = run_l2(inputs, l1res)
    l3res = run_l3(inputs, l2res)
    outs = []
    for c in range(NCORE):
        y = np.asarray(l3res[c]["y"], np.float32)
        outs.append(y.reshape(128, CH, T).transpose(2, 1, 0).reshape(T, D))
    return np.concatenate(outs, 0).reshape(1, SEQ, D).astype(np.float32)


# version 05 (final)
# speedup vs baseline: 1.2765x; 1.0140x over previous
import contextlib
import math

import numpy as np
import ml_dtypes

import concourse.bass as bass
import concourse.mybir as mybir
from concourse.bass_utils import run_bass_kernel_spmd

F32 = mybir.dt.float32
BF16 = mybir.dt.bfloat16
AF = mybir.ActivationFunctionType
ALU = mybir.AluOpType
AX = mybir.AxisListType

NCORE = 8
D = 2048
SEQ = 8192
T = SEQ // NCORE
HW = 512
CH = D // 128
INTER = 8192
ICH = INTER // 128
NH, DK, DV = 16, 128, 128
CHUNK = 128
NCHK = T // CHUNK
SUB = 16
NSUB = CHUNK // SUB
EPS = 1e-5
DENOM = math.sqrt(2048.0) / 8192.0
LN_C = math.log(DK ** -0.5)



class Prog:

    def __init__(self):
        self.nc = bass.Bass("TRN2", target_bir_lowering=False,
                            detect_race_conditions=False)
        self.ctx = contextlib.ExitStack()
        self.hnd = {}
        self.cnt = {}
        self.block = None
        self.haz = {k: {"w": [], "r": [], "last": None}
                    for k in ("vector", "scalar", "gpsimd")}
        for s in ("gd", "mm", "ev", "ac", "st", "vsy", "ssy", "gsy"):
            self.sem(s)

    def sem(self, name):
        if name not in self.hnd:
            self.hnd[name] = self.ctx.enter_context(self.nc.semaphore(name))
            self.cnt[name] = 0
        return self.hnd[name]

    def inc(self, inst, name, k=1):
        inst.then_inc(self.hnd[name], k)
        self.cnt[name] += k
        return self.cnt[name]

    def val(self, name):
        return self.cnt[name]

    def start_block(self):
        self.block = self.ctx.enter_context(self.nc.Block())

    def on(self, eng):
        def deco(fn):
            getattr(self.block, eng)(fn)
        return deco

    def snap(self):
        return dict(self.cnt)

    def wait(self, eng, snap):
        for name, v in snap.items():
            if v > 0:
                eng.wait_ge(self.hnd[name], v)

    def sb(self, name, shape, dt):
        return self.ctx.enter_context(self.nc.sbuf_tensor(name, list(shape), dt))

    def psb(self, name, shape, dt):
        return self.ctx.enter_context(self.nc.psum_tensor(name, list(shape), dt))

    def din(self, name, shape, dt):
        return self.nc.dram_tensor(name, list(shape), dt, kind="ExternalInput").ap()

    def dout(self, name, shape, dt):
        return self.nc.dram_tensor(name, list(shape), dt, kind="ExternalOutput").ap()

    def dtmp(self, name, shape, dt):
        return self.nc.dram_tensor(name, list(shape), dt).ap()


_AUTOSEM = {"vector": "vsy", "scalar": "ssy", "gpsimd": "gsy"}


def _region(a):
    try:
        name = a.tensor.name
        es = mybir.dt.size(a.dtype)
        apl = [(int(s), int(c)) for s, c in a.ap]
    except (AttributeError, TypeError):
        return None
    lo = int(a.offset) * es
    if len(apl) > 1:
        skip = max(range(len(apl)), key=lambda i: apl[i][0])
    else:
        skip = -1
    span = es
    for i, (s, c) in enumerate(apl):
        if i != skip:
            span += es * abs(s) * (c - 1)
    return (name, lo, lo + span)


def _overlap(r1, r2):
    return r1[0] == r2[0] and r1[1] < r2[2] and r2[1] < r1[2]


class Eng:

    def __init__(self, p, e, kind):
        self.p, self.e, self.kind = p, e, kind
        self.h = p.haz[kind]

    def _sync(self):
        h = self.h
        if h["last"] is not None:
            inst, sname, sval = h["last"]
            if sname is None:
                sname = _AUTOSEM[self.kind]
                sval = self.p.inc(inst, sname)
                h["last"] = (inst, sname, sval)
            self.e.wait_ge(self.p.hnd[sname], sval)
        h["w"].clear()
        h["r"].clear()

    def _op(self, call, wr, rd):
        h = self.h
        wn = [r for r in (_region(a) for a in wr) if r]
        rn = [r for r in (_region(a) for a in rd) if r]
        conflict = (
            any(_overlap(r, w) for r in rn for w in h["w"])
            or any(_overlap(w1, w2) for w1 in wn for w2 in h["w"])
            or any(_overlap(w, r) for w in wn for r in h["r"])
        )
        if conflict:
            self._sync()
        inst = call()
        h["w"].extend(wn)
        h["r"].extend(rn)
        h["last"] = (inst, None, None)
        return inst

    def mark(self, sem, k=1):
        h = self.h
        inst, sname, sval = h["last"]
        assert sname is None, "op already has an inc"
        n = self.p.inc(inst, sem, k)
        h["last"] = (inst, sem, n)
        return n

    def barrier(self):
        self._sync()

    def wait_ge(self, semh, v):
        self.e.wait_ge(semh, v)

    def wtok(self, t):
        if t is not None:
            self.e.wait_ge(self.p.hnd[t[0]], t[1])

    def wsnap(self, snap):
        self.p.wait(self.e, snap)

    def tt(self, out, a, b2, op):
        return self._op(lambda: self.e.tensor_tensor(out, a, b2, op), [out], [a, b2])

    def ts(self, out, a, s1, s2, op0, op1=None):
        rd = [a] + [s for s in (s1, s2) if hasattr(s, "tensor")]
        if op1 is None:
            return self._op(lambda: self.e.tensor_scalar(out, a, s1, s2, op0), [out], rd)
        return self._op(lambda: self.e.tensor_scalar(out, a, s1, s2, op0, op1), [out], rd)

    def cp(self, out, a):
        return self._op(lambda: self.e.tensor_copy(out, a), [out], [a])

    def red(self, out, a, op):
        return self._op(lambda: self.e.tensor_reduce(out, a, axis=AX.X, op=op), [out], [a])

    def mset(self, out, v):
        return self._op(lambda: self.e.memset(out, v), [out], [])

    def act(self, out, a, func, bias=None, scale=None, accum_out=None):
        kw = {}
        if bias is not None:
            kw["bias"] = bias
        if scale is not None:
            kw["scale"] = scale
        if accum_out is not None:
            kw["accum_out"] = accum_out
        rd = [a] + [x for x in (bias, scale) if hasattr(x, "tensor")]
        wr = [out] + ([accum_out] if accum_out is not None else [])
        return self._op(lambda: self.e.activation(out, a, func, **kw), wr, rd)


def chs(ch, w=T):
    return slice(ch * w, (ch + 1) * w)


def tok(ch, hh, w=T):
    return slice(ch * w + hh * HW, ch * w + (hh + 1) * HW)


def hsl(hh):
    return slice(hh * HW, (hh + 1) * HW)


def wait_tok(eng, p, t):
    if t is not None:
        eng.wait_ge(p.hnd[t[0]], t[1])


def make_shared(p):
    b = {}
    b["banks"] = [p.psb(f"pb{i}", [128, 512], F32) for i in range(8)]
    b["bank_free"] = [None] * 8
    b["ones_bf"] = p.sb("ones_bf", [128, 1], BF16)
    b["ones_row_f"] = p.sb("ones_row_f", [1, 128], F32)
    b["xbf"] = [p.sb("xbf0", [128, T], BF16)]
    b["xsqb"] = [p.sb("xsqb0", [128, T], BF16)]
    b["rows"] = p.sb("rows", [1, 3 * T], F32)
    b["mb"] = p.sb("mbb", [128, T], F32)
    b["rb"] = p.sb("rbb", [128, T], F32)
    b["aptmp"] = p.sb("aptmp", [128, HW], F32)
    b["strip"] = [p.sb(f"strip{i}", [128, 16 * 512], BF16) for i in range(2)]
    b["strip_free"] = [None, None]
    b["consts"] = p.sb("consts", [128, 4], F32)
    return b


def row(b, i):
    return b["rows"][:, i * T:(i + 1) * T]


def fill_consts(p, b):
    @p.on("vector")
    def _(ve):
        v = Eng(p, ve, "vector")
        v.mset(b["consts"][:, 0:1], EPS)
        v.mset(b["consts"][:, 1:2], 1.0)
        v.mset(b["consts"][:, 2:3], LN_C)
        v.mark("ev")


def load_consts(p, b, pairs):
    @p.on("sync")
    def _(sync):
        for dst, src in pairs:
            p.inc(sync.dma_start(out=dst, in_=src), "lx", 16)


def cast_weight(p, dst_bf, src_f32, pieces=4):
    rows_ = src_f32.shape[0]
    step = rows_ // pieces
    @p.on("gpsimd")
    def _(g):
        for i in range(pieces):
            p.inc(g.dma_start(out=dst_bf[i * step:(i + 1) * step, :],
                              in_=src_f32[i * step:(i + 1) * step, :]), "gd", 16)


def emit_ln(p, b, X, XLN, label):
    ps_sum = [b["banks"][0], b["banks"][1]]
    ps_sq = [b["banks"][2], b["banks"][3]]
    ps_mb = [b["banks"][4], b["banks"][5]]
    ps_rb = [b["banks"][6], b["banks"][7]]
    pre = p.snap()
    mm_by_ch = {}

    for ch in range(CH):
        cur = 0
        hold = {}

        @p.on("vector")
        def _(ve, ch=ch, cur=cur):
            v = Eng(p, ve, "vector")
            if ch == 0:
                v.wsnap(pre)
            if ch >= 1:
                v.wait_ge(p.hnd["mm"], mm_by_ch[ch - 1])
            v.cp(b["xbf"][cur][:, :], X[:, chs(ch)])
            v.tt(b["xsqb"][cur][:, :], X[:, chs(ch)], X[:, chs(ch)], ALU.mult)
            hold["v"] = v.mark("ev")

        @p.on("tensor")
        def _(t, ch=ch, cur=cur):
            if ch == 0:
                for i in range(8):
                    wait_tok(t, p, b["bank_free"][i])
            t.wait_ge(p.hnd["ev"], hold["v"])
            last = None
            for hh in (0, 1):
                t.matmul(ps_sum[hh][0:1, :], b["ones_bf"][:, :], b["xbf"][cur][:, hsl(hh)],
                         start=(ch == 0), stop=(ch == CH - 1))
                last = t.matmul(ps_sq[hh][0:1, :], b["ones_bf"][:, :], b["xsqb"][cur][:, hsl(hh)],
                                start=(ch == 0), stop=(ch == CH - 1))
            mm_by_ch[ch] = p.inc(last, "mm")

    mm_done = p.val("mm")
    hold2 = {}

    @p.on("vector")
    def _(ve):
        v = Eng(p, ve, "vector")
        v.wait_ge(p.hnd["mm"], mm_done)
        for hh in (0, 1):
            v.cp(row(b, 0)[:, hsl(hh)], ps_sum[hh][0:1, :])
            v.cp(row(b, 1)[:, hsl(hh)], ps_sq[hh][0:1, :])
        v.ts(row(b, 0), row(b, 0), 1.0 / D, None, ALU.mult)
        v.ts(row(b, 1), row(b, 1), 1.0 / D, None, ALU.mult)
        v.tt(row(b, 2), row(b, 0), row(b, 0), ALU.mult)
        v.tt(row(b, 1), row(b, 1), row(b, 2), ALU.subtract)
        hold2["v"] = v.mark("ev")

    hold3 = {}

    @p.on("scalar")
    def _(se):
        s = Eng(p, se, "scalar")
        s.wait_ge(p.hnd["ev"], hold2["v"])
        s.act(row(b, 2), row(b, 1), AF.Ln, bias=b["consts"][0:1, 0:1])
        s.act(row(b, 1), row(b, 2), AF.Exp, scale=-0.5)
        hold3["v"] = s.mark("ac")

    @p.on("tensor")
    def _(t):
        t.wait_ge(p.hnd["ac"], hold3["v"])
        last = None
        for hh in (0, 1):
            t.matmul(ps_mb[hh][:, :], b["ones_row_f"][:, :], row(b, 0)[0:1, hsl(hh)],
                     start=True, stop=True)
            last = t.matmul(ps_rb[hh][:, :], b["ones_row_f"][:, :], row(b, 1)[0:1, hsl(hh)],
                            start=True, stop=True)
        p.inc(last, "mm")
    mmb = p.val("mm")

    @p.on("vector")
    def _(ve):
        v = Eng(p, ve, "vector")
        v.wait_ge(p.hnd["mm"], mmb)
        for hh in (0, 1):
            v.cp(b["mb"][:, hsl(hh)], ps_mb[hh][:, :])
            v.cp(b["rb"][:, hsl(hh)], ps_rb[hh][:, :])
        for ch in range(CH):
            for hh in (0, 1):
                v.tt(b["aptmp"][:, :], X[:, tok(ch, hh)], b["mb"][:, hsl(hh)], ALU.subtract)
                v.tt(XLN[:, tok(ch, hh)], b["aptmp"][:, :], b["rb"][:, hsl(hh)], ALU.mult)
        v.mark("ev")
    evd = p.val("ev")
    for i in range(8):
        b["bank_free"][i] = ("ev", evd)
    return ("ev", evd)


def load_strip(p, b, sbuf_idx, w_dram, col0, k_chunks, kc0=0, q="sync"):
    prev = b["strip_free"][sbuf_idx]
    strip = b["strip"][sbuf_idx]
    semn = f"lw{sbuf_idx}"
    p.sem(semn)

    @p.on(q)
    def _(sync):
        wait_tok(sync, p, prev)
        src = w_dram[kc0 * 128:(kc0 + k_chunks) * 128, col0:col0 + 512] \
            .rearrange("(k p) c -> p k c", p=128)
        dst = strip[:, 0:k_chunks * 512].rearrange("p (k c) -> p k c", k=k_chunks)
        p.inc(sync.dma_start(out=dst, in_=src), semn, 16)
    return (semn, p.val(semn)), strip


def proj_group(p, b, strip, ldv, src, k_chunks, start, stop, kc_shift=0):
    mmv = {}

    @p.on("tensor")
    def _(t):
        wait_tok(t, p, ldv)
        for oi in range(4):
            for hh in (0, 1):
                ps = b["banks"][oi * 2 + hh]
                if start:
                    wait_tok(t, p, b["bank_free"][oi * 2 + hh])
                last = None
                for kc in range(k_chunks):
                    last = t.matmul(
                        ps[:, :],
                        strip[:, kc * 512 + oi * 128: kc * 512 + (oi + 1) * 128],
                        src[:, (kc + kc_shift) * T + hh * HW:(kc + kc_shift) * T + (hh + 1) * HW],
                        start=(start and kc == 0), stop=(stop and kc == k_chunks - 1))
                mmv[(oi, hh)] = p.inc(last, "mm")
    return mmv



def build_l1():
    p = Prog()
    p.sem("lx")
    for s in ("lr0", "lr1", "lr2", "sro0", "sro1", "skv0", "skv1"):
        p.sem(s)
    x_in = p.din("x", [D, T], F32)
    wkT = p.din("wkT", [D, D], F32)
    wvT = p.din("wvT", [D, D], F32)
    ones_bf_in = p.din("ones_bf_in", [128, 1], BF16)
    ones_row_f_in = p.din("ones_row_f_in", [1, 128], F32)

    gu_tap = [p.din(f"gu_tap{k}", [512, INTER], F32) for k in range(4)]
    gu_c = [p.din(f"gu_c{k}", [256, INTER], F32) for k in range(4)]
    dn_tap = [p.din(f"dn_tap{k}", [1024, D], F32) for k in range(4)]
    dn_c = [p.din(f"dn_c{k}", [1024, D], F32) for k in range(4)]

    out_ss = p.dout("slicesum", [128, CH], F32)
    out_kvt = p.dout("kvt", [128, CH * T], BF16)
    out_gu = p.dout("recon_gu", [512, INTER], BF16)
    out_dn = p.dout("recon_dn", [1024, D], BF16)

    X = p.sb("X", [128, CH * T], F32)
    XLN = p.sb("XLN", [128, CH * T], BF16)
    KVTILE = [p.sb(f"kvtile{i}", [128, T], BF16) for i in range(2)]
    TMPK = p.sb("TMPK", [128, 4 * T], F32)
    KVF = p.sb("KVF", [128, T], F32)
    CS2 = p.sb("CS2", [128, T], F32)
    SS = p.sb("SS", [128, CH], F32)
    RW = 512
    RSL = [X[:, i * 4096:(i + 1) * 4096] for i in range(3)]
    RT4 = X[:, 12288:14336]
    AB = X[:, 14336:15360]
    ROv = X[:, 15360:15872].bitcast(BF16)
    RO = [ROv[:, 0:512], ROv[:, 512:1024]]

    p.start_block()
    b = make_shared(p)
    fill_consts(p, b)

    load_consts(p, b, [
        (b["ones_bf"][:, :], ones_bf_in[:, :]),
        (b["ones_row_f"][:, :], ones_row_f_in[:, :]),
    ])

    @p.on("sync")
    def _(sync):
        for ch in range(CH):
            p.inc(sync.dma_start(out=X[:, chs(ch)], in_=x_in[ch * 128:(ch + 1) * 128, :]), "lx", 16)

    ln1_tok = emit_ln(p, b, X, XLN, "ln1")

    tiles = []
    for rc in range(4):
        for sg in range(INTER // RW):
            tiles.append(("gu", rc, sg))
    for rc in range(8):
        for sg in range(D // RW):
            tiles.append(("dn", rc, sg))
    rst = {"i": 0, "slot_free": [ln1_tok, ln1_tok, ln1_tok], "ro_free": [None, None]}

    def emit_recon(n):
        for _ in range(n):
            ti = rst["i"]
            if ti >= len(tiles):
                return
            rst["i"] = ti + 1
            kind, rc, sg = tiles[ti]
            slot = ti % 3
            q = ["sync", "scalar"][ti % 2] if ti < 56 else \
                ["sync", "scalar", "gpsimd"][ti % 3]
            semr = f"lr{slot}"
            sfree = rst["slot_free"][slot]

            @p.on(q)
            def _(qe, kind=kind, rc=rc, sg=sg, slot=slot, semr=semr, sfree=sfree):
                wait_tok(qe, p, sfree)
                for k in range(4):
                    if kind == "gu":
                        tapsrc = gu_tap[k][rc * 128:(rc + 1) * 128, sg * RW:(sg + 1) * RW]
                        csrc = gu_c[k][(rc % 2) * 128:(rc % 2 + 1) * 128, sg * RW:(sg + 1) * RW]
                    else:
                        tapsrc = dn_tap[k][rc * 128:(rc + 1) * 128, sg * RW:(sg + 1) * RW]
                        csrc = dn_c[k][rc * 128:(rc + 1) * 128, sg * RW:(sg + 1) * RW]
                    p.inc(qe.dma_start(out=RSL[slot][:, k * RW:(k + 1) * RW], in_=tapsrc), semr, 16)
                    p.inc(qe.dma_start(out=RSL[slot][:, (4 + k) * RW:(5 + k) * RW], in_=csrc), semr, 16)
            ldr = p.val(semr)
            ri = ti % 2
            rfree = rst["ro_free"][ri]

            @p.on("vector")
            def _(ve, slot=slot, ldr=ldr, semr=semr, ri=ri, rfree=rfree):
                v = Eng(p, ve, "vector")
                v.wait_ge(p.hnd[semr], ldr)
                v.wtok(rfree)
                sl = RSL[slot]
                v.tt(RT4[:, :], sl[:, 0:4 * RW], sl[:, 4 * RW:8 * RW], ALU.mult)
                v.tt(AB[:, 0:RW], RT4[:, 0:RW], RT4[:, RW:2 * RW], ALU.add)
                v.tt(AB[:, RW:2 * RW], RT4[:, 2 * RW:3 * RW], RT4[:, 3 * RW:4 * RW], ALU.add)
                v.tt(RO[ri][:, :], AB[:, 0:RW], AB[:, RW:2 * RW], ALU.add)
                v.mark("ev")
            evr = p.val("ev")
            rst["slot_free"][slot] = ("ev", evr)
            sems = f"sro{ri}"

            @p.on("gpsimd")
            def _(g, kind=kind, rc=rc, sg=sg, ri=ri, evr=evr, sems=sems):
                g.wait_ge(p.hnd["ev"], evr)
                dst = out_gu if kind == "gu" else out_dn
                p.inc(g.dma_start(out=dst[rc * 128:(rc + 1) * 128, sg * RW:(sg + 1) * RW],
                                  in_=RO[ri][:, :]), sems, 16)
            rst["ro_free"][ri] = (sems, p.val(sems))

    b["kvtile_free"] = [None, None]
    for g in range(4):
        ldk, stripk = load_strip(p, b, 0, wkT, g * 512, CH, q="gpsimd")
        mmk = proj_group(p, b, stripk, ldk, XLN, CH, start=True, stop=True)

        @p.on("vector")
        def _(ve, mmk=mmk):
            v = Eng(p, ve, "vector")
            for oi in range(4):
                for hh in (0, 1):
                    v.wait_ge(p.hnd["mm"], mmk[(oi, hh)])
                    v.cp(TMPK[:, tok(oi, hh)], b["banks"][oi * 2 + hh][:, :])
            v.mark("ev")
        for i in range(8):
            b["bank_free"][i] = ("ev", p.val("ev"))
        b["strip_free"][0] = ("mm", mmk[(3, 1)])

        ldv_, stripv = load_strip(p, b, 1, wvT, g * 512, CH, q="gpsimd")
        mmv = proj_group(p, b, stripv, ldv_, XLN, CH, start=True, stop=True)
        b["strip_free"][1] = ("mm", mmv[(3, 1)])

        for oi in range(4):
            ch = g * 4 + oi
            ktile = KVTILE[ch % 2]
            ktile_free = b["kvtile_free"][ch % 2]

            @p.on("vector")
            def _(ve, ch=ch, oi=oi, mmv=mmv, ktile=ktile, ktile_free=ktile_free):
                v = Eng(p, ve, "vector")
                for hh in (0, 1):
                    v.wait_ge(p.hnd["mm"], mmv[(oi, hh)])
                    v.tt(KVF[:, hsl(hh)], TMPK[:, tok(oi, hh)],
                         b["banks"][oi * 2 + hh][:, :], ALU.mult)
                v.red(SS[:, ch:ch + 1], KVF[:, :], ALU.add)
                v.wtok(ktile_free)
                bufs = [KVF, CS2]
                for si, s in enumerate([1, 2, 4, 8, 16, 32, 64, 128, 256, 512]):
                    a = bufs[si % 2]
                    o = ktile if si == 9 else bufs[(si + 1) % 2]
                    v.tt(o[:, s:T], a[:, s:T], a[:, 0:T - s], ALU.add)
                    v.cp(o[:, 0:s], a[:, 0:s])
                v.mark("ev")
            b["bank_free"][oi * 2] = ("ev", p.val("ev"))
            b["bank_free"][oi * 2 + 1] = ("ev", p.val("ev"))
            evk = p.val("ev")

            semk = f"skv{ch % 2}"

            @p.on("scalar")
            def _(gq, ch=ch, ktile=ktile, evk=evk, semk=semk):
                gq.wait_ge(p.hnd["ev"], evk)
                p.inc(gq.dma_start(out=out_kvt[:, chs(ch)], in_=ktile[:, :]), semk, 16)
            b["kvtile_free"][ch % 2] = (semk, p.val(semk))
            emit_recon(3)
        emit_recon(2)

    emit_recon(len(tiles))

    fin = p.snap()

    @p.on("gpsimd")
    def _(g):
        p.wait(g, fin)
        p.inc(g.dma_start(out=out_ss[:, :], in_=SS[:, :]), "st", 16)
        g.wait_ge(p.hnd["st"], p.val("st"))

    return p



_GRIDS = {}


def _spline_host(gate_cp, up_cp, down_cp):
    n = INTER * D
    nct = gate_cp.shape[0]
    if "idx" not in _GRIDS:
        t = np.linspace(0.0, nct - 1.0, n, dtype=np.float64)
        i = np.clip(np.floor(t).astype(np.int64), 0, nct - 2)
        f = (t - i).astype(np.float32)
        idx = np.stack([np.clip(i + k, 0, nct - 1) for k in (-1, 0, 1, 2)], axis=0).astype(np.int32)
        f2 = f * f
        f3 = f2 * f
        c0 = 0.5 * (-f + 2.0 * f2 - f3)
        c1 = 0.5 * (2.0 - 5.0 * f2 + 3.0 * f3)
        c2 = 0.5 * (f + 4.0 * f2 - 3.0 * f3)
        c3 = 0.5 * (-f2 + f3)
        c = np.stack([c0, c1, c2, c3], 0).astype(np.float32)
        cguT = np.ascontiguousarray(c.reshape(4, INTER, D).transpose(0, 2, 1))
        cdnT = np.ascontiguousarray(c.reshape(4, D, INTER).transpose(0, 2, 1))
        _GRIDS.update(idx=idx, cguT=cguT, cdnT=cdnT)
    idx = _GRIDS["idx"]
    cguT = _GRIDS["cguT"]
    cdnT = _GRIDS["cdnT"]

    out = {"gu_tap": [[] for _ in range(4)], "gu_c": [[] for _ in range(4)],
           "dn_tap": [[] for _ in range(4)], "dn_c": [[] for _ in range(4)]}
    for k in range(4):
        g_t = np.ascontiguousarray(gate_cp[idx[k]].reshape(INTER, D).T)
        u_t = np.ascontiguousarray(up_cp[idx[k]].reshape(INTER, D).T)
        d_t = np.ascontiguousarray(down_cp[idx[k]].reshape(D, INTER).T)
        for ccore in range(NCORE):
            gr = slice(ccore * 256, (ccore + 1) * 256)
            dr = slice(ccore * 1024, (ccore + 1) * 1024)
            out["gu_tap"][k].append(np.ascontiguousarray(
                np.concatenate([g_t[gr], u_t[gr]], 0)))
            out["gu_c"][k].append(np.ascontiguousarray(cguT[k][gr]))
            out["dn_tap"][k].append(np.ascontiguousarray(d_t[dr]))
            out["dn_c"][k].append(np.ascontiguousarray(cdnT[k][dr]))
    return out


def _fm_slices(x):
    xs = x.reshape(SEQ, D)
    return [np.ascontiguousarray(xs[c * T:(c + 1) * T, :].T) for c in range(NCORE)]


def _consts():
    return {
        "ones_bf_in": np.ones((128, 1), dtype=ml_dtypes.bfloat16),
        "ones_row_f_in": np.ones((1, 128), dtype=np.float32),
    }


def run_l1(inputs):
    x = np.asarray(inputs["L0_attn_a_x"], dtype=np.float32)
    xs = _fm_slices(x)
    wkT = np.ascontiguousarray(np.asarray(inputs["L0_attn_a_Wk"], np.float32).T)
    wvT = np.ascontiguousarray(np.asarray(inputs["L0_attn_a_Wv"], np.float32).T)
    sp = _spline_host(np.asarray(inputs["L0_ffn_gate_cp"], np.float32),
                      np.asarray(inputs["L0_ffn_up_cp"], np.float32),
                      np.asarray(inputs["L0_ffn_down_cp"], np.float32))
    cst = _consts()
    in_maps = []
    for c in range(NCORE):
        m = dict(x=xs[c], wkT=wkT, wvT=wvT, **cst)
        for k in range(4):
            m[f"gu_tap{k}"] = sp["gu_tap"][k][c]
            m[f"gu_c{k}"] = sp["gu_c"][k][c]
            m[f"dn_tap{k}"] = sp["dn_tap"][k][c]
            m[f"dn_c{k}"] = sp["dn_c"][k][c]
        in_maps.append(m)
    prog = build_l1()
    res = run_bass_kernel_spmd(prog.nc, in_maps, core_ids=list(range(NCORE)))
    return res.results


def kernel(**inputs):
    raise NotImplementedError("L2/L3 not yet wired")



_L23_CACHE = {}

class Carver:

    def __init__(self, base_ap, total_bytes):
        self.base = base_ap
        self.total = total_bytes
        self.cur = 0

    def take(self, shape, dt):
        es = mybir.dt.size(dt)
        nb = int(np.prod(shape[1:])) * es if len(shape) > 1 else es
        nb = int(np.prod(shape[1:])) * es
        self.cur = (self.cur + 3) // 4 * 4
        c0 = self.cur // 2
        self.cur += nb
        assert self.cur <= self.total, f"carver overflow {self.cur} > {self.total}"
        view = self.base[:, c0:c0 + nb // 2]
        if dt != BF16:
            view = view.bitcast(dt)
        if len(shape) == 3:
            view = view.rearrange("p (a c) -> p a c", a=shape[1])
        return view

    def reset(self, to=0):
        self.cur = to



def load_stripx(p, buf, free_tok, semn, w_dram, col0, k_chunks, kc0=0):
    p.sem(semn)

    @p.on("sync")
    def _(sync):
        wait_tok(sync, p, free_tok)
        src = w_dram[kc0 * 128:(kc0 + k_chunks) * 128, col0:col0 + 512] \
            .rearrange("(k p) c -> p k c", p=128)
        dst = buf[:, 0:k_chunks * 512].rearrange("p (k c) -> p k c", k=k_chunks)
        p.inc(sync.dma_start(out=dst, in_=src), semn, 16)
    return (semn, p.val(semn))


def emit_ffn(p, b, X, XLN, Hbuf, gate_w, up_w, dn_w, sc_x, out_dst, XR, XOT, FE4,
             xstrips=None):
    S = [b["strip"][0][:, :], b["strip"][1][:, :]] + (xstrips or [])
    NS = len(S)
    SF = [b["strip_free"][0], b["strip_free"][1]] + [None] * (NS - 2)
    @p.on("gpsimd")
    def _(g):
        p.wait(g, {"ev": p.val("ev")})
        for ch in range(CH):
            p.inc(g.dma_start(out=sc_x[:, chs(ch)], in_=X[:, chs(ch)]), "gd", 16)
    spill_done = ("gd", p.val("gd"))

    b["xr_free"] = [None, None]
    b["xot_free"] = [None, None]
    prev_silu_consumed = None
    gidx = 0
    for hh in (0, 1):
        for g in range(ICH // 4):
            sg_i = (2 * gidx) % NS
            su_i = (2 * gidx + 1) % NS
            gidx += 1
            stripg = S[sg_i]
            stripu = S[su_i]
            ldg = load_stripx(p, stripg, SF[sg_i], f"lw{sg_i}", gate_w, g * 512, CH)
            ldu = load_stripx(p, stripu, SF[su_i], f"lw{su_i}", up_w, g * 512, CH)
            mmg = {}

            @p.on("tensor")
            def _(t, g=g, hh=hh, ldg=ldg, ldu=ldu, stripg=stripg, stripu=stripu):
                wait_tok(t, p, ldg)
                if g == 0 and hh == 0:
                    wait_tok(t, p, spill_done)
                for oi in range(4):
                    wait_tok(t, p, b["bank_free"][oi])
                    last = None
                    for kc in range(CH):
                        last = t.matmul(b["banks"][oi][:, :],
                                        stripg[:, kc * 512 + oi * 128: kc * 512 + (oi + 1) * 128],
                                        XLN[:, tok(kc, hh)],
                                        start=(kc == 0), stop=(kc == CH - 1))
                    mmg[("g", oi)] = p.inc(last, "mm")
                wait_tok(t, p, ldu)
                for oi in range(4):
                    wait_tok(t, p, b["bank_free"][4 + oi])
                    last = None
                    for kc in range(CH):
                        last = t.matmul(b["banks"][4 + oi][:, :],
                                        stripu[:, kc * 512 + oi * 128: kc * 512 + (oi + 1) * 128],
                                        XLN[:, tok(kc, hh)],
                                        start=(kc == 0), stop=(kc == CH - 1))
                    mmg[("u", oi)] = p.inc(last, "mm")

            holds = {}

            @p.on("scalar")
            def _(se, mmg=mmg, prev=prev_silu_consumed):
                s = Eng(p, se, "scalar")
                if prev is not None:
                    s.wait_ge(p.hnd["ev"], prev)
                for oi in range(4):
                    s.wait_ge(p.hnd["mm"], mmg[("g", oi)])
                    s.act(FE4[:, oi * HW:(oi + 1) * HW], b["banks"][oi][:, :], AF.Silu)
                    holds[oi] = s.mark("ac")

            evs = {}

            @p.on("vector")
            def _(ve, g=g, hh=hh, mmg=mmg):
                v = Eng(p, ve, "vector")
                for oi in range(4):
                    ic = g * 4 + oi
                    v.wait_ge(p.hnd["ac"], holds[oi])
                    v.wait_ge(p.hnd["mm"], mmg[("u", oi)])
                    v.tt(Hbuf[:, ic * HW:(ic + 1) * HW],
                         FE4[:, oi * HW:(oi + 1) * HW], b["banks"][4 + oi][:, :], ALU.mult)
                    evs[oi] = v.mark("ev")
            evg = p.val("ev")
            prev_silu_consumed = evg
            for oi in range(4):
                b["bank_free"][oi] = ("ac", holds[oi])
                b["bank_free"][4 + oi] = ("ev", evs[oi])
            SF[sg_i] = ("mm", mmg[("g", 3)])
            SF[su_i] = ("mm", mmg[("u", 3)])

        for ocg in range(4):
            mmd = {}
            for piece in range(4):
                sbf = (gidx * 2 + piece) % NS
                stripd = S[sbf]
                ldd = load_stripx(p, stripd, SF[sbf], f"lw{sbf}", dn_w, ocg * 512, CH,
                                  kc0=piece * 16)

                @p.on("tensor")
                def _(t, piece=piece, ldd=ldd, stripd=stripd, ocg=ocg, hh=hh):
                    wait_tok(t, p, ldd)
                    if piece == 0:
                        for i in range(4):
                            wait_tok(t, p, b["bank_free"][i])
                    last = None
                    for oi in range(4):
                        for kc in range(CH):
                            kabs = piece * 16 + kc
                            last = t.matmul(
                                b["banks"][oi][:, :],
                                stripd[:, kc * 512 + oi * 128: kc * 512 + (oi + 1) * 128],
                                Hbuf[:, kabs * HW:(kabs + 1) * HW],
                                start=(kabs == 0), stop=(kabs == ICH - 1),
                                skip_group_check=True)
                        if piece == 3:
                            mmd[oi] = p.inc(last, "mm")
                            last = None
                    if piece != 3:
                        p.inc(last, "mm")
                SF[sbf] = ("mm", p.val("mm"))
            gidx += 2

            for oi in range(4):
                oc = ocg * 4 + oi
                sl = oc % 2

                @p.on("sync")
                def _(sync, oc=oc, hh=hh, sl=sl):
                    wait_tok(sync, p, b["xr_free"][sl])
                    p.inc(sync.dma_start(out=XR[sl][:, :], in_=sc_x[:, tok(oc, hh)]), f"lxr{sl}", 16)
                lxr = p.val(f"lxr{sl}")
                xot_free = b["xot_free"][sl]

                @p.on("vector")
                def _(ve, oi=oi, sl=sl, lxr=lxr, mmd=mmd, xot_free=xot_free):
                    v = Eng(p, ve, "vector")
                    v.wait_ge(p.hnd["mm"], mmd[oi])
                    v.wait_ge(p.hnd[f"lxr{sl}"], lxr)
                    v.wtok(xot_free)
                    v.tt(XOT[sl][:, :], b["banks"][oi][:, :], XR[sl][:, :], ALU.add)
                    v.mark("ev")
                evd = p.val("ev")
                b["xr_free"][sl] = ("ev", evd)
                b["bank_free"][oi] = ("ev", evd)

                semx = "sx3a" if sl == 0 else "sx3b"

                @p.on("gpsimd")
                def _(gq, oc=oc, hh=hh, sl=sl, evd=evd, semx=semx):
                    gq.wait_ge(p.hnd["ev"], evd)
                    p.inc(gq.dma_start(out=out_dst[:, tok(oc, hh)], in_=XOT[sl][:, :]), semx, 16)
                b["xot_free"][sl] = (semx, p.val(semx))



def build_l2():
    p = Prog()
    for s in ("lx", "lkv0", "lkv1", "lxr0", "lxr1", "sx3a", "sx3b", "lx3", "lssg",
              "sqs0", "sqs1", "sk0", "sk1", "sgp0", "sgp1", "svv0", "svv1",
              "lgp0", "lgp1", "lkk0", "lkk1", "lvv0", "lvv1", "lqq0", "lqq1",
              "sqt0", "sqt1", "soc0", "soc1"):
        p.sem(s)
    x_in = p.din("x", [D, T], F32)
    wqaT = p.din("wqaT", [D, D], F32)
    woaT = p.din("woaT", [D, D], F32)
    wqb_in = p.din("wqb", [D, D], F32)
    wfb_in = p.din("wfb", [D, D], F32)
    wib_in = p.din("wib", [D, D], F32)
    kvt_in = p.din("kvt", [128, CH * T], BF16)
    ssg_in = p.din("ssg", [8, D], F32)
    mpfx_in = p.din("mpfx", [8, 1], F32)
    gate_w = p.din("gate_w", [D, INTER], BF16)
    up_w = p.din("up_w", [D, INTER], BF16)
    dn_w = p.din("dn_w", [INTER, D], BF16)
    ones_bf_in = p.din("ones_bf_in", [128, 1], BF16)
    ones_row_f_in = p.din("ones_row_f_in", [1, 128], F32)
    ident_in = p.din("ident_bf", [128, 128], BF16)
    maskf_in = p.din("maskf", [128, 128], F32)

    out_x3 = p.dout("x3", [128, CH * T], F32)
    out_op = p.dout("opart", [128, CH * T], BF16)
    out_qt = p.dout("qtil", [128, NCHK * D], BF16)
    out_st = p.dout("sstate", [128, NH * 128], F32)
    out_ld = p.dout("logd", [128, NH], F32)
    out_gm = p.dout("gamlog", [128, NH * NCHK], F32)

    sc_x = p.dtmp("sc_x", [128, CH * T], F32)
    sc_qs = p.dtmp("sc_qs", [128, CH * T], BF16)
    sc_k = p.dtmp("sc_k", [128, CH * T], BF16)
    sc_v = p.dtmp("sc_v", [128, CH * T], BF16)
    sc_gp = p.dtmp("sc_gp", [128, CH * T], F32)

    ARE = p.sb("ARE", [128, 49152], BF16)
    X = ARE[:, 0:32768].bitcast(F32)
    XLN = ARE[:, 32768:49152]
    Hbuf = ARE[:, 0:32768]
    SCR = p.sb("SCR", [128, 16384], BF16)
    VHAT = SCR[:, :]
    OFF = p.sb("OFF", [128, CH], F32)
    MPFX = p.sb("MPFX", [8, 1], F32)
    IDE = p.sb("IDE", [128, 128], BF16)
    MASKF = p.sb("MASKF", [128, 128], F32)
    KVR = [p.sb(f"kvr{i}", [128, HW], BF16) for i in range(2)]
    XR = [p.sb(f"xr{i}", [128, HW], F32) for i in range(2)]
    XOT = [p.sb(f"xot{i}", [128, HW], F32) for i in range(2)]
    FE4 = p.sb("FE4", [128, 4 * HW], F32)

    p.start_block()
    b = make_shared(p)
    fill_consts(p, b)

    load_consts(p, b, [
        (b["ones_bf"][:, :], ones_bf_in[:, :]),
        (b["ones_row_f"][:, :], ones_row_f_in[:, :]),
        (IDE[:, :], ident_in[:, :]),
        (MASKF[:, :], maskf_in[:, :]),
        (MPFX[:, :], mpfx_in[:, :]),
    ])

    @p.on("sync")
    def _(sync):
        for ch in range(CH):
            p.inc(sync.dma_start(out=X[:, chs(ch)], in_=x_in[ch * 128:(ch + 1) * 128, :]), "lx", 16)

    emit_ln(p, b, X, XLN, "ln1")

    SSG = SCR[:, 0:4096].bitcast(F32)[0:8, 0:2048]

    @p.on("sync")
    def _(sync):
        p.inc(sync.dma_start(out=SSG, in_=ssg_in[:, :]), "lssg", 16)
    ldssg = p.val("lssg")

    @p.on("tensor")
    def _(t):
        t.wait_ge(p.hnd["lssg"], ldssg)
        for i in range(8):
            wait_tok(t, p, b["bank_free"][i])
        last = None
        for ch in range(CH):
            last = t.matmul(b["banks"][0][:, ch:ch + 1],
                            SSG[0:8, ch * 128:(ch + 1) * 128], MPFX[0:8, 0:1],
                            start=True, stop=True)
        p.inc(last, "mm")
    mmoff = p.val("mm")

    @p.on("vector")
    def _(ve):
        v = Eng(p, ve, "vector")
        v.wait_ge(p.hnd["mm"], mmoff)
        v.cp(OFF[:, :], b["banks"][0][:, 0:CH])
        v.mark("ev")
    b["bank_free"][0] = ("ev", p.val("ev"))


    kvr_free = [None, None]
    for g in range(4):
        ldq, stripq = load_strip(p, b, g % 2, wqaT, g * 512, CH, q="gpsimd")
        mmq = proj_group(p, b, stripq, ldq, XLN, CH, start=True, stop=True)
        b["strip_free"][g % 2] = ("mm", mmq[(3, 1)])

        for oi in range(4):
            ch = g * 4 + oi
            for hh in (0, 1):
                sl = (ch * 2 + hh) % 2

                @p.on("sync")
                def _(sync, ch=ch, hh=hh, sl=sl):
                    wait_tok(sync, p, kvr_free[sl])
                    p.inc(sync.dma_start(out=KVR[sl][:, :], in_=kvt_in[:, tok(ch, hh)]),
                          f"lkv{sl}", 16)
                lkvv = p.val(f"lkv{sl}")

                @p.on("vector")
                def _(ve, ch=ch, oi=oi, hh=hh, sl=sl, mmq=mmq, lkvv=lkvv):
                    v = Eng(p, ve, "vector")
                    v.wait_ge(p.hnd[f"lkv{sl}"], lkvv)
                    v.wait_ge(p.hnd["mm"], mmq[(oi, hh)])
                    v.ts(b["aptmp"][:, :], KVR[sl][:, :], OFF[:, ch:ch + 1], DENOM,
                         ALU.add, ALU.mult)
                    v.tt(VHAT[:, tok(ch, hh)], b["banks"][oi * 2 + hh][:, :],
                         b["aptmp"][:, :], ALU.mult)
                    v.mark("ev")
                b["bank_free"][oi * 2 + hh] = ("ev", p.val("ev"))
                kvr_free[sl] = ("ev", p.val("ev"))

    for g in range(4):
        ldo, stripo = load_strip(p, b, g % 2, woaT, g * 512, CH, q="gpsimd")
        mmo = proj_group(p, b, stripo, ldo, VHAT, CH, start=True, stop=True)
        b["strip_free"][g % 2] = ("mm", mmo[(3, 1)])

        @p.on("vector")
        def _(ve, g=g, mmo=mmo):
            v = Eng(p, ve, "vector")
            for oi in range(4):
                oc = g * 4 + oi
                for hh in (0, 1):
                    v.wait_ge(p.hnd["mm"], mmo[(oi, hh)])
                    v.tt(X[:, tok(oc, hh)], b["banks"][oi * 2 + hh][:, :],
                         X[:, tok(oc, hh)], ALU.add)
            v.mark("ev")
        for i in range(8):
            b["bank_free"][i] = ("ev", p.val("ev"))

    emit_ln(p, b, X, XLN, "ln2")

    emit_ffn(p, b, X, XLN, Hbuf, gate_w, up_w, dn_w, sc_x, out_x3, XR, XOT, FE4,
             xstrips=[SCR[:, 0:8192], SCR[:, 8192:16384]])

    fin_ffn = p.snap()

    @p.on("sync")
    def _(sync):
        p.wait(sync, fin_ffn)
        for ch in range(CH):
            p.inc(sync.dma_start(out=X[:, chs(ch)], in_=out_x3[:, chs(ch)]), "lx3", 16)

    emit_ln(p, b, X, XLN, "ln3")

    FEG = SCR[:, 0:8192].bitcast(F32)
    KSTG = SCR[:, 8192:12288]
    QSTG = SCR[:, 12288:13312]
    VSTG = SCR[:, 13312:14336]

    qst_free = [None, None]

    for g in range(4):
        ldq, stripq = load_strip(p, b, g % 2, wqb_in, g * 512, CH, q="gpsimd")
        mmq = proj_group(p, b, stripq, ldq, XLN, CH, start=True, stop=True)
        b["strip_free"][g % 2] = ("mm", mmq[(3, 1)])
        for oi in range(4):
            ch = g * 4 + oi
            for hh in (0, 1):
                sl = (ch * 2 + hh) % 2
                hold = {}

                @p.on("scalar")
                def _(se, oi=oi, hh=hh, sl=sl, mmq=mmq, hold=hold):
                    s = Eng(p, se, "scalar")
                    s.wait_ge(p.hnd["mm"], mmq[(oi, hh)])
                    s.wtok(qst_free[sl])
                    s.act(QSTG[:, sl * HW:(sl + 1) * HW], b["banks"][oi * 2 + hh][:, :], AF.Silu)
                    hold["v"] = s.mark("ac")
                b["bank_free"][oi * 2 + hh] = ("ac", hold["v"])
                semq = f"sqs{sl}"

                @p.on("sync")
                def _(gq, ch=ch, hh=hh, sl=sl, hold=hold, semq=semq):
                    gq.wait_ge(p.hnd["ac"], hold["v"])
                    p.inc(gq.dma_start(out=sc_qs[:, tok(ch, hh)],
                                       in_=QSTG[:, sl * HW:(sl + 1) * HW]), semq, 16)
                qst_free[sl] = (semq, p.val(semq))

    kst_free = None
    gst_free = None

    for g in range(4):
        ldf, stripf = load_strip(p, b, g % 2, wfb_in, g * 512, CH, q="gpsimd")
        mmf = proj_group(p, b, stripf, ldf, XLN, CH, start=True, stop=True)
        b["strip_free"][g % 2] = ("mm", mmf[(3, 1)])
        holds = {}

        @p.on("scalar")
        def _(se, g=g, mmf=mmf, holds=holds, kst_free=kst_free, gst_free=gst_free):
            s = Eng(p, se, "scalar")
            s.wtok(gst_free)
            s.wtok(kst_free)
            for oi in range(4):
                for hh in (0, 1):
                    j = oi * 2 + hh
                    s.wait_ge(p.hnd["mm"], mmf[(oi, hh)])
                    s.act(FEG[:, j * HW:(j + 1) * HW], b["banks"][j][:, :], AF.Exp, scale=-1.0)
            for oi in range(4):
                for hh in (0, 1):
                    j = oi * 2 + hh
                    s.act(KSTG[:, j * HW:(j + 1) * HW], b["banks"][j][:, :],
                          AF.Sigmoid, scale=-1.0)
                    holds[("k", oi, hh)] = s.mark("ac")
            for oi in range(4):
                for hh in (0, 1):
                    j = oi * 2 + hh
                    s.act(FEG[:, j * HW:(j + 1) * HW], FEG[:, j * HW:(j + 1) * HW],
                          AF.Ln, bias=b["consts"][:, 1:2])
                    holds[("gp", oi, hh)] = s.mark("ac")

        for oi in range(4):
            for hh in (0, 1):
                ch = g * 4 + oi
                j = oi * 2 + hh
                b["bank_free"][j] = ("ac", holds[("k", oi, hh)])
                hk = holds[("k", oi, hh)]
                hg = holds[("gp", oi, hh)]
                semk = f"sk{j % 2}"
                semg = f"sgp{j % 2}"

                @p.on("sync")
                def _(gq, ch=ch, hh=hh, j=j, hk=hk, hg=hg, semk=semk, semg=semg):
                    gq.wait_ge(p.hnd["ac"], hk)
                    p.inc(gq.dma_start(out=sc_k[:, tok(ch, hh)],
                                       in_=KSTG[:, j * HW:(j + 1) * HW]), semk, 16)
                    gq.wait_ge(p.hnd["ac"], hg)
                    p.inc(gq.dma_start(out=sc_gp[:, tok(ch, hh)],
                                       in_=FEG[:, j * HW:(j + 1) * HW]), semg, 16)
        kst_free = ("sk1", p.val("sk1"))
        gst_free = ("sgp1", p.val("sgp1"))
        kst_free2 = ("sk0", p.val("sk0"))
        gst_free2 = ("sgp0", p.val("sgp0"))

        @p.on("scalar")
        def _(se, kst_free2=kst_free2, gst_free2=gst_free2):
            se.wait_ge(p.hnd[kst_free2[0]], kst_free2[1])
            se.wait_ge(p.hnd[gst_free2[0]], gst_free2[1])

    vst_free = [None, None]

    for g in range(4):
        ldi, stripi = load_strip(p, b, g % 2, wib_in, g * 512, CH, q="gpsimd")
        mmi = proj_group(p, b, stripi, ldi, XLN, CH, start=True, stop=True)
        b["strip_free"][g % 2] = ("mm", mmi[(3, 1)])
        for oi in range(4):
            ch = g * 4 + oi
            for hh in (0, 1):
                sl = (ch * 2 + hh) % 2
                hold = {}

                @p.on("vector")
                def _(ve, oi=oi, hh=hh, sl=sl, mmi=mmi, hold=hold):
                    v = Eng(p, ve, "vector")
                    v.wait_ge(p.hnd["mm"], mmi[(oi, hh)])
                    v.wtok(vst_free[sl])
                    v.cp(VSTG[:, sl * HW:(sl + 1) * HW], b["banks"][oi * 2 + hh][:, :])
                    hold["v"] = v.mark("ev")
                b["bank_free"][oi * 2 + hh] = ("ev", hold["v"])
                semv = f"svv{sl}"

                @p.on("sync")
                def _(gq, ch=ch, hh=hh, sl=sl, hold=hold, semv=semv):
                    gq.wait_ge(p.hnd["ev"], hold["v"])
                    p.inc(gq.dma_start(out=sc_v[:, tok(ch, hh)],
                                       in_=VSTG[:, sl * HW:(sl + 1) * HW]), semv, 16)
                vst_free[sl] = (semv, p.val(semv))

    cva = Carver(ARE[:, :], 98304)
    HST = cva.take([128, NH, 128], F32)
    GPA = cva.take([128, NH, 128], F32)
    GPB = cva.take([128, NH, 128], F32)
    EKT = cva.take([128, NH, 128], F32)
    SUBT = cva.take([128, NH, 128], F32)
    KHAT = cva.take([128, NH, 128], BF16)
    KTT = cva.take([128, NH, 128], BF16)
    VTT = cva.take([128, NH, 128], BF16)
    HBF = cva.take([128, NH, 128], BF16)
    QTC = cva.take([128, NH, 128], BF16)
    KTIL = cva.take([128, NH, 128], BF16)
    QIB = cva.take([128, NH, SUB], BF16)
    ABM = cva.take([128, NH, 128], BF16)
    OCH = cva.take([128, NH, 128], BF16)
    SUBQ = cva.take([128, NH, SUB], F32)
    EQB = cva.take([128, NH, SUB], F32)
    GCT = cva.take([128, NH], F32)
    NGC = cva.take([128, NH], F32)
    EDEC = cva.take([128, NH], F32)
    SGP = cva.take([128, NH], F32)
    GAMT = cva.take([128, NH, NCHK], F32)
    LOGD = cva.take([128, NH], F32)
    QR3 = [cva.take([128, NH, 128], BF16) for _ in range(2)]
    QIBA = cva.take([128, NH, 128], BF16)
    cvs = Carver(SCR[:, :], 32768)
    GPR = [cvs.take([128, NH, 128], F32) for _ in range(2)]
    KR3 = [cvs.take([128, NH, 128], BF16) for _ in range(2)]
    VR3 = [cvs.take([128, NH, 128], BF16) for _ in range(2)]

    glastart = p.snap()

    @p.on("vector")
    def _(ve):
        v = Eng(p, ve, "vector")
        v.wsnap(glastart)
        v.mset(HST[:, :, :], 0.0)
        v.mset(SGP[:, :], 0.0)
        v.mark("ev")
    gla0 = p.val("ev")

    ring_free = {k: [None, None] for k in ("gp", "kk", "vv", "qq")}
    scmap = {"gp": sc_gp, "kk": sc_k, "vv": sc_v, "qq": sc_qs}
    dstmap = {"gp": GPR, "kk": KR3, "vv": VR3, "qq": QR3}

    def load_chunk(m):
        sl = m % 2
        for key in ("gp", "kk", "vv", "qq"):
            @p.on("sync")
            def _(sync, key=key, m=m, sl=sl):
                if m <= 1:
                    p.wait(sync, {"ev": gla0})
                wait_tok(sync, p, ring_free[key][sl])
                src3 = scmap[key].rearrange("p (h t) -> p h t", t=T)[:, :, m * 128:(m + 1) * 128]
                p.inc(sync.dma_start(out=dstmap[key][sl][:, :, :], in_=src3), f"l{key}{sl}", 16)

    load_chunk(0)
    och_free = [None, None]
    qtc_free = None
    prev_gla_mm = 0
    for m in range(NCHK):
        sl = m % 2
        if m + 1 < NCHK:
            load_chunk(m + 1)
        ldv = {k: p.val(f"l{k}{sl}") for k in ("gp", "kk", "vv", "qq")}

        hold = {}

        @p.on("vector")
        def _(ve, m=m, sl=sl, ldv=ldv, prev_gla_mm=prev_gla_mm):
            v = Eng(p, ve, "vector")
            if prev_gla_mm:
                v.wait_ge(p.hnd["mm"], prev_gla_mm)
            for k in ("gp", "kk", "vv", "qq"):
                v.wait_ge(p.hnd[f"l{k}{sl}"], ldv[k])
            src = GPR[sl]
            pp = [GPA, GPB]
            for si, s in enumerate([1, 2, 4, 8, 16, 32, 64]):
                a = src if si == 0 else pp[(si + 1) % 2]
                o = pp[si % 2]
                v.tt(o[:, :, s:128], a[:, :, s:128], a[:, :, 0:128 - s], ALU.add)
                v.cp(o[:, :, 0:s], a[:, :, 0:s])
            v.cp(GCT[:, :].rearrange("p (h o) -> p h o", o=1), GPA[:, :, 127:128])
            hold["v"] = v.mark("ev")
        evc = p.val("ev")
        gposf = GPA

        @p.on("vector")
        def _(ve):
            v = Eng(p, ve, "vector")
            gctb = GCT[:, :].rearrange("p (h o) -> p h o", o=1).broadcast_to([128, NH, 128])
            v.tt(EKT[:, :, :], gposf[:, :, :], gctb, ALU.subtract)
            v.mark("ev")
        evgp = p.val("ev")
        hold2 = {}

        @p.on("scalar")
        def _(se):
            s = Eng(p, se, "scalar")
            s.wait_ge(p.hnd["ev"], evgp)
            s.act(EDEC[:, :], GCT[:, :], AF.Exp, scale=-1.0)
            s.act(EKT[:, :, :], EKT[:, :, :], AF.Exp)
            hold2["v"] = s.mark("ac")

        hold3 = {}

        @p.on("vector")
        def _(ve, sl=sl):
            v = Eng(p, ve, "vector")
            v.wait_ge(p.hnd["ac"], hold2["v"])
            v.tt(KHAT[:, :, :], KR3[sl][:, :, :], EKT[:, :, :], ALU.mult)
            v.cp(HBF[:, :, :], HST[:, :, :])
            hold3["v"] = v.mark("ev")
        evkh = p.val("ev")

        ps_bf = [b["banks"][i][:, :].bitcast(BF16) for i in range(4)]

        @p.on("tensor")
        def _(t, sl=sl):
            t.wait_ge(p.hnd["ev"], evkh)
            for i in range(4):
                wait_tok(t, p, b["bank_free"][i])
            last = None
            for h in range(NH):
                last = t.transpose(ps_bf[h // 8][:, (h % 8) * 128:(h % 8 + 1) * 128],
                                   KHAT[:, h, :], IDE[:, :])
            for h in range(NH):
                last = t.transpose(ps_bf[2 + h // 8][:, (h % 8) * 128:(h % 8 + 1) * 128],
                                   VR3[sl][:, h, :], IDE[:, :])
            p.inc(last, "mm")
        mmtr = p.val("mm")

        @p.on("vector")
        def _(ve):
            v = Eng(p, ve, "vector")
            v.wait_ge(p.hnd["mm"], mmtr)
            v.cp(KTT[:, 0:8, :], ps_bf[0].rearrange("p (h t) -> p h t", h=8))
            v.cp(KTT[:, 8:16, :], ps_bf[1].rearrange("p (h t) -> p h t", h=8))
            v.cp(VTT[:, 0:8, :], ps_bf[2].rearrange("p (h t) -> p h t", h=8))
            v.cp(VTT[:, 8:16, :], ps_bf[3].rearrange("p (h t) -> p h t", h=8))
            v.mark("ev")
        evtt = p.val("ev")
        for i in range(4):
            b["bank_free"][i] = ("ev", evtt)

        @p.on("tensor")
        def _(t):
            t.wait_ge(p.hnd["ev"], evtt)
            for i in range(4, 8):
                wait_tok(t, p, b["bank_free"][i])
            last = None
            for h in range(NH):
                ps = b["banks"][4 + h // 4]
                last = t.matmul(ps[:, (h % 4) * 128:(h % 4 + 1) * 128],
                                KTT[:, h, :], VTT[:, h, :], start=True, stop=True,
                                skip_group_check=True)
            p.inc(last, "mm")
        mmkv = p.val("mm")

        @p.on("vector")
        def _(ve):
            v = Eng(p, ve, "vector")
            v.wait_ge(p.hnd["mm"], mmkv)
            edb = EDEC[:, :].rearrange("p (h o) -> p h o", o=1).broadcast_to([128, NH, 128])
            v.tt(SUBT[:, :, :], HST[:, :, :], edb, ALU.mult)
            for k in range(4):
                pv = b["banks"][4 + k][:, :].rearrange("p (h t) -> p h t", t=128)
                v.tt(HST[:, 4 * k:4 * (k + 1), :], SUBT[:, 4 * k:4 * (k + 1), :], pv, ALU.add)
            v.mark("ev")
        evst = p.val("ev")
        for i in range(4, 8):
            b["bank_free"][i] = ("ev", evst)

        hold6 = {}

        @p.on("scalar")
        def _(se):
            s = Eng(p, se, "scalar")
            s.wait_ge(p.hnd["ev"], evst)
            s.act(EKT[:, :, :], gposf[:, :, :], AF.Exp, scale=-1.0, bias=b["consts"][:, 2:3])
            hold6["v"] = s.mark("ac")

        @p.on("vector")
        def _(ve, sl=sl, qtc_free=qtc_free):
            v = Eng(p, ve, "vector")
            v.wait_ge(p.hnd["ac"], hold6["v"])
            v.wtok(qtc_free)
            v.tt(QTC[:, :, :], QR3[sl][:, :, :], EKT[:, :, :], ALU.mult)
            v.mark("ev")
        evqt = p.val("ev")
        semqt = f"sqt{m % 2}"

        @p.on("gpsimd")
        def _(gq, m=m, semqt=semqt):
            gq.wait_ge(p.hnd["ev"], evqt)
            p.inc(gq.dma_start(out=out_qt[:, m * D:(m + 1) * D],
                               in_=QTC[:, :, :].rearrange("p h t -> p (h t)")), semqt, 16)
        qtc_free = (semqt, p.val(semqt))

        g4 = gposf.rearrange("p h (i s) -> p h i s", s=SUB)
        s4 = SUBT.rearrange("p h (i s) -> p h i s", s=SUB)

        @p.on("vector")
        def _(ve):
            v = Eng(p, ve, "vector")
            v.wait_ge(p.hnd["ev"], evqt)
            v.ts(SUBT[:, :, 0:SUB], gposf[:, :, 0:SUB], -1.0, None, ALU.mult)
            v.tt(s4[:, :, 1:NSUB, :],
                 g4[:, :, 0:NSUB - 1, SUB - 1:SUB].broadcast_to([128, NH, NSUB - 1, SUB]),
                 g4[:, :, 1:NSUB, :], ALU.subtract)
            v.mark("ev")
        evbq = p.val("ev")

        @p.on("scalar")
        def _(se):
            s = Eng(p, se, "scalar")
            s.wait_ge(p.hnd["ev"], evbq)
            s.act(EKT[:, :, :], SUBT[:, :, :], AF.Exp, bias=b["consts"][:, 2:3])
            s.mark("ac")
        acq = p.val("ac")

        @p.on("vector")
        def _(ve, sl=sl):
            v = Eng(p, ve, "vector")
            v.wait_ge(p.hnd["ac"], acq)
            v.tt(QIBA[:, :, :], QR3[sl][:, :, :], EKT[:, :, :], ALU.mult)
            v.mark("ev")
        evqa = p.val("ev")

        prev_amm = {}
        for I in range(NSUB):
            endi = (I + 1) * SUB
            if I > 0:
                @p.on("vector")
                def _(ve, I=I, endi=endi):
                    v = Eng(p, ve, "vector")
                    v.wait_ge(p.hnd["mm"], prev_amm[I - 1])
                    gb = gposf[:, :, I * SUB - 1:I * SUB]
                    v.tt(SUBT[:, :, 0:endi], gposf[:, :, 0:endi],
                         gb.broadcast_to([128, NH, endi]), ALU.subtract)
                    v.mark("ev")
            evsub = p.val("ev")

            @p.on("scalar")
            def _(se, I=I, endi=endi):
                s = Eng(p, se, "scalar")
                if I == 0:
                    s.wait_ge(p.hnd["ev"], evqa)
                    s.act(EKT[:, :, 0:endi], gposf[:, :, 0:endi], AF.Exp)
                else:
                    s.wait_ge(p.hnd["ev"], evsub)
                    s.act(EKT[:, :, 0:endi], SUBT[:, :, 0:endi], AF.Exp)
                s.mark("ac")
            evek = p.val("ac")

            @p.on("vector")
            def _(ve, I=I, endi=endi, sl=sl):
                v = Eng(p, ve, "vector")
                v.wait_ge(p.hnd["ac"], evek)
                v.tt(KTIL[:, :, 0:endi], KR3[sl][:, :, 0:endi], EKT[:, :, 0:endi], ALU.mult)
                v.mark("ev")
            evqi = p.val("ev")

            @p.on("tensor")
            def _(t, I=I, endi=endi):
                t.wait_ge(p.hnd["ev"], evqi)
                if I == 0:
                    for i in (2, 3, 4, 5):
                        wait_tok(t, p, b["bank_free"][i])
                last = None
                for h in range(NH):
                    ab = b["banks"][2 + h // 4]
                    last = t.matmul(ab[0:endi, (h % 4) * 128 + I * SUB:(h % 4) * 128 + (I + 1) * SUB],
                                    KTIL[:, h, 0:endi], QIBA[:, h, I * SUB:(I + 1) * SUB],
                                    start=True, stop=True, skip_group_check=True)
                prev_amm[I] = p.inc(last, "mm")
        mmA = p.val("mm")

        @p.on("vector")
        def _(ve):
            v = Eng(p, ve, "vector")
            v.wait_ge(p.hnd["mm"], mmA)
            mb3 = MASKF[:, :].rearrange("p (o t) -> p o t", o=1).broadcast_to([128, 4, 128])
            for k in range(4):
                ab = b["banks"][2 + k][:, :].rearrange("p (h t) -> p h t", t=128)
                v.tt(ABM[:, 4 * k:4 * (k + 1), :], ab, mb3, ALU.mult)
            v.mark("ev")
        evab = p.val("ev")
        for i in (2, 3, 4, 5):
            b["bank_free"][i] = ("ev", evab)


        obanks = (2, 3, 6, 7)
        mmo = {}

        @p.on("tensor")
        def _(t, m=m):
            t.wait_ge(p.hnd["ev"], evab)
            for i in obanks:
                wait_tok(t, p, b["bank_free"][i])
            for h in range(NH):
                ob = b["banks"][obanks[h // 4]]
                osl = slice((h % 4) * 128, (h % 4 + 1) * 128)
                t.matmul(ob[:, osl], HBF[:, h, :], QTC[:, h, :],
                         start=True, stop=False, skip_group_check=True)
                last = t.matmul(ob[:, osl], VTT[:, h, :], ABM[:, h, :],
                                start=False, stop=True, skip_group_check=True)
                if h % 4 == 3:
                    mmo[h // 4] = p.inc(last, "mm")

        oevs = {}

        @p.on("vector")
        def _(ve, m=m):
            v = Eng(p, ve, "vector")
            for k in range(4):
                v.wait_ge(p.hnd["mm"], mmo[k])
                if k == 0:
                    v.wtok(och_free[m % 2])
                bv = b["banks"][obanks[k]][:, :].rearrange("p (h t) -> p h t", t=128)
                v.cp(OCH[:, 4 * k:4 * (k + 1), :], bv)
                oevs[k] = v.mark("ev")
        evo = p.val("ev")
        for k in range(4):
            b["bank_free"][obanks[k]] = ("ev", oevs[k])
        prev_gla_mm = p.val("mm")

        semoc = f"soc{m % 2}"

        @p.on("gpsimd")
        def _(gq, m=m, semoc=semoc):
            gq.wait_ge(p.hnd["ev"], evo)
            dst3 = out_op.rearrange("p (h t) -> p h t", t=T)[:, :, m * 128:(m + 1) * 128]
            p.inc(gq.dma_start(out=dst3, in_=OCH[:, :, :]), semoc, 16)
        och_free[m % 2] = (semoc, p.val(semoc))

        @p.on("vector")
        def _(ve, m=m):
            v = Eng(p, ve, "vector")
            v.wait_ge(p.hnd["ev"], evc)
            v.ts(GAMT[:, :, m:m + 1].rearrange("p h o -> p (h o)"), SGP[:, :], -1.0,
                 None, ALU.mult)
            v.tt(SGP[:, :], SGP[:, :], GCT[:, :], ALU.add)
            v.mark("ev")
        endchunk = p.val("ev")
        for k in ("gp", "kk", "vv", "qq"):
            ring_free[k][sl] = ("ev", endchunk)

    @p.on("vector")
    def _(ve):
        v = Eng(p, ve, "vector")
        v.ts(LOGD[:, :], SGP[:, :], -1.0, None, ALU.mult)
        v.mark("ev")

    fin = p.snap()

    @p.on("gpsimd")
    def _(g):
        p.wait(g, fin)
        p.inc(g.dma_start(out=out_st[:, :], in_=HST[:, :, :].rearrange("p h t -> p (h t)")), "st", 16)
        p.inc(g.dma_start(out=out_ld[:, :], in_=LOGD[:, :]), "st", 16)
        p.inc(g.dma_start(out=out_gm[:, :], in_=GAMT[:, :, :].rearrange("p h t -> p (h t)")), "st", 16)
        g.wait_ge(p.hnd["st"], p.val("st"))

    return p


def _l2_host(inputs, l1res):
    x = np.asarray(inputs["L0_attn_a_x"], dtype=np.float32)
    xs = _fm_slices(x)
    wqaT = np.ascontiguousarray(np.asarray(inputs["L0_attn_a_Wq"], np.float32).T)
    woaT = np.ascontiguousarray(np.asarray(inputs["L0_attn_a_Wo"], np.float32).T)
    wqb = np.ascontiguousarray(np.asarray(inputs["L0_attn_b_Wq"], np.float32))
    wfb = np.ascontiguousarray(np.asarray(inputs["L0_attn_b_Wf"], np.float32))
    wib = np.ascontiguousarray(np.asarray(inputs["L0_attn_b_Wi"], np.float32))
    ssg = np.stack([np.ascontiguousarray(l1res[j]["slicesum"]).T.reshape(D)
                    for j in range(NCORE)]).astype(np.float32)
    gate_full = np.ascontiguousarray(np.concatenate(
        [np.asarray(l1res[c]["recon_gu"])[:256] for c in range(NCORE)], 0))
    up_full = np.ascontiguousarray(np.concatenate(
        [np.asarray(l1res[c]["recon_gu"])[256:512] for c in range(NCORE)], 0))
    dn_full = np.ascontiguousarray(np.concatenate(
        [np.asarray(l1res[c]["recon_dn"]) for c in range(NCORE)], 0))
    _L23_CACHE.update(gate_full=gate_full, up_full=up_full, dn_full=dn_full)
    maskf = np.zeros((128, 128), np.float32)
    for tau in range(128):
        for t in range(128):
            if tau <= t:
                maskf[tau, t] = 1.0
    ident = np.eye(128, dtype=ml_dtypes.bfloat16)
    cst = _consts()
    in_maps = []
    for c in range(NCORE):
        mpfx = (np.arange(8) < c).astype(np.float32).reshape(8, 1)
        m = dict(x=xs[c], wqaT=wqaT, woaT=woaT, wqb=wqb, wfb=wfb, wib=wib,
                 kvt=np.ascontiguousarray(np.asarray(l1res[c]["kvt"])),
                 ssg=ssg, mpfx=mpfx, gate_w=gate_full, up_w=up_full, dn_w=dn_full,
                 ident_bf=ident, maskf=maskf, **cst)
        in_maps.append(m)
    return in_maps


def run_l2(inputs, l1res):
    in_maps = _l2_host(inputs, l1res)
    prog = build_l2()
    res = run_bass_kernel_spmd(prog.nc, in_maps, core_ids=list(range(NCORE)))
    return res.results



def build_l3():
    p = Prog()
    for s in ("lx", "lxr0", "lxr1", "sx3a", "sx3b", "lld", "lsst0", "lsst1",
              "lqt0", "lqt1", "lgmt", "lop"):
        p.sem(s)
    x3_in = p.din("x3", [128, CH * T], F32)
    op_in = p.din("opart", [128, CH * T], BF16)
    qt_in = p.din("qtil", [128, NCHK * D], BF16)
    sall_in = p.din("sall", [8 * 128, NH * 128], F32)
    ldall_in = p.din("ldall", [8, D], F32)
    wscanT_in = p.din("wscanT", [8, 8], F32)
    biarep_in = p.din("biarep", [128, 128], F32)
    gamlog_in = p.din("gamlog", [128, NH * NCHK], F32)
    gw_in = p.din("gw", [128, NH], F32)
    wob_in = p.din("wob", [D, D], F32)
    gate_w = p.din("gate_w", [D, INTER], BF16)
    up_w = p.din("up_w", [D, INTER], BF16)
    dn_w = p.din("dn_w", [INTER, D], BF16)
    ones_bf_in = p.din("ones_bf_in", [128, 1], BF16)
    ones_row_f_in = p.din("ones_row_f_in", [1, 128], F32)

    out_y = p.dout("y", [128, CH * T], F32)

    sc_x = p.dtmp("sc_x", [128, CH * T], F32)

    ARE = p.sb("ARE", [128, 49152], BF16)
    X = ARE[:, 0:32768].bitcast(F32)
    XLN = ARE[:, 32768:49152]
    Hbuf = ARE[:, 0:32768]
    SCR = p.sb("SCR", [128, 16384], BF16)
    OBUF = SCR[:, :]
    HIN = XLN[:, 0:4096].bitcast(F32)
    LDA = XLN[:, 4096:8192].bitcast(F32)[0:8, 0:2048]
    SSTR = [XLN[:, 8192 + i * 4096: 8192 + (i + 1) * 4096].bitcast(F32) for i in range(2)]
    QTR = [XLN[:, 8192 + i * 2048: 8192 + (i + 1) * 2048] for i in range(2)]
    XR = [p.sb(f"xr{i}", [128, HW], F32) for i in range(2)]
    XOT = [p.sb(f"xot{i}", [128, HW], F32) for i in range(2)]
    FE4 = p.sb("FE4", [128, 4 * HW], F32)
    WSC = p.sb("WSC", [8, 8], F32)
    BIA = p.sb("BIA", [128, 128], F32)
    GMT = p.sb("GMT", [128, 128], F32)
    GW = p.sb("GW", [128, NH], F32)
    COEF = p.sb("COEF", [128, 128], F32)
    HCB = p.sb("HCB", [128, 128], BF16)
    EG = p.sb("EG", [128, 2 * NH], F32)

    p.start_block()
    b = make_shared(p)
    fill_consts(p, b)

    load_consts(p, b, [
        (b["ones_bf"][:, :], ones_bf_in[:, :]),
        (b["ones_row_f"][:, :], ones_row_f_in[:, :]),
        (WSC[:, :], wscanT_in[:, :]),
        (BIA[:, :], biarep_in[:, :]),
        (GMT[:, :], gamlog_in[:, :]),
        (GW[:, :], gw_in[:, :]),
    ])

    @p.on("sync")
    def _(sync):
        for ch in range(CH):
            p.inc(sync.dma_start(out=X[:, chs(ch)], in_=x3_in[:, chs(ch)]), "lx", 16)
        p.inc(sync.dma_start(out=LDA, in_=ldall_in[:, :]), "lld", 16)

    @p.on("gpsimd")
    def _(g):
        p.inc(g.dma_start(out=OBUF[:, :], in_=op_in[:, :]), "lop", 16)
    lop_done = p.val("lop")

    lld = p.val("lld")
    lxv = p.val("lx")

    @p.on("tensor")
    def _(t):
        t.wait_ge(p.hnd["lld"], lld)
        t.wait_ge(p.hnd["lx"], lxv)
        p.wait(t, {"ev": p.val("ev"), "ac": p.val("ac")})
        last = None
        for h in range(NH):
            last = t.matmul(b["banks"][0][:, h * 8:(h + 1) * 8],
                            LDA[0:8, h * 128:(h + 1) * 128], WSC[0:8, 0:8],
                            start=True, stop=True, skip_group_check=True)
        p.inc(last, "mm")
    mme = p.val("mm")

    hold = {}

    @p.on("vector")
    def _(ve):
        v = Eng(p, ve, "vector")
        v.wait_ge(p.hnd["mm"], mme)
        v.tt(COEF[:, :], b["banks"][0][:, 0:128], BIA[:, :], ALU.add)
        hold["v"] = v.mark("ev")
    b["bank_free"][0] = ("ev", p.val("ev"))

    @p.on("scalar")
    def _(se):
        s = Eng(p, se, "scalar")
        s.wait_ge(p.hnd["ev"], hold["v"])
        s.act(COEF[:, :], COEF[:, :], AF.Exp)
        s.mark("ac")
    ace = p.val("ac")

    TMPH = XLN[:, 4096:8192].bitcast(F32)
    HCB16 = [XLN[:, 4096:6144], XLN[:, 12288:14336]]
    sst_free = [None, None]
    for j in range(NCORE):
        sl = j % 2

        @p.on("sync")
        def _(sync, j=j, sl=sl):
            wait_tok(sync, p, sst_free[sl])
            p.inc(sync.dma_start(out=SSTR[sl][:, :],
                                 in_=sall_in[j * 128:(j + 1) * 128, :]), f"lsst{sl}", 16)
        lss = p.val(f"lsst{sl}")

        @p.on("vector")
        def _(ve, j=j, sl=sl, lss=lss):
            v = Eng(p, ve, "vector")
            v.wait_ge(p.hnd[f"lsst{sl}"], lss)
            if j == 0:
                v.wait_ge(p.hnd["ac"], ace)
            sv = SSTR[sl].rearrange("p (h t) -> p h t", t=128)
            cj = COEF.rearrange("p (h j) -> p h j", j=8)[:, :, j:j + 1] \
                .broadcast_to([128, NH, 128])
            tmp = TMPH.rearrange("p (h t) -> p h t", t=128)
            hv = HIN.rearrange("p (h t) -> p h t", t=128)
            if j == 0:
                v.tt(hv, sv, cj, ALU.mult)
            else:
                v.tt(tmp, sv, cj, ALU.mult)
                v.tt(hv, hv, tmp, ALU.add)
            v.mark("ev")
        sst_free[sl] = ("ev", p.val("ev"))
    hin_done = p.val("ev")

    qtr_free = [None, None]
    hcb_free = [None, None]
    eg_free = [None, None]
    OB3 = OBUF.rearrange("p (h t) -> p h t", t=T)
    for m in range(NCHK):
        sl = m % 2

        @p.on("sync")
        def _(sync, m=m, sl=sl):
            if m <= 1:
                p.wait(sync, {"ev": hin_done})
            wait_tok(sync, p, qtr_free[sl])
            p.inc(sync.dma_start(out=QTR[sl][:, :], in_=qt_in[:, m * D:(m + 1) * D]),
                  f"lqt{sl}", 16)
        lqt = p.val(f"lqt{sl}")

        holdg = {}

        @p.on("scalar")
        def _(se, m=m, sl=sl):
            s = Eng(p, se, "scalar")
            if m == 0:
                s.wait_ge(p.hnd["ev"], hin_done)
            s.wtok(eg_free[sl])
            g3 = GMT[:, :].rearrange("p (h mm) -> p h mm", mm=NCHK)
            s.act(EG[:, sl * NH:(sl + 1) * NH].rearrange("p (h o) -> p h o", o=1),
                  g3[:, :, m:m + 1], AF.Exp)
            holdg["v"] = s.mark("ac")

        holdh = {}

        @p.on("vector")
        def _(ve, m=m, sl=sl, holdg=holdg):
            v = Eng(p, ve, "vector")
            v.wait_ge(p.hnd["ac"], holdg["v"])
            if m == 0:
                v.wait_ge(p.hnd["lop"], lop_done)
            v.wtok(hcb_free[sl])
            egb = EG[:, sl * NH:(sl + 1) * NH] \
                .rearrange("p (h o) -> p h o", o=1).broadcast_to([128, NH, 128])
            v.tt(HCB16[sl].rearrange("p (h t) -> p h t", t=128),
                 HIN.rearrange("p (h t) -> p h t", t=128), egb, ALU.mult)
            holdh["v"] = v.mark("ev")
        eg_free[sl] = ("ev", p.val("ev"))

        bank0 = (m % 2) * 4
        mmc = {}

        @p.on("tensor")
        def _(t, m=m, sl=sl, holdh=holdh, lqt=lqt, bank0=bank0):
            t.wait_ge(p.hnd["ev"], holdh["v"])
            t.wait_ge(p.hnd["lqt" + str(sl)], lqt)
            for i in range(bank0, bank0 + 4):
                wait_tok(t, p, b["bank_free"][i])
            hcv = HCB16[sl].rearrange("p (h t) -> p h t", t=128)
            for h in range(NH):
                ps = b["banks"][bank0 + h // 4]
                last = t.matmul(ps[:, (h % 4) * 128:(h % 4 + 1) * 128],
                                hcv[:, h, :], QTR[sl][:, h * 128:(h + 1) * 128],
                                start=True, stop=True, skip_group_check=True)
                if h % 4 == 3:
                    mmc[h // 4] = p.inc(last, "mm")
        hcb_free[sl] = ("mm", mmc[3])
        qtr_free[sl] = ("mm", mmc[3])

        evcs = {}

        @p.on("vector")
        def _(ve, m=m, bank0=bank0):
            v = Eng(p, ve, "vector")
            for k in range(4):
                v.wait_ge(p.hnd["mm"], mmc[k])
                bv = b["banks"][bank0 + k].rearrange("p (h t) -> p h t", t=128)
                ob = OB3[:, 4 * k:4 * (k + 1), m * 128:(m + 1) * 128]
                v.tt(ob, ob, bv, ALU.add)
                evcs[k] = v.mark("ev")
        for k in range(4):
            b["bank_free"][bank0 + k] = ("ev", evcs[k])

    corr_done = p.val("ev")
    ps_sq = [b["banks"][0], b["banks"][1]]
    mm_by_h = {}
    for h in range(NH):
        holds = {}

        @p.on("vector")
        def _(ve, h=h):
            v = Eng(p, ve, "vector")
            if h == 0:
                v.wait_ge(p.hnd["ev"], corr_done)
            if h >= 1:
                v.wait_ge(p.hnd["mm"], mm_by_h[h - 1])
            v.tt(b["xbf"][0][:, :], OBUF[:, chs(h)], OBUF[:, chs(h)], ALU.mult)
            holds["v"] = v.mark("ev")

        @p.on("tensor")
        def _(t, h=h):
            if h == 0:
                for i in (0, 1):
                    wait_tok(t, p, b["bank_free"][i])
            t.wait_ge(p.hnd["ev"], holds["v"])
            last = None
            for hh in (0, 1):
                last = t.matmul(ps_sq[hh][0:1, :], b["ones_bf"][:, :],
                                b["xbf"][0][:, hsl(hh)],
                                start=(h == 0), stop=(h == NH - 1))
            mm_by_h[h] = p.inc(last, "mm")
    mm_sq = p.val("mm")

    hold2 = {}

    @p.on("vector")
    def _(ve):
        v = Eng(p, ve, "vector")
        v.wait_ge(p.hnd["mm"], mm_sq)
        for hh in (0, 1):
            v.cp(row(b, 0)[:, hsl(hh)], ps_sq[hh][0:1, :])
        v.ts(row(b, 0), row(b, 0), 1.0 / D, None, ALU.mult)
        hold2["v"] = v.mark("ev")

    hold3 = {}

    @p.on("scalar")
    def _(se):
        s = Eng(p, se, "scalar")
        s.wait_ge(p.hnd["ev"], hold2["v"])
        s.act(row(b, 2), row(b, 0), AF.Ln, bias=b["consts"][0:1, 0:1])
        s.act(row(b, 1), row(b, 2), AF.Exp, scale=-0.5)
        hold3["v"] = s.mark("ac")

    @p.on("tensor")
    def _(t):
        t.wait_ge(p.hnd["ac"], hold3["v"])
        for i in (2, 3):
            wait_tok(t, p, b["bank_free"][i])
        last = None
        for hh in (0, 1):
            last = t.matmul(b["banks"][2 + hh][:, :], b["ones_row_f"][:, :],
                            row(b, 1)[0:1, hsl(hh)], start=True, stop=True)
        p.inc(last, "mm")
    mmrb = p.val("mm")

    @p.on("vector")
    def _(ve):
        v = Eng(p, ve, "vector")
        v.wait_ge(p.hnd["mm"], mmrb)
        for hh in (0, 1):
            v.cp(b["rb"][:, hsl(hh)], b["banks"][2 + hh][:, :])
        for h in range(NH):
            for hh in (0, 1):
                v.ts(b["aptmp"][:, :], OBUF[:, tok(h, hh)], GW[:, h:h + 1], None, ALU.mult)
                v.tt(OBUF[:, tok(h, hh)], b["aptmp"][:, :], b["rb"][:, hsl(hh)], ALU.mult)
        v.mark("ev")
    nrm_done = p.val("ev")
    for i in range(8):
        b["bank_free"][i] = ("ev", nrm_done)

    for g in range(4):
        ldo, stripo = load_strip(p, b, g % 2, wob_in, g * 512, CH, q="gpsimd")
        mmo = proj_group(p, b, stripo, ldo, OBUF, CH, start=True, stop=True)
        b["strip_free"][g % 2] = ("mm", mmo[(3, 1)])

        @p.on("vector")
        def _(ve, g=g, mmo=mmo):
            v = Eng(p, ve, "vector")
            for oi in range(4):
                oc = g * 4 + oi
                for hh in (0, 1):
                    v.wait_ge(p.hnd["mm"], mmo[(oi, hh)])
                    v.tt(X[:, tok(oc, hh)], b["banks"][oi * 2 + hh][:, :],
                         X[:, tok(oc, hh)], ALU.add)
            v.mark("ev")
        for i in range(8):
            b["bank_free"][i] = ("ev", p.val("ev"))

    emit_ln(p, b, X, XLN, "ln4")

    emit_ffn(p, b, X, XLN, Hbuf, gate_w, up_w, dn_w, sc_x, out_y, XR, XOT, FE4)

    fin = p.snap()

    @p.on("gpsimd")
    def _(g):
        p.wait(g, fin)

    return p


def _l3_host(inputs, l2res):
    gate_full = _L23_CACHE["gate_full"]
    up_full = _L23_CACHE["up_full"]
    dn_full = _L23_CACHE["dn_full"]
    ldall = np.stack([np.ascontiguousarray(np.asarray(l2res[j]["logd"])).T.reshape(D)
                      for j in range(NCORE)]).astype(np.float32)
    sall = np.ascontiguousarray(np.concatenate(
        [np.asarray(l2res[j]["sstate"]) for j in range(NCORE)], 0))
    gwv = np.asarray(inputs["L0_attn_b_g_weight"], np.float32)
    gw = np.ascontiguousarray(gwv.reshape(NH, 128).T)
    wob = np.ascontiguousarray(np.asarray(inputs["L0_attn_b_Wo"], np.float32))
    cst = _consts()
    in_maps = []
    for c in range(NCORE):
        wscanT = np.zeros((8, 8), np.float32)
        for j in range(8):
            for i in range(8):
                if j < i < c:
                    wscanT[i, j] = 1.0
        bia = np.where(np.arange(8) < c, 0.0, -1e30).astype(np.float32)
        biarep = np.tile(bia, 16)[None, :].repeat(128, 0)
        r = l2res[c]
        m = dict(x3=np.ascontiguousarray(np.asarray(r["x3"])),
                 opart=np.ascontiguousarray(np.asarray(r["opart"])),
                 qtil=np.ascontiguousarray(np.asarray(r["qtil"])),
                 sall=sall, ldall=ldall, wscanT=wscanT, biarep=biarep,
                 gamlog=np.ascontiguousarray(np.asarray(r["gamlog"])),
                 gw=gw, wob=wob, gate_w=gate_full, up_w=up_full, dn_w=dn_full,
                 **cst)
        in_maps.append(m)
    return in_maps


def run_l3(inputs, l2res):
    in_maps = _l3_host(inputs, l2res)
    prog = build_l3()
    res = run_bass_kernel_spmd(prog.nc, in_maps, core_ids=list(range(NCORE)))
    return res.results


def kernel(**inputs):
    l1res = run_l1(inputs)
    l2res = run_l2(inputs, l1res)
    l3res = run_l3(inputs, l2res)
    outs = []
    for c in range(NCORE):
        y = np.asarray(l3res[c]["y"], np.float32)
        outs.append(y.reshape(128, CH, T).transpose(2, 1, 0).reshape(T, D))
    return np.concatenate(outs, 0).reshape(1, SEQ, D).astype(np.float32)


# version 06
# speedup vs baseline: 1.3093x; 1.0256x over previous
import contextlib
import math

import numpy as np
import ml_dtypes

import concourse.bass as bass
import concourse.mybir as mybir
from concourse.bass_utils import run_bass_kernel_spmd

F32 = mybir.dt.float32
BF16 = mybir.dt.bfloat16
AF = mybir.ActivationFunctionType
ALU = mybir.AluOpType
AX = mybir.AxisListType

NCORE = 8
D = 2048
SEQ = 8192
T = SEQ // NCORE
HW = 512
CH = D // 128
INTER = 8192
ICH = INTER // 128
NH, DK, DV = 16, 128, 128
CHUNK = 128
NCHK = T // CHUNK
SUB = 32
NSUB = CHUNK // SUB
EPS = 1e-5
DENOM = math.sqrt(2048.0) / 8192.0
LN_C = math.log(DK ** -0.5)



class Prog:

    def __init__(self):
        self.nc = bass.Bass("TRN2", target_bir_lowering=False,
                            detect_race_conditions=False)
        self.ctx = contextlib.ExitStack()
        self.hnd = {}
        self.cnt = {}
        self.block = None
        self.haz = {k: {"w": [], "r": [], "last": None}
                    for k in ("vector", "scalar", "gpsimd")}
        for s in ("gd", "mm", "ev", "ac", "st", "vsy", "ssy", "gsy"):
            self.sem(s)

    def sem(self, name):
        if name not in self.hnd:
            self.hnd[name] = self.ctx.enter_context(self.nc.semaphore(name))
            self.cnt[name] = 0
        return self.hnd[name]

    def inc(self, inst, name, k=1):
        inst.then_inc(self.hnd[name], k)
        self.cnt[name] += k
        return self.cnt[name]

    def val(self, name):
        return self.cnt[name]

    def start_block(self):
        self.block = self.ctx.enter_context(self.nc.Block())

    def on(self, eng):
        def deco(fn):
            getattr(self.block, eng)(fn)
        return deco

    def snap(self):
        return dict(self.cnt)

    def wait(self, eng, snap):
        for name, v in snap.items():
            if v > 0:
                eng.wait_ge(self.hnd[name], v)

    def sb(self, name, shape, dt):
        return self.ctx.enter_context(self.nc.sbuf_tensor(name, list(shape), dt))

    def psb(self, name, shape, dt):
        return self.ctx.enter_context(self.nc.psum_tensor(name, list(shape), dt))

    def din(self, name, shape, dt):
        return self.nc.dram_tensor(name, list(shape), dt, kind="ExternalInput").ap()

    def dout(self, name, shape, dt):
        return self.nc.dram_tensor(name, list(shape), dt, kind="ExternalOutput").ap()

    def dtmp(self, name, shape, dt):
        return self.nc.dram_tensor(name, list(shape), dt).ap()


_AUTOSEM = {"vector": "vsy", "scalar": "ssy", "gpsimd": "gsy"}


def _region(a):
    try:
        name = a.tensor.name
        es = mybir.dt.size(a.dtype)
        apl = [(int(s), int(c)) for s, c in a.ap]
    except (AttributeError, TypeError):
        return None
    lo = int(a.offset) * es
    if len(apl) > 1:
        skip = max(range(len(apl)), key=lambda i: apl[i][0])
    else:
        skip = -1
    span = es
    for i, (s, c) in enumerate(apl):
        if i != skip:
            span += es * abs(s) * (c - 1)
    return (name, lo, lo + span)


def _overlap(r1, r2):
    return r1[0] == r2[0] and r1[1] < r2[2] and r2[1] < r1[2]


class Eng:

    def __init__(self, p, e, kind):
        self.p, self.e, self.kind = p, e, kind
        self.h = p.haz[kind]

    def _sync(self):
        h = self.h
        if h["last"] is not None:
            inst, sname, sval = h["last"]
            if sname is None:
                sname = _AUTOSEM[self.kind]
                sval = self.p.inc(inst, sname)
                h["last"] = (inst, sname, sval)
            self.e.wait_ge(self.p.hnd[sname], sval)
        h["w"].clear()
        h["r"].clear()

    def _op(self, call, wr, rd):
        h = self.h
        wn = [r for r in (_region(a) for a in wr) if r]
        rn = [r for r in (_region(a) for a in rd) if r]
        conflict = (
            any(_overlap(r, w) for r in rn for w in h["w"])
            or any(_overlap(w1, w2) for w1 in wn for w2 in h["w"])
            or any(_overlap(w, r) for w in wn for r in h["r"])
        )
        if conflict:
            self._sync()
        inst = call()
        h["w"].extend(wn)
        h["r"].extend(rn)
        h["last"] = (inst, None, None)
        return inst

    def mark(self, sem, k=1):
        h = self.h
        inst, sname, sval = h["last"]
        assert sname is None, "op already has an inc"
        n = self.p.inc(inst, sem, k)
        h["last"] = (inst, sem, n)
        return n

    def barrier(self):
        self._sync()

    def wait_ge(self, semh, v):
        self.e.wait_ge(semh, v)

    def wtok(self, t):
        if t is not None:
            self.e.wait_ge(self.p.hnd[t[0]], t[1])

    def wsnap(self, snap):
        self.p.wait(self.e, snap)

    def tt(self, out, a, b2, op):
        return self._op(lambda: self.e.tensor_tensor(out, a, b2, op), [out], [a, b2])

    def ts(self, out, a, s1, s2, op0, op1=None):
        rd = [a] + [s for s in (s1, s2) if hasattr(s, "tensor")]
        if op1 is None:
            return self._op(lambda: self.e.tensor_scalar(out, a, s1, s2, op0), [out], rd)
        return self._op(lambda: self.e.tensor_scalar(out, a, s1, s2, op0, op1), [out], rd)

    def cp(self, out, a):
        return self._op(lambda: self.e.tensor_copy(out, a), [out], [a])

    def red(self, out, a, op):
        return self._op(lambda: self.e.tensor_reduce(out, a, axis=AX.X, op=op), [out], [a])

    def mset(self, out, v):
        return self._op(lambda: self.e.memset(out, v), [out], [])

    def act(self, out, a, func, bias=None, scale=None, accum_out=None):
        kw = {}
        if bias is not None:
            kw["bias"] = bias
        if scale is not None:
            kw["scale"] = scale
        if accum_out is not None:
            kw["accum_out"] = accum_out
        rd = [a] + [x for x in (bias, scale) if hasattr(x, "tensor")]
        wr = [out] + ([accum_out] if accum_out is not None else [])
        return self._op(lambda: self.e.activation(out, a, func, **kw), wr, rd)


def chs(ch, w=T):
    return slice(ch * w, (ch + 1) * w)


def tok(ch, hh, w=T):
    return slice(ch * w + hh * HW, ch * w + (hh + 1) * HW)


def hsl(hh):
    return slice(hh * HW, (hh + 1) * HW)


def wait_tok(eng, p, t):
    if t is not None:
        eng.wait_ge(p.hnd[t[0]], t[1])


def make_shared(p):
    b = {}
    b["banks"] = [p.psb(f"pb{i}", [128, 512], F32) for i in range(8)]
    b["bank_free"] = [None] * 8
    b["ones_bf"] = p.sb("ones_bf", [128, 1], BF16)
    b["ones_row_f"] = p.sb("ones_row_f", [1, 128], F32)
    b["xbf"] = [p.sb("xbf0", [128, T], BF16)]
    b["xsqb"] = [p.sb("xsqb0", [128, T], BF16)]
    b["rows"] = p.sb("rows", [1, 3 * T], F32)
    b["mb"] = p.sb("mbb", [128, T], F32)
    b["rb"] = p.sb("rbb", [128, T], F32)
    b["aptmp"] = p.sb("aptmp", [128, HW], F32)
    b["strip"] = [p.sb(f"strip{i}", [128, 16 * 512], BF16) for i in range(2)]
    b["strip_free"] = [None, None]
    b["consts"] = p.sb("consts", [128, 4], F32)
    return b


def row(b, i):
    return b["rows"][:, i * T:(i + 1) * T]


def fill_consts(p, b):
    @p.on("vector")
    def _(ve):
        v = Eng(p, ve, "vector")
        v.mset(b["consts"][:, 0:1], EPS)
        v.mset(b["consts"][:, 1:2], 1.0)
        v.mset(b["consts"][:, 2:3], LN_C)
        v.mark("ev")


def load_consts(p, b, pairs):
    @p.on("sync")
    def _(sync):
        for dst, src in pairs:
            p.inc(sync.dma_start(out=dst, in_=src), "lx", 16)


def cast_weight(p, dst_bf, src_f32, pieces=4):
    rows_ = src_f32.shape[0]
    step = rows_ // pieces
    @p.on("gpsimd")
    def _(g):
        for i in range(pieces):
            p.inc(g.dma_start(out=dst_bf[i * step:(i + 1) * step, :],
                              in_=src_f32[i * step:(i + 1) * step, :]), "gd", 16)


def emit_ln(p, b, X, XLN, label):
    ps_sum = [b["banks"][0], b["banks"][1]]
    ps_sq = [b["banks"][2], b["banks"][3]]
    ps_mb = [b["banks"][4], b["banks"][5]]
    ps_rb = [b["banks"][6], b["banks"][7]]
    pre = p.snap()
    mm_by_ch = {}

    for ch in range(CH):
        cur = 0
        hold = {}

        @p.on("vector")
        def _(ve, ch=ch, cur=cur):
            v = Eng(p, ve, "vector")
            if ch == 0:
                v.wsnap(pre)
            if ch >= 1:
                v.wait_ge(p.hnd["mm"], mm_by_ch[ch - 1])
            v.cp(b["xbf"][cur][:, :], X[:, chs(ch)])
            v.tt(b["xsqb"][cur][:, :], X[:, chs(ch)], X[:, chs(ch)], ALU.mult)
            hold["v"] = v.mark("ev")

        @p.on("tensor")
        def _(t, ch=ch, cur=cur):
            if ch == 0:
                for i in range(8):
                    wait_tok(t, p, b["bank_free"][i])
            t.wait_ge(p.hnd["ev"], hold["v"])
            last = None
            for hh in (0, 1):
                t.matmul(ps_sum[hh][0:1, :], b["ones_bf"][:, :], b["xbf"][cur][:, hsl(hh)],
                         start=(ch == 0), stop=(ch == CH - 1))
                last = t.matmul(ps_sq[hh][0:1, :], b["ones_bf"][:, :], b["xsqb"][cur][:, hsl(hh)],
                                start=(ch == 0), stop=(ch == CH - 1))
            mm_by_ch[ch] = p.inc(last, "mm")

    mm_done = p.val("mm")
    hold2 = {}

    @p.on("vector")
    def _(ve):
        v = Eng(p, ve, "vector")
        v.wait_ge(p.hnd["mm"], mm_done)
        for hh in (0, 1):
            v.cp(row(b, 0)[:, hsl(hh)], ps_sum[hh][0:1, :])
            v.cp(row(b, 1)[:, hsl(hh)], ps_sq[hh][0:1, :])
        v.ts(row(b, 0), row(b, 0), 1.0 / D, None, ALU.mult)
        v.ts(row(b, 1), row(b, 1), 1.0 / D, None, ALU.mult)
        v.tt(row(b, 2), row(b, 0), row(b, 0), ALU.mult)
        v.tt(row(b, 1), row(b, 1), row(b, 2), ALU.subtract)
        hold2["v"] = v.mark("ev")

    hold3 = {}

    @p.on("scalar")
    def _(se):
        s = Eng(p, se, "scalar")
        s.wait_ge(p.hnd["ev"], hold2["v"])
        s.act(row(b, 2), row(b, 1), AF.Ln, bias=b["consts"][0:1, 0:1])
        s.act(row(b, 1), row(b, 2), AF.Exp, scale=-0.5)
        hold3["v"] = s.mark("ac")

    @p.on("tensor")
    def _(t):
        t.wait_ge(p.hnd["ac"], hold3["v"])
        last = None
        for hh in (0, 1):
            t.matmul(ps_mb[hh][:, :], b["ones_row_f"][:, :], row(b, 0)[0:1, hsl(hh)],
                     start=True, stop=True)
            last = t.matmul(ps_rb[hh][:, :], b["ones_row_f"][:, :], row(b, 1)[0:1, hsl(hh)],
                            start=True, stop=True)
        p.inc(last, "mm")
    mmb = p.val("mm")

    @p.on("vector")
    def _(ve):
        v = Eng(p, ve, "vector")
        v.wait_ge(p.hnd["mm"], mmb)
        for hh in (0, 1):
            v.cp(b["mb"][:, hsl(hh)], ps_mb[hh][:, :])
            v.cp(b["rb"][:, hsl(hh)], ps_rb[hh][:, :])
        for ch in range(CH):
            for hh in (0, 1):
                v.tt(b["aptmp"][:, :], X[:, tok(ch, hh)], b["mb"][:, hsl(hh)], ALU.subtract)
                v.tt(XLN[:, tok(ch, hh)], b["aptmp"][:, :], b["rb"][:, hsl(hh)], ALU.mult)
        v.mark("ev")
    evd = p.val("ev")
    for i in range(8):
        b["bank_free"][i] = ("ev", evd)
    return ("ev", evd)


def load_strip(p, b, sbuf_idx, w_dram, col0, k_chunks, kc0=0, q="sync"):
    prev = b["strip_free"][sbuf_idx]
    strip = b["strip"][sbuf_idx]
    semn = f"lw{sbuf_idx}"
    p.sem(semn)

    @p.on(q)
    def _(sync):
        wait_tok(sync, p, prev)
        src = w_dram[kc0 * 128:(kc0 + k_chunks) * 128, col0:col0 + 512] \
            .rearrange("(k p) c -> p k c", p=128)
        dst = strip[:, 0:k_chunks * 512].rearrange("p (k c) -> p k c", k=k_chunks)
        p.inc(sync.dma_start(out=dst, in_=src), semn, 16)
    return (semn, p.val(semn)), strip


def proj_group(p, b, strip, ldv, src, k_chunks, start, stop, kc_shift=0):
    mmv = {}

    @p.on("tensor")
    def _(t):
        wait_tok(t, p, ldv)
        for oi in range(4):
            for hh in (0, 1):
                ps = b["banks"][oi * 2 + hh]
                if start:
                    wait_tok(t, p, b["bank_free"][oi * 2 + hh])
                last = None
                for kc in range(k_chunks):
                    last = t.matmul(
                        ps[:, :],
                        strip[:, kc * 512 + oi * 128: kc * 512 + (oi + 1) * 128],
                        src[:, (kc + kc_shift) * T + hh * HW:(kc + kc_shift) * T + (hh + 1) * HW],
                        start=(start and kc == 0), stop=(stop and kc == k_chunks - 1))
                mmv[(oi, hh)] = p.inc(last, "mm")
    return mmv



def build_l1():
    p = Prog()
    p.sem("lx")
    for s in ("lr0", "lr1", "lr2", "sro0", "sro1", "skv0", "skv1"):
        p.sem(s)
    x_in = p.din("x", [D, T], F32)
    wkT = p.din("wkT", [D, D], F32)
    wvT = p.din("wvT", [D, D], F32)
    ones_bf_in = p.din("ones_bf_in", [128, 1], BF16)
    ones_row_f_in = p.din("ones_row_f_in", [1, 128], F32)

    gu_tap = [p.din(f"gu_tap{k}", [512, INTER], F32) for k in range(4)]
    gu_c = [p.din(f"gu_c{k}", [256, INTER], F32) for k in range(4)]
    dn_tap = [p.din(f"dn_tap{k}", [1024, D], F32) for k in range(4)]
    dn_c = [p.din(f"dn_c{k}", [1024, D], F32) for k in range(4)]

    out_ss = p.dout("slicesum", [128, CH], F32)
    out_kvt = p.dout("kvt", [128, CH * T], BF16)
    out_gu = p.dout("recon_gu", [512, INTER], BF16)
    out_dn = p.dout("recon_dn", [1024, D], BF16)

    X = p.sb("X", [128, CH * T], F32)
    XLN = p.sb("XLN", [128, CH * T], BF16)
    KVTILE = [p.sb(f"kvtile{i}", [128, T], BF16) for i in range(2)]
    TMPK = p.sb("TMPK", [128, 4 * T], F32)
    KVF = p.sb("KVF", [128, T], F32)
    CS2 = p.sb("CS2", [128, T], F32)
    SS = p.sb("SS", [128, CH], F32)
    RW = 512
    RSL = [X[:, i * 4096:(i + 1) * 4096] for i in range(3)]
    RT4 = X[:, 12288:14336]
    AB = X[:, 14336:15360]
    ROv = X[:, 15360:15872].bitcast(BF16)
    RO = [ROv[:, 0:512], ROv[:, 512:1024]]

    p.start_block()
    b = make_shared(p)
    fill_consts(p, b)

    load_consts(p, b, [
        (b["ones_bf"][:, :], ones_bf_in[:, :]),
        (b["ones_row_f"][:, :], ones_row_f_in[:, :]),
    ])

    @p.on("sync")
    def _(sync):
        for ch in range(CH):
            p.inc(sync.dma_start(out=X[:, chs(ch)], in_=x_in[ch * 128:(ch + 1) * 128, :]), "lx", 16)

    ln1_tok = emit_ln(p, b, X, XLN, "ln1")

    tiles = []
    for rc in range(4):
        for sg in range(INTER // RW):
            tiles.append(("gu", rc, sg))
    for rc in range(8):
        for sg in range(D // RW):
            tiles.append(("dn", rc, sg))
    rst = {"i": 0, "slot_free": [ln1_tok, ln1_tok, ln1_tok], "ro_free": [None, None]}

    def emit_recon(n):
        for _ in range(n):
            ti = rst["i"]
            if ti >= len(tiles):
                return
            rst["i"] = ti + 1
            kind, rc, sg = tiles[ti]
            slot = ti % 3
            q = ["sync", "scalar"][ti % 2] if ti < 56 else \
                ["sync", "scalar", "gpsimd"][ti % 3]
            semr = f"lr{slot}"
            sfree = rst["slot_free"][slot]

            @p.on(q)
            def _(qe, kind=kind, rc=rc, sg=sg, slot=slot, semr=semr, sfree=sfree):
                wait_tok(qe, p, sfree)
                for k in range(4):
                    if kind == "gu":
                        tapsrc = gu_tap[k][rc * 128:(rc + 1) * 128, sg * RW:(sg + 1) * RW]
                        csrc = gu_c[k][(rc % 2) * 128:(rc % 2 + 1) * 128, sg * RW:(sg + 1) * RW]
                    else:
                        tapsrc = dn_tap[k][rc * 128:(rc + 1) * 128, sg * RW:(sg + 1) * RW]
                        csrc = dn_c[k][rc * 128:(rc + 1) * 128, sg * RW:(sg + 1) * RW]
                    p.inc(qe.dma_start(out=RSL[slot][:, k * RW:(k + 1) * RW], in_=tapsrc), semr, 16)
                    p.inc(qe.dma_start(out=RSL[slot][:, (4 + k) * RW:(5 + k) * RW], in_=csrc), semr, 16)
            ldr = p.val(semr)
            ri = ti % 2
            rfree = rst["ro_free"][ri]

            @p.on("vector")
            def _(ve, slot=slot, ldr=ldr, semr=semr, ri=ri, rfree=rfree):
                v = Eng(p, ve, "vector")
                v.wait_ge(p.hnd[semr], ldr)
                v.wtok(rfree)
                sl = RSL[slot]
                v.tt(RT4[:, :], sl[:, 0:4 * RW], sl[:, 4 * RW:8 * RW], ALU.mult)
                v.tt(AB[:, 0:RW], RT4[:, 0:RW], RT4[:, RW:2 * RW], ALU.add)
                v.tt(AB[:, RW:2 * RW], RT4[:, 2 * RW:3 * RW], RT4[:, 3 * RW:4 * RW], ALU.add)
                v.tt(RO[ri][:, :], AB[:, 0:RW], AB[:, RW:2 * RW], ALU.add)
                v.mark("ev")
            evr = p.val("ev")
            rst["slot_free"][slot] = ("ev", evr)
            sems = f"sro{ri}"

            @p.on("gpsimd")
            def _(g, kind=kind, rc=rc, sg=sg, ri=ri, evr=evr, sems=sems):
                g.wait_ge(p.hnd["ev"], evr)
                dst = out_gu if kind == "gu" else out_dn
                p.inc(g.dma_start(out=dst[rc * 128:(rc + 1) * 128, sg * RW:(sg + 1) * RW],
                                  in_=RO[ri][:, :]), sems, 16)
            rst["ro_free"][ri] = (sems, p.val(sems))

    b["kvtile_free"] = [None, None]
    for g in range(4):
        ldk, stripk = load_strip(p, b, 0, wkT, g * 512, CH, q="gpsimd")
        mmk = proj_group(p, b, stripk, ldk, XLN, CH, start=True, stop=True)

        @p.on("vector")
        def _(ve, mmk=mmk):
            v = Eng(p, ve, "vector")
            for oi in range(4):
                for hh in (0, 1):
                    v.wait_ge(p.hnd["mm"], mmk[(oi, hh)])
                    v.cp(TMPK[:, tok(oi, hh)], b["banks"][oi * 2 + hh][:, :])
            v.mark("ev")
        for i in range(8):
            b["bank_free"][i] = ("ev", p.val("ev"))
        b["strip_free"][0] = ("mm", mmk[(3, 1)])

        ldv_, stripv = load_strip(p, b, 1, wvT, g * 512, CH, q="gpsimd")
        mmv = proj_group(p, b, stripv, ldv_, XLN, CH, start=True, stop=True)
        b["strip_free"][1] = ("mm", mmv[(3, 1)])

        for oi in range(4):
            ch = g * 4 + oi
            ktile = KVTILE[ch % 2]
            ktile_free = b["kvtile_free"][ch % 2]

            @p.on("vector")
            def _(ve, ch=ch, oi=oi, mmv=mmv, ktile=ktile, ktile_free=ktile_free):
                v = Eng(p, ve, "vector")
                for hh in (0, 1):
                    v.wait_ge(p.hnd["mm"], mmv[(oi, hh)])
                    v.tt(KVF[:, hsl(hh)], TMPK[:, tok(oi, hh)],
                         b["banks"][oi * 2 + hh][:, :], ALU.mult)
                v.red(SS[:, ch:ch + 1], KVF[:, :], ALU.add)
                v.wtok(ktile_free)
                bufs = [KVF, CS2]
                for si, s in enumerate([1, 2, 4, 8, 16, 32, 64, 128, 256, 512]):
                    a = bufs[si % 2]
                    o = ktile if si == 9 else bufs[(si + 1) % 2]
                    v.tt(o[:, s:T], a[:, s:T], a[:, 0:T - s], ALU.add)
                    v.cp(o[:, 0:s], a[:, 0:s])
                v.mark("ev")
            b["bank_free"][oi * 2] = ("ev", p.val("ev"))
            b["bank_free"][oi * 2 + 1] = ("ev", p.val("ev"))
            evk = p.val("ev")

            semk = f"skv{ch % 2}"

            @p.on("scalar")
            def _(gq, ch=ch, ktile=ktile, evk=evk, semk=semk):
                gq.wait_ge(p.hnd["ev"], evk)
                p.inc(gq.dma_start(out=out_kvt[:, chs(ch)], in_=ktile[:, :]), semk, 16)
            b["kvtile_free"][ch % 2] = (semk, p.val(semk))
            emit_recon(3)
        emit_recon(2)

    emit_recon(len(tiles))

    fin = p.snap()

    @p.on("gpsimd")
    def _(g):
        p.wait(g, fin)
        p.inc(g.dma_start(out=out_ss[:, :], in_=SS[:, :]), "st", 16)
        g.wait_ge(p.hnd["st"], p.val("st"))

    return p



_GRIDS = {}


def _spline_host(gate_cp, up_cp, down_cp):
    n = INTER * D
    nct = gate_cp.shape[0]
    if "idx" not in _GRIDS:
        t = np.linspace(0.0, nct - 1.0, n, dtype=np.float64)
        i = np.clip(np.floor(t).astype(np.int64), 0, nct - 2)
        f = (t - i).astype(np.float32)
        idx = np.stack([np.clip(i + k, 0, nct - 1) for k in (-1, 0, 1, 2)], axis=0).astype(np.int32)
        f2 = f * f
        f3 = f2 * f
        c0 = 0.5 * (-f + 2.0 * f2 - f3)
        c1 = 0.5 * (2.0 - 5.0 * f2 + 3.0 * f3)
        c2 = 0.5 * (f + 4.0 * f2 - 3.0 * f3)
        c3 = 0.5 * (-f2 + f3)
        c = np.stack([c0, c1, c2, c3], 0).astype(np.float32)
        cguT = np.ascontiguousarray(c.reshape(4, INTER, D).transpose(0, 2, 1))
        cdnT = np.ascontiguousarray(c.reshape(4, D, INTER).transpose(0, 2, 1))
        _GRIDS.update(idx=idx, cguT=cguT, cdnT=cdnT)
    idx = _GRIDS["idx"]
    cguT = _GRIDS["cguT"]
    cdnT = _GRIDS["cdnT"]

    out = {"gu_tap": [[] for _ in range(4)], "gu_c": [[] for _ in range(4)],
           "dn_tap": [[] for _ in range(4)], "dn_c": [[] for _ in range(4)]}
    for k in range(4):
        g_t = np.ascontiguousarray(gate_cp[idx[k]].reshape(INTER, D).T)
        u_t = np.ascontiguousarray(up_cp[idx[k]].reshape(INTER, D).T)
        d_t = np.ascontiguousarray(down_cp[idx[k]].reshape(D, INTER).T)
        for ccore in range(NCORE):
            gr = slice(ccore * 256, (ccore + 1) * 256)
            dr = slice(ccore * 1024, (ccore + 1) * 1024)
            out["gu_tap"][k].append(np.ascontiguousarray(
                np.concatenate([g_t[gr], u_t[gr]], 0)))
            out["gu_c"][k].append(np.ascontiguousarray(cguT[k][gr]))
            out["dn_tap"][k].append(np.ascontiguousarray(d_t[dr]))
            out["dn_c"][k].append(np.ascontiguousarray(cdnT[k][dr]))
    return out


def _fm_slices(x):
    xs = x.reshape(SEQ, D)
    return [np.ascontiguousarray(xs[c * T:(c + 1) * T, :].T) for c in range(NCORE)]


def _consts():
    return {
        "ones_bf_in": np.ones((128, 1), dtype=ml_dtypes.bfloat16),
        "ones_row_f_in": np.ones((1, 128), dtype=np.float32),
    }


def run_l1(inputs):
    x = np.asarray(inputs["L0_attn_a_x"], dtype=np.float32)
    xs = _fm_slices(x)
    wkT = np.ascontiguousarray(np.asarray(inputs["L0_attn_a_Wk"], np.float32).T)
    wvT = np.ascontiguousarray(np.asarray(inputs["L0_attn_a_Wv"], np.float32).T)
    sp = _spline_host(np.asarray(inputs["L0_ffn_gate_cp"], np.float32),
                      np.asarray(inputs["L0_ffn_up_cp"], np.float32),
                      np.asarray(inputs["L0_ffn_down_cp"], np.float32))
    cst = _consts()
    in_maps = []
    for c in range(NCORE):
        m = dict(x=xs[c], wkT=wkT, wvT=wvT, **cst)
        for k in range(4):
            m[f"gu_tap{k}"] = sp["gu_tap"][k][c]
            m[f"gu_c{k}"] = sp["gu_c"][k][c]
            m[f"dn_tap{k}"] = sp["dn_tap"][k][c]
            m[f"dn_c{k}"] = sp["dn_c"][k][c]
        in_maps.append(m)
    prog = build_l1()
    res = run_bass_kernel_spmd(prog.nc, in_maps, core_ids=list(range(NCORE)))
    return res.results


def kernel(**inputs):
    raise NotImplementedError("L2/L3 not yet wired")



_L23_CACHE = {}

class Carver:

    def __init__(self, base_ap, total_bytes):
        self.base = base_ap
        self.total = total_bytes
        self.cur = 0

    def take(self, shape, dt):
        es = mybir.dt.size(dt)
        nb = int(np.prod(shape[1:])) * es if len(shape) > 1 else es
        nb = int(np.prod(shape[1:])) * es
        self.cur = (self.cur + 3) // 4 * 4
        c0 = self.cur // 2
        self.cur += nb
        assert self.cur <= self.total, f"carver overflow {self.cur} > {self.total}"
        view = self.base[:, c0:c0 + nb // 2]
        if dt != BF16:
            view = view.bitcast(dt)
        if len(shape) == 3:
            view = view.rearrange("p (a c) -> p a c", a=shape[1])
        return view

    def reset(self, to=0):
        self.cur = to



def load_stripx(p, buf, free_tok, semn, w_dram, col0, k_chunks, kc0=0):
    p.sem(semn)

    @p.on("sync")
    def _(sync):
        wait_tok(sync, p, free_tok)
        src = w_dram[kc0 * 128:(kc0 + k_chunks) * 128, col0:col0 + 512] \
            .rearrange("(k p) c -> p k c", p=128)
        dst = buf[:, 0:k_chunks * 512].rearrange("p (k c) -> p k c", k=k_chunks)
        p.inc(sync.dma_start(out=dst, in_=src), semn, 16)
    return (semn, p.val(semn))


def emit_ffn(p, b, X, XLN, Hbuf, gate_w, up_w, dn_w, sc_x, out_dst, XR, XOT, FE4,
             xstrips=None):
    S = [b["strip"][0][:, :], b["strip"][1][:, :]] + (xstrips or [])
    NS = len(S)
    SF = [b["strip_free"][0], b["strip_free"][1]] + [None] * (NS - 2)
    @p.on("gpsimd")
    def _(g):
        p.wait(g, {"ev": p.val("ev")})
        for ch in range(CH):
            p.inc(g.dma_start(out=sc_x[:, chs(ch)], in_=X[:, chs(ch)]), "gd", 16)
    spill_done = ("gd", p.val("gd"))

    b["xr_free"] = [None, None]
    b["xot_free"] = [None, None]
    prev_silu_consumed = None
    gidx = 0
    for hh in (0, 1):
        for g in range(ICH // 4):
            sg_i = (2 * gidx) % NS
            su_i = (2 * gidx + 1) % NS
            gidx += 1
            stripg = S[sg_i]
            stripu = S[su_i]
            ldg = load_stripx(p, stripg, SF[sg_i], f"lw{sg_i}", gate_w, g * 512, CH)
            ldu = load_stripx(p, stripu, SF[su_i], f"lw{su_i}", up_w, g * 512, CH)
            mmg = {}

            @p.on("tensor")
            def _(t, g=g, hh=hh, ldg=ldg, ldu=ldu, stripg=stripg, stripu=stripu):
                wait_tok(t, p, ldg)
                if g == 0 and hh == 0:
                    wait_tok(t, p, spill_done)
                for oi in range(4):
                    wait_tok(t, p, b["bank_free"][oi])
                    last = None
                    for kc in range(CH):
                        last = t.matmul(b["banks"][oi][:, :],
                                        stripg[:, kc * 512 + oi * 128: kc * 512 + (oi + 1) * 128],
                                        XLN[:, tok(kc, hh)],
                                        start=(kc == 0), stop=(kc == CH - 1))
                    mmg[("g", oi)] = p.inc(last, "mm")
                wait_tok(t, p, ldu)
                for oi in range(4):
                    wait_tok(t, p, b["bank_free"][4 + oi])
                    last = None
                    for kc in range(CH):
                        last = t.matmul(b["banks"][4 + oi][:, :],
                                        stripu[:, kc * 512 + oi * 128: kc * 512 + (oi + 1) * 128],
                                        XLN[:, tok(kc, hh)],
                                        start=(kc == 0), stop=(kc == CH - 1))
                    mmg[("u", oi)] = p.inc(last, "mm")

            holds = {}

            @p.on("scalar")
            def _(se, mmg=mmg, prev=prev_silu_consumed):
                s = Eng(p, se, "scalar")
                if prev is not None:
                    s.wait_ge(p.hnd["ev"], prev)
                for oi in range(4):
                    s.wait_ge(p.hnd["mm"], mmg[("g", oi)])
                    s.act(FE4[:, oi * HW:(oi + 1) * HW], b["banks"][oi][:, :], AF.Silu)
                    holds[oi] = s.mark("ac")

            evs = {}

            @p.on("vector")
            def _(ve, g=g, hh=hh, mmg=mmg):
                v = Eng(p, ve, "vector")
                for oi in range(4):
                    ic = g * 4 + oi
                    v.wait_ge(p.hnd["ac"], holds[oi])
                    v.wait_ge(p.hnd["mm"], mmg[("u", oi)])
                    v.tt(Hbuf[:, ic * HW:(ic + 1) * HW],
                         FE4[:, oi * HW:(oi + 1) * HW], b["banks"][4 + oi][:, :], ALU.mult)
                    evs[oi] = v.mark("ev")
            evg = p.val("ev")
            prev_silu_consumed = evg
            for oi in range(4):
                b["bank_free"][oi] = ("ac", holds[oi])
                b["bank_free"][4 + oi] = ("ev", evs[oi])
            SF[sg_i] = ("mm", mmg[("g", 3)])
            SF[su_i] = ("mm", mmg[("u", 3)])

        for ocg in range(4):
            mmd = {}
            for piece in range(4):
                sbf = (gidx * 2 + piece) % NS
                stripd = S[sbf]
                ldd = load_stripx(p, stripd, SF[sbf], f"lw{sbf}", dn_w, ocg * 512, CH,
                                  kc0=piece * 16)

                @p.on("tensor")
                def _(t, piece=piece, ldd=ldd, stripd=stripd, ocg=ocg, hh=hh):
                    wait_tok(t, p, ldd)
                    if piece == 0:
                        for i in range(4):
                            wait_tok(t, p, b["bank_free"][i])
                    last = None
                    for oi in range(4):
                        for kc in range(CH):
                            kabs = piece * 16 + kc
                            last = t.matmul(
                                b["banks"][oi][:, :],
                                stripd[:, kc * 512 + oi * 128: kc * 512 + (oi + 1) * 128],
                                Hbuf[:, kabs * HW:(kabs + 1) * HW],
                                start=(kabs == 0), stop=(kabs == ICH - 1),
                                skip_group_check=True)
                        if piece == 3:
                            mmd[oi] = p.inc(last, "mm")
                            last = None
                    if piece != 3:
                        p.inc(last, "mm")
                SF[sbf] = ("mm", p.val("mm"))
            gidx += 2

            for oi in range(4):
                oc = ocg * 4 + oi
                sl = oc % 2

                @p.on("sync")
                def _(sync, oc=oc, hh=hh, sl=sl):
                    wait_tok(sync, p, b["xr_free"][sl])
                    p.inc(sync.dma_start(out=XR[sl][:, :], in_=sc_x[:, tok(oc, hh)]), f"lxr{sl}", 16)
                lxr = p.val(f"lxr{sl}")
                xot_free = b["xot_free"][sl]

                @p.on("vector")
                def _(ve, oi=oi, sl=sl, lxr=lxr, mmd=mmd, xot_free=xot_free):
                    v = Eng(p, ve, "vector")
                    v.wait_ge(p.hnd["mm"], mmd[oi])
                    v.wait_ge(p.hnd[f"lxr{sl}"], lxr)
                    v.wtok(xot_free)
                    v.tt(XOT[sl][:, :], b["banks"][oi][:, :], XR[sl][:, :], ALU.add)
                    v.mark("ev")
                evd = p.val("ev")
                b["xr_free"][sl] = ("ev", evd)
                b["bank_free"][oi] = ("ev", evd)

                semx = "sx3a" if sl == 0 else "sx3b"

                @p.on("gpsimd")
                def _(gq, oc=oc, hh=hh, sl=sl, evd=evd, semx=semx):
                    gq.wait_ge(p.hnd["ev"], evd)
                    p.inc(gq.dma_start(out=out_dst[:, tok(oc, hh)], in_=XOT[sl][:, :]), semx, 16)
                b["xot_free"][sl] = (semx, p.val(semx))



def build_l2():
    p = Prog()
    for s in ("lx", "lkv0", "lkv1", "lxr0", "lxr1", "sx3a", "sx3b", "lx3", "lssg",
              "sqs0", "sqs1", "sk0", "sk1", "sgp0", "sgp1", "svv0", "svv1",
              "lgp0", "lgp1", "lkk0", "lkk1", "lvv0", "lvv1", "lqq0", "lqq1",
              "sqt0", "sqt1", "soc0", "soc1"):
        p.sem(s)
    x_in = p.din("x", [D, T], F32)
    wqaT = p.din("wqaT", [D, D], F32)
    woaT = p.din("woaT", [D, D], F32)
    wqb_in = p.din("wqb", [D, D], F32)
    wfb_in = p.din("wfb", [D, D], F32)
    wib_in = p.din("wib", [D, D], F32)
    kvt_in = p.din("kvt", [128, CH * T], BF16)
    ssg_in = p.din("ssg", [8, D], F32)
    mpfx_in = p.din("mpfx", [8, 1], F32)
    gate_w = p.din("gate_w", [D, INTER], BF16)
    up_w = p.din("up_w", [D, INTER], BF16)
    dn_w = p.din("dn_w", [INTER, D], BF16)
    ones_bf_in = p.din("ones_bf_in", [128, 1], BF16)
    ones_row_f_in = p.din("ones_row_f_in", [1, 128], F32)
    ident_in = p.din("ident_bf", [128, 128], BF16)
    maskf_in = p.din("maskf", [128, 128], F32)

    out_x3 = p.dout("x3", [128, CH * T], F32)
    out_op = p.dout("opart", [128, CH * T], BF16)
    out_qt = p.dout("qtil", [128, NCHK * D], BF16)
    out_st = p.dout("sstate", [128, NH * 128], F32)
    out_ld = p.dout("logd", [128, NH], F32)
    out_gm = p.dout("gamlog", [128, NH * NCHK], F32)

    sc_x = p.dtmp("sc_x", [128, CH * T], F32)
    sc_qs = p.dtmp("sc_qs", [128, CH * T], BF16)
    sc_k = p.dtmp("sc_k", [128, CH * T], BF16)
    sc_v = p.dtmp("sc_v", [128, CH * T], BF16)
    sc_gp = p.dtmp("sc_gp", [128, CH * T], F32)

    ARE = p.sb("ARE", [128, 49152], BF16)
    X = ARE[:, 0:32768].bitcast(F32)
    XLN = ARE[:, 32768:49152]
    Hbuf = ARE[:, 0:32768]
    SCR = p.sb("SCR", [128, 16384], BF16)
    VHAT = SCR[:, :]
    OFF = p.sb("OFF", [128, CH], F32)
    MPFX = p.sb("MPFX", [8, 1], F32)
    IDE = p.sb("IDE", [128, 128], BF16)
    MASKF = p.sb("MASKF", [128, 128], F32)
    KVR = [p.sb(f"kvr{i}", [128, HW], BF16) for i in range(2)]
    XR = [p.sb(f"xr{i}", [128, HW], F32) for i in range(2)]
    XOT = [p.sb(f"xot{i}", [128, HW], F32) for i in range(2)]
    FE4 = p.sb("FE4", [128, 4 * HW], F32)

    p.start_block()
    b = make_shared(p)
    fill_consts(p, b)

    load_consts(p, b, [
        (b["ones_bf"][:, :], ones_bf_in[:, :]),
        (b["ones_row_f"][:, :], ones_row_f_in[:, :]),
        (IDE[:, :], ident_in[:, :]),
        (MASKF[:, :], maskf_in[:, :]),
        (MPFX[:, :], mpfx_in[:, :]),
    ])

    @p.on("sync")
    def _(sync):
        for ch in range(CH):
            p.inc(sync.dma_start(out=X[:, chs(ch)], in_=x_in[ch * 128:(ch + 1) * 128, :]), "lx", 16)

    emit_ln(p, b, X, XLN, "ln1")

    SSG = SCR[:, 0:4096].bitcast(F32)[0:8, 0:2048]

    @p.on("sync")
    def _(sync):
        p.inc(sync.dma_start(out=SSG, in_=ssg_in[:, :]), "lssg", 16)
    ldssg = p.val("lssg")

    @p.on("tensor")
    def _(t):
        t.wait_ge(p.hnd["lssg"], ldssg)
        for i in range(8):
            wait_tok(t, p, b["bank_free"][i])
        last = None
        for ch in range(CH):
            last = t.matmul(b["banks"][0][:, ch:ch + 1],
                            SSG[0:8, ch * 128:(ch + 1) * 128], MPFX[0:8, 0:1],
                            start=True, stop=True)
        p.inc(last, "mm")
    mmoff = p.val("mm")

    @p.on("vector")
    def _(ve):
        v = Eng(p, ve, "vector")
        v.wait_ge(p.hnd["mm"], mmoff)
        v.cp(OFF[:, :], b["banks"][0][:, 0:CH])
        v.mark("ev")
    b["bank_free"][0] = ("ev", p.val("ev"))


    kvr_free = [None, None]
    for g in range(4):
        ldq, stripq = load_strip(p, b, g % 2, wqaT, g * 512, CH, q="gpsimd")
        mmq = proj_group(p, b, stripq, ldq, XLN, CH, start=True, stop=True)
        b["strip_free"][g % 2] = ("mm", mmq[(3, 1)])

        for oi in range(4):
            ch = g * 4 + oi
            for hh in (0, 1):
                sl = (ch * 2 + hh) % 2

                @p.on("sync")
                def _(sync, ch=ch, hh=hh, sl=sl):
                    wait_tok(sync, p, kvr_free[sl])
                    p.inc(sync.dma_start(out=KVR[sl][:, :], in_=kvt_in[:, tok(ch, hh)]),
                          f"lkv{sl}", 16)
                lkvv = p.val(f"lkv{sl}")

                @p.on("vector")
                def _(ve, ch=ch, oi=oi, hh=hh, sl=sl, mmq=mmq, lkvv=lkvv):
                    v = Eng(p, ve, "vector")
                    v.wait_ge(p.hnd[f"lkv{sl}"], lkvv)
                    v.wait_ge(p.hnd["mm"], mmq[(oi, hh)])
                    v.ts(b["aptmp"][:, :], KVR[sl][:, :], OFF[:, ch:ch + 1], DENOM,
                         ALU.add, ALU.mult)
                    v.tt(VHAT[:, tok(ch, hh)], b["banks"][oi * 2 + hh][:, :],
                         b["aptmp"][:, :], ALU.mult)
                    v.mark("ev")
                b["bank_free"][oi * 2 + hh] = ("ev", p.val("ev"))
                kvr_free[sl] = ("ev", p.val("ev"))

    for g in range(4):
        ldo, stripo = load_strip(p, b, g % 2, woaT, g * 512, CH, q="gpsimd")
        mmo = proj_group(p, b, stripo, ldo, VHAT, CH, start=True, stop=True)
        b["strip_free"][g % 2] = ("mm", mmo[(3, 1)])

        @p.on("vector")
        def _(ve, g=g, mmo=mmo):
            v = Eng(p, ve, "vector")
            for oi in range(4):
                oc = g * 4 + oi
                for hh in (0, 1):
                    v.wait_ge(p.hnd["mm"], mmo[(oi, hh)])
                    v.tt(X[:, tok(oc, hh)], b["banks"][oi * 2 + hh][:, :],
                         X[:, tok(oc, hh)], ALU.add)
            v.mark("ev")
        for i in range(8):
            b["bank_free"][i] = ("ev", p.val("ev"))

    emit_ln(p, b, X, XLN, "ln2")

    emit_ffn(p, b, X, XLN, Hbuf, gate_w, up_w, dn_w, sc_x, out_x3, XR, XOT, FE4,
             xstrips=[SCR[:, 0:8192], SCR[:, 8192:16384]])

    fin_ffn = p.snap()

    @p.on("sync")
    def _(sync):
        p.wait(sync, fin_ffn)
        for ch in range(CH):
            p.inc(sync.dma_start(out=X[:, chs(ch)], in_=out_x3[:, chs(ch)]), "lx3", 16)

    emit_ln(p, b, X, XLN, "ln3")

    FEG = SCR[:, 0:8192].bitcast(F32)
    KSTG = SCR[:, 8192:12288]
    QSTG = SCR[:, 12288:13312]
    VSTG = SCR[:, 13312:14336]

    qst_free = [None, None]

    for g in range(4):
        ldq, stripq = load_strip(p, b, g % 2, wqb_in, g * 512, CH, q="gpsimd")
        mmq = proj_group(p, b, stripq, ldq, XLN, CH, start=True, stop=True)
        b["strip_free"][g % 2] = ("mm", mmq[(3, 1)])
        for oi in range(4):
            ch = g * 4 + oi
            for hh in (0, 1):
                sl = (ch * 2 + hh) % 2
                hold = {}

                @p.on("scalar")
                def _(se, oi=oi, hh=hh, sl=sl, mmq=mmq, hold=hold):
                    s = Eng(p, se, "scalar")
                    s.wait_ge(p.hnd["mm"], mmq[(oi, hh)])
                    s.wtok(qst_free[sl])
                    s.act(QSTG[:, sl * HW:(sl + 1) * HW], b["banks"][oi * 2 + hh][:, :], AF.Silu)
                    hold["v"] = s.mark("ac")
                b["bank_free"][oi * 2 + hh] = ("ac", hold["v"])
                semq = f"sqs{sl}"

                @p.on("sync")
                def _(gq, ch=ch, hh=hh, sl=sl, hold=hold, semq=semq):
                    gq.wait_ge(p.hnd["ac"], hold["v"])
                    p.inc(gq.dma_start(out=sc_qs[:, tok(ch, hh)],
                                       in_=QSTG[:, sl * HW:(sl + 1) * HW]), semq, 16)
                qst_free[sl] = (semq, p.val(semq))

    kst_free = None
    gst_free = None

    for g in range(4):
        ldf, stripf = load_strip(p, b, g % 2, wfb_in, g * 512, CH, q="gpsimd")
        mmf = proj_group(p, b, stripf, ldf, XLN, CH, start=True, stop=True)
        b["strip_free"][g % 2] = ("mm", mmf[(3, 1)])
        holds = {}

        @p.on("scalar")
        def _(se, g=g, mmf=mmf, holds=holds, kst_free=kst_free, gst_free=gst_free):
            s = Eng(p, se, "scalar")
            s.wtok(gst_free)
            s.wtok(kst_free)
            for oi in range(4):
                for hh in (0, 1):
                    j = oi * 2 + hh
                    s.wait_ge(p.hnd["mm"], mmf[(oi, hh)])
                    s.act(FEG[:, j * HW:(j + 1) * HW], b["banks"][j][:, :], AF.Exp, scale=-1.0)
            for oi in range(4):
                for hh in (0, 1):
                    j = oi * 2 + hh
                    s.act(KSTG[:, j * HW:(j + 1) * HW], b["banks"][j][:, :],
                          AF.Sigmoid, scale=-1.0)
                    holds[("k", oi, hh)] = s.mark("ac")
            for oi in range(4):
                for hh in (0, 1):
                    j = oi * 2 + hh
                    s.act(FEG[:, j * HW:(j + 1) * HW], FEG[:, j * HW:(j + 1) * HW],
                          AF.Ln, bias=b["consts"][:, 1:2])
                    holds[("gp", oi, hh)] = s.mark("ac")

        for oi in range(4):
            for hh in (0, 1):
                ch = g * 4 + oi
                j = oi * 2 + hh
                b["bank_free"][j] = ("ac", holds[("k", oi, hh)])
                hk = holds[("k", oi, hh)]
                hg = holds[("gp", oi, hh)]
                semk = f"sk{j % 2}"
                semg = f"sgp{j % 2}"

                @p.on("sync")
                def _(gq, ch=ch, hh=hh, j=j, hk=hk, hg=hg, semk=semk, semg=semg):
                    gq.wait_ge(p.hnd["ac"], hk)
                    p.inc(gq.dma_start(out=sc_k[:, tok(ch, hh)],
                                       in_=KSTG[:, j * HW:(j + 1) * HW]), semk, 16)
                    gq.wait_ge(p.hnd["ac"], hg)
                    p.inc(gq.dma_start(out=sc_gp[:, tok(ch, hh)],
                                       in_=FEG[:, j * HW:(j + 1) * HW]), semg, 16)
        kst_free = ("sk1", p.val("sk1"))
        gst_free = ("sgp1", p.val("sgp1"))
        kst_free2 = ("sk0", p.val("sk0"))
        gst_free2 = ("sgp0", p.val("sgp0"))

        @p.on("scalar")
        def _(se, kst_free2=kst_free2, gst_free2=gst_free2):
            se.wait_ge(p.hnd[kst_free2[0]], kst_free2[1])
            se.wait_ge(p.hnd[gst_free2[0]], gst_free2[1])

    vst_free = [None, None]

    for g in range(4):
        ldi, stripi = load_strip(p, b, g % 2, wib_in, g * 512, CH, q="gpsimd")
        mmi = proj_group(p, b, stripi, ldi, XLN, CH, start=True, stop=True)
        b["strip_free"][g % 2] = ("mm", mmi[(3, 1)])
        for oi in range(4):
            ch = g * 4 + oi
            for hh in (0, 1):
                sl = (ch * 2 + hh) % 2
                hold = {}

                @p.on("vector")
                def _(ve, oi=oi, hh=hh, sl=sl, mmi=mmi, hold=hold):
                    v = Eng(p, ve, "vector")
                    v.wait_ge(p.hnd["mm"], mmi[(oi, hh)])
                    v.wtok(vst_free[sl])
                    v.cp(VSTG[:, sl * HW:(sl + 1) * HW], b["banks"][oi * 2 + hh][:, :])
                    hold["v"] = v.mark("ev")
                b["bank_free"][oi * 2 + hh] = ("ev", hold["v"])
                semv = f"svv{sl}"

                @p.on("sync")
                def _(gq, ch=ch, hh=hh, sl=sl, hold=hold, semv=semv):
                    gq.wait_ge(p.hnd["ev"], hold["v"])
                    p.inc(gq.dma_start(out=sc_v[:, tok(ch, hh)],
                                       in_=VSTG[:, sl * HW:(sl + 1) * HW]), semv, 16)
                vst_free[sl] = (semv, p.val(semv))

    cva = Carver(ARE[:, :], 98304)
    HST = cva.take([128, NH, 128], F32)
    GPA = cva.take([128, NH, 128], F32)
    GPB = cva.take([128, NH, 128], F32)
    EKT = cva.take([128, NH, 128], F32)
    SUBT = cva.take([128, NH, 128], F32)
    KHAT = cva.take([128, NH, 128], BF16)
    KTT = cva.take([128, NH, 128], BF16)
    VTT = cva.take([128, NH, 128], BF16)
    HBF = cva.take([128, NH, 128], BF16)
    QTC = cva.take([128, NH, 128], BF16)
    KTIL = cva.take([128, NH, 128], BF16)
    QIB = cva.take([128, NH, SUB], BF16)
    ABM = cva.take([128, NH, 128], BF16)
    OCH = cva.take([128, NH, 128], BF16)
    SUBQ = cva.take([128, NH, SUB], F32)
    EQB = cva.take([128, NH, SUB], F32)
    GCT = cva.take([128, NH], F32)
    NGC = cva.take([128, NH], F32)
    EDEC = cva.take([128, NH], F32)
    SGP = cva.take([128, NH], F32)
    GAMT = cva.take([128, NH, NCHK], F32)
    LOGD = cva.take([128, NH], F32)
    QR3 = [cva.take([128, NH, 128], BF16) for _ in range(2)]
    QIBA = cva.take([128, NH, 128], BF16)
    cvs = Carver(SCR[:, :], 32768)
    GPR = [cvs.take([128, NH, 128], F32) for _ in range(2)]
    KR3 = [cvs.take([128, NH, 128], BF16) for _ in range(2)]
    VR3 = [cvs.take([128, NH, 128], BF16) for _ in range(2)]

    glastart = p.snap()

    @p.on("vector")
    def _(ve):
        v = Eng(p, ve, "vector")
        v.wsnap(glastart)
        v.mset(HST[:, :, :], 0.0)
        v.mset(SGP[:, :], 0.0)
        v.mark("ev")
    gla0 = p.val("ev")

    ring_free = {k: [None, None] for k in ("gp", "kk", "vv", "qq")}
    scmap = {"gp": sc_gp, "kk": sc_k, "vv": sc_v, "qq": sc_qs}
    dstmap = {"gp": GPR, "kk": KR3, "vv": VR3, "qq": QR3}

    def load_chunk(m):
        sl = m % 2
        for key in ("gp", "kk", "vv", "qq"):
            @p.on("sync")
            def _(sync, key=key, m=m, sl=sl):
                if m <= 1:
                    p.wait(sync, {"ev": gla0})
                wait_tok(sync, p, ring_free[key][sl])
                src3 = scmap[key].rearrange("p (h t) -> p h t", t=T)[:, :, m * 128:(m + 1) * 128]
                p.inc(sync.dma_start(out=dstmap[key][sl][:, :, :], in_=src3), f"l{key}{sl}", 16)

    load_chunk(0)
    och_free = [None, None]
    qtc_free = None
    prev_gla_mm = 0
    for m in range(NCHK):
        sl = m % 2
        if m + 1 < NCHK:
            load_chunk(m + 1)
        ldv = {k: p.val(f"l{k}{sl}") for k in ("gp", "kk", "vv", "qq")}

        hold = {}

        @p.on("vector")
        def _(ve, m=m, sl=sl, ldv=ldv, prev_gla_mm=prev_gla_mm):
            v = Eng(p, ve, "vector")
            if prev_gla_mm:
                v.wait_ge(p.hnd["mm"], prev_gla_mm)
            for k in ("gp", "kk", "vv", "qq"):
                v.wait_ge(p.hnd[f"l{k}{sl}"], ldv[k])
            src = GPR[sl]
            pp = [GPA, GPB]
            for si, s in enumerate([1, 2, 4, 8, 16, 32, 64]):
                a = src if si == 0 else pp[(si + 1) % 2]
                o = pp[si % 2]
                v.tt(o[:, :, s:128], a[:, :, s:128], a[:, :, 0:128 - s], ALU.add)
                v.cp(o[:, :, 0:s], a[:, :, 0:s])
            v.cp(GCT[:, :].rearrange("p (h o) -> p h o", o=1), GPA[:, :, 127:128])
            hold["v"] = v.mark("ev")
        evc = p.val("ev")
        gposf = GPA

        @p.on("vector")
        def _(ve):
            v = Eng(p, ve, "vector")
            gctb = GCT[:, :].rearrange("p (h o) -> p h o", o=1).broadcast_to([128, NH, 128])
            v.tt(EKT[:, :, :], gposf[:, :, :], gctb, ALU.subtract)
            v.mark("ev")
        evgp = p.val("ev")
        hold2 = {}

        @p.on("scalar")
        def _(se):
            s = Eng(p, se, "scalar")
            s.wait_ge(p.hnd["ev"], evgp)
            s.act(EDEC[:, :], GCT[:, :], AF.Exp, scale=-1.0)
            s.act(EKT[:, :, :], EKT[:, :, :], AF.Exp)
            hold2["v"] = s.mark("ac")

        hold3 = {}

        @p.on("vector")
        def _(ve, sl=sl):
            v = Eng(p, ve, "vector")
            v.wait_ge(p.hnd["ac"], hold2["v"])
            v.tt(KHAT[:, :, :], KR3[sl][:, :, :], EKT[:, :, :], ALU.mult)
            v.cp(HBF[:, :, :], HST[:, :, :])
            hold3["v"] = v.mark("ev")
        evkh = p.val("ev")

        ps_bf = [b["banks"][i][:, :].bitcast(BF16) for i in range(4)]

        @p.on("tensor")
        def _(t, sl=sl):
            t.wait_ge(p.hnd["ev"], evkh)
            for i in range(4):
                wait_tok(t, p, b["bank_free"][i])
            last = None
            for h in range(NH):
                last = t.transpose(ps_bf[h // 8][:, (h % 8) * 128:(h % 8 + 1) * 128],
                                   KHAT[:, h, :], IDE[:, :])
            for h in range(NH):
                last = t.transpose(ps_bf[2 + h // 8][:, (h % 8) * 128:(h % 8 + 1) * 128],
                                   VR3[sl][:, h, :], IDE[:, :])
            p.inc(last, "mm")
        mmtr = p.val("mm")

        @p.on("vector")
        def _(ve):
            v = Eng(p, ve, "vector")
            v.wait_ge(p.hnd["mm"], mmtr)
            v.cp(KTT[:, 0:8, :], ps_bf[0].rearrange("p (h t) -> p h t", h=8))
            v.cp(KTT[:, 8:16, :], ps_bf[1].rearrange("p (h t) -> p h t", h=8))
            v.cp(VTT[:, 0:8, :], ps_bf[2].rearrange("p (h t) -> p h t", h=8))
            v.cp(VTT[:, 8:16, :], ps_bf[3].rearrange("p (h t) -> p h t", h=8))
            v.mark("ev")
        evtt = p.val("ev")
        for i in range(4):
            b["bank_free"][i] = ("ev", evtt)

        @p.on("tensor")
        def _(t):
            t.wait_ge(p.hnd["ev"], evtt)
            for i in range(4, 8):
                wait_tok(t, p, b["bank_free"][i])
            last = None
            for h in range(NH):
                ps = b["banks"][4 + h // 4]
                last = t.matmul(ps[:, (h % 4) * 128:(h % 4 + 1) * 128],
                                KTT[:, h, :], VTT[:, h, :], start=True, stop=True,
                                skip_group_check=True)
            p.inc(last, "mm")
        mmkv = p.val("mm")

        @p.on("vector")
        def _(ve):
            v = Eng(p, ve, "vector")
            v.wait_ge(p.hnd["mm"], mmkv)
            edb = EDEC[:, :].rearrange("p (h o) -> p h o", o=1).broadcast_to([128, NH, 128])
            v.tt(SUBT[:, :, :], HST[:, :, :], edb, ALU.mult)
            for k in range(4):
                pv = b["banks"][4 + k][:, :].rearrange("p (h t) -> p h t", t=128)
                v.tt(HST[:, 4 * k:4 * (k + 1), :], SUBT[:, 4 * k:4 * (k + 1), :], pv, ALU.add)
            v.mark("ev")
        evst = p.val("ev")
        for i in range(4, 8):
            b["bank_free"][i] = ("ev", evst)

        hold6 = {}

        @p.on("scalar")
        def _(se):
            s = Eng(p, se, "scalar")
            s.wait_ge(p.hnd["ev"], evst)
            s.act(EKT[:, :, :], gposf[:, :, :], AF.Exp, scale=-1.0, bias=b["consts"][:, 2:3])
            hold6["v"] = s.mark("ac")

        @p.on("vector")
        def _(ve, sl=sl, qtc_free=qtc_free):
            v = Eng(p, ve, "vector")
            v.wait_ge(p.hnd["ac"], hold6["v"])
            v.wtok(qtc_free)
            v.tt(QTC[:, :, :], QR3[sl][:, :, :], EKT[:, :, :], ALU.mult)
            v.mark("ev")
        evqt = p.val("ev")
        semqt = f"sqt{m % 2}"

        @p.on("gpsimd")
        def _(gq, m=m, semqt=semqt):
            gq.wait_ge(p.hnd["ev"], evqt)
            p.inc(gq.dma_start(out=out_qt[:, m * D:(m + 1) * D],
                               in_=QTC[:, :, :].rearrange("p h t -> p (h t)")), semqt, 16)
        qtc_free = (semqt, p.val(semqt))

        g4 = gposf.rearrange("p h (i s) -> p h i s", s=SUB)
        s4 = SUBT.rearrange("p h (i s) -> p h i s", s=SUB)

        @p.on("vector")
        def _(ve):
            v = Eng(p, ve, "vector")
            v.wait_ge(p.hnd["ev"], evqt)
            v.ts(SUBT[:, :, 0:SUB], gposf[:, :, 0:SUB], -1.0, None, ALU.mult)
            v.tt(s4[:, :, 1:NSUB, :],
                 g4[:, :, 0:NSUB - 1, SUB - 1:SUB].broadcast_to([128, NH, NSUB - 1, SUB]),
                 g4[:, :, 1:NSUB, :], ALU.subtract)
            v.mark("ev")
        evbq = p.val("ev")

        @p.on("scalar")
        def _(se):
            s = Eng(p, se, "scalar")
            s.wait_ge(p.hnd["ev"], evbq)
            s.act(EKT[:, :, :], SUBT[:, :, :], AF.Exp, bias=b["consts"][:, 2:3])
            s.mark("ac")
        acq = p.val("ac")

        @p.on("vector")
        def _(ve, sl=sl):
            v = Eng(p, ve, "vector")
            v.wait_ge(p.hnd["ac"], acq)
            v.tt(QIBA[:, :, :], QR3[sl][:, :, :], EKT[:, :, :], ALU.mult)
            v.mark("ev")
        evqa = p.val("ev")

        prev_amm = {}
        for I in range(NSUB):
            endi = (I + 1) * SUB
            if I > 0:
                @p.on("vector")
                def _(ve, I=I, endi=endi):
                    v = Eng(p, ve, "vector")
                    v.wait_ge(p.hnd["mm"], prev_amm[I - 1])
                    gb = gposf[:, :, I * SUB - 1:I * SUB]
                    v.tt(SUBT[:, :, 0:endi], gposf[:, :, 0:endi],
                         gb.broadcast_to([128, NH, endi]), ALU.subtract)
                    v.mark("ev")
            evsub = p.val("ev")

            @p.on("scalar")
            def _(se, I=I, endi=endi):
                s = Eng(p, se, "scalar")
                if I == 0:
                    s.wait_ge(p.hnd["ev"], evqa)
                    s.act(EKT[:, :, 0:endi], gposf[:, :, 0:endi], AF.Exp)
                else:
                    s.wait_ge(p.hnd["ev"], evsub)
                    s.act(EKT[:, :, 0:endi], SUBT[:, :, 0:endi], AF.Exp)
                s.mark("ac")
            evek = p.val("ac")

            @p.on("vector")
            def _(ve, I=I, endi=endi, sl=sl):
                v = Eng(p, ve, "vector")
                v.wait_ge(p.hnd["ac"], evek)
                v.tt(KTIL[:, :, 0:endi], KR3[sl][:, :, 0:endi], EKT[:, :, 0:endi], ALU.mult)
                v.mark("ev")
            evqi = p.val("ev")

            @p.on("tensor")
            def _(t, I=I, endi=endi):
                t.wait_ge(p.hnd["ev"], evqi)
                if I == 0:
                    for i in (2, 3, 4, 5):
                        wait_tok(t, p, b["bank_free"][i])
                last = None
                for h in range(NH):
                    ab = b["banks"][2 + h // 4]
                    last = t.matmul(ab[0:endi, (h % 4) * 128 + I * SUB:(h % 4) * 128 + (I + 1) * SUB],
                                    KTIL[:, h, 0:endi], QIBA[:, h, I * SUB:(I + 1) * SUB],
                                    start=True, stop=True, skip_group_check=True)
                prev_amm[I] = p.inc(last, "mm")
        mmA = p.val("mm")

        @p.on("vector")
        def _(ve):
            v = Eng(p, ve, "vector")
            v.wait_ge(p.hnd["mm"], mmA)
            mb3 = MASKF[:, :].rearrange("p (o t) -> p o t", o=1).broadcast_to([128, 4, 128])
            for k in range(4):
                ab = b["banks"][2 + k][:, :].rearrange("p (h t) -> p h t", t=128)
                v.tt(ABM[:, 4 * k:4 * (k + 1), :], ab, mb3, ALU.mult)
            v.mark("ev")
        evab = p.val("ev")
        for i in (2, 3, 4, 5):
            b["bank_free"][i] = ("ev", evab)


        obanks = (2, 3, 6, 7)
        mmo = {}

        @p.on("tensor")
        def _(t, m=m):
            t.wait_ge(p.hnd["ev"], evab)
            for i in obanks:
                wait_tok(t, p, b["bank_free"][i])
            for h in range(NH):
                ob = b["banks"][obanks[h // 4]]
                osl = slice((h % 4) * 128, (h % 4 + 1) * 128)
                t.matmul(ob[:, osl], HBF[:, h, :], QTC[:, h, :],
                         start=True, stop=False, skip_group_check=True)
                last = t.matmul(ob[:, osl], VTT[:, h, :], ABM[:, h, :],
                                start=False, stop=True, skip_group_check=True)
                if h % 4 == 3:
                    mmo[h // 4] = p.inc(last, "mm")

        oevs = {}

        @p.on("vector")
        def _(ve, m=m):
            v = Eng(p, ve, "vector")
            for k in range(4):
                v.wait_ge(p.hnd["mm"], mmo[k])
                if k == 0:
                    v.wtok(och_free[m % 2])
                bv = b["banks"][obanks[k]][:, :].rearrange("p (h t) -> p h t", t=128)
                v.cp(OCH[:, 4 * k:4 * (k + 1), :], bv)
                oevs[k] = v.mark("ev")
        evo = p.val("ev")
        for k in range(4):
            b["bank_free"][obanks[k]] = ("ev", oevs[k])
        prev_gla_mm = p.val("mm")

        semoc = f"soc{m % 2}"

        @p.on("gpsimd")
        def _(gq, m=m, semoc=semoc):
            gq.wait_ge(p.hnd["ev"], evo)
            dst3 = out_op.rearrange("p (h t) -> p h t", t=T)[:, :, m * 128:(m + 1) * 128]
            p.inc(gq.dma_start(out=dst3, in_=OCH[:, :, :]), semoc, 16)
        och_free[m % 2] = (semoc, p.val(semoc))

        @p.on("vector")
        def _(ve, m=m):
            v = Eng(p, ve, "vector")
            v.wait_ge(p.hnd["ev"], evc)
            v.ts(GAMT[:, :, m:m + 1].rearrange("p h o -> p (h o)"), SGP[:, :], -1.0,
                 None, ALU.mult)
            v.tt(SGP[:, :], SGP[:, :], GCT[:, :], ALU.add)
            v.mark("ev")
        endchunk = p.val("ev")
        for k in ("gp", "kk", "vv", "qq"):
            ring_free[k][sl] = ("ev", endchunk)

    @p.on("vector")
    def _(ve):
        v = Eng(p, ve, "vector")
        v.ts(LOGD[:, :], SGP[:, :], -1.0, None, ALU.mult)
        v.mark("ev")

    fin = p.snap()

    @p.on("gpsimd")
    def _(g):
        p.wait(g, fin)
        p.inc(g.dma_start(out=out_st[:, :], in_=HST[:, :, :].rearrange("p h t -> p (h t)")), "st", 16)
        p.inc(g.dma_start(out=out_ld[:, :], in_=LOGD[:, :]), "st", 16)
        p.inc(g.dma_start(out=out_gm[:, :], in_=GAMT[:, :, :].rearrange("p h t -> p (h t)")), "st", 16)
        g.wait_ge(p.hnd["st"], p.val("st"))

    return p


def _l2_host(inputs, l1res):
    x = np.asarray(inputs["L0_attn_a_x"], dtype=np.float32)
    xs = _fm_slices(x)
    wqaT = np.ascontiguousarray(np.asarray(inputs["L0_attn_a_Wq"], np.float32).T)
    woaT = np.ascontiguousarray(np.asarray(inputs["L0_attn_a_Wo"], np.float32).T)
    wqb = np.ascontiguousarray(np.asarray(inputs["L0_attn_b_Wq"], np.float32))
    wfb = np.ascontiguousarray(np.asarray(inputs["L0_attn_b_Wf"], np.float32))
    wib = np.ascontiguousarray(np.asarray(inputs["L0_attn_b_Wi"], np.float32))
    ssg = np.stack([np.ascontiguousarray(l1res[j]["slicesum"]).T.reshape(D)
                    for j in range(NCORE)]).astype(np.float32)
    gate_full = np.ascontiguousarray(np.concatenate(
        [np.asarray(l1res[c]["recon_gu"])[:256] for c in range(NCORE)], 0))
    up_full = np.ascontiguousarray(np.concatenate(
        [np.asarray(l1res[c]["recon_gu"])[256:512] for c in range(NCORE)], 0))
    dn_full = np.ascontiguousarray(np.concatenate(
        [np.asarray(l1res[c]["recon_dn"]) for c in range(NCORE)], 0))
    _L23_CACHE.update(gate_full=gate_full, up_full=up_full, dn_full=dn_full)
    maskf = np.zeros((128, 128), np.float32)
    for tau in range(128):
        for t in range(128):
            if tau <= t:
                maskf[tau, t] = 1.0
    ident = np.eye(128, dtype=ml_dtypes.bfloat16)
    cst = _consts()
    in_maps = []
    for c in range(NCORE):
        mpfx = (np.arange(8) < c).astype(np.float32).reshape(8, 1)
        m = dict(x=xs[c], wqaT=wqaT, woaT=woaT, wqb=wqb, wfb=wfb, wib=wib,
                 kvt=np.ascontiguousarray(np.asarray(l1res[c]["kvt"])),
                 ssg=ssg, mpfx=mpfx, gate_w=gate_full, up_w=up_full, dn_w=dn_full,
                 ident_bf=ident, maskf=maskf, **cst)
        in_maps.append(m)
    return in_maps


def run_l2(inputs, l1res):
    in_maps = _l2_host(inputs, l1res)
    prog = build_l2()
    res = run_bass_kernel_spmd(prog.nc, in_maps, core_ids=list(range(NCORE)))
    return res.results



def build_l3():
    p = Prog()
    for s in ("lx", "lxr0", "lxr1", "sx3a", "sx3b", "lld", "lsst0", "lsst1",
              "lqt0", "lqt1", "lgmt", "lop"):
        p.sem(s)
    x3_in = p.din("x3", [128, CH * T], F32)
    op_in = p.din("opart", [128, CH * T], BF16)
    qt_in = p.din("qtil", [128, NCHK * D], BF16)
    sall_in = p.din("sall", [8 * 128, NH * 128], F32)
    ldall_in = p.din("ldall", [8, D], F32)
    wscanT_in = p.din("wscanT", [8, 8], F32)
    biarep_in = p.din("biarep", [128, 128], F32)
    gamlog_in = p.din("gamlog", [128, NH * NCHK], F32)
    gw_in = p.din("gw", [128, NH], F32)
    wob_in = p.din("wob", [D, D], F32)
    gate_w = p.din("gate_w", [D, INTER], BF16)
    up_w = p.din("up_w", [D, INTER], BF16)
    dn_w = p.din("dn_w", [INTER, D], BF16)
    ones_bf_in = p.din("ones_bf_in", [128, 1], BF16)
    ones_row_f_in = p.din("ones_row_f_in", [1, 128], F32)

    out_y = p.dout("y", [128, CH * T], F32)

    sc_x = p.dtmp("sc_x", [128, CH * T], F32)

    ARE = p.sb("ARE", [128, 49152], BF16)
    X = ARE[:, 0:32768].bitcast(F32)
    XLN = ARE[:, 32768:49152]
    Hbuf = ARE[:, 0:32768]
    SCR = p.sb("SCR", [128, 16384], BF16)
    OBUF = SCR[:, :]
    HIN = XLN[:, 0:4096].bitcast(F32)
    LDA = XLN[:, 4096:8192].bitcast(F32)[0:8, 0:2048]
    SSTR = [XLN[:, 8192 + i * 4096: 8192 + (i + 1) * 4096].bitcast(F32) for i in range(2)]
    QTR = [XLN[:, 8192 + i * 2048: 8192 + (i + 1) * 2048] for i in range(2)]
    XR = [p.sb(f"xr{i}", [128, HW], F32) for i in range(2)]
    XOT = [p.sb(f"xot{i}", [128, HW], F32) for i in range(2)]
    FE4 = p.sb("FE4", [128, 4 * HW], F32)
    WSC = p.sb("WSC", [8, 8], F32)
    BIA = p.sb("BIA", [128, 128], F32)
    GMT = p.sb("GMT", [128, 128], F32)
    GW = p.sb("GW", [128, NH], F32)
    COEF = p.sb("COEF", [128, 128], F32)
    HCB = p.sb("HCB", [128, 128], BF16)
    EG = p.sb("EG", [128, 2 * NH], F32)

    p.start_block()
    b = make_shared(p)
    fill_consts(p, b)

    load_consts(p, b, [
        (b["ones_bf"][:, :], ones_bf_in[:, :]),
        (b["ones_row_f"][:, :], ones_row_f_in[:, :]),
        (WSC[:, :], wscanT_in[:, :]),
        (BIA[:, :], biarep_in[:, :]),
        (GMT[:, :], gamlog_in[:, :]),
        (GW[:, :], gw_in[:, :]),
    ])

    @p.on("sync")
    def _(sync):
        for ch in range(CH):
            p.inc(sync.dma_start(out=X[:, chs(ch)], in_=x3_in[:, chs(ch)]), "lx", 16)
        p.inc(sync.dma_start(out=LDA, in_=ldall_in[:, :]), "lld", 16)

    @p.on("gpsimd")
    def _(g):
        p.inc(g.dma_start(out=OBUF[:, :], in_=op_in[:, :]), "lop", 16)
    lop_done = p.val("lop")

    lld = p.val("lld")
    lxv = p.val("lx")

    @p.on("tensor")
    def _(t):
        t.wait_ge(p.hnd["lld"], lld)
        t.wait_ge(p.hnd["lx"], lxv)
        p.wait(t, {"ev": p.val("ev"), "ac": p.val("ac")})
        last = None
        for h in range(NH):
            last = t.matmul(b["banks"][0][:, h * 8:(h + 1) * 8],
                            LDA[0:8, h * 128:(h + 1) * 128], WSC[0:8, 0:8],
                            start=True, stop=True, skip_group_check=True)
        p.inc(last, "mm")
    mme = p.val("mm")

    hold = {}

    @p.on("vector")
    def _(ve):
        v = Eng(p, ve, "vector")
        v.wait_ge(p.hnd["mm"], mme)
        v.tt(COEF[:, :], b["banks"][0][:, 0:128], BIA[:, :], ALU.add)
        hold["v"] = v.mark("ev")
    b["bank_free"][0] = ("ev", p.val("ev"))

    @p.on("scalar")
    def _(se):
        s = Eng(p, se, "scalar")
        s.wait_ge(p.hnd["ev"], hold["v"])
        s.act(COEF[:, :], COEF[:, :], AF.Exp)
        s.mark("ac")
    ace = p.val("ac")

    TMPH = XLN[:, 4096:8192].bitcast(F32)
    HCB16 = [XLN[:, 4096:6144], XLN[:, 12288:14336]]
    sst_free = [None, None]
    for j in range(NCORE):
        sl = j % 2

        @p.on("sync")
        def _(sync, j=j, sl=sl):
            wait_tok(sync, p, sst_free[sl])
            p.inc(sync.dma_start(out=SSTR[sl][:, :],
                                 in_=sall_in[j * 128:(j + 1) * 128, :]), f"lsst{sl}", 16)
        lss = p.val(f"lsst{sl}")

        @p.on("vector")
        def _(ve, j=j, sl=sl, lss=lss):
            v = Eng(p, ve, "vector")
            v.wait_ge(p.hnd[f"lsst{sl}"], lss)
            if j == 0:
                v.wait_ge(p.hnd["ac"], ace)
            sv = SSTR[sl].rearrange("p (h t) -> p h t", t=128)
            cj = COEF.rearrange("p (h j) -> p h j", j=8)[:, :, j:j + 1] \
                .broadcast_to([128, NH, 128])
            tmp = TMPH.rearrange("p (h t) -> p h t", t=128)
            hv = HIN.rearrange("p (h t) -> p h t", t=128)
            if j == 0:
                v.tt(hv, sv, cj, ALU.mult)
            else:
                v.tt(tmp, sv, cj, ALU.mult)
                v.tt(hv, hv, tmp, ALU.add)
            v.mark("ev")
        sst_free[sl] = ("ev", p.val("ev"))
    hin_done = p.val("ev")

    qtr_free = [None, None]
    hcb_free = [None, None]
    eg_free = [None, None]
    OB3 = OBUF.rearrange("p (h t) -> p h t", t=T)
    for m in range(NCHK):
        sl = m % 2

        @p.on("sync")
        def _(sync, m=m, sl=sl):
            if m <= 1:
                p.wait(sync, {"ev": hin_done})
            wait_tok(sync, p, qtr_free[sl])
            p.inc(sync.dma_start(out=QTR[sl][:, :], in_=qt_in[:, m * D:(m + 1) * D]),
                  f"lqt{sl}", 16)
        lqt = p.val(f"lqt{sl}")

        holdg = {}

        @p.on("scalar")
        def _(se, m=m, sl=sl):
            s = Eng(p, se, "scalar")
            if m == 0:
                s.wait_ge(p.hnd["ev"], hin_done)
            s.wtok(eg_free[sl])
            g3 = GMT[:, :].rearrange("p (h mm) -> p h mm", mm=NCHK)
            s.act(EG[:, sl * NH:(sl + 1) * NH].rearrange("p (h o) -> p h o", o=1),
                  g3[:, :, m:m + 1], AF.Exp)
            holdg["v"] = s.mark("ac")

        holdh = {}

        @p.on("vector")
        def _(ve, m=m, sl=sl, holdg=holdg):
            v = Eng(p, ve, "vector")
            v.wait_ge(p.hnd["ac"], holdg["v"])
            if m == 0:
                v.wait_ge(p.hnd["lop"], lop_done)
            v.wtok(hcb_free[sl])
            egb = EG[:, sl * NH:(sl + 1) * NH] \
                .rearrange("p (h o) -> p h o", o=1).broadcast_to([128, NH, 128])
            v.tt(HCB16[sl].rearrange("p (h t) -> p h t", t=128),
                 HIN.rearrange("p (h t) -> p h t", t=128), egb, ALU.mult)
            holdh["v"] = v.mark("ev")
        eg_free[sl] = ("ev", p.val("ev"))

        bank0 = (m % 2) * 4
        mmc = {}

        @p.on("tensor")
        def _(t, m=m, sl=sl, holdh=holdh, lqt=lqt, bank0=bank0):
            t.wait_ge(p.hnd["ev"], holdh["v"])
            t.wait_ge(p.hnd["lqt" + str(sl)], lqt)
            for i in range(bank0, bank0 + 4):
                wait_tok(t, p, b["bank_free"][i])
            hcv = HCB16[sl].rearrange("p (h t) -> p h t", t=128)
            for h in range(NH):
                ps = b["banks"][bank0 + h // 4]
                last = t.matmul(ps[:, (h % 4) * 128:(h % 4 + 1) * 128],
                                hcv[:, h, :], QTR[sl][:, h * 128:(h + 1) * 128],
                                start=True, stop=True, skip_group_check=True)
                if h % 4 == 3:
                    mmc[h // 4] = p.inc(last, "mm")
        hcb_free[sl] = ("mm", mmc[3])
        qtr_free[sl] = ("mm", mmc[3])

        evcs = {}

        @p.on("vector")
        def _(ve, m=m, bank0=bank0):
            v = Eng(p, ve, "vector")
            for k in range(4):
                v.wait_ge(p.hnd["mm"], mmc[k])
                bv = b["banks"][bank0 + k].rearrange("p (h t) -> p h t", t=128)
                ob = OB3[:, 4 * k:4 * (k + 1), m * 128:(m + 1) * 128]
                v.tt(ob, ob, bv, ALU.add)
                evcs[k] = v.mark("ev")
        for k in range(4):
            b["bank_free"][bank0 + k] = ("ev", evcs[k])

    corr_done = p.val("ev")
    ps_sq = [b["banks"][0], b["banks"][1]]
    mm_by_h = {}
    for h in range(NH):
        holds = {}

        @p.on("vector")
        def _(ve, h=h):
            v = Eng(p, ve, "vector")
            if h == 0:
                v.wait_ge(p.hnd["ev"], corr_done)
            if h >= 1:
                v.wait_ge(p.hnd["mm"], mm_by_h[h - 1])
            v.tt(b["xbf"][0][:, :], OBUF[:, chs(h)], OBUF[:, chs(h)], ALU.mult)
            holds["v"] = v.mark("ev")

        @p.on("tensor")
        def _(t, h=h):
            if h == 0:
                for i in (0, 1):
                    wait_tok(t, p, b["bank_free"][i])
            t.wait_ge(p.hnd["ev"], holds["v"])
            last = None
            for hh in (0, 1):
                last = t.matmul(ps_sq[hh][0:1, :], b["ones_bf"][:, :],
                                b["xbf"][0][:, hsl(hh)],
                                start=(h == 0), stop=(h == NH - 1))
            mm_by_h[h] = p.inc(last, "mm")
    mm_sq = p.val("mm")

    hold2 = {}

    @p.on("vector")
    def _(ve):
        v = Eng(p, ve, "vector")
        v.wait_ge(p.hnd["mm"], mm_sq)
        for hh in (0, 1):
            v.cp(row(b, 0)[:, hsl(hh)], ps_sq[hh][0:1, :])
        v.ts(row(b, 0), row(b, 0), 1.0 / D, None, ALU.mult)
        hold2["v"] = v.mark("ev")

    hold3 = {}

    @p.on("scalar")
    def _(se):
        s = Eng(p, se, "scalar")
        s.wait_ge(p.hnd["ev"], hold2["v"])
        s.act(row(b, 2), row(b, 0), AF.Ln, bias=b["consts"][0:1, 0:1])
        s.act(row(b, 1), row(b, 2), AF.Exp, scale=-0.5)
        hold3["v"] = s.mark("ac")

    @p.on("tensor")
    def _(t):
        t.wait_ge(p.hnd["ac"], hold3["v"])
        for i in (2, 3):
            wait_tok(t, p, b["bank_free"][i])
        last = None
        for hh in (0, 1):
            last = t.matmul(b["banks"][2 + hh][:, :], b["ones_row_f"][:, :],
                            row(b, 1)[0:1, hsl(hh)], start=True, stop=True)
        p.inc(last, "mm")
    mmrb = p.val("mm")

    @p.on("vector")
    def _(ve):
        v = Eng(p, ve, "vector")
        v.wait_ge(p.hnd["mm"], mmrb)
        for hh in (0, 1):
            v.cp(b["rb"][:, hsl(hh)], b["banks"][2 + hh][:, :])
        for h in range(NH):
            for hh in (0, 1):
                v.ts(b["aptmp"][:, :], OBUF[:, tok(h, hh)], GW[:, h:h + 1], None, ALU.mult)
                v.tt(OBUF[:, tok(h, hh)], b["aptmp"][:, :], b["rb"][:, hsl(hh)], ALU.mult)
        v.mark("ev")
    nrm_done = p.val("ev")
    for i in range(8):
        b["bank_free"][i] = ("ev", nrm_done)

    for g in range(4):
        ldo, stripo = load_strip(p, b, g % 2, wob_in, g * 512, CH, q="gpsimd")
        mmo = proj_group(p, b, stripo, ldo, OBUF, CH, start=True, stop=True)
        b["strip_free"][g % 2] = ("mm", mmo[(3, 1)])

        @p.on("vector")
        def _(ve, g=g, mmo=mmo):
            v = Eng(p, ve, "vector")
            for oi in range(4):
                oc = g * 4 + oi
                for hh in (0, 1):
                    v.wait_ge(p.hnd["mm"], mmo[(oi, hh)])
                    v.tt(X[:, tok(oc, hh)], b["banks"][oi * 2 + hh][:, :],
                         X[:, tok(oc, hh)], ALU.add)
            v.mark("ev")
        for i in range(8):
            b["bank_free"][i] = ("ev", p.val("ev"))

    emit_ln(p, b, X, XLN, "ln4")

    emit_ffn(p, b, X, XLN, Hbuf, gate_w, up_w, dn_w, sc_x, out_y, XR, XOT, FE4)

    fin = p.snap()

    @p.on("gpsimd")
    def _(g):
        p.wait(g, fin)

    return p


def _l3_host(inputs, l2res):
    gate_full = _L23_CACHE["gate_full"]
    up_full = _L23_CACHE["up_full"]
    dn_full = _L23_CACHE["dn_full"]
    ldall = np.stack([np.ascontiguousarray(np.asarray(l2res[j]["logd"])).T.reshape(D)
                      for j in range(NCORE)]).astype(np.float32)
    sall = np.ascontiguousarray(np.concatenate(
        [np.asarray(l2res[j]["sstate"]) for j in range(NCORE)], 0))
    gwv = np.asarray(inputs["L0_attn_b_g_weight"], np.float32)
    gw = np.ascontiguousarray(gwv.reshape(NH, 128).T)
    wob = np.ascontiguousarray(np.asarray(inputs["L0_attn_b_Wo"], np.float32))
    cst = _consts()
    in_maps = []
    for c in range(NCORE):
        wscanT = np.zeros((8, 8), np.float32)
        for j in range(8):
            for i in range(8):
                if j < i < c:
                    wscanT[i, j] = 1.0
        bia = np.where(np.arange(8) < c, 0.0, -1e30).astype(np.float32)
        biarep = np.tile(bia, 16)[None, :].repeat(128, 0)
        r = l2res[c]
        m = dict(x3=np.ascontiguousarray(np.asarray(r["x3"])),
                 opart=np.ascontiguousarray(np.asarray(r["opart"])),
                 qtil=np.ascontiguousarray(np.asarray(r["qtil"])),
                 sall=sall, ldall=ldall, wscanT=wscanT, biarep=biarep,
                 gamlog=np.ascontiguousarray(np.asarray(r["gamlog"])),
                 gw=gw, wob=wob, gate_w=gate_full, up_w=up_full, dn_w=dn_full,
                 **cst)
        in_maps.append(m)
    return in_maps


def run_l3(inputs, l2res):
    in_maps = _l3_host(inputs, l2res)
    prog = build_l3()
    res = run_bass_kernel_spmd(prog.nc, in_maps, core_ids=list(range(NCORE)))
    return res.results


def kernel(**inputs):
    l1res = run_l1(inputs)
    l2res = run_l2(inputs, l1res)
    l3res = run_l3(inputs, l2res)
    outs = []
    for c in range(NCORE):
        y = np.asarray(l3res[c]["y"], np.float32)
        outs.append(y.reshape(128, CH, T).transpose(2, 1, 0).reshape(T, D))
    return np.concatenate(outs, 0).reshape(1, SEQ, D).astype(np.float32)
